# version 05
# baseline (speedup 1.0000x reference)
import contextlib
from collections import defaultdict

import numpy as np
import ml_dtypes

import concourse.bass as bass
import concourse.mybir as mybir
from concourse.bass_utils import run_bass_kernel_spmd

F32 = mybir.dt.float32
BF16 = mybir.dt.bfloat16
U8 = mybir.dt.uint8
AF = mybir.ActivationFunctionType
ALU = mybir.AluOpType
AX = mybir.AxisListType
NP_BF16 = ml_dtypes.bfloat16

NC = 8
DM = 4096
DIN = 8192
NST = 16
TOK = 8192
TSH = TOK // NC
LSEQ = 4096
DSH = DIN // NC


class Ev:
    __slots__ = ("sem_key", "val")

    def __init__(self, sem_key, val):
        self.sem_key = sem_key
        self.val = val


class Prog:

    AUTO = {"vector", "scalar", "gpsimd", "tensor"}

    def __init__(self, nc):
        self.nc = nc
        self.items = defaultdict(list)
        self.sem_count = defaultdict(int)
        self.sem_keys = set()
        self.last = {}
        self.floor = []
        self._pid = {}

    def barrier(self):
        self.floor = [Ev(k, v) for k, v in self.sem_count.items()]

    def pid(self, e):
        key = id(e)
        if key not in self._pid:
            self._pid[key] = e.partition_id()
        return self._pid[key]

    def _rec(self, engine, fn, deps, sem_key, inc, chain):
        deps = [d for d in (deps or []) if d is not None] + self.floor
        if chain and engine in self.AUTO and engine in self.last:
            deps.append(self.last[engine])
        self.sem_keys.add(sem_key)
        self.sem_count[sem_key] += inc
        ev = Ev(sem_key, self.sem_count[sem_key])
        self.items[engine].append((fn, deps, sem_key, inc))
        if engine in self.AUTO:
            self.last[engine] = ev
        return ev

    def op(self, engine, fn, deps=None, chain=True):
        return self._rec(engine, fn, deps, "c_" + engine, 1, chain)

    def dma(self, queue, fn, deps=None, slot=None):
        key = "d_" + queue + ("" if slot is None else "_" + str(slot))
        return self._rec(queue, fn, deps, key, 16, False)

    def cc(self, fn, deps=None):
        return self._rec("gpsimd", fn, deps, "cc", 1, True)

    def build(self, final_engine="sync"):
        nc = self.nc
        ctx = contextlib.ExitStack()
        sems = {k: ctx.enter_context(nc.semaphore(k)) for k in sorted(self.sem_keys)}
        finals = sorted(self.sem_count.items())

        def emit_stream(eng, engine_name):
            hw = defaultdict(int)
            for (fn, deps, sem_key, inc) in self.items[engine_name]:
                need = {}
                for d in deps:
                    if d.val > need.get(d.sem_key, 0):
                        need[d.sem_key] = d.val
                for k, v in sorted(need.items()):
                    if v > hw[k]:
                        eng.wait_ge(sems[k], v)
                        hw[k] = v
                inst = fn(eng)
                if inst is None:
                    raise RuntimeError(f"op on {engine_name} returned None")
                inst.then_inc(sems[sem_key], inc)

        with ctx, nc.Block() as block:
            for engine_name in ["sync", "vector", "scalar", "tensor", "gpsimd"]:
                if not self.items[engine_name] and engine_name != final_engine:
                    continue
                dec = getattr(block, engine_name)

                def mk(en):
                    def _f(eng):
                        emit_stream(eng, en)
                        if en == final_engine:
                            for k, v in finals:
                                eng.wait_ge(sems[k], v)
                    return _f
                dec(mk(engine_name))
        return nc


def _mk():
    return bass.Bass("TRN2", target_bir_lowering=False)


def _t(a):
    return np.ascontiguousarray(a.T)


def _run(nc, in_maps):
    return run_bass_kernel_spmd(nc, in_maps, core_ids=list(range(NC))).results


IDENT_BF16 = np.eye(128, dtype=NP_BF16)
IDENT_F32 = np.eye(128, dtype=np.float32)


def emit_l0_prep(p, io, pref="P1_"):
    nc = p.nc
    x, ident, xnT = io["x"], io["ident"], io["xnT"]
    ctx = contextlib.ExitStack()
    sb = lambda n, s, d=F32: ctx.enter_context(nc.sbuf_tensor(pref + n, s, d))
    ps = lambda n, s, d=F32: ctx.enter_context(nc.psum_tensor(pref + n, s, d))
    NT = TSH // 128
    xt = [sb(f"xt{i}", [128, DM]) for i in range(2)]
    sq = sb("sq", [128, DM])
    stat = sb("stat", [128, 8])
    xnt = [sb(f"xnt{i}", [128, DM], BF16) for i in range(2)]
    sid = sb("sid", [128, 128], BF16)
    ptr = [ps(f"ptr{i}", [128, 1024], BF16) for i in range(2)]
    ot = [sb(f"ot{i}", [128, 128], BF16) for i in range(4)]
    scst = sb("scst", [128, 2])

    if True:
        eid = p.dma("sync", lambda e: e.dma_start(out=sid[:], in_=ident), slot="id")
        ecst = p.op("vector", lambda e: e.memset(scst[:, 0:1], 1e-5))
        cons = [None, None]
        xn_cons = [None, None]
        tr_cons = [None, None]
        st_ev = [None] * 4
        for i in range(NT):
            sl = i % 2
            xb, xnb = xt[sl], xnt[sl]
            eld = p.dma("sync", lambda e, xb=xb, i=i: e.dma_start(out=xb[:], in_=x[i * 128:(i + 1) * 128, :]),
                        deps=[cons[sl]], slot=f"x{sl}")
            es1 = p.op("vector", lambda e, xb=xb: e.tensor_reduce(out=stat[:, 0:1], in_=xb[:], axis=AX.X, op=ALU.add), [eld])
            esq = p.op("scalar", lambda e, xb=xb: e.activation(out=sq[:], in_=xb[:], func=AF.Square, accum_out=stat[:, 1:2]), [eld, es1])
            em = p.op("vector", lambda e: e.tensor_scalar(out=stat[:, 2:3], in0=stat[:, 0:1], scalar1=1.0 / DM, scalar2=None, op0=ALU.mult), [esq])
            em2 = p.op("vector", lambda e: e.tensor_scalar(out=stat[:, 3:4], in0=stat[:, 1:2], scalar1=1.0 / DM, scalar2=None, op0=ALU.mult), [esq])
            emm = p.op("vector", lambda e: e.tensor_tensor(out=stat[:, 4:5], in0=stat[:, 2:3], in1=stat[:, 2:3], op=ALU.mult))
            evar = p.op("vector", lambda e: e.tensor_tensor(out=stat[:, 5:6], in0=stat[:, 3:4], in1=stat[:, 4:5], op=ALU.subtract))
            esd = p.op("scalar", lambda e: e.activation(out=stat[:, 6:7], in_=stat[:, 5:6], func=AF.Sqrt, bias=scst[:, 0:1]), [evar, ecst])
            ers = p.op("vector", lambda e: e.reciprocal(out=stat[:, 7:8], in_=stat[:, 6:7]), [esd])
            exn = p.op("vector", lambda e, xb=xb, xnb=xnb: e.tensor_scalar(out=xnb[:], in0=xb[:], scalar1=stat[:, 2:3], scalar2=stat[:, 7:8], op0=ALU.subtract, op1=ALU.mult), [xn_cons[sl]])
            cons[sl] = exn
            last_tr = None
            for j in range(DM // 128):
                pj = j % 2
                etr = p.op("tensor", lambda e, j=j, pj=pj, xnb=xnb: e.transpose(out=ptr[pj][:, 0:128], in_=xnb[:, j * 128:(j + 1) * 128], identity=sid[:]),
                           [exn, eid, tr_cons[pj]])
                k = (i * 32 + j) % 4
                eev = p.op("scalar", lambda e, k=k, pj=pj: e.copy(out=ot[k][:], in_=ptr[pj][:, 0:128]), [etr, st_ev[k]])
                tr_cons[pj] = eev
                st_ev[k] = p.dma("scalar", lambda e, k=k, i=i, j=j: e.dma_start(out=xnT[j * 128:(j + 1) * 128, i * 128:(i + 1) * 128], in_=ot[k][:]),
                                 deps=[eev], slot=f"st{k}")
                last_tr = eev
            xn_cons[sl] = last_tr
    ctx.close()


def emit_casts(p, pairs, pref="PC_"):
    nc = p.nc
    ctx = contextlib.ExitStack()
    wmax = max(c for (_, _, _, c) in pairs)
    cb = [ctx.enter_context(nc.sbuf_tensor(pref + f"cb{i}", [128, wmax], BF16)) for i in range(2)]
    ld = [None, None]
    st = [None, None]
    k = 0
    for (iap, oap, rows, cols) in pairs:
        for r in range(0, rows, 128):
            rsz = min(128, rows - r)
            sl = k % 2
            eld = p.dma("gpsimd", lambda e, sl=sl, rsz=rsz, r=r, iap=iap, cols=cols: e.dma_start(
                out=cb[sl][0:rsz, 0:cols], in_=iap[r:r + rsz, :]), deps=[st[sl]], slot=f"ci{sl}")
            ld[sl] = eld
            st[sl] = p.dma("scalar", lambda e, sl=sl, rsz=rsz, r=r, oap=oap, cols=cols: e.dma_start(
                out=oap[r:r + rsz, :], in_=cb[sl][0:rsz, 0:cols]), deps=[eld], slot=f"co{sl}")
            k += 1
    ctx.close()


CAST_SPECS = [
    ("woT", DIN // NC, DM),
    ("woT2", DM // NC, DM),
    ("kern", DM // NC, DM),
    ("wqT", DM // NC, 6144),
    ("utT", 4032 // NC, DM),
    ("kw1T", 2048 // NC, 2048),
    ("vw1T", 2048 // NC, 2048),
]


def build_l0_prep():
    nc = _mk()
    io = {
        "x": nc.dram_tensor("x", [TSH, DM], F32, kind="ExternalInput").ap(),
        "ident": nc.dram_tensor("ident", [128, 128], BF16, kind="ExternalInput").ap(),
        "xnT": nc.dram_tensor("xnT", [DM, TSH], BF16, kind="ExternalOutput").ap(),
    }
    pairs = []
    for nm, rows, cols in CAST_SPECS:
        iap = nc.dram_tensor("ci_" + nm, [rows, cols], F32, kind="ExternalInput").ap()
        oap = nc.dram_tensor("co_" + nm, [rows, cols], BF16, kind="ExternalOutput").ap()
        pairs.append((iap, oap, rows, cols))
    p = Prog(nc)
    emit_l0_prep(p, io)
    p.barrier()
    emit_casts(p, pairs)
    p.build()
    return nc


def host_prep_inputs(inputs):
    x = np.asarray(inputs["L0_attn_hidden_states"]).reshape(TOK, DM)
    return x


def emit_l0_inproj(p, io, pref="P2_"):
    nc = p.nc
    xnT, w_x, w_z, cw, cb, wxpT = io["xnT"], io["w_x"], io["w_z"], io["cw"], io["cb"], io["wxpT"]
    xc, zs = io["xc"], io["zs"]
    ssp_store_ap = io["ssp_store_ap"]
    ctx = contextlib.ExitStack()
    sb = lambda n, s, d=F32: ctx.enter_context(nc.sbuf_tensor(pref + n, s, d))
    ps = lambda n, s, d=F32: ctx.enter_context(nc.psum_tensor(pref + n, s, d))
    NTC = 16
    TC = 512
    w_sb = sb("w_sb", [128, 32 * 1024], BF16)
    wxp_sb = sb("wxp_sb", [128, 8 * 288], BF16)
    cw_sb = sb("cw_sb", [128, 8 * 4])
    cb_sb = sb("cb_sb", [128, 8])
    xk = [sb(f"xk{i}", [128, 32 * TC], BF16) for i in range(2)]
    xcp = [sb(f"xcp{i}", [128, TC + 3]) for i in range(8)]
    cs1 = sb("cs1", [128, TC])
    cs2 = sb("cs2", [128, TC])
    xcs = [sb(f"xcs{i}", [128, TC], BF16) for i in range(8)]
    zst = [sb(f"zst{i}", [128, TC], BF16) for i in range(2)]
    ssps = [sb(f"ssps{i}", [128, TC]) for i in range(3)]
    pb = [ps(f"pb{i}", [128, 512]) for i in range(4)]

    if True:
        ewx = p.dma("gpsimd", lambda e: e.dma_start(
            out=w_sb[:].rearrange("p (ht e) -> p ht e", ht=32), in_=w_x.rearrange("(ht p) e -> p ht e", p=128)), slot="w")
        ewxp = p.dma("gpsimd", lambda e: e.dma_start(
            out=wxp_sb[:].rearrange("p (dt e) -> p dt e", dt=8), in_=wxpT.rearrange("(dt p) e -> p dt e", p=128)), slot="wxp")
        ecw = p.dma("sync", lambda e: e.dma_start(
            out=cw_sb[:].rearrange("p (dt k) -> p dt k", dt=8), in_=cw.rearrange("(dt p) k -> p dt k", p=128)), slot="cw")
        ecb = p.dma("sync", lambda e: e.dma_start(
            out=cb_sb[:], in_=cb.rearrange("(dt p) -> p dt", p=128), allow_slow_non_contiguous=True), slot="cb")
        einit = []
        for et in range(8):
            einit.append(p.op("vector", lambda e, et=et: e.memset(xcp[et][:, 0:3], 0.0)))
        xk_cons = [None, None]
        bank_cons = [None] * 4
        xcs_cons = [None] * 8
        zst_cons = [None, None]
        ssp_store = [None] * 3
        xc_store = [None] * 8

        def xn_load(tc, sl, deps):
            evs = []
            for q in range(4):
                evs.append(p.dma("sync", lambda e, tc=tc, sl=sl, q=q: e.dma_start(
                    out=xk[sl][:].rearrange("p (ht t) -> p ht t", ht=32)[:, q * 8:(q + 1) * 8, :],
                    in_=xnT.rearrange("(ht p) t -> p ht t", p=128)[:, q * 8:(q + 1) * 8, tc * TC:(tc + 1) * TC]),
                    deps=deps, slot=f"xk{sl}{q}"))
            return evs

        for tc in range(NTC):
            sl = tc % 2
            elds = xn_load(tc, sl, [xk_cons[sl]])
            last_mm_of_chunk = None
            chunk_sil = []
            for et in range(8):
                bk = et % 4
                mm = None
                for ht in range(32):
                    mm = p.op("tensor", lambda e, ht=ht, et=et, sl=sl, bk=bk: e.matmul(
                        out=pb[bk][:],
                        lhsT=w_sb[:, ht * 1024 + et * 128: ht * 1024 + et * 128 + 128],
                        rhs=xk[sl][:, ht * TC:(ht + 1) * TC],
                        start=(ht == 0), stop=(ht == 31)),
                        deps=([ewx, bank_cons[bk]] + elds) if ht == 0 else None, chain=(ht == 0))
                last_mm_of_chunk = mm
                eev = p.op("scalar", lambda e, et=et, bk=bk: e.copy(out=xcp[et][:, 3:3 + TC], in_=pb[bk][:]), [mm])
                bank_cons[bk] = eev
                e1 = p.op("vector", lambda e, et=et: e.tensor_scalar(
                    out=cs1[:], in0=xcp[et][:, 3:3 + TC], scalar1=cw_sb[:, et * 4 + 3:et * 4 + 4],
                    scalar2=cb_sb[:, et:et + 1], op0=ALU.mult, op1=ALU.add), [eev, ecw, ecb])
                e2 = p.op("vector", lambda e, et=et: e.scalar_tensor_tensor(
                    out=cs2[:], in0=xcp[et][:, 2:2 + TC], scalar=cw_sb[:, et * 4 + 2:et * 4 + 3],
                    in1=cs1[:], op0=ALU.mult, op1=ALU.add))
                e3 = p.op("vector", lambda e, et=et: e.scalar_tensor_tensor(
                    out=cs1[:], in0=xcp[et][:, 1:1 + TC], scalar=cw_sb[:, et * 4 + 1:et * 4 + 2],
                    in1=cs2[:], op0=ALU.mult, op1=ALU.add))
                e4 = p.op("vector", lambda e, et=et: e.scalar_tensor_tensor(
                    out=cs2[:], in0=xcp[et][:, 0:0 + TC], scalar=cw_sb[:, et * 4 + 0:et * 4 + 1],
                    in1=cs1[:], op0=ALU.mult, op1=ALU.add))
                if tc == NTC // 2 - 1:
                    etl = p.op("vector", lambda e, et=et: e.memset(xcp[et][:, 0:3], 0.0), [e4])
                else:
                    etl = p.op("vector", lambda e, et=et: e.tensor_copy(out=xcp[et][:, 0:3], in_=xcp[et][:, TC:TC + 3]), [e4])
                esl = p.op("scalar", lambda e, et=et: e.activation(out=xcs[et][:], in_=cs2[:], func=AF.Silu), [e4, xcs_cons[et]])
                chunk_sil.append(esl)
                xc_store[et] = p.dma("scalar", lambda e, et=et, tc=tc: e.dma_start(
                    out=xc[et * 128:(et + 1) * 128, tc * TC:(tc + 1) * TC], in_=xcs[et][:]), deps=[esl], slot=f"xc{et}")
            for mt, msz in ((0, 128), (1, 128), (2, 32)):
                bk = mt % 4
                mm = None
                for kt in range(8):
                    mm = p.op("tensor", lambda e, mt=mt, msz=msz, kt=kt, bk=bk: e.matmul(
                        out=pb[bk][0:msz, :],
                        lhsT=wxp_sb[:, kt * 288 + mt * 128: kt * 288 + mt * 128 + msz],
                        rhs=xcs[kt][:],
                        start=(kt == 0), stop=(kt == 7)),
                        deps=([ewxp, bank_cons[bk], last_mm_of_chunk] + chunk_sil) if kt == 0 else None,
                        chain=(kt == 0))
                eev = p.op("scalar", lambda e, mt=mt, msz=msz, bk=bk: e.copy(out=ssps[mt][0:msz, :], in_=pb[bk][0:msz, :]), [mm, ssp_store[mt]])
                bank_cons[bk] = eev
                ssp_store[mt] = p.dma("scalar", lambda e, mt=mt, msz=msz, tc=tc: e.dma_start(
                    out=ssp_store_ap(mt, msz, tc), in_=ssps[mt][0:msz, :]), deps=[eev], slot=f"sp{mt}")
            for et in range(8):
                xcs_cons[et] = Ev("c_tensor", p.sem_count["c_tensor"])
            xk_cons[sl] = Ev("c_tensor", p.sem_count["c_tensor"])
        ewz = p.dma("gpsimd", lambda e: e.dma_start(
            out=w_sb[:].rearrange("p (ht e) -> p ht e", ht=32), in_=w_z.rearrange("(ht p) e -> p ht e", p=128)),
            deps=[Ev("c_tensor", p.sem_count["c_tensor"])], slot="w")
        for tc in range(NTC):
            sl = tc % 2
            elds = xn_load(tc, sl, [xk_cons[sl]])
            for et in range(8):
                bk = et % 4
                mm = None
                for ht in range(32):
                    mm = p.op("tensor", lambda e, ht=ht, et=et, sl=sl, bk=bk: e.matmul(
                        out=pb[bk][:],
                        lhsT=w_sb[:, ht * 1024 + et * 128: ht * 1024 + et * 128 + 128],
                        rhs=xk[sl][:, ht * TC:(ht + 1) * TC],
                        start=(ht == 0), stop=(ht == 31)),
                        deps=([ewz, bank_cons[bk]] + elds) if ht == 0 else None, chain=(ht == 0))
                zsl = et % 2
                esl = p.op("scalar", lambda e, zsl=zsl, bk=bk: e.activation(out=zst[zsl][:], in_=pb[bk][:], func=AF.Silu), [mm, zst_cons[zsl]])
                bank_cons[bk] = esl
                zst_cons[zsl] = p.dma("scalar", lambda e, zsl=zsl, et=et, tc=tc: e.dma_start(
                    out=zs[et * 128:(et + 1) * 128, tc * TC:(tc + 1) * TC], in_=zst[zsl][:]), deps=[esl], slot=f"zs{zsl}")
            xk_cons[sl] = Ev("c_tensor", p.sem_count["c_tensor"])
    ctx.close()


def build_l0_inproj():
    nc = _mk()
    io = {
        "xnT": nc.dram_tensor("xnT", [DM, TOK], BF16, kind="ExternalInput").ap(),
        "w_x": nc.dram_tensor("w_x", [DM, DSH], F32, kind="ExternalInput").ap(),
        "w_z": nc.dram_tensor("w_z", [DM, DSH], F32, kind="ExternalInput").ap(),
        "cw": nc.dram_tensor("cw", [DSH, 4], F32, kind="ExternalInput").ap(),
        "cb": nc.dram_tensor("cb", [DSH], F32, kind="ExternalInput").ap(),
        "wxpT": nc.dram_tensor("wxpT", [DSH, 288], F32, kind="ExternalInput").ap(),
        "xc": nc.dram_tensor("xc", [DSH, TOK], BF16, kind="ExternalOutput").ap(),
        "zs": nc.dram_tensor("zs", [DSH, TOK], BF16, kind="ExternalOutput").ap(),
    }
    ssp = nc.dram_tensor("ssp", [288, TOK], F32, kind="ExternalOutput").ap()
    io["ssp_store_ap"] = lambda mt, msz, tc: ssp[mt * 128:mt * 128 + msz, tc * 512:(tc + 1) * 512]
    p = Prog(nc)
    emit_l0_inproj(p, io)
    p.build()
    return nc


def emit_l0_ssmmid(p, io, pref="P3_"):
    nc = p.nc
    lnw, dtWT, dtb = io["lnw"], io["dtWT"], io["dtb"]
    ident, identf = io["ident"], io["identf"]
    dt_o, Bn_o, Cn_o = io["dt_o"], io["Bn_o"], io["Cn_o"]
    pst = io.get("pst")
    ssp_my = io.get("ssp_my")
    ctx = contextlib.ExitStack()
    sb = lambda n, s, d=F32: ctx.enter_context(nc.sbuf_tensor(pref + n, s, d))
    ps = lambda n, s, d=F32: ctx.enter_context(nc.psum_tensor(pref + n, s, d))
    NT = TSH // 128
    acc = [sb(f"acc{i}", [128, TSH]) for i in range(3)]
    prt = sb("prt", [128, TSH])
    sid = sb("sid", [128, 128], BF16)
    sidf = sb("sidf", [128, 128], F32)
    ptr = [ps(f"ptr{i}", [128, 512]) for i in range(2)]
    ptb = [ps(f"ptb{i}", [128, 1024], BF16) for i in range(2)]
    stok = [sb(f"stok{i}", [128, 288]) for i in range(2)]
    sq = sb("sq", [128, 288])
    st8 = sb("st8", [128, 12])
    lnb = sb("lnb", [128, 288])
    nrm = sb("nrm", [128, 288])
    nrmb = sb("nrmb", [128, 288], BF16)
    dtr = sb("dtr", [128, 2 * TSH], BF16)
    bcn = sb("bcn", [32, TSH])
    dtw = sb("dtw", [128, 2 * DIN], BF16)
    dtbs = sb("dtbs", [128, 64])
    one = sb("one", [128, 1])
    sp1 = sb("sp1", [128, 512])
    sp2 = sb("sp2", [128, 512])
    sp3 = sb("sp3", [128, 512])
    dttile = [sb(f"dtt{i}", [128, TSH]) for i in range(2)]
    pdt = [ps(f"pdt{i}", [128, 512]) for i in range(2)]

    if True:
        eid = p.dma("sync", lambda e: e.dma_start(out=sid[:], in_=ident), slot="id")
        eidf = p.dma("sync", lambda e: e.dma_start(out=sidf[:], in_=identf), slot="idf")
        elnb = p.dma("gpsimd", lambda e: e.dma_start(out=lnb[:], in_=lnw.partition_broadcast(128).squeeze(1)), slot="lnb")
        edtw = p.dma("gpsimd", lambda e: e.dma_start(out=dtw[:].rearrange("p (kt e) -> p kt e", kt=2),
                                                     in_=dtWT.rearrange("(kt p) e -> p kt e", p=128)), slot="dtw")
        edtb = p.dma("sync", lambda e: e.dma_start(out=dtbs[:], in_=dtb.rearrange("(mt p) -> p mt", p=128),
                                                   allow_slow_non_contiguous=True), slot="dtb")
        eone = p.op("vector", lambda e: e.memset(one[:], 1.0))
        eacc = []
        prt_cons = None
        if ssp_my is not None:
            for rt, rsz in ((0, 128), (1, 128), (2, 32)):
                eacc.append(p.dma("sync", lambda e, rt=rt, rsz=rsz: e.dma_start(
                    out=acc[rt][0:rsz, :], in_=ssp_my[rt * 128:rt * 128 + rsz, :]), slot=f"pa{rt}"))
        else:
            for rt, rsz in ((0, 128), (1, 128), (2, 32)):
                ea = p.dma("sync", lambda e, rt=rt, rsz=rsz: e.dma_start(out=acc[rt][0:rsz, :], in_=pst[0, rt * 128:rt * 128 + rsz, :]), slot=f"pa{rt}")
                ev = None
                for c in range(1, NC):
                    el = p.dma("sync", lambda e, c=c, rt=rt, rsz=rsz: e.dma_start(out=prt[0:rsz, :], in_=pst[c, rt * 128:rt * 128 + rsz, :]),
                               deps=[prt_cons], slot=f"pb{rt}")
                    ev = p.op("vector", lambda e, rt=rt, rsz=rsz: e.tensor_tensor(out=acc[rt][0:rsz, :], in0=acc[rt][0:rsz, :], in1=prt[0:rsz, :], op=ALU.add),
                              [el, ea])
                    prt_cons = ev
                eacc.append(ev)
        dtr_evs = []
        bcn_evs = []
        stok_cons = [None, None]
        ptr_cons = [None, None]
        ptb_cons = [None, None]
        for it in range(NT):
            sl = it % 2
            etr_ev = []
            for rt, rsz in ((0, 128), (1, 128), (2, 32)):
                pj = rt % 2
                etr = p.op("tensor", lambda e, rt=rt, rsz=rsz, it=it, pj=pj: e.transpose(
                    out=ptr[pj][0:128, 0:rsz], in_=acc[rt][0:rsz, it * 128:(it + 1) * 128], identity=sidf[0:rsz, 0:rsz]),
                    [eacc[rt], eidf, stok_cons[sl], ptr_cons[pj]])
                eev = p.op("scalar", lambda e, rt=rt, rsz=rsz, sl=sl, pj=pj: e.copy(
                    out=stok[sl][:, rt * 128:rt * 128 + rsz], in_=ptr[pj][0:128, 0:rsz]), [etr])
                ptr_cons[pj] = eev
                etr_ev.append(eev)
            esq = p.op("scalar", lambda e, sl=sl: e.activation(out=sq[:], in_=stok[sl][:], func=AF.Square), etr_ev)
            segs = ((0, 256, 0), (256, 16, 1), (272, 16, 2))
            einv = []
            for (o, n, ix) in segs:
                er = p.op("vector", lambda e, o=o, n=n, ix=ix: e.tensor_reduce(out=st8[:, ix:ix + 1], in_=sq[:, o:o + n], axis=AX.X, op=ALU.add), [esq])
                em = p.op("vector", lambda e, n=n, ix=ix: e.tensor_scalar(out=st8[:, 3 + ix:4 + ix], in0=st8[:, ix:ix + 1], scalar1=1.0 / n, scalar2=1e-6, op0=ALU.mult, op1=ALU.add))
                es = p.op("scalar", lambda e, ix=ix: e.activation(out=st8[:, 6 + ix:7 + ix], in_=st8[:, 3 + ix:4 + ix], func=AF.Sqrt), [em])
                ei = p.op("vector", lambda e, ix=ix: e.reciprocal(out=st8[:, 9 + ix:10 + ix], in_=st8[:, 6 + ix:7 + ix]), [es])
                einv.append(ei)
            en = []
            for (o, n, ix) in segs:
                en.append(p.op("vector", lambda e, o=o, n=n, ix=ix, sl=sl: e.tensor_scalar(
                    out=nrm[:, o:o + n], in0=stok[sl][:, o:o + n], scalar1=st8[:, 9 + ix:10 + ix], scalar2=None, op0=ALU.mult), [einv[ix]]))
            stok_cons[sl] = en[-1]
            ew = p.op("vector", lambda e: e.tensor_tensor(out=nrm[:], in0=nrm[:], in1=lnb[:], op=ALU.mult), en + [elnb])
            ebf = p.op("vector", lambda e: e.tensor_copy(out=nrmb[:], in_=nrm[:]), [ew])
            for j in range(2):
                pj = j % 2
                etr = p.op("tensor", lambda e, j=j, pj=pj: e.transpose(out=ptb[pj][:, 0:128], in_=nrmb[:, j * 128:(j + 1) * 128], identity=sid[:]), [ebf, ptb_cons[pj], eid])
                ee = p.op("scalar", lambda e, j=j, it=it, pj=pj: e.copy(out=dtr[:, j * TSH + it * 128: j * TSH + (it + 1) * 128], in_=ptb[pj][:, 0:128]), [etr])
                ptb_cons[pj] = ee
                dtr_evs.append(ee)
            etrb = p.op("tensor", lambda e: e.transpose(out=ptr[0][0:32, 0:128], in_=nrm[:, 256:288], identity=sidf[:]), [ew, ptr_cons[0]])
            eeb = p.op("scalar", lambda e, it=it: e.copy(out=bcn[:, it * 128:(it + 1) * 128], in_=ptr[0][0:32, 0:128]), [etrb])
            ptr_cons[0] = eeb
            bcn_evs.append(eeb)
        eb_st = p.dma("sync", lambda e: e.dma_start(out=Bn_o, in_=bcn[0:16, :]), bcn_evs, slot="bno")
        ec_st = p.dma("sync", lambda e: e.dma_start(out=Cn_o, in_=bcn[16:32, :]), bcn_evs, slot="cno")
        dt_cons = [None, None]
        pdt_cons = [None, None]
        prev_ed = None
        for mt in range(DIN // 128):
            sl = mt % 2
            for tch in range(2):
                bk = tch
                mm = None
                for kt in range(2):
                    mm = p.op("tensor", lambda e, mt=mt, tch=tch, kt=kt, bk=bk: e.matmul(
                        out=pdt[bk][:], lhsT=dtw[:, kt * DIN + mt * 128: kt * DIN + (mt + 1) * 128],
                        rhs=dtr[:, kt * TSH + tch * 512: kt * TSH + (tch + 1) * 512],
                        start=(kt == 0), stop=(kt == 1)),
                        deps=([edtw, pdt_cons[bk]] + dtr_evs) if kt == 0 else None, chain=(kt == 0))
                er = p.op("scalar", lambda e, mt=mt, bk=bk: e.activation(out=sp1[:], in_=pdt[bk][:], func=AF.Relu, bias=dtbs[:, mt:mt + 1]), [mm, edtb, prev_ed])
                ea = p.op("scalar", lambda e, mt=mt, bk=bk: e.activation(out=sp2[:], in_=pdt[bk][:], func=AF.Abs, bias=dtbs[:, mt:mt + 1]))
                pdt_cons[bk] = ea
                ee = p.op("scalar", lambda e: e.activation(out=sp3[:], in_=sp2[:], func=AF.Exp, scale=-1.0))
                el = p.op("scalar", lambda e: e.activation(out=sp2[:], in_=sp3[:], func=AF.Ln, bias=one[:, 0:1]), [eone])
                ed = p.op("vector", lambda e, sl=sl, tch=tch: e.tensor_tensor(out=dttile[sl][:, tch * 512:(tch + 1) * 512], in0=sp1[:], in1=sp2[:], op=ALU.add),
                          [el, dt_cons[sl]])
                prev_ed = ed
                if tch == 1:
                    dt_cons[sl] = p.dma("scalar", lambda e, mt=mt, sl=sl: e.dma_start(
                        out=dt_o[mt * 128:(mt + 1) * 128, :], in_=dttile[sl][:]), deps=[ed], slot=f"dt{sl}")
    ctx.close()


def build_l0_ssmmid():
    nc = _mk()
    io = {
        "pst": nc.dram_tensor("pst", [NC, 288, TSH], F32, kind="ExternalInput").ap(),
        "lnw": nc.dram_tensor("lnw", [1, 288], F32, kind="ExternalInput").ap(),
        "dtWT": nc.dram_tensor("dtWT", [256, DIN], F32, kind="ExternalInput").ap(),
        "dtb": nc.dram_tensor("dtb", [DIN], F32, kind="ExternalInput").ap(),
        "ident": nc.dram_tensor("ident", [128, 128], BF16, kind="ExternalInput").ap(),
        "identf": nc.dram_tensor("identf", [128, 128], F32, kind="ExternalInput").ap(),
        "dt_o": nc.dram_tensor("dt_o", [DIN, TSH], F32, kind="ExternalOutput").ap(),
        "Bn_o": nc.dram_tensor("Bn_o", [NST, TSH], F32, kind="ExternalOutput").ap(),
        "Cn_o": nc.dram_tensor("Cn_o", [NST, TSH], F32, kind="ExternalOutput").ap(),
    }
    p = Prog(nc)
    emit_l0_ssmmid(p, io)
    p.build()
    return nc


def emit_l0_ffnbuild(p, io, pref="P4_"):
    nc = p.nc
    ISH = DM // NC
    RT = 4032
    vh, st, utT, vtop, utopT = io["vh"], io["st"], io["utT"], io["vtop"], io["utopT"]
    stop_, alpha, beta, vhr, urT, srd, rmat = io["stop_"], io["alpha"], io["beta"], io["vhr"], io["urT"], io["srd"], io["rmat"]
    weT = io["weT"]
    ctx = contextlib.ExitStack()
    sb = lambda n, s, d=F32: ctx.enter_context(nc.sbuf_tensor(pref + n, s, d))
    ps = lambda n, s, d=F32: ctx.enter_context(nc.psum_tensor(pref + n, s, d))
    vh_sb = sb("vh_sb", [128, 32 * ISH], BF16)
    st_sb = sb("st_sb", [128, 32])
    vtop_sb = sb("vtop_sb", [64, ISH], BF16)
    sh_sb = sb("sh_sb", [64, 4])
    vhr_sb = sb("vhr_sb", [4, ISH], BF16)
    ur_sb = sb("ur_sb", [4, DM], BF16)
    srd_sb = sb("srd_sb", [4, 3])
    r_sb = sb("r_sb", [4, 4], BF16)
    mt_sb = sb("mt_sb", [4, DM], BF16)
    rhs_t = [sb(f"rhs{i}", [128, 512], BF16) for i in range(2)]
    rtop = sb("rtop", [64, 512], BF16)
    outt = [sb(f"outt{i}", [128, 512], BF16) for i in range(4)]
    pmm = [ps(f"pmm{i}", [128, 512]) for i in range(5)]

    if True:
        evh1 = p.dma("gpsimd", lambda e: e.dma_start(
            out=vh_sb[:].rearrange("p (kt i) -> p kt i", kt=32)[:, 0:31, :],
            in_=vh[0:3968, :].rearrange("(kt p) i -> p kt i", p=128)), slot="vh")
        evh2 = p.dma("gpsimd", lambda e: e.dma_start(
            out=vh_sb[0:64, :].rearrange("p (kt i) -> p kt i", kt=32)[:, 31:32, :],
            in_=vh[3968:4032, :].rearrange("(kt p) i -> p kt i", p=64)), slot="vh")
        est1 = p.dma("sync", lambda e: e.dma_start(out=st_sb[:, 0:31], in_=st[0:3968].rearrange("(kt p) -> p kt", p=128),
                                                   allow_slow_non_contiguous=True), slot="st")
        est2 = p.dma("sync", lambda e: e.dma_start(out=st_sb[0:64, 31:32], in_=st[3968:4032].rearrange("(kt p) -> p kt", p=64),
                                                   allow_slow_non_contiguous=True), slot="st")
        evt = p.dma("gpsimd", lambda e: e.dma_start(out=vtop_sb[:], in_=vtop), slot="vt")
        esh = p.dma("sync", lambda e: e.dma_start(out=sh_sb[:, 0:1], in_=stop_), slot="sh")
        esa = p.dma("sync", lambda e: e.dma_start(out=sh_sb[:, 1:2], in_=alpha), slot="sh2")
        esb2 = p.dma("sync", lambda e: e.dma_start(out=sh_sb[:, 2:3], in_=beta), slot="sh3")
        evr = p.dma("gpsimd", lambda e: e.dma_start(out=vhr_sb[:], in_=vhr), slot="vr")
        eur = p.dma("gpsimd", lambda e: e.dma_start(out=ur_sb[:], in_=urT), slot="ur")
        esd = p.dma("sync", lambda e: e.dma_start(out=srd_sb[:, 0:2], in_=srd), slot="sd")
        erm = p.dma("gpsimd", lambda e: e.dma_start(out=r_sb[:], in_=rmat), slot="rm")
        escl = []
        for kt in range(32):
            kz = 128 if kt < 31 else 64
            escl.append(p.op("vector", lambda e, kt=kt, kz=kz: e.tensor_scalar(
                out=vh_sb[0:kz, kt * ISH:(kt + 1) * ISH], in0=vh_sb[0:kz, kt * ISH:(kt + 1) * ISH],
                scalar1=st_sb[0:kz, kt:kt + 1], scalar2=None, op0=ALU.mult), [evh1, evh2, est1, est2]))
        e1 = p.op("vector", lambda e: e.tensor_tensor(out=sh_sb[:, 3:4], in0=sh_sb[:, 0:1], in1=sh_sb[:, 1:2], op=ALU.mult), [esh, esa, esb2])
        e2 = p.op("vector", lambda e: e.tensor_tensor(out=sh_sb[:, 3:4], in0=sh_sb[:, 3:4], in1=sh_sb[:, 2:3], op=ALU.add))
        e3 = p.op("scalar", lambda e: e.activation(out=sh_sb[:, 3:4], in_=sh_sb[:, 3:4], func=AF.Relu), [e2])
        e4 = p.op("vector", lambda e: e.tensor_scalar(out=vtop_sb[:], in0=vtop_sb[:], scalar1=sh_sb[:, 3:4], scalar2=None, op0=ALU.mult), [e3, evt])
        e5 = p.op("scalar", lambda e: e.activation(out=srd_sb[:, 2:3], in_=srd_sb[:, 1:2], func=AF.Relu), [esd])
        e6 = p.op("vector", lambda e: e.tensor_tensor(out=srd_sb[:, 2:3], in0=srd_sb[:, 2:3], in1=srd_sb[:, 0:1], op=ALU.mult), [e5])
        e7 = p.op("vector", lambda e: e.tensor_scalar(out=ur_sb[:], in0=ur_sb[:], scalar1=srd_sb[:, 2:3], scalar2=None, op0=ALU.mult), [e6, eur])
        emt_ev = []
        for oc in range(8):
            emm = p.op("tensor", lambda e, oc=oc: e.matmul(out=pmm[4][0:4, :], lhsT=r_sb[:], rhs=ur_sb[:, oc * 512:(oc + 1) * 512],
                                                           start=True, stop=True), [e7, erm] + ([emt_ev[-1]] if emt_ev else []))
            emt_ev.append(p.op("scalar", lambda e, oc=oc: e.copy(out=mt_sb[:, oc * 512:(oc + 1) * 512], in_=pmm[4][0:4, :]), [emm]))
        rhs_cons = [None, None]
        outt_cons = [None] * 4
        bank_cons = [None] * 4
        rt_cons = None
        for oc in range(8):
            mm_last = [None] * 4
            for kt in range(32):
                sl = kt % 2
                ksz = 128 if kt < 31 else 64
                eld = p.dma("gpsimd", lambda e, kt=kt, oc=oc, sl=sl, ksz=ksz: e.dma_start(
                    out=rhs_t[sl][0:ksz, :], in_=utT[kt * 128:kt * 128 + ksz, oc * 512:(oc + 1) * 512]),
                    deps=[rhs_cons[sl]], slot=f"r{sl}")
                for mt in range(4):
                    mm_last[mt] = p.op("tensor", lambda e, kt=kt, mt=mt, sl=sl, ksz=ksz: e.matmul(
                        out=pmm[mt][:], lhsT=vh_sb[0:ksz, kt * ISH + mt * 128: kt * ISH + (mt + 1) * 128],
                        rhs=rhs_t[sl][0:ksz, :], start=(kt == 0), stop=False),
                        deps=([eld] + escl + [bank_cons[mt]]) if kt == 0 else [eld], chain=(kt == 0))
                rhs_cons[sl] = Ev("c_tensor", p.sem_count["c_tensor"])
            sl = 0
            eldt = p.dma("gpsimd", lambda e, oc=oc: e.dma_start(out=rtop[:], in_=utopT[:, oc * 512:(oc + 1) * 512]),
                         deps=[rt_cons], slot="rt")
            for mt in range(4):
                mm_last[mt] = p.op("tensor", lambda e, mt=mt, oc=oc: e.matmul(
                    out=pmm[mt][:], lhsT=vtop_sb[:, mt * 128:(mt + 1) * 128], rhs=rtop[:],
                    start=False, stop=False), deps=[eldt, e4] if mt == 0 else None, chain=(mt == 0))
            rt_cons = Ev("c_tensor", p.sem_count["c_tensor"])
            for mt in range(4):
                mm_last[mt] = p.op("tensor", lambda e, mt=mt, oc=oc: e.matmul(
                    out=pmm[mt][:], lhsT=vhr_sb[:, mt * 128:(mt + 1) * 128], rhs=mt_sb[:, oc * 512:(oc + 1) * 512],
                    start=False, stop=True), deps=([evr] + emt_ev) if mt == 0 else None, chain=(mt == 0))
            for mt in range(4):
                eev = p.op("scalar", lambda e, mt=mt: e.copy(out=outt[mt][:], in_=pmm[mt][:]), [mm_last[mt], outt_cons[mt]])
                bank_cons[mt] = eev
                outt_cons[mt] = p.dma("scalar", lambda e, mt=mt, oc=oc: e.dma_start(
                    out=weT[mt * 128:(mt + 1) * 128, oc * 512:(oc + 1) * 512], in_=outt[mt][:]), deps=[eev], slot=f"o{mt}")
    ctx.close()


def build_l0_ffnbuild():
    nc = _mk()
    ISH = DM // NC
    RT = 4032
    io = {
        "vh": nc.dram_tensor("vh", [RT, ISH], F32, kind="ExternalInput").ap(),
        "st": nc.dram_tensor("st", [RT], F32, kind="ExternalInput").ap(),
        "utT": nc.dram_tensor("utT", [RT, DM], BF16, kind="ExternalInput").ap(),
        "vtop": nc.dram_tensor("vtop", [64, ISH], F32, kind="ExternalInput").ap(),
        "utopT": nc.dram_tensor("utopT", [64, DM], F32, kind="ExternalInput").ap(),
        "stop_": nc.dram_tensor("stop_", [64, 1], F32, kind="ExternalInput").ap(),
        "alpha": nc.dram_tensor("alpha", [64, 1], F32, kind="ExternalInput").ap(),
        "beta": nc.dram_tensor("beta", [64, 1], F32, kind="ExternalInput").ap(),
        "vhr": nc.dram_tensor("vhr", [4, ISH], F32, kind="ExternalInput").ap(),
        "urT": nc.dram_tensor("urT", [4, DM], F32, kind="ExternalInput").ap(),
        "srd": nc.dram_tensor("srd", [4, 2], F32, kind="ExternalInput").ap(),
        "rmat": nc.dram_tensor("rmat", [4, 4], F32, kind="ExternalInput").ap(),
        "weT": nc.dram_tensor("weT", [ISH, DM], BF16, kind="ExternalOutput").ap(),
    }
    p = Prog(nc)
    emit_l0_ffnbuild(p, io)
    p.build()
    return nc


def emit_l0_scan(p, io, pref="P5_"):
    nc = p.nc
    xc, zsd, alog, dpar, yg = io["xc"], io["zsd"], io["alog"], io["dpar"], io["yg"]
    dt_ap = io["dt_ap"]
    bn_ap = io["bn_ap"]
    cn_ap = io["cn_ap"]
    ctx = contextlib.ExitStack()
    sb = lambda n, s, d=F32: ctx.enter_context(nc.sbuf_tensor(pref + n, s, d))
    TC = 512
    NCH = TOK // TC
    a_sb = sb("a_sb", [128, 8 * NST])
    d_sb = sb("d_sb", [128, 8])
    bb = sb("bb", [128, NST * TC])
    cb2 = sb("cb2", [128, NST * TC])
    dtc = sb("dtc", [128, TC])
    xcc = sb("xcc", [128, TC], BF16)
    zsc = sb("zsc", [128, TC], BF16)
    ub = sb("ub", [128, TC])
    dab = sb("dab", [128, NST * TC])
    dbx = sb("dbx", [128, NST * TC])
    hb = sb("hb", [128, NST * TC])
    tmp16 = sb("tmp16", [128, NST])
    hinit = sb("hinit", [128, 8 * NST])
    yb = sb("yb", [128, TC])
    g1 = sb("g1", [128, TC])
    ygt = sb("ygt", [128, TC], BF16)

    if True:
        ea = p.dma("sync", lambda e: e.dma_start(out=a_sb[:].rearrange("p (dt n) -> p dt n", dt=8),
                                                 in_=alog.rearrange("(dt p) n -> p dt n", p=128)), slot="al")
        eax = p.op("scalar", lambda e: e.activation(out=a_sb[:], in_=a_sb[:], func=AF.Exp), [ea])
        ean = p.op("vector", lambda e: e.tensor_scalar(out=a_sb[:], in0=a_sb[:], scalar1=-1.0, scalar2=None, op0=ALU.mult), [eax])
        ed = p.dma("sync", lambda e: e.dma_start(out=d_sb[:], in_=dpar.rearrange("(dt p) -> p dt", p=128),
                                                 allow_slow_non_contiguous=True), slot="dp")
        bc_cons = None
        ld_cons = [[], [], []]
        st_cons = None
        buf_cons = None
        for b in range(2):
            ehz = p.op("vector", lambda e: e.memset(hinit[:], 0.0), [buf_cons])
            for ch in range(NCH // 2):
                off = b * (TOK // 2) + ch * TC
                ebb = p.dma("gpsimd", lambda e, off=off: e.dma_start(
                    out=bb[:].rearrange("p (n t) -> p n t", n=NST),
                    in_=bn_ap(off).partition_broadcast(128)), deps=[bc_cons], slot="bb")
                ecb = p.dma("gpsimd", lambda e, off=off: e.dma_start(
                    out=cb2[:].rearrange("p (n t) -> p n t", n=NST),
                    in_=cn_ap(off).partition_broadcast(128)), deps=[bc_cons], slot="cb")
                for dt_i in range(8):
                    ro = dt_i * 128
                    e_dt = p.dma("sync", lambda e, ro=ro, off=off: e.dma_start(out=dtc[:], in_=dt_ap(e, p, ro, off)),
                                 deps=ld_cons[0], slot="dt")
                    e_xc = p.dma("sync", lambda e, ro=ro, off=off: e.dma_start(out=xcc[:], in_=xc[ro:ro + 128, off:off + TC]),
                                 deps=ld_cons[1], slot="xc")
                    e_zs = p.dma("sync", lambda e, ro=ro, off=off: e.dma_start(out=zsc[:], in_=zsd[ro:ro + 128, off:off + TC]),
                                 deps=ld_cons[2], slot="zs")
                    edas = []
                    for n in range(NST):
                        edas.append(p.op("scalar", lambda e, n=n, dt_i=dt_i: e.activation(
                            out=dab[:, n * TC:(n + 1) * TC], in_=dtc[:], func=AF.Exp,
                            scale=a_sb[:, dt_i * NST + n: dt_i * NST + n + 1]), [e_dt, ean, buf_cons]))
                    eu = p.op("vector", lambda e: e.tensor_tensor(out=ub[:], in0=dtc[:], in1=xcc[:], op=ALU.mult), [e_dt, e_xc, buf_cons])
                    edbx = p.op("vector", lambda e: e.tensor_tensor(
                        out=dbx[:].rearrange("p (n t) -> p n t", n=NST),
                        in0=bb[:].rearrange("p (n t) -> p n t", n=NST),
                        in1=ub[:].unsqueeze(1).broadcast_to([128, NST, TC]), op=ALU.mult), [eu, ebb])
                    ef1 = p.op("vector", lambda e, dt_i=dt_i: e.tensor_tensor(
                        out=tmp16[:].unsqueeze(2), in0=dab[:].rearrange("p (n t) -> p n t", n=NST)[:, :, 0:1],
                        in1=hinit[:, dt_i * NST:(dt_i + 1) * NST].unsqueeze(2), op=ALU.mult), edas + [edbx])
                    ef2 = p.op("vector", lambda e: e.tensor_tensor(
                        out=dbx[:].rearrange("p (n t) -> p n t", n=NST)[:, :, 0:1],
                        in0=dbx[:].rearrange("p (n t) -> p n t", n=NST)[:, :, 0:1],
                        in1=tmp16[:].unsqueeze(2), op=ALU.add))
                    ef3 = p.op("vector", lambda e: e.memset(dab[:].rearrange("p (n t) -> p n t", n=NST)[:, :, 0:1], 0.0))
                    esc = p.op("vector", lambda e: e.tensor_tensor_scan(
                        out=hb[:], data0=dab[:], data1=dbx[:], initial=0.0, op0=ALU.mult, op1=ALU.add), [ef3, ef2])
                    eh = p.op("vector", lambda e, dt_i=dt_i: e.tensor_copy(
                        out=hinit[:, dt_i * NST:(dt_i + 1) * NST].unsqueeze(2),
                        in_=hb[:].rearrange("p (n t) -> p n t", n=NST)[:, :, TC - 1:TC]), [esc])
                    ep = p.op("vector", lambda e: e.tensor_tensor(
                        out=dbx[:].rearrange("p (n t) -> p n t", n=NST),
                        in0=hb[:].rearrange("p (n t) -> p n t", n=NST),
                        in1=cb2[:].rearrange("p (n t) -> p n t", n=NST), op=ALU.mult), [esc, ecb])
                    ey = p.op("vector", lambda e: e.tensor_reduce(
                        out=yb[:], in_=dbx[:].rearrange("p (n t) -> p t n", n=NST), axis=AX.X, op=ALU.add), [ep])
                    eg1 = p.op("vector", lambda e, dt_i=dt_i: e.scalar_tensor_tensor(
                        out=g1[:], in0=xcc[:], scalar=d_sb[:, dt_i:dt_i + 1], in1=yb[:], op0=ALU.mult, op1=ALU.add), [ey, ed])
                    eg2 = p.op("vector", lambda e: e.tensor_tensor(out=ygt[:], in0=g1[:], in1=zsc[:], op=ALU.mult), [e_zs, st_cons])
                    st_cons = p.dma("scalar", lambda e, ro=ro, off=off: e.dma_start(
                        out=yg[ro:ro + 128, off:off + TC], in_=ygt[:]), deps=[eg2], slot="st")
                    ld_cons = [[eu, edas[-1]], [eg1], [eg2]]
                    buf_cons = eg1
                bc_cons = Ev("c_vector", p.sem_count["c_vector"])
    ctx.close()


def build_l0_scan():
    nc = _mk()
    io = {
        "xc": nc.dram_tensor("xc", [DSH, TOK], BF16, kind="ExternalInput").ap(),
        "zsd": nc.dram_tensor("zsd", [DSH, TOK], BF16, kind="ExternalInput").ap(),
        "alog": nc.dram_tensor("alog", [DSH, NST], F32, kind="ExternalInput").ap(),
        "dpar": nc.dram_tensor("dpar", [DSH], F32, kind="ExternalInput").ap(),
        "yg": nc.dram_tensor("yg", [DSH, TOK], BF16, kind="ExternalOutput").ap(),
    }
    dtd = nc.dram_tensor("dtd", [DSH, TOK], F32, kind="ExternalInput").ap()
    Bn = nc.dram_tensor("Bn", [NST, TOK], F32, kind="ExternalInput").ap()
    Cn = nc.dram_tensor("Cn", [NST, TOK], F32, kind="ExternalInput").ap()
    io["dt_ap"] = lambda e, p, ro, off: dtd[ro:ro + 128, off:off + 512]
    io["bn_ap"] = lambda off: Bn[:, off:off + 512]
    io["cn_ap"] = lambda off: Cn[:, off:off + 512]
    p = Prog(nc)
    emit_l0_scan(p, io)
    p.build()
    return nc


def emit_l0_tail(p, io, pref="P6_"):
    nc = p.nc
    woT, weT, xnT, x, fbias, ident = io["woT"], io["weT"], io["xnT"], io["x"], io["fbias"], io["ident"]
    xnew, xn1T = io["xnew"], io["xn1T"]
    ygd_ap = io["ygd_ap"]
    ctx = contextlib.ExitStack()
    sb = lambda n, s, d=F32: ctx.enter_context(nc.sbuf_tensor(pref + n, s, d))
    ps = lambda n, s, d=F32: ctx.enter_context(nc.psum_tensor(pref + n, s, d))
    yg_tg = sb("yg_tg", [128, 64 * 512], BF16)
    xn_tg = sb("xn_tg", [128, 32 * 512], BF16)
    wot = [sb(f"wot{i}", [128, 512], BF16) for i in range(2)]
    wet = [sb(f"wet{i}", [128, 512], BF16) for i in range(2)]
    bias_bc = sb("bias_bc", [128, DM])
    xt = [sb(f"xt{i}", [128, 512]) for i in range(4)]
    a_ev = [sb(f"a_ev{i}", [128, 512]) for i in range(4)]
    xo = [sb(f"xo{i}", [128, 512]) for i in range(4)]
    pa = [ps(f"pa{i}", [128, 512]) for i in range(4)]
    pf = [ps(f"pf{i}", [128, 512]) for i in range(4)]
    xnw = [sb(f"xnw{i}", [128, DM]) for i in range(2)]
    sq = sb("sq", [128, DM])
    stat = sb("stat", [128, 8])
    x1b = [sb(f"x1b{i}", [128, DM], BF16) for i in range(2)]
    sid = sb("sid", [128, 128], BF16)
    ot = [sb(f"ot{i}", [128, 128], BF16) for i in range(4)]
    scst = sb("scst", [128, 1])
    ptb = [pa[0][:].bitcast(BF16), pa[1][:].bitcast(BF16)]

    if True:
        eid = p.dma("sync", lambda e: e.dma_start(out=sid[:], in_=ident), slot="id")
        ebias = p.dma("gpsimd", lambda e: e.dma_start(out=bias_bc[:], in_=fbias.partition_broadcast(128).squeeze(1)), slot="fb")
        ecst = p.op("vector", lambda e: e.memset(scst[:], 1e-5))
        yg_cons = None
        xn_cons = None
        w_cons = [None, None]
        we_cons = [None, None]
        xt_cons = [None] * 4
        pa_cons = [None] * 4
        pf_cons = [None] * 4
        aev_cons = [None] * 4
        xo_cons = [None] * 4
        for tg in range(2):
            eygs = []
            for q in range(8):
                eygs.append(p.dma("sync", lambda e, tg=tg, q=q: e.dma_start(
                    out=yg_tg[:].rearrange("p (kt t) -> p kt t", kt=64)[:, q * 8:(q + 1) * 8, :],
                    in_=ygd_ap(e, p, q, tg)),
                    deps=[yg_cons], slot=f"yg{q}"))
            exns = []
            for q in range(4):
                exns.append(p.dma("sync", lambda e, tg=tg, q=q: e.dma_start(
                    out=xn_tg[:].rearrange("p (kt t) -> p kt t", kt=32)[:, q * 8:(q + 1) * 8, :],
                    in_=xnT.rearrange("(kt p) t -> p kt t", p=128)[:, q * 8:(q + 1) * 8, tg * 512:(tg + 1) * 512]),
                    deps=[xn_cons], slot=f"xn{q}"))
            for ch in range(8):
                mma = [None] * 4
                for kt in range(64):
                    sl = kt % 2
                    ew = p.dma("gpsimd", lambda e, kt=kt, ch=ch, sl=sl: e.dma_start(
                        out=wot[sl][:], in_=woT[kt * 128:(kt + 1) * 128, ch * 512:(ch + 1) * 512]),
                        deps=[w_cons[sl]], slot=f"w{sl}")
                    for tt in range(4):
                        mma[tt] = p.op("tensor", lambda e, kt=kt, tt=tt, sl=sl: e.matmul(
                            out=pa[tt][:], lhsT=yg_tg[:, kt * 512 + tt * 128: kt * 512 + (tt + 1) * 128],
                            rhs=wot[sl][:], start=(kt == 0), stop=(kt == 63)),
                            deps=([ew, pa_cons[tt]] + eygs) if kt == 0 else [ew], chain=(kt == 0))
                    w_cons[sl] = Ev("c_tensor", p.sem_count["c_tensor"])
                mmf = [None] * 4
                for kt in range(32):
                    sl = kt % 2
                    ew = p.dma("sync", lambda e, kt=kt, ch=ch, sl=sl: e.dma_start(
                        out=wet[sl][:], in_=weT[kt * 128:(kt + 1) * 128, ch * 512:(ch + 1) * 512]),
                        deps=[we_cons[sl]], slot=f"v{sl}")
                    for tt in range(4):
                        mmf[tt] = p.op("tensor", lambda e, kt=kt, tt=tt, sl=sl: e.matmul(
                            out=pf[tt][:], lhsT=xn_tg[:, kt * 512 + tt * 128: kt * 512 + (tt + 1) * 128],
                            rhs=wet[sl][:], start=(kt == 0), stop=(kt == 31)),
                            deps=([ew, pf_cons[tt]] + exns) if kt == 0 else [ew], chain=(kt == 0))
                    we_cons[sl] = Ev("c_tensor", p.sem_count["c_tensor"])
                for tt in range(4):
                    ex = p.dma("sync", lambda e, tg=tg, tt=tt, ch=ch: e.dma_start(
                        out=xt[tt][:], in_=x[tg * 512 + tt * 128: tg * 512 + (tt + 1) * 128, ch * 512:(ch + 1) * 512]),
                        deps=[xt_cons[tt]], slot=f"x{tt}")
                    ea = p.op("scalar", lambda e, tt=tt: e.copy(out=a_ev[tt][:], in_=pa[tt][:]), [mma[tt], aev_cons[tt]])
                    pa_cons[tt] = ea
                    e1 = p.op("vector", lambda e, tt=tt: e.tensor_tensor(out=xo[tt][:], in0=a_ev[tt][:], in1=pf[tt][:], op=ALU.add), [ea, mmf[tt], xo_cons[tt]])
                    pf_cons[tt] = e1
                    aev_cons[tt] = e1
                    e2 = p.op("vector", lambda e, tt=tt: e.tensor_tensor(out=xo[tt][:], in0=xo[tt][:], in1=xt[tt][:], op=ALU.add), [ex])
                    e3 = p.op("vector", lambda e, tt=tt, ch=ch: e.tensor_tensor(out=xo[tt][:], in0=xo[tt][:], in1=bias_bc[:, ch * 512:(ch + 1) * 512], op=ALU.add), [ebias])
                    xt_cons[tt] = e2
                    xo_cons[tt] = p.dma("scalar", lambda e, tg=tg, tt=tt, ch=ch: e.dma_start(
                        out=xnew[tg * 512 + tt * 128: tg * 512 + (tt + 1) * 128, ch * 512:(ch + 1) * 512], in_=xo[tt][:]),
                        deps=[e3], slot=f"xo{tt}")
            yg_cons = Ev("c_tensor", p.sem_count["c_tensor"])
            xn_cons = yg_cons
        allst = [Ev(k, v) for k, v in p.sem_count.items() if k.startswith("d_scalar_xo")]
        cons2 = [None, None]
        x1_cons = [None, None]
        tr_cons = [None, None]
        st_ev = [None] * 4
        for i in range(TSH // 128):
            sl = i % 2
            xb, xnb = xnw[sl], x1b[sl]
            eld = p.dma("sync", lambda e, xb=xb, i=i: e.dma_start(out=xb[:], in_=xnew[i * 128:(i + 1) * 128, :]),
                        deps=[cons2[sl]] + allst, slot=f"r{sl}")
            es1 = p.op("vector", lambda e, xb=xb: e.tensor_reduce(out=stat[:, 0:1], in_=xb[:], axis=AX.X, op=ALU.add), [eld])
            esq = p.op("scalar", lambda e, xb=xb: e.activation(out=sq[:], in_=xb[:], func=AF.Square, accum_out=stat[:, 1:2]), [eld, es1])
            em = p.op("vector", lambda e: e.tensor_scalar(out=stat[:, 2:3], in0=stat[:, 0:1], scalar1=1.0 / DM, scalar2=None, op0=ALU.mult), [esq])
            em2 = p.op("vector", lambda e: e.tensor_scalar(out=stat[:, 3:4], in0=stat[:, 1:2], scalar1=1.0 / DM, scalar2=None, op0=ALU.mult), [esq])
            emm = p.op("vector", lambda e: e.tensor_tensor(out=stat[:, 4:5], in0=stat[:, 2:3], in1=stat[:, 2:3], op=ALU.mult))
            evar = p.op("vector", lambda e: e.tensor_tensor(out=stat[:, 5:6], in0=stat[:, 3:4], in1=stat[:, 4:5], op=ALU.subtract))
            esd = p.op("scalar", lambda e: e.activation(out=stat[:, 6:7], in_=stat[:, 5:6], func=AF.Sqrt, bias=scst[:, 0:1]), [evar, ecst])
            ers = p.op("vector", lambda e: e.reciprocal(out=stat[:, 7:8], in_=stat[:, 6:7]), [esd])
            exn = p.op("vector", lambda e, xb=xb, xnb=xnb: e.tensor_scalar(out=xnb[:], in0=xb[:], scalar1=stat[:, 2:3], scalar2=stat[:, 7:8], op0=ALU.subtract, op1=ALU.mult), [x1_cons[sl]])
            cons2[sl] = exn
            last = None
            for j in range(DM // 128):
                pj = j % 2
                etr = p.op("tensor", lambda e, j=j, pj=pj, xnb=xnb: e.transpose(out=ptb[pj][:, 0:128], in_=xnb[:, j * 128:(j + 1) * 128], identity=sid[:]),
                           [exn, eid, tr_cons[pj], pa_cons[pj]])
                k2 = (i * 32 + j) % 4
                eev = p.op("scalar", lambda e, k2=k2, pj=pj: e.copy(out=ot[k2][:], in_=ptb[pj][:, 0:128]), [etr, st_ev[k2]])
                pa_cons[pj] = eev
                tr_cons[pj] = eev
                st_ev[k2] = p.dma("scalar", lambda e, k2=k2, i=i, j=j: e.dma_start(
                    out=xn1T[j * 128:(j + 1) * 128, i * 128:(i + 1) * 128], in_=ot[k2][:]), deps=[eev], slot=f"s{k2}")
                last = eev
            x1_cons[sl] = last
    ctx.close()


def build_l0_tail():
    nc = _mk()
    io = {
        "woT": nc.dram_tensor("woT", [DIN, DM], BF16, kind="ExternalInput").ap(),
        "weT": nc.dram_tensor("weT", [DM, DM], BF16, kind="ExternalInput").ap(),
        "xnT": nc.dram_tensor("xnT", [DM, TSH], BF16, kind="ExternalInput").ap(),
        "x": nc.dram_tensor("x", [TSH, DM], F32, kind="ExternalInput").ap(),
        "fbias": nc.dram_tensor("fbias", [1, DM], F32, kind="ExternalInput").ap(),
        "ident": nc.dram_tensor("ident", [128, 128], BF16, kind="ExternalInput").ap(),
        "xnew": nc.dram_tensor("xnew", [TSH, DM], F32, kind="ExternalOutput").ap(),
        "xn1T": nc.dram_tensor("xn1T", [DM, TSH], BF16, kind="ExternalOutput").ap(),
    }
    ygd = nc.dram_tensor("ygd", [DIN, TSH], BF16, kind="ExternalInput").ap()
    io["ygd_ap"] = lambda e, p, q, tg: ygd.rearrange("(kt p) t -> p kt t", p=128)[:, q * 8:(q + 1) * 8, tg * 512:(tg + 1) * 512]
    p = Prog(nc)
    emit_l0_tail(p, io)
    p.build()
    return nc


def emit_l1_qkv(p, io, pref="P7_"):
    nc = p.nc
    xn1T, wqT, wgT, bg, identf = io["xn1T"], io["wqT"], io["wgT"], io["bg"], io["identf"]
    qkv, gatesT = io["qkv"], io["gatesT"]
    ctx = contextlib.ExitStack()
    sb = lambda n, s, d=F32: ctx.enter_context(nc.sbuf_tensor(pref + n, s, d))
    ps = lambda n, s, d=F32: ctx.enter_context(nc.psum_tensor(pref + n, s, d))
    xn_sb = sb("xn_sb", [128, 32 * TSH], BF16)
    wq_t = [sb(f"wq{i}", [128, 32 * 128], BF16) for i in range(2)]
    wg_sb = sb("wg_sb", [128, 32 * 96], BF16)
    bg_sb = sb("bg_sb", [96, 1])
    out_t = [sb(f"out{i}", [128, 512], BF16) for i in range(4)]
    gt_sb = sb("gt_sb", [96, TSH])
    gto = sb("gto", [128, 96])
    sidf = sb("sidf", [128, 128], F32)
    pq = [ps(f"pq{i}", [128, 512]) for i in range(2)]
    pg = ps("pg", [128, 512])

    if True:
        eidf = p.dma("sync", lambda e: e.dma_start(out=sidf[:], in_=identf), slot="idf")
        exn = []
        for q in range(4):
            exn.append(p.dma("sync", lambda e, q=q: e.dma_start(
                out=xn_sb[:].rearrange("p (kt t) -> p kt t", kt=32)[:, q * 8:(q + 1) * 8, :],
                in_=xn1T.rearrange("(kt p) t -> p kt t", p=128)[:, q * 8:(q + 1) * 8, :]), slot=f"xn{q}"))
        ewg = p.dma("gpsimd", lambda e: e.dma_start(
            out=wg_sb[:].rearrange("p (kt m) -> p kt m", kt=32),
            in_=wgT.rearrange("(kt p) m -> p kt m", p=128)), slot="wg")
        ebg = p.dma("sync", lambda e: e.dma_start(out=bg_sb[:], in_=bg), slot="bg")
        wq_cons = [None, None]
        out_cons = [None] * 4
        pq_cons = [None, None]
        for mt in range(48):
            sl = mt % 2
            ew = p.dma("gpsimd", lambda e, mt=mt, sl=sl: e.dma_start(
                out=wq_t[sl][:].rearrange("p (kt m) -> p kt m", kt=32),
                in_=wqT.rearrange("(kt p) m -> p kt m", p=128)[:, :, mt * 128:(mt + 1) * 128]),
                deps=[wq_cons[sl]], slot=f"wq{sl}")
            for tch in range(2):
                bk = tch
                mm = None
                for kt in range(32):
                    mm = p.op("tensor", lambda e, kt=kt, mt=mt, tch=tch, sl=sl, bk=bk: e.matmul(
                        out=pq[bk][:], lhsT=wq_t[sl][:, kt * 128:(kt + 1) * 128],
                        rhs=xn_sb[:, kt * TSH + tch * 512: kt * TSH + (tch + 1) * 512],
                        start=(kt == 0), stop=(kt == 31)),
                        deps=([ew, pq_cons[bk]] + exn) if kt == 0 else None, chain=(kt == 0))
                ob = (mt * 2 + tch) % 4
                eev = p.op("scalar", lambda e, ob=ob, bk=bk: e.copy(out=out_t[ob][:], in_=pq[bk][:]), [mm, out_cons[ob]])
                pq_cons[bk] = eev
                out_cons[ob] = p.dma("scalar", lambda e, ob=ob, mt=mt, tch=tch: e.dma_start(
                    out=qkv[mt * 128:(mt + 1) * 128, tch * 512:(tch + 1) * 512], in_=out_t[ob][:]), deps=[eev], slot=f"o{ob}")
            wq_cons[sl] = Ev("c_tensor", p.sem_count["c_tensor"])
        eg_ev = []
        for tch in range(2):
            mm = None
            for kt in range(32):
                mm = p.op("tensor", lambda e, kt=kt, tch=tch: e.matmul(
                    out=pg[0:96, :], lhsT=wg_sb[:, kt * 96:(kt + 1) * 96],
                    rhs=xn_sb[:, kt * TSH + tch * 512: kt * TSH + (tch + 1) * 512],
                    start=(kt == 0), stop=(kt == 31)),
                    deps=([ewg] + exn + eg_ev) if kt == 0 else None, chain=(kt == 0))
            eg_ev.append(p.op("scalar", lambda e, tch=tch: e.activation(
                out=gt_sb[:, tch * 512:(tch + 1) * 512], in_=pg[0:96, :], func=AF.Sigmoid, bias=bg_sb[:]), [mm, ebg]))
        gt_st = None
        for it2 in range(TSH // 128):
            etr = p.op("tensor", lambda e, it2=it2: e.transpose(out=pg[0:128, 0:96], in_=gt_sb[:, it2 * 128:(it2 + 1) * 128],
                                                               identity=sidf[0:96, 0:96]), eg_ev + [eidf, gt_st])
            eev = p.op("scalar", lambda e: e.copy(out=gto[:, 0:96], in_=pg[0:128, 0:96]), [etr])
            gt_st = p.dma("scalar", lambda e, it2=it2: e.dma_start(out=gatesT[it2 * 128:(it2 + 1) * 128, :], in_=gto[:, 0:96]), deps=[eev], slot="gt")
    ctx.close()


def build_l1_qkv():
    nc = _mk()
    io = {
        "xn1T": nc.dram_tensor("xn1T", [DM, TSH], BF16, kind="ExternalInput").ap(),
        "wqT": nc.dram_tensor("wqT", [DM, 6144], BF16, kind="ExternalInput").ap(),
        "wgT": nc.dram_tensor("wgT", [DM, 96], F32, kind="ExternalInput").ap(),
        "bg": nc.dram_tensor("bg", [96, 1], F32, kind="ExternalInput").ap(),
        "identf": nc.dram_tensor("identf", [128, 128], F32, kind="ExternalInput").ap(),
        "qkv": nc.dram_tensor("qkv", [6144, TSH], BF16, kind="ExternalOutput").ap(),
        "gatesT": nc.dram_tensor("gatesT", [TSH, 96], F32, kind="ExternalOutput").ap(),
    }
    p = Prog(nc)
    emit_l1_qkv(p, io)
    p.build()
    return nc


NEGC = -1.0e30
SCL = 0.08838834764831845


def emit_l1_nsa(p, io, pref="P8_"):
    nc = p.nc
    load_qg = io["load_qg"]
    load_kv = io["load_kv"]
    gt_ap = io["gt_ap"]
    kw1T = io["kw1T"]
    kb1 = io["kb1"]
    kw2T = io["kw2T"]
    kb2 = io["kb2"]
    vw1T = io["vw1T"]
    vb1 = io["vb1"]
    vw2T = io["vw2T"]
    vb2 = io["vb2"]
    kpos = io["kpos"]
    vpos = io["vpos"]
    memk = io["memk"]
    memv = io["memv"]
    cosf = io["cosf"]
    sinf = io["sinf"]
    cmneg = io["cmneg"]
    baneg = io["baneg"]
    bzneg = io["bzneg"]
    diagf = io["diagf"]
    identb = io["identb"]
    A_out = io["A_out"]
    imp_d = io["imp_d"]

    ctx = contextlib.ExitStack()
    sb = lambda n, s, d=F32: ctx.enter_context(nc.sbuf_tensor(pref + n, s, d))
    ps = lambda n, s, d=F32: ctx.enter_context(nc.psum_tensor(pref + n, s, d))
    k_sb = [sb(f"k{u}", [128, LSEQ], BF16) for u in range(2)]
    v_sb = [sb(f"v{u}", [128, LSEQ], BF16) for u in range(2)]
    rk_sb = [sb(f"rk{u}", [128, LSEQ], BF16) for u in range(2)]
    ck_sb = [sb(f"ck{u}", [128, 257], BF16) for u in range(2)]
    cv_sb = [sb(f"cv{u}", [128, 257], BF16) for u in range(2)]
    cvt_a = [sb(f"cvta{u}", [128, 2 * 128], BF16) for u in range(2)]
    cvt_b = [sb(f"cvtb{u}", [1, 128], BF16) for u in range(2)]
    vt_sb = sb("vt_sb", [128, 32 * 128], BF16)
    cos_sb = sb("cos_sb", [128, LSEQ], BF16)
    sin_sb = sb("sin_sb", [128, LSEQ], BF16)
    qg_sb = sb("qg_sb", [128, LSEQ], BF16)
    qsw_sb = sb("qsw_sb", [128, LSEQ], BF16)
    rtmp2 = sb("rtmp2", [128, LSEQ])
    rq_sb = sb("rq_sb", [128, LSEQ], BF16)
    w1_sb = [sb(f"w1_{i}", [128, 16 * 128], BF16) for i in range(2)]
    w2_sb = sb("w2_sb", [128, 16 * 128], BF16)
    b1_sb = sb("b1_sb", [128, 16])
    b2_sb = sb("b2_sb", [128, 1])
    pos_sb = sb("pos_sb", [128, 16])
    mem_sb = sb("mem_sb", [128, 2])
    ckin = sb("ckin", [128, 16 * 256], BF16)
    hid = sb("hid", [128, 16 * 256], BF16)
    impw = sb("impw", [128, 257])
    impl = sb("impl", [128, 257])
    bm8 = sb("bm8", [128, 32 * 256], U8)
    bmneg_t = sb("bmneg_t", [128, 256])
    dneg = sb("dneg", [128, 128])
    cmtf = sb("cmtf", [128, 257])
    ba_sb = sb("ba_sb", [128, 256])
    bz_sb = sb("bz_sb", [128, 128])
    dg_sb = sb("dg_sb", [128, 128])
    gt_it = sb("gt_it", [128, 24])
    fsm = sb("fsm", [128, LSEQ])
    fp_sb = sb("fp_sb", [128, LSEQ], BF16)
    pts = sb("pts", [128, 32 * 128], BF16)
    stats = sb("stats", [128, 10])
    m8 = sb("m8", [128, 8])
    mwork = sb("mwork", [128, 256])
    fo_sb = sb("fo_sb", [128, 128])
    so_sb = sb("so_sb", [128, 128])
    co_sb = sb("co_sb", [128, 128])
    abuf = sb("abuf", [128, 128], BF16)
    aold = sb("aold", [128, 128], BF16)
    sid = sb("sid", [128, 128], BF16)
    pp = [ps(f"pp{i}", [128, 512]) for i in range(4)]
    ptb = [ps(f"ptb{i}", [128, 1024], BF16) for i in range(2)]

    if True:
        E = p.op
        D = p.dma
        eid = D("sync", lambda e: e.dma_start(out=sid[:], in_=identb), slot="id")
        ecos = D("sync", lambda e: e.dma_start(out=cos_sb[:], in_=cosf), slot="cos")
        esin = D("sync", lambda e: e.dma_start(out=sin_sb[:], in_=sinf), slot="sin")
        eba = D("sync", lambda e: e.dma_start(out=ba_sb[:], in_=baneg), slot="ba")
        ebz = D("sync", lambda e: e.dma_start(out=bz_sb[:], in_=bzneg), slot="bz")
        edg = D("sync", lambda e: e.dma_start(out=dg_sb[:], in_=diagf), slot="dg")
        ekv = []
        for u in range(2):
            ev_k, ev_v = load_kv(p, u, k_sb[u], v_sb[u])
            ekv.append(ev_k)
            ekv.append(ev_v)

        ck_done = [None, None]
        cv_done = [None, None]
        w_cons = None
        pos_cons = None
        mem_cons = None
        pbc = [None] * 4
        ptc = [None] * 2
        for which in ("k", "v"):
            wsrcT, bb1, wsrc2, bb2, possrc, memsrc = ((kw1T, kb1, kw2T, kb2, kpos, memk) if which == "k"
                                                      else (vw1T, vb1, vw2T, vb2, vpos, memv))
            ew1l = [None, None]
            ew2 = D("gpsimd", lambda e, wsrc2=wsrc2: e.dma_start(
                out=w2_sb[:].rearrange("p (kt m) -> p kt m", kt=16),
                in_=wsrc2.rearrange("(kt p) m -> p kt m", p=128)), deps=[w_cons], slot="w2")
            eb1 = D("sync", lambda e, bb1=bb1: e.dma_start(out=b1_sb[:], in_=bb1.rearrange("(kt p) o -> p (kt o)", p=128),
                                                           allow_slow_non_contiguous=True), deps=[mem_cons], slot="b1")
            eb2 = D("sync", lambda e, bb2=bb2: e.dma_start(out=b2_sb[:], in_=bb2), deps=[mem_cons], slot="b2")
            for u in range(2):
                src = k_sb[u] if which == "k" else v_sb[u]
                epos = D("sync", lambda e, u=u, possrc=possrc: e.dma_start(out=pos_sb[:], in_=possrc[u]), deps=[pos_cons], slot="pos")
                emem = D("sync", lambda e, u=u, memsrc=memsrc: e.dma_start(out=mem_sb[:, 0:1], in_=memsrc[u]), deps=[mem_cons], slot="mem")
                eci = []
                for c in range(16):
                    eci.append(E("vector", lambda e, c=c, src=src: e.tensor_scalar(
                        out=ckin[:, c * 256:(c + 1) * 256],
                        in0=src[:].rearrange("p (nc cb) -> p cb nc", cb=16)[:, c, :],
                        scalar1=pos_sb[:, c:c + 1], scalar2=None, op0=ALU.add), [epos, ekv[u * 2 + (0 if which == "k" else 1)]]))
                pos_cons = eci[-1]
                ehid = []
                for mt2 in range(16):
                    bk = mt2 % 2
                    wsl = mt2 % 2
                    ew1 = D("gpsimd", lambda e, mt2=mt2, wsl=wsl, wsrcT=wsrcT: e.dma_start(
                        out=w1_sb[wsl][:].rearrange("p (kt m) -> p kt m", kt=16),
                        in_=wsrcT.rearrange("(kt p) m -> p kt m", p=128)[:, :, mt2 * 128:(mt2 + 1) * 128]),
                        deps=[ew1l[wsl], w_cons], slot=f"w1{wsl}")
                    mm = None
                    for kt in range(16):
                        mm = E("tensor", lambda e, mt2=mt2, kt=kt, bk=bk, wsl=wsl: e.matmul(
                            out=pp[bk][:, 0:256], lhsT=w1_sb[wsl][:, kt * 128:(kt + 1) * 128],
                            rhs=ckin[:, kt * 256:(kt + 1) * 256], start=(kt == 0), stop=(kt == 15)),
                            deps=([ew1, pbc[bk]] + eci) if kt == 0 else None, chain=(kt == 0))
                    ew1l[wsl] = Ev("c_tensor", p.sem_count["c_tensor"])
                    ehid.append(E("scalar", lambda e, mt2=mt2, bk=bk: e.activation(
                        out=hid[:, mt2 * 256:(mt2 + 1) * 256], in_=pp[bk][:, 0:256], func=AF.Relu,
                        bias=b1_sb[:, mt2:mt2 + 1]), [mm, eb1]))
                    pbc[bk] = ehid[-1]
                mm = None
                for kt in range(16):
                    mm = E("tensor", lambda e, kt=kt: e.matmul(
                        out=pp[2][:, 0:256], lhsT=w2_sb[:, kt * 128:(kt + 1) * 128],
                        rhs=hid[:, kt * 256:(kt + 1) * 256], start=(kt == 0), stop=(kt == 15)),
                        deps=([ew2, pbc[2]] + ehid) if kt == 0 else None, chain=(kt == 0))
                dst = ck_sb[u] if which == "k" else cv_sb[u]
                ec1 = E("scalar", lambda e, dst=dst: e.activation(out=dst[:, 1:257], in_=pp[2][:, 0:256], func=AF.Identity, bias=b2_sb[:]), [mm, eb2])
                pbc[2] = ec1
                ec2 = E("vector", lambda e, dst=dst: e.tensor_copy(out=dst[:, 0:1], in_=mem_sb[:, 0:1]), [emem, ec1])
                mem_cons = ec2
                if which == "k":
                    ck_done[u] = ec2
                else:
                    cv_done[u] = ec2
            w_cons = Ev("c_tensor", p.sem_count["c_tensor"])
        cvt_done = [None, None]
        for u in range(2):
            ee = []
            for jt, (o, n) in enumerate(((0, 128), (128, 128))):
                etr = E("tensor", lambda e, u=u, o=o: e.transpose(out=ptb[0][:, 0:128], in_=cv_sb[u][:, o:o + 128], identity=sid[:]),
                        [cv_done[u], eid, cvt_done[u], ptc[0]])
                ee.append(E("scalar", lambda e, u=u, jt=jt: e.copy(out=cvt_a[u][:, jt * 128:(jt + 1) * 128], in_=ptb[0][:, 0:128]), [etr]))
                ptc[0] = ee[-1]
            etr = E("tensor", lambda e, u=u: e.transpose(out=ptb[0][0:1, 0:128], in_=cv_sb[u][:, 256:257], identity=sid[:]), ee + [ptc[0]])
            ee.append(E("scalar", lambda e, u=u: e.copy(out=cvt_b[u][:], in_=ptb[0][0:1, 0:128]), [etr]))
            ptc[0] = ee[-1]
            cvt_done[u] = ee[-1]

        def softmax_f(W, deps, scl, ofs=0):
            e1 = E("vector", lambda e, W=W, ofs=ofs: e.tensor_reduce(out=stats[:, ofs:ofs + 1], in_=fsm[:, 0:W], axis=AX.X, op=ALU.max), deps)
            e2 = E("vector", lambda e, ofs=ofs: e.tensor_scalar(out=stats[:, ofs + 1:ofs + 2], in0=stats[:, ofs:ofs + 1], scalar1=-scl, scalar2=None, op0=ALU.mult))
            e3 = E("scalar", lambda e, W=W, scl=scl, ofs=ofs: e.activation(out=fp_sb[:, 0:W], in_=fsm[:, 0:W], func=AF.Exp,
                                                                           scale=scl, bias=stats[:, ofs + 1:ofs + 2], accum_out=stats[:, ofs + 2:ofs + 3]), [e2])
            e4 = E("vector", lambda e, ofs=ofs: e.reciprocal(out=stats[:, ofs + 3:ofs + 4], in_=stats[:, ofs + 2:ofs + 3]), [e3])
            return e3, e4

        a_st = {}
        rope_cons = None
        qg_cons = None
        unit_mm = None
        vt_cons = None
        imp_ld_cons = None
        imp_st_cons = None
        cmt_cons = None
        gt_cons = None
        aold_cons = None
        abuf_cons = None
        impd_guard = None
        for u in range(2):
            er1 = E("vector", lambda e, u=u: e.tensor_copy(out=qsw_sb[0:64, :], in_=k_sb[u][64:128, :]), [ekv[u * 2], rope_cons])
            er2 = E("vector", lambda e, u=u: e.tensor_copy(out=qsw_sb[64:128, :], in_=k_sb[u][0:64, :]))
            er3 = E("vector", lambda e, u=u: e.tensor_tensor(out=fsm[:], in0=k_sb[u][:], in1=cos_sb[:], op=ALU.mult), [ecos])
            er4 = E("vector", lambda e: e.tensor_tensor(out=rtmp2[:], in0=qsw_sb[:], in1=sin_sb[:], op=ALU.mult), [esin])
            er5 = E("vector", lambda e, u=u: e.tensor_tensor(out=rk_sb[u][:], in0=fsm[:], in1=rtmp2[:], op=ALU.add))
            evt = []
            for jt in range(32):
                etr = E("tensor", lambda e, u=u, jt=jt: e.transpose(out=ptb[jt % 2][:, 0:128], in_=v_sb[u][:, jt * 128:(jt + 1) * 128], identity=sid[:]),
                        [ekv[u * 2 + 1], eid, vt_cons, ptc[jt % 2]])
                evt.append(E("scalar", lambda e, jt=jt: e.copy(out=vt_sb[:, jt * 128:(jt + 1) * 128], in_=ptb[jt % 2][:, 0:128]), [etr]))
                ptc[jt % 2] = evt[-1]
            evt_done = evt[-1]
            imp_ev = [impd_guard] * 32
            for g in range(4):
                eq = load_qg(p, u, g, qg_sb, [qg_cons])
                for it in range(32):
                    ecm = D("sync", lambda e, it=it: e.dma_start(out=cmtf[:], in_=cmneg[it * 128:(it + 1) * 128, :]), slot="cmt",
                            deps=[cmt_cons])
                    mm = E("tensor", lambda e, u=u, g=g, it=it: e.matmul(
                        out=pp[0][:, 0:257], lhsT=qg_sb[:, it * 128:(it + 1) * 128], rhs=ck_sb[u][:],
                        start=True, stop=True), [eq, ck_done[u], unit_mm, pbc[0]])
                    esel = E("vector", lambda e: e.tensor_tensor(out=fsm[:, 0:257], in0=pp[0][:, 0:257], in1=cmtf[:], op=ALU.add), [mm, ecm])
                    pbc[0] = esel
                    unit_mm = esel
                    cmt_cons = esel
                    if g == 0:
                        imp_ev[it] = D("scalar", lambda e, it=it: e.dma_start(out=imp_d[it], in_=fsm[:, 0:257]), deps=[esel, imp_ev[it]], slot="imps")
                        imp_st_cons = imp_ev[it]
                    else:
                        eil = D("sync", lambda e, it=it: e.dma_start(out=impl[:], in_=imp_d[it]), deps=[imp_ev[it], imp_ld_cons], slot="impl")
                        eia = E("vector", lambda e: e.tensor_tensor(out=impw[:], in0=impl[:], in1=fsm[:, 0:257], op=ALU.add), [eil, esel, imp_st_cons])
                        imp_ld_cons = eia
                        imp_ev[it] = D("scalar", lambda e, it=it: e.dma_start(out=imp_d[it], in_=impw[:]), deps=[eia], slot="imps")
                        imp_st_cons = imp_ev[it]
                    e3, e4 = softmax_f(257, [esel], SCL)
                    eptl = []
                    for jt, (o, n) in enumerate(((0, 128), (128, 128), (256, 1))):
                        etr = E("tensor", lambda e, o=o, n=n, jt=jt: e.transpose(
                            out=ptb[jt % 2][0:n, 0:128], in_=fp_sb[:, o:o + n], identity=sid[:]), [e3, ptc[jt % 2]])
                        eptl.append(E("scalar", lambda e, jt=jt, n=n: e.copy(out=pts[0:n, jt * 128:(jt + 1) * 128], in_=ptb[jt % 2][0:n, 0:128]), [etr]))
                        ptc[jt % 2] = eptl[-1]
                    mmv = E("tensor", lambda e, u=u: e.matmul(out=pp[1][:, 0:128], lhsT=pts[:, 0:128], rhs=cvt_a[u][:, 0:128], start=True, stop=False), [eptl[0], cvt_done[u], pbc[1]])
                    mmv = E("tensor", lambda e, u=u: e.matmul(out=pp[1][:, 0:128], lhsT=pts[:, 128:256], rhs=cvt_a[u][:, 128:256], start=False, stop=False), [eptl[1]])
                    mmv = E("tensor", lambda e, u=u: e.matmul(out=pp[1][:, 0:128], lhsT=pts[0:1, 256:384], rhs=cvt_b[u][:], start=False, stop=True), [eptl[2]])
                    egt = D("sync", lambda e, it=it: e.dma_start(out=gt_it[:], in_=gt_ap(e, p, it)), slot="gtit",
                            deps=[gt_cons])
                    eco = E("scalar", lambda e: e.copy(out=co_sb[:], in_=pp[1][:, 0:128]), [mmv])
                    pbc[1] = eco
                    col = (u * 4 + g) * 3
                    eab = E("vector", lambda e, col=col: e.tensor_scalar(
                        out=abuf[:], in0=co_sb[:], scalar1=stats[:, 3:4], scalar2=gt_it[:, col:col + 1], op0=ALU.mult, op1=ALU.mult), [eco, e4, egt, abuf_cons])
                    gt_cons = eab
                    a_st[("a", u, g, it)] = D("scalar", lambda e, u=u, g=g, it=it: e.dma_start(
                        out=A_out[it * 128:(it + 1) * 128, (u * 4 + g) * 128:(u * 4 + g + 1) * 128], in_=abuf[:]), deps=[eab], slot="ast")
                    abuf_cons = a_st[("a", u, g, it)]
                qg_cons = Ev("c_tensor", p.sem_count["c_tensor"])
            t4ld = None
            for it in range(32):
                eil = D("sync", lambda e, it=it: e.dma_start(out=impw[:], in_=imp_d[it]),
                        deps=[imp_ev[it], t4ld, imp_ld_cons, imp_st_cons], slot="impl")
                et1 = E("vector", lambda e: e.memset(impw[:, 0:1], -1000.0 / (0.25 * SCL)), [eil])
                et2 = E("vector", lambda e: e.tensor_scalar(out=impw[:], in0=impw[:], scalar1=0.25 * SCL, scalar2=None, op0=ALU.mult))
                et3 = E("vector", lambda e: e.tensor_reduce(out=stats[:, 0:1], in_=impw[:], axis=AX.X, op=ALU.max))
                et4 = E("vector", lambda e: e.tensor_scalar(out=stats[:, 1:2], in0=stats[:, 0:1], scalar1=-1.0, scalar2=None, op0=ALU.mult))
                et5 = E("scalar", lambda e: e.activation(out=impw[:], in_=impw[:], func=AF.Exp,
                                                         bias=stats[:, 1:2], accum_out=stats[:, 2:3]), [et4])
                et6 = E("vector", lambda e: e.reciprocal(out=stats[:, 3:4], in_=stats[:, 2:3]), [et5])
                et7 = E("vector", lambda e: e.tensor_scalar(out=impw[:], in0=impw[:], scalar1=stats[:, 3:4], scalar2=None, op0=ALU.mult), [et6])
                et8 = E("vector", lambda e: e.max(out=m8[:], in_=impw[:, 1:257]), [et7])
                et9 = E("vector", lambda e: e.tensor_scalar(out=mwork[:], in0=impw[:, 1:257], scalar1=m8[:, 3:4], scalar2=None, op0=ALU.is_ge), [et8])
                eta = E("vector", lambda e: e.tensor_scalar(out=fsm[:, 0:256], in0=impw[:, 1:257], scalar1=1e-10, scalar2=None, op0=ALU.is_gt))
                t4ld = eta
                etb = E("vector", lambda e: e.tensor_tensor(out=mwork[:], in0=mwork[:], in1=fsm[:, 0:256], op=ALU.mult))
                E("vector", lambda e, it=it: e.tensor_scalar(out=bm8[:, it * 256:(it + 1) * 256], in0=mwork[:], scalar1=0.5, scalar2=None, op0=ALU.is_gt))
                impd_guard = eil
            bm_done = Ev("c_vector", p.sem_count["c_vector"])
            for g in range(4):
                eq = load_qg(p, u, g, qg_sb, [qg_cons])
                eq1 = E("vector", lambda e: e.tensor_copy(out=qsw_sb[0:64, :], in_=qg_sb[64:128, :]), [eq, rope_cons])
                eq2 = E("vector", lambda e: e.tensor_copy(out=qsw_sb[64:128, :], in_=qg_sb[0:64, :]))
                eq3 = E("vector", lambda e: e.tensor_tensor(out=fsm[:], in0=qg_sb[:], in1=cos_sb[:], op=ALU.mult))
                eq4 = E("vector", lambda e: e.tensor_tensor(out=rtmp2[:], in0=qsw_sb[:], in1=sin_sb[:], op=ALU.mult))
                eq5 = E("vector", lambda e: e.tensor_tensor(out=rq_sb[:], in0=fsm[:], in1=rtmp2[:], op=ALU.add))
                rope_cons = eq5
                bmn_cons = None
                dneg_cons = None
                for it in range(32):
                    W = (it + 1) * 128
                    ebm = E("vector", lambda e, it=it: e.tensor_scalar(
                        out=bmneg_t[:], in0=bm8[:, it * 256:(it + 1) * 256], scalar1=1e30, scalar2=-1e30,
                        op0=ALU.mult, op1=ALU.add), [bm_done, bmn_cons])
                    edc = E("vector", lambda e, it=it: e.tensor_tensor(
                        out=dneg[:].rearrange("p (nb c) -> p nb c", c=16),
                        in0=dg_sb[:].rearrange("p (nb c) -> p nb c", c=16),
                        in1=bm8[:, it * 256 + it * 8: it * 256 + it * 8 + 8].unsqueeze(2).broadcast_to([128, 8, 16]),
                        op=ALU.max), [edg, bm_done, dneg_cons])
                    edc2 = E("vector", lambda e: e.tensor_scalar(out=dneg[:], in0=dneg[:], scalar1=1e30, scalar2=-1e30,
                                                                 op0=ALU.mult, op1=ALU.add))
                    esels = []
                    for jc in range((W + 511) // 512):
                        wd = min(512, W - jc * 512)
                        nb = wd // 16
                        mm = E("tensor", lambda e, u=u, it=it, jc=jc, wd=wd: e.matmul(
                            out=pp[0][:, 0:wd], lhsT=rq_sb[:, it * 128:(it + 1) * 128],
                            rhs=rk_sb[u][:, jc * 512:jc * 512 + wd], start=True, stop=True), [eq5, er5, unit_mm, pbc[0]])
                        esel = E("vector", lambda e, it=it, jc=jc, wd=wd, nb=nb: e.tensor_tensor(
                            out=fsm[:, jc * 512:jc * 512 + wd].rearrange("p (nb c) -> p nb c", c=16),
                            in0=pp[0][:, 0:wd].rearrange("p (nb c) -> p nb c", c=16),
                            in1=bmneg_t[:, jc * 32:jc * 32 + nb].unsqueeze(2).broadcast_to([128, nb, 16]),
                            op=ALU.add), [ebm, mm])
                        pbc[0] = esel
                        unit_mm = esel
                        esels.append(esel)
                    bmn_cons = esels[-1]
                    dof = (W - 128) % 512
                    jcl = (W - 128) // 512
                    esel2 = E("vector", lambda e, W=W, dof=dof, jcl=jcl: e.tensor_tensor(
                        out=fsm[:, W - 128:W], in0=pp[0][:, dof:dof + 128], in1=dneg[:], op=ALU.add), [edc2, esels[-1]])
                    dneg_cons = esel2
                    pbc[0] = esel2
                    e3, e4 = softmax_f(W, [esel2] + esels, SCL)
                    eptl = []
                    for jt in range(it + 1):
                        pj = jt % 2
                        etr = E("tensor", lambda e, jt=jt, pj=pj: e.transpose(
                            out=ptb[pj][:, 0:128], in_=fp_sb[:, jt * 128:(jt + 1) * 128], identity=sid[:]),
                            [e3, ptc[pj]])
                        eptl.append(E("scalar", lambda e, jt=jt, pj=pj: e.copy(out=pts[:, jt * 128:(jt + 1) * 128], in_=ptb[pj][:, 0:128]), [etr]))
                        ptc[pj] = eptl[-1]
                    mmv = None
                    for jt in range(it + 1):
                        mmv = E("tensor", lambda e, jt=jt, it=it: e.matmul(
                            out=pp[1][:, 0:128], lhsT=pts[:, jt * 128:(jt + 1) * 128], rhs=vt_sb[:, jt * 128:(jt + 1) * 128],
                            start=(jt == 0), stop=(jt == it)), deps=([evt_done, pbc[1]] + eptl) if jt == 0 else None, chain=(jt == 0))
                    efo = E("scalar", lambda e: e.copy(out=fo_sb[:], in_=pp[1][:, 0:128]), [mmv])
                    pbc[1] = efo
                    col = (u * 4 + g) * 3
                    egt = D("sync", lambda e, it=it: e.dma_start(out=gt_it[:], in_=gt_ap(e, p, it)), slot="gtit",
                            deps=[gt_cons])
                    ef1 = E("vector", lambda e, col=col: e.tensor_scalar(
                        out=fo_sb[:], in0=fo_sb[:], scalar1=stats[:, 3:4], scalar2=gt_it[:, col + 1:col + 2], op0=ALU.mult, op1=ALU.mult), [efo, e4, egt])
                    if it == 0:
                        mm = E("tensor", lambda e, u=u: e.matmul(out=pp[2][:, 0:128], lhsT=rq_sb[:, 0:128], rhs=rk_sb[u][:, 0:128], start=True, stop=True), [eq5, er5, ef1, pbc[2]])
                        esel = E("vector", lambda e: e.tensor_tensor(out=fsm[:, 0:128], in0=pp[2][:, 0:128], in1=bz_sb[:], op=ALU.add), [ebz, mm])
                        sW = 128
                    else:
                        mm = E("tensor", lambda e, u=u, it=it: e.matmul(out=pp[2][:, 0:256], lhsT=rq_sb[:, it * 128:(it + 1) * 128],
                                                                        rhs=rk_sb[u][:, (it - 1) * 128:(it + 1) * 128], start=True, stop=True), [eq5, er5, ef1, pbc[2]])
                        esel = E("vector", lambda e: e.tensor_tensor(out=fsm[:, 0:256], in0=pp[2][:, 0:256], in1=ba_sb[:], op=ALU.add), [eba, mm])
                        sW = 256
                    pbc[2] = esel
                    unit_mm = esel
                    e3s, e4s = softmax_f(sW, [esel], SCL, ofs=4)
                    eptl2 = []
                    for jt in range(sW // 128):
                        pj = jt % 2
                        etr = E("tensor", lambda e, jt=jt, pj=pj: e.transpose(
                            out=ptb[pj][:, 0:128], in_=fp_sb[:, jt * 128:(jt + 1) * 128], identity=sid[:]),
                            [e3s, ptc[pj]])
                        eptl2.append(E("scalar", lambda e, jt=jt, pj=pj: e.copy(out=pts[:, jt * 128:(jt + 1) * 128], in_=ptb[pj][:, 0:128]), [etr]))
                        ptc[pj] = eptl2[-1]
                    mmv = None
                    for jt in range(sW // 128):
                        vjt = it - 1 + jt if it > 0 else 0
                        mmv = E("tensor", lambda e, jt=jt, vjt=vjt, sW=sW: e.matmul(
                            out=pp[3][:, 0:128], lhsT=pts[:, jt * 128:(jt + 1) * 128], rhs=vt_sb[:, vjt * 128:(vjt + 1) * 128],
                            start=(jt == 0), stop=(jt == sW // 128 - 1)), deps=([evt_done, pbc[3]] + eptl2) if jt == 0 else None, chain=(jt == 0))
                    eso = E("scalar", lambda e: e.copy(out=so_sb[:], in_=pp[3][:, 0:128]), [mmv])
                    pbc[3] = eso
                    es5 = E("vector", lambda e, col=col: e.tensor_scalar(
                        out=so_sb[:], in0=so_sb[:], scalar1=stats[:, 7:8], scalar2=gt_it[:, col + 2:col + 3], op0=ALU.mult, op1=ALU.mult), [eso, e4s])
                    gt_cons = es5
                    eao = D("sync", lambda e, u=u, g=g, it=it: e.dma_start(
                        out=aold[:], in_=A_out[it * 128:(it + 1) * 128, (u * 4 + g) * 128:(u * 4 + g + 1) * 128]),
                        deps=[a_st[("a", u, g, it)], aold_cons], slot="aold")
                    ec1 = E("vector", lambda e: e.tensor_tensor(out=fo_sb[:], in0=fo_sb[:], in1=so_sb[:], op=ALU.add), [ef1, es5])
                    ec2 = E("vector", lambda e: e.tensor_tensor(out=abuf[:], in0=fo_sb[:], in1=aold[:], op=ALU.add), [eao, abuf_cons])
                    aold_cons = ec2
                    a_st[("a2", u, g, it)] = D("scalar", lambda e, u=u, g=g, it=it: e.dma_start(
                        out=A_out[it * 128:(it + 1) * 128, (u * 4 + g) * 128:(u * 4 + g + 1) * 128], in_=abuf[:]), deps=[ec2], slot="ast2")
                    abuf_cons = a_st[("a2", u, g, it)]
                qg_cons = Ev("c_tensor", p.sem_count["c_tensor"])
            vt_cons = Ev("c_tensor", p.sem_count["c_tensor"])
    ctx.close()


def build_l1_nsa():
    nc = _mk()
    io = {}
    for nm, shp, dt in [("kw1T", [2048, 2048], BF16), ("kb1", [2048, 1], F32), ("kw2T", [2048, 128], F32),
                        ("kb2", [128, 1], F32), ("vw1T", [2048, 2048], BF16), ("vb1", [2048, 1], F32),
                        ("vw2T", [2048, 128], F32), ("vb2", [128, 1], F32), ("kpos", [2, 128, 16], F32),
                        ("vpos", [2, 128, 16], F32), ("memk", [2, 128, 1], F32), ("memv", [2, 128, 1], F32),
                        ("cosf", [128, LSEQ], BF16), ("sinf", [128, LSEQ], BF16), ("cmneg", [LSEQ, 257], F32),
                        ("baneg", [128, 256], F32), ("bzneg", [128, 128], F32), ("diagf", [128, 128], F32),
                        ("identb", [128, 128], BF16)]:
        io[nm] = nc.dram_tensor(nm, shp, dt, kind="ExternalInput").ap()
    io["A_out"] = nc.dram_tensor("A_out", [LSEQ, 1024], BF16, kind="ExternalOutput").ap()
    io["imp_d"] = nc.dram_tensor("imp_d", [32, 128, 257], F32).ap()
    q_in = nc.dram_tensor("q_in", [2, 512, LSEQ], BF16, kind="ExternalInput").ap()
    k_in = nc.dram_tensor("k_in", [2, 128, LSEQ], BF16, kind="ExternalInput").ap()
    v_in = nc.dram_tensor("v_in", [2, 128, LSEQ], BF16, kind="ExternalInput").ap()
    gtb = nc.dram_tensor("gtb", [LSEQ, 96], F32, kind="ExternalInput").ap()
    io["load_qg"] = lambda p, u, g, dst, deps: p.dma("sync", lambda e: e.dma_start(out=dst[:], in_=q_in[u, g * 128:(g + 1) * 128, :]), deps=deps, slot="qg")
    def _load_kv(p, u, k_dst, v_dst):
        ev_k = p.dma("sync", lambda e: e.dma_start(out=k_dst[:], in_=k_in[u]), slot=f"k{u}")
        ev_v = p.dma("sync", lambda e: e.dma_start(out=v_dst[:], in_=v_in[u]), slot=f"v{u}")
        return ev_k, ev_v
    io["load_kv"] = _load_kv
    io["gt_ap"] = lambda e, p, it: gtb[it * 128:(it + 1) * 128, 0:24]
    p = Prog(nc)
    emit_l1_nsa(p, io)
    p.build()
    return nc


def emit_l1_tail(p, io, pref="P9_"):
    nc = p.nc
    woT2, xn1T, kern, grow = io["woT2"], io["xn1T"], io["kern"], io["grow"]
    loraA, loraBT, shAT, shBT = io["loraA"], io["loraBT"], io["shAT"], io["shBT"]
    fbias1, xnew, ident, onesb, outf = io["fbias1"], io["xnew"], io["ident"], io["onesb"], io["outf"]
    load_at = io["load_at"]
    ctx = contextlib.ExitStack()
    sb = lambda n, s, d=F32: ctx.enter_context(nc.sbuf_tensor(pref + n, s, d))
    ps = lambda n, s, d=F32: ctx.enter_context(nc.psum_tensor(pref + n, s, d))
    at_tg = sb("at_tg", [128, 32 * 512], BF16)
    xn_tg = sb("xn_tg", [128, 32 * 512], BF16)
    wot = [sb(f"wot{i}", [128, 512], BF16) for i in range(2)]
    ket = [sb(f"ket{i}", [128, 512], BF16) for i in range(2)]
    la_t = sb("la_t", [32, 512], BF16)
    sid = sb("sid", [128, 128], BF16)
    ones_sb = sb("ones_sb", [1, 32], BF16)
    grow_b = sb("grow_b", [1, 16], BF16)
    grep = sb("grep", [32, 16])
    i32 = sb("i32", [32, 32], BF16)
    ign = sb("ign", [32, 32], BF16)
    aeff = sb("aeff", [32, DM], BF16)
    aefT = sb("aefT", [128, 32 * 32], BF16)
    befT = sb("befT", [32, DM], BF16)
    shb_sb = sb("shb_sb", [32, DM], BF16)
    sha_t = sb("sha_t", [128, 32 * 32], BF16)
    ax_sb = sb("ax_sb", [32, TSH], BF16)
    axs_sb = sb("axs_sb", [32, TSH], BF16)
    bias_bc = sb("bias_bc", [128, DM])
    xt = [sb(f"xt{i}", [128, 512]) for i in range(4)]
    a_ev = [sb(f"a_ev{i}", [128, 512]) for i in range(4)]
    xo = [sb(f"xo{i}", [128, 512]) for i in range(4)]
    pa = [ps(f"pa{i}", [128, 512]) for i in range(4)]
    pf = [ps(f"pf{i}", [128, 512]) for i in range(4)]

    if True:
        E = p.op
        D = p.dma
        eid = D("sync", lambda e: e.dma_start(out=sid[:], in_=ident), slot="id")
        eon = D("sync", lambda e: e.dma_start(out=ones_sb[:], in_=onesb), slot="on")
        ebias = D("gpsimd", lambda e: e.dma_start(out=bias_bc[:], in_=fbias1.partition_broadcast(128).squeeze(1)), slot="fb")
        egr = D("gpsimd", lambda e: e.dma_start(out=grow_b[:], in_=grow), slot="gr")
        eshb = D("gpsimd", lambda e: e.dma_start(out=shb_sb[:], in_=shBT), slot="shb")
        esha = D("gpsimd", lambda e: e.dma_start(out=sha_t[:].rearrange("p (kt r) -> p kt r", kt=32),
                                                 in_=shAT.rearrange("(kt p) r -> p kt r", p=128)), slot="sha")
        exn = []
        for tg in range(2):
            for q in range(4):
                exn.append(D("sync", lambda e, tg=tg, q=q: e.dma_start(
                    out=xn_tg[:].rearrange("p (kt t) -> p kt t", kt=32)[:, q * 8:(q + 1) * 8, tg * 0:512] if False else
                        xn_tg[:].rearrange("p (kt t) -> p kt t", kt=32)[:, q * 8:(q + 1) * 8, :],
                    in_=xn1T.rearrange("(kt p) t -> p kt t", p=128)[:, q * 8:(q + 1) * 8, tg * 512:(tg + 1) * 512]), slot=f"xn{q}"))
            break
        mmg = E("tensor", lambda e: e.matmul(out=pa[0][0:32, 0:16], lhsT=ones_sb[:], rhs=grow_b[:], start=True, stop=True), [eon, egr])
        egrep = E("scalar", lambda e: e.copy(out=grep[:], in_=pa[0][0:32, 0:16]), [mmg])
        ei32 = D("sync", lambda e: e.dma_start(out=i32[:], in_=ident[0:32, 0:32]), slot="i32")
        la_cons = None
        ae_done = []
        for dst, srcT in ((aeff, loraA), (befT, loraBT)):
            for och in range(8):
                mm = None
                for n in range(16):
                    eg = E("vector", lambda e, n=n: e.tensor_scalar(out=ign[:], in0=i32[:], scalar1=grep[:, n:n + 1], scalar2=None, op0=ALU.mult), [egrep, ei32, la_cons])
                    el = D("gpsimd", lambda e, n=n, och=och, srcT=srcT: e.dma_start(
                        out=la_t[:], in_=srcT[n, :, och * 512:(och + 1) * 512]), deps=[la_cons], slot="la")
                    mm = E("tensor", lambda e, n=n, och=och, dst=dst: e.matmul(
                        out=pa[1][0:32, :], lhsT=ign[:], rhs=la_t[:], start=(n == 0), stop=(n == 15)),
                        [eg, el] + ([ae_done[-1]] if (n == 0 and ae_done) else []))
                    la_cons = mm
                ae_done.append(E("scalar", lambda e, och=och, dst=dst: e.copy(out=dst[:, och * 512:(och + 1) * 512], in_=pa[1][0:32, :]), [mm]))
        atr = []
        for kt in range(32):
            etr = E("tensor", lambda e, kt=kt: e.transpose(out=pa[2][:].bitcast(BF16)[0:128, 0:32], in_=aeff[:, kt * 128:(kt + 1) * 128], identity=sid[0:32, 0:32]),
                    [ae_done[7], eid] + ([atr[-1]] if atr else []))
            atr.append(E("scalar", lambda e, kt=kt: e.copy(out=aefT[:, kt * 32:(kt + 1) * 32], in_=pa[2][:].bitcast(BF16)[0:128, 0:32]), [etr]))
        at_cons = None
        xn_cons = exn[-1]
        w_cons = [None, None]
        ke_cons = [None, None]
        xt_cons = [None] * 4
        pa_cons = [None, [atr[-1]], [la_cons, egrep], [atr[-1]]]
        pa_cons = [None, atr[-1], la_cons, atr[-1]]
        pf_cons = [None] * 4
        aev_cons = [None] * 4
        xo_cons = [None] * 4
        ax_done = None
        for tg in range(2):
            if tg == 1:
                exn = []
                for q in range(4):
                    exn.append(D("sync", lambda e, q=q: e.dma_start(
                        out=xn_tg[:].rearrange("p (kt t) -> p kt t", kt=32)[:, q * 8:(q + 1) * 8, :],
                        in_=xn1T.rearrange("(kt p) t -> p kt t", p=128)[:, q * 8:(q + 1) * 8, 512:1024]),
                        deps=[xn_cons], slot=f"xn{q}"))
            eats = load_at(p, tg, at_tg, [at_cons], (sid, eid))
            mm = None
            for kt in range(32):
                mm = E("tensor", lambda e, kt=kt: e.matmul(out=pa[2][0:32, :], lhsT=aefT[:, kt * 32:(kt + 1) * 32],
                                                           rhs=xn_tg[:, kt * 512:(kt + 1) * 512], start=(kt == 0), stop=(kt == 31)),
                       deps=([pa_cons[2], atr[-1]] + exn) if kt == 0 else None, chain=(kt == 0))
            eax = E("scalar", lambda e, tg=tg: e.copy(out=ax_sb[:, tg * 512:(tg + 1) * 512], in_=pa[2][0:32, :]), [mm])
            mm = None
            for kt in range(32):
                mm = E("tensor", lambda e, kt=kt: e.matmul(out=pa[2][0:32, :], lhsT=sha_t[:, kt * 32:(kt + 1) * 32],
                                                           rhs=xn_tg[:, kt * 512:(kt + 1) * 512], start=(kt == 0), stop=(kt == 31)),
                       deps=[eax, esha] if kt == 0 else None, chain=(kt == 0))
            eaxs = E("scalar", lambda e, tg=tg: e.copy(out=axs_sb[:, tg * 512:(tg + 1) * 512], in_=pa[2][0:32, :]), [mm])
            pa_cons[2] = eaxs
            ax_done = eaxs
            for ch in range(8):
                mma = [None] * 4
                for kt in range(32):
                    sl = kt % 2
                    ew = D("gpsimd", lambda e, kt=kt, ch=ch, sl=sl: e.dma_start(
                        out=wot[sl][:], in_=woT2[kt * 128:(kt + 1) * 128, ch * 512:(ch + 1) * 512]),
                        deps=[w_cons[sl]], slot=f"w{sl}")
                    for tt in range(4):
                        mma[tt] = E("tensor", lambda e, kt=kt, tt=tt, sl=sl: e.matmul(
                            out=pa[tt][:], lhsT=at_tg[:, kt * 512 + tt * 128: kt * 512 + (tt + 1) * 128],
                            rhs=wot[sl][:], start=(kt == 0), stop=(kt == 31)),
                            deps=([ew, pa_cons[tt]] + eats) if kt == 0 else [ew], chain=(kt == 0))
                    w_cons[sl] = Ev("c_tensor", p.sem_count["c_tensor"])
                mmf = [None] * 4
                for kt in range(32):
                    sl = kt % 2
                    ew = D("gpsimd", lambda e, kt=kt, ch=ch, sl=sl: e.dma_start(
                        out=ket[sl][:], in_=kern[kt * 128:(kt + 1) * 128, ch * 512:(ch + 1) * 512]),
                        deps=[ke_cons[sl]], slot=f"v{sl}")
                    for tt in range(4):
                        mmf[tt] = E("tensor", lambda e, kt=kt, tt=tt, sl=sl: e.matmul(
                            out=pf[tt][:], lhsT=xn_tg[:, kt * 512 + tt * 128: kt * 512 + (tt + 1) * 128],
                            rhs=ket[sl][:], start=(kt == 0), stop=False),
                            deps=([ew, pf_cons[tt]] + exn) if kt == 0 else [ew], chain=(kt == 0))
                    ke_cons[sl] = Ev("c_tensor", p.sem_count["c_tensor"])
                for tt in range(4):
                    mmf[tt] = E("tensor", lambda e, tt=tt, tg=tg, ch=ch: e.matmul(
                        out=pf[tt][:], lhsT=ax_sb[:, tg * 512 + tt * 128: tg * 512 + (tt + 1) * 128],
                        rhs=befT[:, ch * 512:(ch + 1) * 512], start=False, stop=False),
                        deps=[ax_done, ae_done[-1]] if tt == 0 else None, chain=(tt == 0))
                for tt in range(4):
                    mmf[tt] = E("tensor", lambda e, tt=tt, tg=tg, ch=ch: e.matmul(
                        out=pf[tt][:], lhsT=axs_sb[:, tg * 512 + tt * 128: tg * 512 + (tt + 1) * 128],
                        rhs=shb_sb[:, ch * 512:(ch + 1) * 512], start=False, stop=True),
                        deps=[ax_done, eshb] if tt == 0 else None, chain=(tt == 0))
                for tt in range(4):
                    ex = D("sync", lambda e, tg=tg, tt=tt, ch=ch: e.dma_start(
                        out=xt[tt][:], in_=xnew[tg * 512 + tt * 128: tg * 512 + (tt + 1) * 128, ch * 512:(ch + 1) * 512]),
                        deps=[xt_cons[tt]], slot=f"x{tt}")
                    ea = E("scalar", lambda e, tt=tt: e.copy(out=a_ev[tt][:], in_=pa[tt][:]), [mma[tt], aev_cons[tt]])
                    pa_cons[tt] = ea
                    e1 = E("vector", lambda e, tt=tt: e.tensor_tensor(out=xo[tt][:], in0=a_ev[tt][:], in1=pf[tt][:], op=ALU.add), [ea, mmf[tt], xo_cons[tt]])
                    pf_cons[tt] = e1
                    aev_cons[tt] = e1
                    e2 = E("vector", lambda e, tt=tt: e.tensor_tensor(out=xo[tt][:], in0=xo[tt][:], in1=xt[tt][:], op=ALU.add), [ex])
                    e3 = E("vector", lambda e, tt=tt, ch=ch: e.tensor_tensor(out=xo[tt][:], in0=xo[tt][:], in1=bias_bc[:, ch * 512:(ch + 1) * 512], op=ALU.add), [ebias])
                    xt_cons[tt] = e2
                    xo_cons[tt] = D("scalar", lambda e, tg=tg, tt=tt, ch=ch: e.dma_start(
                        out=outf[tg * 512 + tt * 128: tg * 512 + (tt + 1) * 128, ch * 512:(ch + 1) * 512], in_=xo[tt][:]),
                        deps=[e3], slot=f"xo{tt}")
            at_cons = Ev("c_tensor", p.sem_count["c_tensor"])
            xn_cons = at_cons
    ctx.close()


def build_l1_tail():
    nc = _mk()
    io = {}
    for nm, shp, dt in [("woT2", [DM, DM], BF16), ("xn1T", [DM, TSH], BF16), ("kern", [DM, DM], BF16),
                        ("grow", [1, 16], F32), ("loraA", [16, 32, DM], F32), ("loraBT", [16, 32, DM], F32),
                        ("shAT", [DM, 32], F32), ("shBT", [32, DM], F32), ("fbias1", [1, DM], F32),
                        ("xnew", [TSH, DM], F32), ("ident", [128, 128], BF16), ("onesb", [1, 32], BF16)]:
        io[nm] = nc.dram_tensor(nm, shp, dt, kind="ExternalInput").ap()
    io["outf"] = nc.dram_tensor("outf", [TSH, DM], F32, kind="ExternalOutput").ap()
    AT = nc.dram_tensor("AT", [DM, TSH], BF16, kind="ExternalInput").ap()
    def _load_at(p, tg, at_tg, deps, helpers):
        evs = []
        for q in range(4):
            evs.append(p.dma("sync", lambda e, tg=tg, q=q: e.dma_start(
                out=at_tg[:].rearrange("p (kt t) -> p kt t", kt=32)[:, q * 8:(q + 1) * 8, :],
                in_=AT.rearrange("(kt p) t -> p kt t", p=128)[:, q * 8:(q + 1) * 8, tg * 512:(tg + 1) * 512]),
                deps=deps, slot=f"at{q}"))
        return evs
    io["load_at"] = _load_at
    p = Prog(nc)
    emit_l1_tail(p, io)
    p.build()
    return nc


PERM128 = np.concatenate([np.arange(0, 128, 2), np.arange(1, 128, 2)])
ONES32 = np.ones((1, 32), dtype=NP_BF16)


def _nsa_consts():
    inv = 1.0 / 10000.0 ** (np.arange(0, 128, 2, dtype=np.float64) / 128)
    ang = np.arange(LSEQ, dtype=np.float64)[:, None] * inv[None, :]
    cos = np.cos(ang).astype(np.float32).T
    sin = np.sin(ang).astype(np.float32).T
    cosf = np.concatenate([cos, cos], 0).astype(NP_BF16)
    sinf = np.concatenate([-sin, sin], 0).astype(NP_BF16)
    ck_seq = np.concatenate([[-1], (np.arange(256) + 1) * 16 - 1])
    cmneg = np.where(ck_seq[None, :] < np.arange(LSEQ)[:, None], 0.0, -1e30).astype(np.float32)
    r = np.arange(128)[:, None]
    jj = np.arange(256)[None, :]
    baneg = np.where((jj >= r + 64) & (jj <= r + 128), 0.0, -1e30).astype(np.float32)
    jz = np.arange(128)[None, :]
    bzneg = np.where((jz <= r) & (jz >= r - 64), 0.0, -1e30).astype(np.float32)
    diagf = (((r // 16) == (jz // 16)) & (jz <= r)).astype(np.float32)
    return cosf, sinf, cmneg, baneg, bzneg, diagf


def kernel(**ins):
    C = np.ascontiguousarray
    x = np.asarray(ins["L0_attn_hidden_states"]).reshape(TOK, DM)
    qk_perm = np.arange(6144)
    for h in range(40):
        qk_perm[h * 128:(h + 1) * 128] = h * 128 + PERM128
    kw1p = ins["L1_attn_k_w1"].reshape(2048, 16, 128)[:, :, PERM128].reshape(2048, 2048)
    cast_src = {
        "woT": C(ins["L0_attn_out_proj_w"].T),
        "woT2": C(ins["L1_attn_w_o"].T),
        "kern": np.asarray(ins["L1_ffn_kernel"]),
        "wqT": C(ins["L1_attn_w_qkv"][qk_perm].T),
        "utT": C(ins["L0_ffn_U_tail"].T),
        "kw1T": C(kw1p.T),
        "vw1T": C(ins["L1_attn_v_w1"].T),
    }
    nc = build_l0_prep()
    in_maps = []
    for c in range(NC):
        d = {"x": x[c * TSH:(c + 1) * TSH], "ident": IDENT_BF16}
        for nm, rows, cols in CAST_SPECS:
            d["ci_" + nm] = C(cast_src[nm][c * rows:(c + 1) * rows])
        in_maps.append(d)
    res = _run(nc, in_maps)
    xnT_full = np.concatenate([np.asarray(r["xnT"]) for r in res], axis=1)
    wbf = {nm: np.concatenate([np.asarray(r["co_" + nm]) for r in res], axis=0)
           for nm, _, _ in CAST_SPECS}
    W = ins["L0_attn_in_proj_w"]
    nc = build_l0_inproj()
    res = _run(nc, [{
        "xnT": xnT_full,
        "w_x": C(W[:, c * DSH:(c + 1) * DSH]),
        "w_z": C(W[:, DIN + c * DSH: DIN + (c + 1) * DSH]),
        "cw": C(ins["L0_attn_conv_w"][c * DSH:(c + 1) * DSH]),
        "cb": C(ins["L0_attn_conv_b"][c * DSH:(c + 1) * DSH]),
        "wxpT": C(ins["L0_attn_x_proj_w"].T[c * DSH:(c + 1) * DSH]),
    } for c in range(NC)])
    xc_sh = [np.asarray(r["xc"]) for r in res]
    zs_sh = [np.asarray(r["zs"]) for r in res]
    ssp_all = np.stack([np.asarray(r["ssp"]) for r in res])
    lnw = np.concatenate([ins["L0_attn_dt_ln_w"], ins["L0_attn_B_ln_w"], ins["L0_attn_C_ln_w"]])[None, :].astype(np.float32)
    dtWT = C(ins["L0_attn_dt_proj_w"].T)
    nc = build_l0_ssmmid()
    res = _run(nc, [{
        "pst": C(ssp_all[:, :, c * TSH:(c + 1) * TSH]), "lnw": lnw, "dtWT": dtWT,
        "dtb": ins["L0_attn_dt_bias"], "ident": IDENT_BF16, "identf": IDENT_F32,
    } for c in range(NC)])
    dt_full = np.concatenate([np.asarray(r["dt_o"]) for r in res], axis=1)
    Bn_full = np.concatenate([np.asarray(r["Bn_o"]) for r in res], axis=1)
    Cn_full = np.concatenate([np.asarray(r["Cn_o"]) for r in res], axis=1)
    ISH = DM // NC
    nc = build_l0_ffnbuild()
    res = _run(nc, [{
        "vh": C(ins["L0_ffn_Vh_tail"][:, c * ISH:(c + 1) * ISH]), "st": ins["L0_ffn_S_tail"],
        "utT": wbf["utT"],
        "vtop": C(ins["L0_ffn_Vh_top"][:, c * ISH:(c + 1) * ISH]), "utopT": C(ins["L0_ffn_U_top"].T),
        "stop_": ins["L0_ffn_S_top"][:, None], "alpha": ins["L0_ffn_alpha"][:, None],
        "beta": ins["L0_ffn_beta"][:, None],
        "vhr": C(ins["L0_ffn_Vh_tail_r"][:, c * ISH:(c + 1) * ISH]), "urT": C(ins["L0_ffn_U_tail_r"].T),
        "srd": np.stack([ins["L0_ffn_S_tail_r"], ins["L0_ffn_D"]], axis=1), "rmat": ins["L0_ffn_R"],
    } for c in range(NC)])
    weT_full = np.concatenate([np.asarray(r["weT"]) for r in res], axis=0)
    nc = build_l0_scan()
    res = _run(nc, [{
        "xc": xc_sh[c], "zsd": zs_sh[c], "dtd": C(dt_full[c * DSH:(c + 1) * DSH]),
        "alog": C(ins["L0_attn_A_log"][c * DSH:(c + 1) * DSH]),
        "dpar": C(ins["L0_attn_D_param"][c * DSH:(c + 1) * DSH]),
        "Bn": Bn_full, "Cn": Cn_full,
    } for c in range(NC)])
    yg_full = np.concatenate([np.asarray(r["yg"]) for r in res], axis=0)
    nc = build_l0_tail()
    res = _run(nc, [{
        "ygd": C(yg_full[:, c * TSH:(c + 1) * TSH]), "woT": wbf["woT"], "weT": weT_full,
        "xnT": C(xnT_full[:, c * TSH:(c + 1) * TSH]), "x": x[c * TSH:(c + 1) * TSH],
        "fbias": ins["L0_ffn_bias"][None, :], "ident": IDENT_BF16,
    } for c in range(NC)])
    xnew_sh = [np.asarray(r["xnew"]) for r in res]
    xn1T_full = np.concatenate([np.asarray(r["xn1T"]) for r in res], axis=1)
    wgT = C(ins["L1_attn_w_gate"].T)
    nc = build_l1_qkv()
    res = _run(nc, [{
        "xn1T": C(xn1T_full[:, c * TSH:(c + 1) * TSH]), "wqT": wbf["wqT"], "wgT": wgT,
        "bg": ins["L1_attn_b_gate"][:, None], "identf": IDENT_F32,
    } for c in range(NC)])
    qkvT_full = np.concatenate([np.asarray(r["qkv"]) for r in res], axis=1)
    gates_full = np.concatenate([np.asarray(r["gatesT"]) for r in res], axis=0)
    COSF, SINF, CMNEG, BANEG, BZNEG, DIAGF = _nsa_consts()
    kw2p = ins["L1_attn_k_w2"][PERM128]
    wcom = {
        "kw1T": wbf["kw1T"], "kb1": ins["L1_attn_k_b1"][:, None],
        "kw2T": C(kw2p.T), "kb2": ins["L1_attn_k_b2"][PERM128][:, None],
        "vw1T": wbf["vw1T"], "vb1": ins["L1_attn_v_b1"][:, None],
        "vw2T": C(ins["L1_attn_v_w2"].T), "vb2": ins["L1_attn_v_b2"][:, None],
        "cosf": COSF, "sinf": SINF, "cmneg": CMNEG, "baneg": BANEG, "bzneg": BZNEG,
        "diagf": DIAGF, "identb": IDENT_BF16,
    }
    in_maps = []
    for c in range(NC):
        b = c // 4
        kvhs = [2 * (c % 4), 2 * (c % 4) + 1]
        cols = slice(b * LSEQ, (b + 1) * LSEQ)
        q_in = np.zeros((2, 512, LSEQ), dtype=NP_BF16)
        k_in = np.zeros((2, 128, LSEQ), dtype=NP_BF16)
        v_in = np.zeros((2, 128, LSEQ), dtype=NP_BF16)
        kpos = np.zeros((2, 128, 16), np.float32)
        vpos = np.zeros((2, 128, 16), np.float32)
        memk = np.zeros((2, 128, 1), np.float32)
        memv = np.zeros((2, 128, 1), np.float32)
        for u, kvh in enumerate(kvhs):
            for g in range(4):
                hq = kvh * 4 + g
                q_in[u, g * 128:(g + 1) * 128] = qkvT_full[hq * 128:(hq + 1) * 128, cols]
            k_in[u] = qkvT_full[4096 + kvh * 128: 4096 + (kvh + 1) * 128, cols]
            v_in[u] = qkvT_full[5120 + kvh * 128: 5120 + (kvh + 1) * 128, cols]
            kpos[u] = ins["L1_attn_k_pos"][kvh].T[PERM128]
            vpos[u] = ins["L1_attn_v_pos"][kvh].T
            memk[u] = ins["L1_attn_mem_kv"][0, kvh].T[PERM128]
            memv[u] = ins["L1_attn_mem_kv"][1, kvh].T
        colperm = []
        for u2, kvh2 in enumerate(kvhs):
            for g2 in range(4):
                for c2 in range(3):
                    colperm.append((kvh2 * 4 + g2) * 3 + c2)
        colperm += [i for i in range(96) if i not in colperm]
        d = {"q_in": q_in, "k_in": k_in, "v_in": v_in,
             "gtb": C(gates_full[cols][:, colperm]),
             "kpos": kpos, "vpos": vpos, "memk": memk, "memv": memv}
        d.update(wcom)
        in_maps.append(d)
    nc = build_l1_nsa()
    res = _run(nc, in_maps)
    A_tok = np.zeros((TOK, DM), dtype=NP_BF16)
    for c in range(NC):
        b = c // 4
        kvhs = [2 * (c % 4), 2 * (c % 4) + 1]
        a = np.asarray(res[c]["A_out"])
        for u, kvh in enumerate(kvhs):
            for g in range(4):
                hq = kvh * 4 + g
                A_tok[b * LSEQ:(b + 1) * LSEQ, hq * 128:(hq + 1) * 128] = a[:, (u * 4 + g) * 128:(u * 4 + g + 1) * 128]
    AT_full = C(A_tok.T)
    loraBT = C(np.transpose(ins["L1_ffn_lora_B"], (0, 2, 1)))
    shAT = C(ins["L1_ffn_lora_A_shared"].T)
    nc = build_l1_tail()
    res = _run(nc, [{
        "AT": C(AT_full[:, c * TSH:(c + 1) * TSH]), "woT2": wbf["woT2"],
        "xn1T": C(xn1T_full[:, c * TSH:(c + 1) * TSH]), "kern": wbf["kern"],
        "grow": ins["L1_ffn_gate"][c][None, :], "loraA": ins["L1_ffn_lora_A"],
        "loraBT": loraBT, "shAT": shAT, "shBT": C(ins["L1_ffn_lora_B_shared"].T),
        "fbias1": ins["L1_ffn_bias"][None, :], "xnew": xnew_sh[c],
        "ident": IDENT_BF16, "onesb": ONES32,
    } for c in range(NC)])
    out = np.concatenate([np.asarray(r["outf"]) for r in res], axis=0)
    return out.reshape(2, LSEQ, DM).astype(np.float32)


# version 06
# speedup vs baseline: 1.1419x; 1.1419x over previous
import contextlib
from collections import defaultdict

import numpy as np
import ml_dtypes

import concourse.bass as bass
import concourse.mybir as mybir
from concourse.bass_utils import run_bass_kernel_spmd

F32 = mybir.dt.float32
BF16 = mybir.dt.bfloat16
U8 = mybir.dt.uint8
AF = mybir.ActivationFunctionType
ALU = mybir.AluOpType
AX = mybir.AxisListType
NP_BF16 = ml_dtypes.bfloat16

NC = 8
DM = 4096
DIN = 8192
NST = 16
TOK = 8192
TSH = TOK // NC
LSEQ = 4096
DSH = DIN // NC


class Ev:
    __slots__ = ("sem_key", "val")

    def __init__(self, sem_key, val):
        self.sem_key = sem_key
        self.val = val


class Prog:

    AUTO = {"vector", "scalar", "gpsimd", "tensor"}

    def __init__(self, nc):
        self.nc = nc
        self.items = defaultdict(list)
        self.sem_count = defaultdict(int)
        self.sem_keys = set()
        self.last = {}
        self.floor = []
        self._pid = {}

    def barrier(self):
        self.floor = [Ev(k, v) for k, v in self.sem_count.items()]

    def pid(self, e):
        key = id(e)
        if key not in self._pid:
            self._pid[key] = e.partition_id()
        return self._pid[key]

    def _rec(self, engine, fn, deps, sem_key, inc, chain):
        deps = [d for d in (deps or []) if d is not None] + self.floor
        if chain and engine in self.AUTO and engine in self.last:
            deps.append(self.last[engine])
        self.sem_keys.add(sem_key)
        self.sem_count[sem_key] += inc
        ev = Ev(sem_key, self.sem_count[sem_key])
        self.items[engine].append((fn, deps, sem_key, inc))
        if engine in self.AUTO:
            self.last[engine] = ev
        return ev

    def op(self, engine, fn, deps=None, chain=True):
        return self._rec(engine, fn, deps, "c_" + engine, 1, chain)

    def dma(self, queue, fn, deps=None, slot=None):
        key = "d_" + queue + ("" if slot is None else "_" + str(slot))
        return self._rec(queue, fn, deps, key, 16, False)

    def cc(self, fn, deps=None):
        return self._rec("gpsimd", fn, deps, "cc", 1, True)

    def build(self, final_engine="sync"):
        nc = self.nc
        ctx = contextlib.ExitStack()
        sems = {k: ctx.enter_context(nc.semaphore(k)) for k in sorted(self.sem_keys)}
        finals = sorted(self.sem_count.items())

        def emit_stream(eng, engine_name):
            hw = defaultdict(int)
            for (fn, deps, sem_key, inc) in self.items[engine_name]:
                need = {}
                for d in deps:
                    if d.val > need.get(d.sem_key, 0):
                        need[d.sem_key] = d.val
                for k, v in sorted(need.items()):
                    if v > hw[k]:
                        eng.wait_ge(sems[k], v)
                        hw[k] = v
                inst = fn(eng)
                if inst is None:
                    raise RuntimeError(f"op on {engine_name} returned None")
                inst.then_inc(sems[sem_key], inc)

        with ctx, nc.Block() as block:
            for engine_name in ["sync", "vector", "scalar", "tensor", "gpsimd"]:
                if not self.items[engine_name] and engine_name != final_engine:
                    continue
                dec = getattr(block, engine_name)

                def mk(en):
                    def _f(eng):
                        emit_stream(eng, en)
                        if en == final_engine:
                            for k, v in finals:
                                eng.wait_ge(sems[k], v)
                    return _f
                dec(mk(engine_name))
        return nc


def _mk():
    return bass.Bass("TRN2", target_bir_lowering=False)


def _t(a):
    return np.ascontiguousarray(a.T)


def _run(nc, in_maps):
    return run_bass_kernel_spmd(nc, in_maps, core_ids=list(range(NC))).results


IDENT_BF16 = np.eye(128, dtype=NP_BF16)
IDENT_F32 = np.eye(128, dtype=np.float32)


def emit_l0_prep(p, io, pref="P1_"):
    nc = p.nc
    x, ident, xnT = io["x"], io["ident"], io["xnT"]
    ctx = contextlib.ExitStack()
    sb = lambda n, s, d=F32: ctx.enter_context(nc.sbuf_tensor(pref + n, s, d))
    ps = lambda n, s, d=F32: ctx.enter_context(nc.psum_tensor(pref + n, s, d))
    NT = TSH // 128
    xt = [sb(f"xt{i}", [128, DM]) for i in range(2)]
    sq = sb("sq", [128, DM])
    stat = sb("stat", [128, 8])
    xnt = [sb(f"xnt{i}", [128, DM], BF16) for i in range(2)]
    sid = sb("sid", [128, 128], BF16)
    ptr = [ps(f"ptr{i}", [128, 1024], BF16) for i in range(2)]
    ot = [sb(f"ot{i}", [128, 128], BF16) for i in range(4)]
    scst = sb("scst", [128, 2])

    if True:
        eid = p.dma("sync", lambda e: e.dma_start(out=sid[:], in_=ident), slot="id")
        ecst = p.op("vector", lambda e: e.memset(scst[:, 0:1], 1e-5))
        cons = [None, None]
        xn_cons = [None, None]
        tr_cons = [None, None]
        st_ev = [None] * 4
        for i in range(NT):
            sl = i % 2
            xb, xnb = xt[sl], xnt[sl]
            eld = p.dma("sync", lambda e, xb=xb, i=i: e.dma_start(out=xb[:], in_=x[i * 128:(i + 1) * 128, :]),
                        deps=[cons[sl]], slot=f"x{sl}")
            es1 = p.op("vector", lambda e, xb=xb: e.tensor_reduce(out=stat[:, 0:1], in_=xb[:], axis=AX.X, op=ALU.add), [eld])
            esq = p.op("scalar", lambda e, xb=xb: e.activation(out=sq[:], in_=xb[:], func=AF.Square, accum_out=stat[:, 1:2]), [eld, es1])
            em = p.op("vector", lambda e: e.tensor_scalar(out=stat[:, 2:3], in0=stat[:, 0:1], scalar1=1.0 / DM, scalar2=None, op0=ALU.mult), [esq])
            em2 = p.op("vector", lambda e: e.tensor_scalar(out=stat[:, 3:4], in0=stat[:, 1:2], scalar1=1.0 / DM, scalar2=None, op0=ALU.mult), [esq])
            emm = p.op("vector", lambda e: e.tensor_tensor(out=stat[:, 4:5], in0=stat[:, 2:3], in1=stat[:, 2:3], op=ALU.mult))
            evar = p.op("vector", lambda e: e.tensor_tensor(out=stat[:, 5:6], in0=stat[:, 3:4], in1=stat[:, 4:5], op=ALU.subtract))
            esd = p.op("scalar", lambda e: e.activation(out=stat[:, 6:7], in_=stat[:, 5:6], func=AF.Sqrt, bias=scst[:, 0:1]), [evar, ecst])
            ers = p.op("vector", lambda e: e.reciprocal(out=stat[:, 7:8], in_=stat[:, 6:7]), [esd])
            exn = p.op("vector", lambda e, xb=xb, xnb=xnb: e.tensor_scalar(out=xnb[:], in0=xb[:], scalar1=stat[:, 2:3], scalar2=stat[:, 7:8], op0=ALU.subtract, op1=ALU.mult), [xn_cons[sl]])
            cons[sl] = exn
            last_tr = None
            for j in range(DM // 128):
                pj = j % 2
                etr = p.op("tensor", lambda e, j=j, pj=pj, xnb=xnb: e.transpose(out=ptr[pj][:, 0:128], in_=xnb[:, j * 128:(j + 1) * 128], identity=sid[:]),
                           [exn, eid, tr_cons[pj]])
                k = (i * 32 + j) % 4
                eev = p.op("scalar", lambda e, k=k, pj=pj: e.copy(out=ot[k][:], in_=ptr[pj][:, 0:128]), [etr, st_ev[k]])
                tr_cons[pj] = eev
                st_ev[k] = p.dma("scalar", lambda e, k=k, i=i, j=j: e.dma_start(out=xnT[j * 128:(j + 1) * 128, i * 128:(i + 1) * 128], in_=ot[k][:]),
                                 deps=[eev], slot=f"st{k}")
                last_tr = eev
            xn_cons[sl] = last_tr
    ctx.close()


def emit_casts(p, pairs, pref="PC_"):
    nc = p.nc
    ctx = contextlib.ExitStack()
    wmax = max(c for (_, _, _, c) in pairs)
    cb = [ctx.enter_context(nc.sbuf_tensor(pref + f"cb{i}", [128, wmax], BF16)) for i in range(2)]
    ld = [None, None]
    st = [None, None]
    k = 0
    for (iap, oap, rows, cols) in pairs:
        for r in range(0, rows, 128):
            rsz = min(128, rows - r)
            sl = k % 2
            eld = p.dma("gpsimd", lambda e, sl=sl, rsz=rsz, r=r, iap=iap, cols=cols: e.dma_start(
                out=cb[sl][0:rsz, 0:cols], in_=iap[r:r + rsz, :]), deps=[st[sl]], slot=f"ci{sl}")
            ld[sl] = eld
            st[sl] = p.dma("scalar", lambda e, sl=sl, rsz=rsz, r=r, oap=oap, cols=cols: e.dma_start(
                out=oap[r:r + rsz, :], in_=cb[sl][0:rsz, 0:cols]), deps=[eld], slot=f"co{sl}")
            k += 1
    ctx.close()


CAST_SPECS = [
    ("woT", DIN // NC, DM),
    ("woT2", DM // NC, DM),
    ("kern", DM // NC, DM),
    ("wqT", DM // NC, 6144),
    ("utT", 4032 // NC, DM),
    ("kw1T", 2048 // NC, 2048),
    ("vw1T", 2048 // NC, 2048),
]


def build_l0_prep():
    nc = _mk()
    io = {
        "x": nc.dram_tensor("x", [TSH, DM], F32, kind="ExternalInput").ap(),
        "ident": nc.dram_tensor("ident", [128, 128], BF16, kind="ExternalInput").ap(),
        "xnT": nc.dram_tensor("xnT", [DM, TSH], BF16, kind="ExternalOutput").ap(),
    }
    pairs = []
    for nm, rows, cols in CAST_SPECS:
        iap = nc.dram_tensor("ci_" + nm, [rows, cols], F32, kind="ExternalInput").ap()
        oap = nc.dram_tensor("co_" + nm, [rows, cols], BF16, kind="ExternalOutput").ap()
        pairs.append((iap, oap, rows, cols))
    p = Prog(nc)
    emit_l0_prep(p, io)
    p.barrier()
    emit_casts(p, pairs)
    p.build()
    return nc


def host_prep_inputs(inputs):
    x = np.asarray(inputs["L0_attn_hidden_states"]).reshape(TOK, DM)
    return x


def emit_l0_inproj(p, io, pref="P2_"):
    nc = p.nc
    xnT, w_x, w_z, cw, cb, wxpT = io["xnT"], io["w_x"], io["w_z"], io["cw"], io["cb"], io["wxpT"]
    xc, zs = io["xc"], io["zs"]
    ssp_store_ap = io["ssp_store_ap"]
    ctx = contextlib.ExitStack()
    sb = lambda n, s, d=F32: ctx.enter_context(nc.sbuf_tensor(pref + n, s, d))
    ps = lambda n, s, d=F32: ctx.enter_context(nc.psum_tensor(pref + n, s, d))
    NTC = 16
    TC = 512
    w_sb = sb("w_sb", [128, 32 * 1024], BF16)
    wxp_sb = sb("wxp_sb", [128, 8 * 288], BF16)
    cw_sb = sb("cw_sb", [128, 8 * 4])
    cb_sb = sb("cb_sb", [128, 8])
    xk = [sb(f"xk{i}", [128, 32 * TC], BF16) for i in range(2)]
    xcp = [sb(f"xcp{i}", [128, TC + 3]) for i in range(8)]
    cs1 = sb("cs1", [128, TC])
    cs2 = sb("cs2", [128, TC])
    xcs = [sb(f"xcs{i}", [128, TC], BF16) for i in range(8)]
    zst = [sb(f"zst{i}", [128, TC], BF16) for i in range(2)]
    ssps = [sb(f"ssps{i}", [128, TC]) for i in range(3)]
    pb = [ps(f"pb{i}", [128, 512]) for i in range(4)]

    if True:
        ewx = p.dma("gpsimd", lambda e: e.dma_start(
            out=w_sb[:].rearrange("p (ht e) -> p ht e", ht=32), in_=w_x.rearrange("(ht p) e -> p ht e", p=128)), slot="w")
        ewxp = p.dma("gpsimd", lambda e: e.dma_start(
            out=wxp_sb[:].rearrange("p (dt e) -> p dt e", dt=8), in_=wxpT.rearrange("(dt p) e -> p dt e", p=128)), slot="wxp")
        ecw = p.dma("sync", lambda e: e.dma_start(
            out=cw_sb[:].rearrange("p (dt k) -> p dt k", dt=8), in_=cw.rearrange("(dt p) k -> p dt k", p=128)), slot="cw")
        ecb = p.dma("sync", lambda e: e.dma_start(
            out=cb_sb[:], in_=cb.rearrange("(dt p) -> p dt", p=128), allow_slow_non_contiguous=True), slot="cb")
        einit = []
        for et in range(8):
            einit.append(p.op("vector", lambda e, et=et: e.memset(xcp[et][:, 0:3], 0.0)))
        xk_cons = [None, None]
        bank_cons = [None] * 4
        xcs_cons = [None] * 8
        zst_cons = [None, None]
        ssp_store = [None] * 3
        xc_store = [None] * 8

        def xn_load(tc, sl, deps):
            evs = []
            for q in range(4):
                evs.append(p.dma("sync", lambda e, tc=tc, sl=sl, q=q: e.dma_start(
                    out=xk[sl][:].rearrange("p (ht t) -> p ht t", ht=32)[:, q * 8:(q + 1) * 8, :],
                    in_=xnT.rearrange("(ht p) t -> p ht t", p=128)[:, q * 8:(q + 1) * 8, tc * TC:(tc + 1) * TC]),
                    deps=deps, slot=f"xk{sl}{q}"))
            return evs

        for tc in range(NTC):
            sl = tc % 2
            elds = xn_load(tc, sl, [xk_cons[sl]])
            last_mm_of_chunk = None
            chunk_sil = []
            for et in range(8):
                bk = et % 4
                mm = None
                for ht in range(32):
                    mm = p.op("tensor", lambda e, ht=ht, et=et, sl=sl, bk=bk: e.matmul(
                        out=pb[bk][:],
                        lhsT=w_sb[:, ht * 1024 + et * 128: ht * 1024 + et * 128 + 128],
                        rhs=xk[sl][:, ht * TC:(ht + 1) * TC],
                        start=(ht == 0), stop=(ht == 31)),
                        deps=([ewx, bank_cons[bk]] + elds) if ht == 0 else None, chain=(ht == 0))
                last_mm_of_chunk = mm
                eev = p.op("scalar", lambda e, et=et, bk=bk: e.copy(out=xcp[et][:, 3:3 + TC], in_=pb[bk][:]), [mm])
                bank_cons[bk] = eev
                e1 = p.op("vector", lambda e, et=et: e.tensor_scalar(
                    out=cs1[:], in0=xcp[et][:, 3:3 + TC], scalar1=cw_sb[:, et * 4 + 3:et * 4 + 4],
                    scalar2=cb_sb[:, et:et + 1], op0=ALU.mult, op1=ALU.add), [eev, ecw, ecb])
                e2 = p.op("vector", lambda e, et=et: e.scalar_tensor_tensor(
                    out=cs2[:], in0=xcp[et][:, 2:2 + TC], scalar=cw_sb[:, et * 4 + 2:et * 4 + 3],
                    in1=cs1[:], op0=ALU.mult, op1=ALU.add))
                e3 = p.op("vector", lambda e, et=et: e.scalar_tensor_tensor(
                    out=cs1[:], in0=xcp[et][:, 1:1 + TC], scalar=cw_sb[:, et * 4 + 1:et * 4 + 2],
                    in1=cs2[:], op0=ALU.mult, op1=ALU.add))
                e4 = p.op("vector", lambda e, et=et: e.scalar_tensor_tensor(
                    out=cs2[:], in0=xcp[et][:, 0:0 + TC], scalar=cw_sb[:, et * 4 + 0:et * 4 + 1],
                    in1=cs1[:], op0=ALU.mult, op1=ALU.add))
                if tc == NTC // 2 - 1:
                    etl = p.op("vector", lambda e, et=et: e.memset(xcp[et][:, 0:3], 0.0), [e4])
                else:
                    etl = p.op("vector", lambda e, et=et: e.tensor_copy(out=xcp[et][:, 0:3], in_=xcp[et][:, TC:TC + 3]), [e4])
                esl = p.op("scalar", lambda e, et=et: e.activation(out=xcs[et][:], in_=cs2[:], func=AF.Silu), [e4, xcs_cons[et]])
                chunk_sil.append(esl)
                xc_store[et] = p.dma("scalar", lambda e, et=et, tc=tc: e.dma_start(
                    out=xc[et * 128:(et + 1) * 128, tc * TC:(tc + 1) * TC], in_=xcs[et][:]), deps=[esl], slot=f"xc{et}")
            for mt, msz in ((0, 128), (1, 128), (2, 32)):
                bk = mt % 4
                mm = None
                for kt in range(8):
                    mm = p.op("tensor", lambda e, mt=mt, msz=msz, kt=kt, bk=bk: e.matmul(
                        out=pb[bk][0:msz, :],
                        lhsT=wxp_sb[:, kt * 288 + mt * 128: kt * 288 + mt * 128 + msz],
                        rhs=xcs[kt][:],
                        start=(kt == 0), stop=(kt == 7)),
                        deps=([ewxp, bank_cons[bk], last_mm_of_chunk] + chunk_sil) if kt == 0 else None,
                        chain=(kt == 0))
                eev = p.op("scalar", lambda e, mt=mt, msz=msz, bk=bk: e.copy(out=ssps[mt][0:msz, :], in_=pb[bk][0:msz, :]), [mm, ssp_store[mt]])
                bank_cons[bk] = eev
                ssp_store[mt] = p.dma("scalar", lambda e, mt=mt, msz=msz, tc=tc: e.dma_start(
                    out=ssp_store_ap(mt, msz, tc), in_=ssps[mt][0:msz, :]), deps=[eev], slot=f"sp{mt}")
            for et in range(8):
                xcs_cons[et] = Ev("c_tensor", p.sem_count["c_tensor"])
            xk_cons[sl] = Ev("c_tensor", p.sem_count["c_tensor"])
        ewz = p.dma("gpsimd", lambda e: e.dma_start(
            out=w_sb[:].rearrange("p (ht e) -> p ht e", ht=32), in_=w_z.rearrange("(ht p) e -> p ht e", p=128)),
            deps=[Ev("c_tensor", p.sem_count["c_tensor"])], slot="w")
        for tc in range(NTC):
            sl = tc % 2
            elds = xn_load(tc, sl, [xk_cons[sl]])
            for et in range(8):
                bk = et % 4
                mm = None
                for ht in range(32):
                    mm = p.op("tensor", lambda e, ht=ht, et=et, sl=sl, bk=bk: e.matmul(
                        out=pb[bk][:],
                        lhsT=w_sb[:, ht * 1024 + et * 128: ht * 1024 + et * 128 + 128],
                        rhs=xk[sl][:, ht * TC:(ht + 1) * TC],
                        start=(ht == 0), stop=(ht == 31)),
                        deps=([ewz, bank_cons[bk]] + elds) if ht == 0 else None, chain=(ht == 0))
                zsl = et % 2
                esl = p.op("scalar", lambda e, zsl=zsl, bk=bk: e.activation(out=zst[zsl][:], in_=pb[bk][:], func=AF.Silu), [mm, zst_cons[zsl]])
                bank_cons[bk] = esl
                zst_cons[zsl] = p.dma("scalar", lambda e, zsl=zsl, et=et, tc=tc: e.dma_start(
                    out=zs[et * 128:(et + 1) * 128, tc * TC:(tc + 1) * TC], in_=zst[zsl][:]), deps=[esl], slot=f"zs{zsl}")
            xk_cons[sl] = Ev("c_tensor", p.sem_count["c_tensor"])
    ctx.close()


def build_l0_inproj():
    nc = _mk()
    io = {
        "xnT": nc.dram_tensor("xnT", [DM, TOK], BF16, kind="ExternalInput").ap(),
        "w_x": nc.dram_tensor("w_x", [DM, DSH], F32, kind="ExternalInput").ap(),
        "w_z": nc.dram_tensor("w_z", [DM, DSH], F32, kind="ExternalInput").ap(),
        "cw": nc.dram_tensor("cw", [DSH, 4], F32, kind="ExternalInput").ap(),
        "cb": nc.dram_tensor("cb", [DSH], F32, kind="ExternalInput").ap(),
        "wxpT": nc.dram_tensor("wxpT", [DSH, 288], F32, kind="ExternalInput").ap(),
        "xc": nc.dram_tensor("xc", [DSH, TOK], BF16, kind="ExternalOutput").ap(),
        "zs": nc.dram_tensor("zs", [DSH, TOK], BF16, kind="ExternalOutput").ap(),
    }
    ssp = nc.dram_tensor("ssp", [288, TOK], F32, kind="ExternalOutput").ap()
    io["ssp_store_ap"] = lambda mt, msz, tc: ssp[mt * 128:mt * 128 + msz, tc * 512:(tc + 1) * 512]
    p = Prog(nc)
    emit_l0_inproj(p, io)
    p.build()
    return nc


def emit_l0_ssmmid(p, io, pref="P3_"):
    nc = p.nc
    lnw, dtWT, dtb = io["lnw"], io["dtWT"], io["dtb"]
    ident, identf = io["ident"], io["identf"]
    dt_o, Bn_o, Cn_o = io["dt_o"], io["Bn_o"], io["Cn_o"]
    pst = io.get("pst")
    ssp_my = io.get("ssp_my")
    ctx = contextlib.ExitStack()
    sb = lambda n, s, d=F32: ctx.enter_context(nc.sbuf_tensor(pref + n, s, d))
    ps = lambda n, s, d=F32: ctx.enter_context(nc.psum_tensor(pref + n, s, d))
    NT = TSH // 128
    acc = [sb(f"acc{i}", [128, TSH]) for i in range(3)]
    prt = sb("prt", [128, TSH])
    sid = sb("sid", [128, 128], BF16)
    sidf = sb("sidf", [128, 128], F32)
    ptr = [ps(f"ptr{i}", [128, 512]) for i in range(2)]
    ptb = [ps(f"ptb{i}", [128, 1024], BF16) for i in range(2)]
    stok = [sb(f"stok{i}", [128, 288]) for i in range(2)]
    sq = sb("sq", [128, 288])
    st8 = sb("st8", [128, 12])
    lnb = sb("lnb", [128, 288])
    nrm = sb("nrm", [128, 288])
    nrmb = sb("nrmb", [128, 288], BF16)
    dtr = sb("dtr", [128, 2 * TSH], BF16)
    bcn = sb("bcn", [32, TSH])
    dtw = sb("dtw", [128, 2 * DIN], BF16)
    dtbs = sb("dtbs", [128, 64])
    one = sb("one", [128, 1])
    sp1 = sb("sp1", [128, 512])
    sp2 = sb("sp2", [128, 512])
    sp3 = sb("sp3", [128, 512])
    dttile = [sb(f"dtt{i}", [128, TSH]) for i in range(2)]
    pdt = [ps(f"pdt{i}", [128, 512]) for i in range(2)]

    if True:
        eid = p.dma("sync", lambda e: e.dma_start(out=sid[:], in_=ident), slot="id")
        eidf = p.dma("sync", lambda e: e.dma_start(out=sidf[:], in_=identf), slot="idf")
        elnb = p.dma("gpsimd", lambda e: e.dma_start(out=lnb[:], in_=lnw.partition_broadcast(128).squeeze(1)), slot="lnb")
        edtw = p.dma("gpsimd", lambda e: e.dma_start(out=dtw[:].rearrange("p (kt e) -> p kt e", kt=2),
                                                     in_=dtWT.rearrange("(kt p) e -> p kt e", p=128)), slot="dtw")
        edtb = p.dma("sync", lambda e: e.dma_start(out=dtbs[:], in_=dtb.rearrange("(mt p) -> p mt", p=128),
                                                   allow_slow_non_contiguous=True), slot="dtb")
        eone = p.op("vector", lambda e: e.memset(one[:], 1.0))
        eacc = []
        prt_cons = None
        if ssp_my is not None:
            for rt, rsz in ((0, 128), (1, 128), (2, 32)):
                eacc.append(p.dma("sync", lambda e, rt=rt, rsz=rsz: e.dma_start(
                    out=acc[rt][0:rsz, :], in_=ssp_my[rt * 128:rt * 128 + rsz, :]), slot=f"pa{rt}"))
        else:
            for rt, rsz in ((0, 128), (1, 128), (2, 32)):
                ea = p.dma("sync", lambda e, rt=rt, rsz=rsz: e.dma_start(out=acc[rt][0:rsz, :], in_=pst[0, rt * 128:rt * 128 + rsz, :]), slot=f"pa{rt}")
                ev = None
                for c in range(1, NC):
                    el = p.dma("sync", lambda e, c=c, rt=rt, rsz=rsz: e.dma_start(out=prt[0:rsz, :], in_=pst[c, rt * 128:rt * 128 + rsz, :]),
                               deps=[prt_cons], slot=f"pb{rt}")
                    ev = p.op("vector", lambda e, rt=rt, rsz=rsz: e.tensor_tensor(out=acc[rt][0:rsz, :], in0=acc[rt][0:rsz, :], in1=prt[0:rsz, :], op=ALU.add),
                              [el, ea])
                    prt_cons = ev
                eacc.append(ev)
        dtr_evs = []
        bcn_evs = []
        stok_cons = [None, None]
        ptr_cons = [None, None]
        ptb_cons = [None, None]
        for it in range(NT):
            sl = it % 2
            etr_ev = []
            for rt, rsz in ((0, 128), (1, 128), (2, 32)):
                pj = rt % 2
                etr = p.op("tensor", lambda e, rt=rt, rsz=rsz, it=it, pj=pj: e.transpose(
                    out=ptr[pj][0:128, 0:rsz], in_=acc[rt][0:rsz, it * 128:(it + 1) * 128], identity=sidf[0:rsz, 0:rsz]),
                    [eacc[rt], eidf, stok_cons[sl], ptr_cons[pj]])
                eev = p.op("scalar", lambda e, rt=rt, rsz=rsz, sl=sl, pj=pj: e.copy(
                    out=stok[sl][:, rt * 128:rt * 128 + rsz], in_=ptr[pj][0:128, 0:rsz]), [etr])
                ptr_cons[pj] = eev
                etr_ev.append(eev)
            esq = p.op("scalar", lambda e, sl=sl: e.activation(out=sq[:], in_=stok[sl][:], func=AF.Square), etr_ev)
            segs = ((0, 256, 0), (256, 16, 1), (272, 16, 2))
            einv = []
            for (o, n, ix) in segs:
                er = p.op("vector", lambda e, o=o, n=n, ix=ix: e.tensor_reduce(out=st8[:, ix:ix + 1], in_=sq[:, o:o + n], axis=AX.X, op=ALU.add), [esq])
                em = p.op("vector", lambda e, n=n, ix=ix: e.tensor_scalar(out=st8[:, 3 + ix:4 + ix], in0=st8[:, ix:ix + 1], scalar1=1.0 / n, scalar2=1e-6, op0=ALU.mult, op1=ALU.add))
                es = p.op("scalar", lambda e, ix=ix: e.activation(out=st8[:, 6 + ix:7 + ix], in_=st8[:, 3 + ix:4 + ix], func=AF.Sqrt), [em])
                ei = p.op("vector", lambda e, ix=ix: e.reciprocal(out=st8[:, 9 + ix:10 + ix], in_=st8[:, 6 + ix:7 + ix]), [es])
                einv.append(ei)
            en = []
            for (o, n, ix) in segs:
                en.append(p.op("vector", lambda e, o=o, n=n, ix=ix, sl=sl: e.tensor_scalar(
                    out=nrm[:, o:o + n], in0=stok[sl][:, o:o + n], scalar1=st8[:, 9 + ix:10 + ix], scalar2=None, op0=ALU.mult), [einv[ix]]))
            stok_cons[sl] = en[-1]
            ew = p.op("vector", lambda e: e.tensor_tensor(out=nrm[:], in0=nrm[:], in1=lnb[:], op=ALU.mult), en + [elnb])
            ebf = p.op("vector", lambda e: e.tensor_copy(out=nrmb[:], in_=nrm[:]), [ew])
            for j in range(2):
                pj = j % 2
                etr = p.op("tensor", lambda e, j=j, pj=pj: e.transpose(out=ptb[pj][:, 0:128], in_=nrmb[:, j * 128:(j + 1) * 128], identity=sid[:]), [ebf, ptb_cons[pj], eid])
                ee = p.op("scalar", lambda e, j=j, it=it, pj=pj: e.copy(out=dtr[:, j * TSH + it * 128: j * TSH + (it + 1) * 128], in_=ptb[pj][:, 0:128]), [etr])
                ptb_cons[pj] = ee
                dtr_evs.append(ee)
            etrb = p.op("tensor", lambda e: e.transpose(out=ptr[0][0:32, 0:128], in_=nrm[:, 256:288], identity=sidf[:]), [ew, ptr_cons[0]])
            eeb = p.op("scalar", lambda e, it=it: e.copy(out=bcn[:, it * 128:(it + 1) * 128], in_=ptr[0][0:32, 0:128]), [etrb])
            ptr_cons[0] = eeb
            bcn_evs.append(eeb)
        eb_st = p.dma("sync", lambda e: e.dma_start(out=Bn_o, in_=bcn[0:16, :]), bcn_evs, slot="bno")
        ec_st = p.dma("sync", lambda e: e.dma_start(out=Cn_o, in_=bcn[16:32, :]), bcn_evs, slot="cno")
        dt_cons = [None, None]
        pdt_cons = [None, None]
        prev_ed = None
        for mt in range(DIN // 128):
            sl = mt % 2
            for tch in range(2):
                bk = tch
                mm = None
                for kt in range(2):
                    mm = p.op("tensor", lambda e, mt=mt, tch=tch, kt=kt, bk=bk: e.matmul(
                        out=pdt[bk][:], lhsT=dtw[:, kt * DIN + mt * 128: kt * DIN + (mt + 1) * 128],
                        rhs=dtr[:, kt * TSH + tch * 512: kt * TSH + (tch + 1) * 512],
                        start=(kt == 0), stop=(kt == 1)),
                        deps=([edtw, pdt_cons[bk]] + dtr_evs) if kt == 0 else None, chain=(kt == 0))
                er = p.op("scalar", lambda e, mt=mt, bk=bk: e.activation(out=sp1[:], in_=pdt[bk][:], func=AF.Relu, bias=dtbs[:, mt:mt + 1]), [mm, edtb, prev_ed])
                ea = p.op("scalar", lambda e, mt=mt, bk=bk: e.activation(out=sp2[:], in_=pdt[bk][:], func=AF.Abs, bias=dtbs[:, mt:mt + 1]))
                pdt_cons[bk] = ea
                ee = p.op("scalar", lambda e: e.activation(out=sp3[:], in_=sp2[:], func=AF.Exp, scale=-1.0))
                el = p.op("scalar", lambda e: e.activation(out=sp2[:], in_=sp3[:], func=AF.Ln, bias=one[:, 0:1]), [eone])
                ed = p.op("vector", lambda e, sl=sl, tch=tch: e.tensor_tensor(out=dttile[sl][:, tch * 512:(tch + 1) * 512], in0=sp1[:], in1=sp2[:], op=ALU.add),
                          [el, dt_cons[sl]])
                prev_ed = ed
                if tch == 1:
                    dt_cons[sl] = p.dma("scalar", lambda e, mt=mt, sl=sl: e.dma_start(
                        out=dt_o[mt * 128:(mt + 1) * 128, :], in_=dttile[sl][:]), deps=[ed], slot=f"dt{sl}")
    ctx.close()


def build_l0_ssmmid():
    nc = _mk()
    io = {
        "pst": nc.dram_tensor("pst", [NC, 288, TSH], F32, kind="ExternalInput").ap(),
        "lnw": nc.dram_tensor("lnw", [1, 288], F32, kind="ExternalInput").ap(),
        "dtWT": nc.dram_tensor("dtWT", [256, DIN], F32, kind="ExternalInput").ap(),
        "dtb": nc.dram_tensor("dtb", [DIN], F32, kind="ExternalInput").ap(),
        "ident": nc.dram_tensor("ident", [128, 128], BF16, kind="ExternalInput").ap(),
        "identf": nc.dram_tensor("identf", [128, 128], F32, kind="ExternalInput").ap(),
        "dt_o": nc.dram_tensor("dt_o", [DIN, TSH], F32, kind="ExternalOutput").ap(),
        "Bn_o": nc.dram_tensor("Bn_o", [NST, TSH], F32, kind="ExternalOutput").ap(),
        "Cn_o": nc.dram_tensor("Cn_o", [NST, TSH], F32, kind="ExternalOutput").ap(),
    }
    p = Prog(nc)
    emit_l0_ssmmid(p, io)
    p.build()
    return nc


def emit_l0_ffnbuild(p, io, pref="P4_"):
    nc = p.nc
    ISH = DM // NC
    RT = 4032
    vh, st, utT, vtop, utopT = io["vh"], io["st"], io["utT"], io["vtop"], io["utopT"]
    stop_, alpha, beta, vhr, urT, srd, rmat = io["stop_"], io["alpha"], io["beta"], io["vhr"], io["urT"], io["srd"], io["rmat"]
    weT = io["weT"]
    ctx = contextlib.ExitStack()
    sb = lambda n, s, d=F32: ctx.enter_context(nc.sbuf_tensor(pref + n, s, d))
    ps = lambda n, s, d=F32: ctx.enter_context(nc.psum_tensor(pref + n, s, d))
    vh_sb = sb("vh_sb", [128, 32 * ISH], BF16)
    st_sb = sb("st_sb", [128, 32])
    vtop_sb = sb("vtop_sb", [64, ISH], BF16)
    sh_sb = sb("sh_sb", [64, 4])
    vhr_sb = sb("vhr_sb", [4, ISH], BF16)
    ur_sb = sb("ur_sb", [4, DM], BF16)
    srd_sb = sb("srd_sb", [4, 3])
    r_sb = sb("r_sb", [4, 4], BF16)
    mt_sb = sb("mt_sb", [4, DM], BF16)
    rhs_t = [sb(f"rhs{i}", [128, 512], BF16) for i in range(2)]
    rtop = sb("rtop", [64, 512], BF16)
    outt = [sb(f"outt{i}", [128, 512], BF16) for i in range(4)]
    pmm = [ps(f"pmm{i}", [128, 512]) for i in range(5)]

    if True:
        evh1 = p.dma("gpsimd", lambda e: e.dma_start(
            out=vh_sb[:].rearrange("p (kt i) -> p kt i", kt=32)[:, 0:31, :],
            in_=vh[0:3968, :].rearrange("(kt p) i -> p kt i", p=128)), slot="vh")
        evh2 = p.dma("gpsimd", lambda e: e.dma_start(
            out=vh_sb[0:64, :].rearrange("p (kt i) -> p kt i", kt=32)[:, 31:32, :],
            in_=vh[3968:4032, :].rearrange("(kt p) i -> p kt i", p=64)), slot="vh")
        est1 = p.dma("sync", lambda e: e.dma_start(out=st_sb[:, 0:31], in_=st[0:3968].rearrange("(kt p) -> p kt", p=128),
                                                   allow_slow_non_contiguous=True), slot="st")
        est2 = p.dma("sync", lambda e: e.dma_start(out=st_sb[0:64, 31:32], in_=st[3968:4032].rearrange("(kt p) -> p kt", p=64),
                                                   allow_slow_non_contiguous=True), slot="st")
        evt = p.dma("gpsimd", lambda e: e.dma_start(out=vtop_sb[:], in_=vtop), slot="vt")
        esh = p.dma("sync", lambda e: e.dma_start(out=sh_sb[:, 0:1], in_=stop_), slot="sh")
        esa = p.dma("sync", lambda e: e.dma_start(out=sh_sb[:, 1:2], in_=alpha), slot="sh2")
        esb2 = p.dma("sync", lambda e: e.dma_start(out=sh_sb[:, 2:3], in_=beta), slot="sh3")
        evr = p.dma("gpsimd", lambda e: e.dma_start(out=vhr_sb[:], in_=vhr), slot="vr")
        eur = p.dma("gpsimd", lambda e: e.dma_start(out=ur_sb[:], in_=urT), slot="ur")
        esd = p.dma("sync", lambda e: e.dma_start(out=srd_sb[:, 0:2], in_=srd), slot="sd")
        erm = p.dma("gpsimd", lambda e: e.dma_start(out=r_sb[:], in_=rmat), slot="rm")
        escl = []
        for kt in range(32):
            kz = 128 if kt < 31 else 64
            escl.append(p.op("vector", lambda e, kt=kt, kz=kz: e.tensor_scalar(
                out=vh_sb[0:kz, kt * ISH:(kt + 1) * ISH], in0=vh_sb[0:kz, kt * ISH:(kt + 1) * ISH],
                scalar1=st_sb[0:kz, kt:kt + 1], scalar2=None, op0=ALU.mult), [evh1, evh2, est1, est2]))
        e1 = p.op("vector", lambda e: e.tensor_tensor(out=sh_sb[:, 3:4], in0=sh_sb[:, 0:1], in1=sh_sb[:, 1:2], op=ALU.mult), [esh, esa, esb2])
        e2 = p.op("vector", lambda e: e.tensor_tensor(out=sh_sb[:, 3:4], in0=sh_sb[:, 3:4], in1=sh_sb[:, 2:3], op=ALU.add))
        e3 = p.op("scalar", lambda e: e.activation(out=sh_sb[:, 3:4], in_=sh_sb[:, 3:4], func=AF.Relu), [e2])
        e4 = p.op("vector", lambda e: e.tensor_scalar(out=vtop_sb[:], in0=vtop_sb[:], scalar1=sh_sb[:, 3:4], scalar2=None, op0=ALU.mult), [e3, evt])
        e5 = p.op("scalar", lambda e: e.activation(out=srd_sb[:, 2:3], in_=srd_sb[:, 1:2], func=AF.Relu), [esd])
        e6 = p.op("vector", lambda e: e.tensor_tensor(out=srd_sb[:, 2:3], in0=srd_sb[:, 2:3], in1=srd_sb[:, 0:1], op=ALU.mult), [e5])
        e7 = p.op("vector", lambda e: e.tensor_scalar(out=ur_sb[:], in0=ur_sb[:], scalar1=srd_sb[:, 2:3], scalar2=None, op0=ALU.mult), [e6, eur])
        emt_ev = []
        for oc in range(8):
            emm = p.op("tensor", lambda e, oc=oc: e.matmul(out=pmm[4][0:4, :], lhsT=r_sb[:], rhs=ur_sb[:, oc * 512:(oc + 1) * 512],
                                                           start=True, stop=True), [e7, erm] + ([emt_ev[-1]] if emt_ev else []))
            emt_ev.append(p.op("scalar", lambda e, oc=oc: e.copy(out=mt_sb[:, oc * 512:(oc + 1) * 512], in_=pmm[4][0:4, :]), [emm]))
        rhs_cons = [None, None]
        outt_cons = [None] * 4
        bank_cons = [None] * 4
        rt_cons = None
        for oc in range(8):
            mm_last = [None] * 4
            for kt in range(32):
                sl = kt % 2
                ksz = 128 if kt < 31 else 64
                eld = p.dma("gpsimd", lambda e, kt=kt, oc=oc, sl=sl, ksz=ksz: e.dma_start(
                    out=rhs_t[sl][0:ksz, :], in_=utT[kt * 128:kt * 128 + ksz, oc * 512:(oc + 1) * 512]),
                    deps=[rhs_cons[sl]], slot=f"r{sl}")
                for mt in range(4):
                    mm_last[mt] = p.op("tensor", lambda e, kt=kt, mt=mt, sl=sl, ksz=ksz: e.matmul(
                        out=pmm[mt][:], lhsT=vh_sb[0:ksz, kt * ISH + mt * 128: kt * ISH + (mt + 1) * 128],
                        rhs=rhs_t[sl][0:ksz, :], start=(kt == 0), stop=False),
                        deps=([eld] + escl + [bank_cons[mt]]) if kt == 0 else [eld], chain=(kt == 0))
                rhs_cons[sl] = Ev("c_tensor", p.sem_count["c_tensor"])
            sl = 0
            eldt = p.dma("gpsimd", lambda e, oc=oc: e.dma_start(out=rtop[:], in_=utopT[:, oc * 512:(oc + 1) * 512]),
                         deps=[rt_cons], slot="rt")
            for mt in range(4):
                mm_last[mt] = p.op("tensor", lambda e, mt=mt, oc=oc: e.matmul(
                    out=pmm[mt][:], lhsT=vtop_sb[:, mt * 128:(mt + 1) * 128], rhs=rtop[:],
                    start=False, stop=False), deps=[eldt, e4] if mt == 0 else None, chain=(mt == 0))
            rt_cons = Ev("c_tensor", p.sem_count["c_tensor"])
            for mt in range(4):
                mm_last[mt] = p.op("tensor", lambda e, mt=mt, oc=oc: e.matmul(
                    out=pmm[mt][:], lhsT=vhr_sb[:, mt * 128:(mt + 1) * 128], rhs=mt_sb[:, oc * 512:(oc + 1) * 512],
                    start=False, stop=True), deps=([evr] + emt_ev) if mt == 0 else None, chain=(mt == 0))
            for mt in range(4):
                eev = p.op("scalar", lambda e, mt=mt: e.copy(out=outt[mt][:], in_=pmm[mt][:]), [mm_last[mt], outt_cons[mt]])
                bank_cons[mt] = eev
                outt_cons[mt] = p.dma("scalar", lambda e, mt=mt, oc=oc: e.dma_start(
                    out=weT[mt * 128:(mt + 1) * 128, oc * 512:(oc + 1) * 512], in_=outt[mt][:]), deps=[eev], slot=f"o{mt}")
    ctx.close()


def build_l0_ffnbuild():
    nc = _mk()
    ISH = DM // NC
    RT = 4032
    io = {
        "vh": nc.dram_tensor("vh", [RT, ISH], F32, kind="ExternalInput").ap(),
        "st": nc.dram_tensor("st", [RT], F32, kind="ExternalInput").ap(),
        "utT": nc.dram_tensor("utT", [RT, DM], BF16, kind="ExternalInput").ap(),
        "vtop": nc.dram_tensor("vtop", [64, ISH], F32, kind="ExternalInput").ap(),
        "utopT": nc.dram_tensor("utopT", [64, DM], F32, kind="ExternalInput").ap(),
        "stop_": nc.dram_tensor("stop_", [64, 1], F32, kind="ExternalInput").ap(),
        "alpha": nc.dram_tensor("alpha", [64, 1], F32, kind="ExternalInput").ap(),
        "beta": nc.dram_tensor("beta", [64, 1], F32, kind="ExternalInput").ap(),
        "vhr": nc.dram_tensor("vhr", [4, ISH], F32, kind="ExternalInput").ap(),
        "urT": nc.dram_tensor("urT", [4, DM], F32, kind="ExternalInput").ap(),
        "srd": nc.dram_tensor("srd", [4, 2], F32, kind="ExternalInput").ap(),
        "rmat": nc.dram_tensor("rmat", [4, 4], F32, kind="ExternalInput").ap(),
        "weT": nc.dram_tensor("weT", [ISH, DM], BF16, kind="ExternalOutput").ap(),
    }
    p = Prog(nc)
    emit_l0_ffnbuild(p, io)
    p.build()
    return nc


def emit_l0_scan(p, io, pref="P5_"):
    nc = p.nc
    xc, zsd, alog, dpar, yg = io["xc"], io["zsd"], io["alog"], io["dpar"], io["yg"]
    dt_ap = io["dt_ap"]
    bn_ap = io["bn_ap"]
    cn_ap = io["cn_ap"]
    ctx = contextlib.ExitStack()
    sb = lambda n, s, d=F32: ctx.enter_context(nc.sbuf_tensor(pref + n, s, d))
    TC = 512
    NCH = TOK // TC
    a_sb = sb("a_sb", [128, 8 * NST])
    d_sb = sb("d_sb", [128, 8])
    bb = sb("bb", [128, NST * TC])
    cb2 = sb("cb2", [128, NST * TC])
    dtc = sb("dtc", [128, TC])
    xcc = sb("xcc", [128, TC], BF16)
    zsc = sb("zsc", [128, TC], BF16)
    ub = sb("ub", [128, TC])
    dab = sb("dab", [128, NST * TC])
    dbx = sb("dbx", [128, NST * TC])
    hb = sb("hb", [128, NST * TC])
    tmp16 = sb("tmp16", [128, NST])
    hinit = sb("hinit", [128, 8 * NST])
    yb = sb("yb", [128, TC])
    g1 = sb("g1", [128, TC])
    ygt = sb("ygt", [128, TC], BF16)

    if True:
        ea = p.dma("sync", lambda e: e.dma_start(out=a_sb[:].rearrange("p (dt n) -> p dt n", dt=8),
                                                 in_=alog.rearrange("(dt p) n -> p dt n", p=128)), slot="al")
        eax = p.op("scalar", lambda e: e.activation(out=a_sb[:], in_=a_sb[:], func=AF.Exp), [ea])
        ean = p.op("vector", lambda e: e.tensor_scalar(out=a_sb[:], in0=a_sb[:], scalar1=-1.0, scalar2=None, op0=ALU.mult), [eax])
        ed = p.dma("sync", lambda e: e.dma_start(out=d_sb[:], in_=dpar.rearrange("(dt p) -> p dt", p=128),
                                                 allow_slow_non_contiguous=True), slot="dp")
        bc_cons = None
        ld_cons = [[], [], []]
        st_cons = None
        buf_cons = None
        for b in range(2):
            ehz = p.op("vector", lambda e: e.memset(hinit[:], 0.0), [buf_cons])
            for ch in range(NCH // 2):
                off = b * (TOK // 2) + ch * TC
                ebb = p.dma("gpsimd", lambda e, off=off: e.dma_start(
                    out=bb[:].rearrange("p (n t) -> p n t", n=NST),
                    in_=bn_ap(off).partition_broadcast(128)), deps=[bc_cons], slot="bb")
                ecb = p.dma("gpsimd", lambda e, off=off: e.dma_start(
                    out=cb2[:].rearrange("p (n t) -> p n t", n=NST),
                    in_=cn_ap(off).partition_broadcast(128)), deps=[bc_cons], slot="cb")
                for dt_i in range(8):
                    ro = dt_i * 128
                    e_dt = p.dma("sync", lambda e, ro=ro, off=off: e.dma_start(out=dtc[:], in_=dt_ap(e, p, ro, off)),
                                 deps=ld_cons[0], slot="dt")
                    e_xc = p.dma("sync", lambda e, ro=ro, off=off: e.dma_start(out=xcc[:], in_=xc[ro:ro + 128, off:off + TC]),
                                 deps=ld_cons[1], slot="xc")
                    e_zs = p.dma("sync", lambda e, ro=ro, off=off: e.dma_start(out=zsc[:], in_=zsd[ro:ro + 128, off:off + TC]),
                                 deps=ld_cons[2], slot="zs")
                    edas = []
                    for n in range(NST):
                        edas.append(p.op("scalar", lambda e, n=n, dt_i=dt_i: e.activation(
                            out=dab[:, n * TC:(n + 1) * TC], in_=dtc[:], func=AF.Exp,
                            scale=a_sb[:, dt_i * NST + n: dt_i * NST + n + 1]), [e_dt, ean, buf_cons]))
                    eu = p.op("vector", lambda e: e.tensor_tensor(out=ub[:], in0=dtc[:], in1=xcc[:], op=ALU.mult), [e_dt, e_xc, buf_cons])
                    edbx = p.op("vector", lambda e: e.tensor_tensor(
                        out=dbx[:].rearrange("p (n t) -> p n t", n=NST),
                        in0=bb[:].rearrange("p (n t) -> p n t", n=NST),
                        in1=ub[:].unsqueeze(1).broadcast_to([128, NST, TC]), op=ALU.mult), [eu, ebb])
                    ef1 = p.op("vector", lambda e, dt_i=dt_i: e.tensor_tensor(
                        out=tmp16[:].unsqueeze(2), in0=dab[:].rearrange("p (n t) -> p n t", n=NST)[:, :, 0:1],
                        in1=hinit[:, dt_i * NST:(dt_i + 1) * NST].unsqueeze(2), op=ALU.mult), edas + [edbx])
                    ef2 = p.op("vector", lambda e: e.tensor_tensor(
                        out=dbx[:].rearrange("p (n t) -> p n t", n=NST)[:, :, 0:1],
                        in0=dbx[:].rearrange("p (n t) -> p n t", n=NST)[:, :, 0:1],
                        in1=tmp16[:].unsqueeze(2), op=ALU.add))
                    ef3 = p.op("vector", lambda e: e.memset(dab[:].rearrange("p (n t) -> p n t", n=NST)[:, :, 0:1], 0.0))
                    esc = p.op("vector", lambda e: e.tensor_tensor_scan(
                        out=hb[:], data0=dab[:], data1=dbx[:], initial=0.0, op0=ALU.mult, op1=ALU.add), [ef3, ef2])
                    eh = p.op("vector", lambda e, dt_i=dt_i: e.tensor_copy(
                        out=hinit[:, dt_i * NST:(dt_i + 1) * NST].unsqueeze(2),
                        in_=hb[:].rearrange("p (n t) -> p n t", n=NST)[:, :, TC - 1:TC]), [esc])
                    ep = p.op("vector", lambda e: e.tensor_tensor(
                        out=dbx[:].rearrange("p (n t) -> p n t", n=NST),
                        in0=hb[:].rearrange("p (n t) -> p n t", n=NST),
                        in1=cb2[:].rearrange("p (n t) -> p n t", n=NST), op=ALU.mult), [esc, ecb])
                    ey = p.op("vector", lambda e: e.tensor_reduce(
                        out=yb[:], in_=dbx[:].rearrange("p (n t) -> p t n", n=NST), axis=AX.X, op=ALU.add), [ep])
                    eg1 = p.op("vector", lambda e, dt_i=dt_i: e.scalar_tensor_tensor(
                        out=g1[:], in0=xcc[:], scalar=d_sb[:, dt_i:dt_i + 1], in1=yb[:], op0=ALU.mult, op1=ALU.add), [ey, ed])
                    eg2 = p.op("vector", lambda e: e.tensor_tensor(out=ygt[:], in0=g1[:], in1=zsc[:], op=ALU.mult), [e_zs, st_cons])
                    st_cons = p.dma("scalar", lambda e, ro=ro, off=off: e.dma_start(
                        out=yg[ro:ro + 128, off:off + TC], in_=ygt[:]), deps=[eg2], slot="st")
                    ld_cons = [[eu, edas[-1]], [eg1], [eg2]]
                    buf_cons = eg1
                bc_cons = Ev("c_vector", p.sem_count["c_vector"])
    ctx.close()


def build_l0_scan():
    nc = _mk()
    io = {
        "xc": nc.dram_tensor("xc", [DSH, TOK], BF16, kind="ExternalInput").ap(),
        "zsd": nc.dram_tensor("zsd", [DSH, TOK], BF16, kind="ExternalInput").ap(),
        "alog": nc.dram_tensor("alog", [DSH, NST], F32, kind="ExternalInput").ap(),
        "dpar": nc.dram_tensor("dpar", [DSH], F32, kind="ExternalInput").ap(),
        "yg": nc.dram_tensor("yg", [DSH, TOK], BF16, kind="ExternalOutput").ap(),
    }
    dtd = nc.dram_tensor("dtd", [DSH, TOK], F32, kind="ExternalInput").ap()
    Bn = nc.dram_tensor("Bn", [NST, TOK], F32, kind="ExternalInput").ap()
    Cn = nc.dram_tensor("Cn", [NST, TOK], F32, kind="ExternalInput").ap()
    io["dt_ap"] = lambda e, p, ro, off: dtd[ro:ro + 128, off:off + 512]
    io["bn_ap"] = lambda off: Bn[:, off:off + 512]
    io["cn_ap"] = lambda off: Cn[:, off:off + 512]
    p = Prog(nc)
    emit_l0_scan(p, io)
    p.build()
    return nc


def emit_l0_tail(p, io, pref="P6_"):
    nc = p.nc
    woT, weT, xnT, x, fbias, ident = io["woT"], io["weT"], io["xnT"], io["x"], io["fbias"], io["ident"]
    xnew, xn1T = io["xnew"], io["xn1T"]
    ygd_ap = io["ygd_ap"]
    ctx = contextlib.ExitStack()
    sb = lambda n, s, d=F32: ctx.enter_context(nc.sbuf_tensor(pref + n, s, d))
    ps = lambda n, s, d=F32: ctx.enter_context(nc.psum_tensor(pref + n, s, d))
    yg_tg = sb("yg_tg", [128, 64 * 512], BF16)
    xn_tg = sb("xn_tg", [128, 32 * 512], BF16)
    wot = [sb(f"wot{i}", [128, 512], BF16) for i in range(4)]
    wet = [sb(f"wet{i}", [128, 512], BF16) for i in range(4)]
    bias_bc = sb("bias_bc", [128, DM])
    xt = [sb(f"xt{i}", [128, 512]) for i in range(4)]
    a_ev = [sb(f"a_ev{i}", [128, 512]) for i in range(4)]
    xo = [sb(f"xo{i}", [128, 512]) for i in range(4)]
    sqs = sb("sqs", [128, 512])
    stacc = sb("stacc", [128, 16])
    stmp = sb("stmp", [128, 8])
    pa = [ps(f"pa{i}", [128, 512]) for i in range(4)]
    pf = [ps(f"pf{i}", [128, 512]) for i in range(4)]
    xnw = [sb(f"xnw{i}", [128, DM]) for i in range(2)]
    stat = sb("stat", [128, 8])
    x1b = [sb(f"x1b{i}", [128, DM], BF16) for i in range(2)]
    sid = sb("sid", [128, 128], BF16)
    otb = [sb(f"otb{i}", [128, 1024], BF16) for i in range(2)]
    scst = sb("scst", [128, 1])
    ptb = [pa[0][:].bitcast(BF16), pa[1][:].bitcast(BF16)]

    if True:
        eid = p.dma("sync", lambda e: e.dma_start(out=sid[:], in_=ident), slot="id")
        ebias = p.dma("gpsimd", lambda e: e.dma_start(out=bias_bc[:], in_=fbias.partition_broadcast(128).squeeze(1)), slot="fb")
        ecst = p.op("vector", lambda e: e.memset(scst[:], 1e-5))
        eacc0 = p.op("vector", lambda e: e.memset(stacc[:], 0.0))
        yg_cons = None
        xn_cons = None
        w_cons = [None] * 4
        we_cons = [None] * 4
        xt_cons = [None] * 4
        pa_cons = [None] * 4
        pf_cons = [None] * 4
        aev_cons = [None] * 4
        xo_cons = [None] * 4
        st_cons = [None] * 4
        stq = [None] * 4
        acc_done = [None] * 8
        row_done = [[None] * 4 for _ in range(2)]
        for tg in range(2):
            eygs = []
            for q in range(8):
                eygs.append(p.dma("sync", lambda e, tg=tg, q=q: e.dma_start(
                    out=yg_tg[:].rearrange("p (kt t) -> p kt t", kt=64)[:, q * 8:(q + 1) * 8, :],
                    in_=ygd_ap(e, p, q, tg)),
                    deps=[yg_cons], slot=f"yg{q}"))
            exns = []
            for q in range(4):
                exns.append(p.dma("sync", lambda e, tg=tg, q=q: e.dma_start(
                    out=xn_tg[:].rearrange("p (kt t) -> p kt t", kt=32)[:, q * 8:(q + 1) * 8, :],
                    in_=xnT.rearrange("(kt p) t -> p kt t", p=128)[:, q * 8:(q + 1) * 8, tg * 512:(tg + 1) * 512]),
                    deps=[xn_cons], slot=f"xn{q}"))
            for ch in range(8):
                mma = [None] * 4
                for kt in range(64):
                    sl = kt % 4
                    qn = "gpsimd" if kt % 2 == 0 else "sync"
                    ew = p.dma(qn, lambda e, kt=kt, ch=ch, sl=sl: e.dma_start(
                        out=wot[sl][:], in_=woT[kt * 128:(kt + 1) * 128, ch * 512:(ch + 1) * 512]),
                        deps=[w_cons[sl]], slot=f"w{sl}")
                    for tt in range(4):
                        mma[tt] = p.op("tensor", lambda e, kt=kt, tt=tt, sl=sl: e.matmul(
                            out=pa[tt][:], lhsT=yg_tg[:, kt * 512 + tt * 128: kt * 512 + (tt + 1) * 128],
                            rhs=wot[sl][:], start=(kt == 0), stop=(kt == 63)),
                            deps=([ew, pa_cons[tt]] + eygs) if kt == 0 else [ew], chain=(kt == 0))
                    w_cons[sl] = Ev("c_tensor", p.sem_count["c_tensor"])
                mmf = [None] * 4
                for kt in range(32):
                    sl = kt % 4
                    qn = "scalar" if kt % 2 == 0 else "gpsimd"
                    ew = p.dma(qn, lambda e, kt=kt, ch=ch, sl=sl: e.dma_start(
                        out=wet[sl][:], in_=weT[kt * 128:(kt + 1) * 128, ch * 512:(ch + 1) * 512]),
                        deps=[we_cons[sl]], slot=f"v{sl}")
                    for tt in range(4):
                        mmf[tt] = p.op("tensor", lambda e, kt=kt, tt=tt, sl=sl: e.matmul(
                            out=pf[tt][:], lhsT=xn_tg[:, kt * 512 + tt * 128: kt * 512 + (tt + 1) * 128],
                            rhs=wet[sl][:], start=(kt == 0), stop=(kt == 31)),
                            deps=([ew, pf_cons[tt]] + exns) if kt == 0 else [ew], chain=(kt == 0))
                    we_cons[sl] = Ev("c_tensor", p.sem_count["c_tensor"])
                for tt in range(4):
                    i = tg * 4 + tt
                    ex = p.dma("sync", lambda e, tg=tg, tt=tt, ch=ch: e.dma_start(
                        out=xt[tt][:], in_=x[tg * 512 + tt * 128: tg * 512 + (tt + 1) * 128, ch * 512:(ch + 1) * 512]),
                        deps=[xt_cons[tt]], slot=f"x{tt}")
                    ea = p.op("scalar", lambda e, tt=tt: e.copy(out=a_ev[tt][:], in_=pa[tt][:]), [mma[tt], aev_cons[tt]])
                    pa_cons[tt] = ea
                    e1 = p.op("vector", lambda e, tt=tt: e.tensor_tensor(out=xo[tt][:], in0=a_ev[tt][:], in1=pf[tt][:], op=ALU.add), [ea, mmf[tt], xo_cons[tt], st_cons[tt]])
                    pf_cons[tt] = e1
                    aev_cons[tt] = e1
                    e2 = p.op("vector", lambda e, tt=tt: e.tensor_tensor(out=xo[tt][:], in0=xo[tt][:], in1=xt[tt][:], op=ALU.add), [ex])
                    e3 = p.op("vector", lambda e, tt=tt, ch=ch: e.tensor_tensor(out=xo[tt][:], in0=xo[tt][:], in1=bias_bc[:, ch * 512:(ch + 1) * 512], op=ALU.add), [ebias])
                    xt_cons[tt] = e2
                    est = p.dma("scalar", lambda e, tg=tg, tt=tt, ch=ch: e.dma_start(
                        out=xnew[tg * 512 + tt * 128: tg * 512 + (tt + 1) * 128, ch * 512:(ch + 1) * 512], in_=xo[tt][:]),
                        deps=[e3], slot=f"xo{tt}")
                    st_cons[tt] = est
                    ev1 = p.op("vector", lambda e, tt=tt: e.tensor_reduce(out=stmp[:, tt:tt + 1], in_=xo[tt][:], axis=AX.X, op=ALU.add), [e3])
                    ev2 = p.op("vector", lambda e, i=i, tt=tt: e.tensor_tensor(out=stacc[:, 2 * i:2 * i + 1], in0=stacc[:, 2 * i:2 * i + 1], in1=stmp[:, tt:tt + 1], op=ALU.add), [ev1, eacc0])
                    es1 = p.op("scalar", lambda e, tt=tt: e.activation(out=sqs[:], in_=xo[tt][:], func=AF.Square, accum_out=stmp[:, 4 + tt:5 + tt]), [e3, stq[tt]])
                    ev3 = p.op("vector", lambda e, i=i, tt=tt: e.tensor_tensor(out=stacc[:, 2 * i + 1:2 * i + 2], in0=stacc[:, 2 * i + 1:2 * i + 2], in1=stmp[:, 4 + tt:5 + tt], op=ALU.add), [es1, eacc0])
                    stq[tt] = ev3
                    xo_cons[tt] = ev3
                    acc_done[i] = ev3
                    row_done[tg][tt] = est
            yg_cons = Ev("c_tensor", p.sem_count["c_tensor"])
            xn_cons = yg_cons
        cons2 = [None, None]
        x1_cons = [None, None]
        tr_cons = [None, None]
        ot_cons = [None, None]
        for i in range(TSH // 128):
            sl = i % 2
            xb, xnb = xnw[sl], x1b[sl]
            eld = p.dma("sync", lambda e, xb=xb, i=i: e.dma_start(out=xb[:], in_=xnew[i * 128:(i + 1) * 128, :]),
                        deps=[cons2[sl], row_done[i // 4][i % 4]], slot=f"r{sl}")
            em = p.op("vector", lambda e, i=i: e.tensor_scalar(out=stat[:, 2:3], in0=stacc[:, 2 * i:2 * i + 1], scalar1=1.0 / DM, scalar2=None, op0=ALU.mult), [acc_done[i]])
            em2 = p.op("vector", lambda e, i=i: e.tensor_scalar(out=stat[:, 3:4], in0=stacc[:, 2 * i + 1:2 * i + 2], scalar1=1.0 / DM, scalar2=None, op0=ALU.mult))
            emm = p.op("vector", lambda e: e.tensor_tensor(out=stat[:, 4:5], in0=stat[:, 2:3], in1=stat[:, 2:3], op=ALU.mult))
            evar = p.op("vector", lambda e: e.tensor_tensor(out=stat[:, 5:6], in0=stat[:, 3:4], in1=stat[:, 4:5], op=ALU.subtract))
            esd = p.op("scalar", lambda e: e.activation(out=stat[:, 6:7], in_=stat[:, 5:6], func=AF.Sqrt, bias=scst[:, 0:1]), [evar, ecst])
            ers = p.op("vector", lambda e: e.reciprocal(out=stat[:, 7:8], in_=stat[:, 6:7]), [esd])
            exn = p.op("vector", lambda e, xb=xb, xnb=xnb: e.tensor_scalar(out=xnb[:], in0=xb[:], scalar1=stat[:, 2:3], scalar2=stat[:, 7:8], op0=ALU.subtract, op1=ALU.mult), [x1_cons[sl], eld, ers])
            cons2[sl] = exn
            last = None
            for jg in range(4):
                og = jg % 2
                evs = []
                for jj in range(8):
                    j = jg * 8 + jj
                    pj = j % 2
                    etr = p.op("tensor", lambda e, j=j, pj=pj, xnb=xnb: e.transpose(out=ptb[pj][:, 0:128], in_=xnb[:, j * 128:(j + 1) * 128], identity=sid[:]),
                               [exn, eid, tr_cons[pj], pa_cons[pj]])
                    eev = p.op("scalar", lambda e, jj=jj, og=og, pj=pj: e.copy(out=otb[og][:, jj * 128:(jj + 1) * 128], in_=ptb[pj][:, 0:128]),
                               [etr, ot_cons[og] if jj == 0 else None])
                    pa_cons[pj] = eev
                    tr_cons[pj] = eev
                    evs.append(eev)
                    last = eev
                ot_cons[og] = p.dma("scalar", lambda e, jg=jg, og=og, i=i: e.dma_start(
                    out=xn1T.rearrange("(jt p) t -> p jt t", p=128)[:, jg * 8:(jg + 1) * 8, i * 128:(i + 1) * 128],
                    in_=otb[og][:].rearrange("p (jt t) -> p jt t", jt=8)), deps=evs, slot=f"s{og}")
            x1_cons[sl] = last
    ctx.close()


def build_l0_tail():
    nc = _mk()
    io = {
        "woT": nc.dram_tensor("woT", [DIN, DM], BF16, kind="ExternalInput").ap(),
        "weT": nc.dram_tensor("weT", [DM, DM], BF16, kind="ExternalInput").ap(),
        "xnT": nc.dram_tensor("xnT", [DM, TSH], BF16, kind="ExternalInput").ap(),
        "x": nc.dram_tensor("x", [TSH, DM], F32, kind="ExternalInput").ap(),
        "fbias": nc.dram_tensor("fbias", [1, DM], F32, kind="ExternalInput").ap(),
        "ident": nc.dram_tensor("ident", [128, 128], BF16, kind="ExternalInput").ap(),
        "xnew": nc.dram_tensor("xnew", [TSH, DM], F32, kind="ExternalOutput").ap(),
        "xn1T": nc.dram_tensor("xn1T", [DM, TSH], BF16, kind="ExternalOutput").ap(),
    }
    ygd = nc.dram_tensor("ygd", [DIN, TSH], BF16, kind="ExternalInput").ap()
    io["ygd_ap"] = lambda e, p, q, tg: ygd.rearrange("(kt p) t -> p kt t", p=128)[:, q * 8:(q + 1) * 8, tg * 512:(tg + 1) * 512]
    p = Prog(nc)
    emit_l0_tail(p, io)
    p.build()
    return nc


def emit_l1_qkv(p, io, pref="P7_"):
    nc = p.nc
    xn1T, wqT, wgT, bg, identf = io["xn1T"], io["wqT"], io["wgT"], io["bg"], io["identf"]
    qkv, gatesT = io["qkv"], io["gatesT"]
    ctx = contextlib.ExitStack()
    sb = lambda n, s, d=F32: ctx.enter_context(nc.sbuf_tensor(pref + n, s, d))
    ps = lambda n, s, d=F32: ctx.enter_context(nc.psum_tensor(pref + n, s, d))
    xn_sb = sb("xn_sb", [128, 32 * TSH], BF16)
    wq_t = [sb(f"wq{i}", [128, 32 * 128], BF16) for i in range(2)]
    wg_sb = sb("wg_sb", [128, 32 * 96], BF16)
    bg_sb = sb("bg_sb", [96, 1])
    out_t = [sb(f"out{i}", [128, 512], BF16) for i in range(4)]
    gt_sb = sb("gt_sb", [96, TSH])
    gto = sb("gto", [128, 96])
    sidf = sb("sidf", [128, 128], F32)
    pq = [ps(f"pq{i}", [128, 512]) for i in range(2)]
    pg = ps("pg", [128, 512])

    if True:
        eidf = p.dma("sync", lambda e: e.dma_start(out=sidf[:], in_=identf), slot="idf")
        exn = []
        for q in range(4):
            exn.append(p.dma("sync", lambda e, q=q: e.dma_start(
                out=xn_sb[:].rearrange("p (kt t) -> p kt t", kt=32)[:, q * 8:(q + 1) * 8, :],
                in_=xn1T.rearrange("(kt p) t -> p kt t", p=128)[:, q * 8:(q + 1) * 8, :]), slot=f"xn{q}"))
        ewg = p.dma("gpsimd", lambda e: e.dma_start(
            out=wg_sb[:].rearrange("p (kt m) -> p kt m", kt=32),
            in_=wgT.rearrange("(kt p) m -> p kt m", p=128)), slot="wg")
        ebg = p.dma("sync", lambda e: e.dma_start(out=bg_sb[:], in_=bg), slot="bg")
        wq_cons = [None, None]
        out_cons = [None] * 4
        pq_cons = [None, None]
        for mt in range(48):
            sl = mt % 2
            ew = p.dma("gpsimd", lambda e, mt=mt, sl=sl: e.dma_start(
                out=wq_t[sl][:].rearrange("p (kt m) -> p kt m", kt=32),
                in_=wqT.rearrange("(kt p) m -> p kt m", p=128)[:, :, mt * 128:(mt + 1) * 128]),
                deps=[wq_cons[sl]], slot=f"wq{sl}")
            for tch in range(2):
                bk = tch
                mm = None
                for kt in range(32):
                    mm = p.op("tensor", lambda e, kt=kt, mt=mt, tch=tch, sl=sl, bk=bk: e.matmul(
                        out=pq[bk][:], lhsT=wq_t[sl][:, kt * 128:(kt + 1) * 128],
                        rhs=xn_sb[:, kt * TSH + tch * 512: kt * TSH + (tch + 1) * 512],
                        start=(kt == 0), stop=(kt == 31)),
                        deps=([ew, pq_cons[bk]] + exn) if kt == 0 else None, chain=(kt == 0))
                ob = (mt * 2 + tch) % 4
                eev = p.op("scalar", lambda e, ob=ob, bk=bk: e.copy(out=out_t[ob][:], in_=pq[bk][:]), [mm, out_cons[ob]])
                pq_cons[bk] = eev
                out_cons[ob] = p.dma("scalar", lambda e, ob=ob, mt=mt, tch=tch: e.dma_start(
                    out=qkv[mt * 128:(mt + 1) * 128, tch * 512:(tch + 1) * 512], in_=out_t[ob][:]), deps=[eev], slot=f"o{ob}")
            wq_cons[sl] = Ev("c_tensor", p.sem_count["c_tensor"])
        eg_ev = []
        for tch in range(2):
            mm = None
            for kt in range(32):
                mm = p.op("tensor", lambda e, kt=kt, tch=tch: e.matmul(
                    out=pg[0:96, :], lhsT=wg_sb[:, kt * 96:(kt + 1) * 96],
                    rhs=xn_sb[:, kt * TSH + tch * 512: kt * TSH + (tch + 1) * 512],
                    start=(kt == 0), stop=(kt == 31)),
                    deps=([ewg] + exn + eg_ev) if kt == 0 else None, chain=(kt == 0))
            eg_ev.append(p.op("scalar", lambda e, tch=tch: e.activation(
                out=gt_sb[:, tch * 512:(tch + 1) * 512], in_=pg[0:96, :], func=AF.Sigmoid, bias=bg_sb[:]), [mm, ebg]))
        gt_st = None
        for it2 in range(TSH // 128):
            etr = p.op("tensor", lambda e, it2=it2: e.transpose(out=pg[0:128, 0:96], in_=gt_sb[:, it2 * 128:(it2 + 1) * 128],
                                                               identity=sidf[0:96, 0:96]), eg_ev + [eidf, gt_st])
            eev = p.op("scalar", lambda e: e.copy(out=gto[:, 0:96], in_=pg[0:128, 0:96]), [etr])
            gt_st = p.dma("scalar", lambda e, it2=it2: e.dma_start(out=gatesT[it2 * 128:(it2 + 1) * 128, :], in_=gto[:, 0:96]), deps=[eev], slot="gt")
    ctx.close()


def build_l1_qkv():
    nc = _mk()
    io = {
        "xn1T": nc.dram_tensor("xn1T", [DM, TSH], BF16, kind="ExternalInput").ap(),
        "wqT": nc.dram_tensor("wqT", [DM, 6144], BF16, kind="ExternalInput").ap(),
        "wgT": nc.dram_tensor("wgT", [DM, 96], F32, kind="ExternalInput").ap(),
        "bg": nc.dram_tensor("bg", [96, 1], F32, kind="ExternalInput").ap(),
        "identf": nc.dram_tensor("identf", [128, 128], F32, kind="ExternalInput").ap(),
        "qkv": nc.dram_tensor("qkv", [6144, TSH], BF16, kind="ExternalOutput").ap(),
        "gatesT": nc.dram_tensor("gatesT", [TSH, 96], F32, kind="ExternalOutput").ap(),
    }
    p = Prog(nc)
    emit_l1_qkv(p, io)
    p.build()
    return nc


NEGC = -1.0e30
SCL = 0.08838834764831845


def emit_l1_nsa(p, io, pref="P8_"):
    nc = p.nc
    load_qg = io["load_qg"]
    load_kv = io["load_kv"]
    gt_ap = io["gt_ap"]
    kw1T = io["kw1T"]
    kb1 = io["kb1"]
    kw2T = io["kw2T"]
    kb2 = io["kb2"]
    vw1T = io["vw1T"]
    vb1 = io["vb1"]
    vw2T = io["vw2T"]
    vb2 = io["vb2"]
    kpos = io["kpos"]
    vpos = io["vpos"]
    memk = io["memk"]
    memv = io["memv"]
    cosf = io["cosf"]
    sinf = io["sinf"]
    cmneg = io["cmneg"]
    baneg = io["baneg"]
    bzneg = io["bzneg"]
    diagf = io["diagf"]
    identb = io["identb"]
    A_out = io["A_out"]
    imp_d = io["imp_d"]

    ctx = contextlib.ExitStack()
    sb = lambda n, s, d=F32: ctx.enter_context(nc.sbuf_tensor(pref + n, s, d))
    ps = lambda n, s, d=F32: ctx.enter_context(nc.psum_tensor(pref + n, s, d))
    k_sb = [sb(f"k{u}", [128, LSEQ], BF16) for u in range(2)]
    v_sb = [sb(f"v{u}", [128, LSEQ], BF16) for u in range(2)]
    rk_sb = [sb(f"rk{u}", [128, LSEQ], BF16) for u in range(2)]
    ck_sb = [sb(f"ck{u}", [128, 257], BF16) for u in range(2)]
    cv_sb = [sb(f"cv{u}", [128, 257], BF16) for u in range(2)]
    cvt_a = [sb(f"cvta{u}", [128, 2 * 128], BF16) for u in range(2)]
    cvt_b = [sb(f"cvtb{u}", [1, 128], BF16) for u in range(2)]
    vt_sb = sb("vt_sb", [128, 32 * 128], BF16)
    cos_sb = sb("cos_sb", [128, LSEQ], BF16)
    sin_sb = sb("sin_sb", [128, LSEQ], BF16)
    qg_sb = sb("qg_sb", [128, LSEQ], BF16)
    qsw_sb = sb("qsw_sb", [128, LSEQ], BF16)
    rtmp2 = sb("rtmp2", [128, LSEQ])
    rq_sb = sb("rq_sb", [128, LSEQ], BF16)
    w1_sb = [sb(f"w1_{i}", [128, 16 * 128], BF16) for i in range(2)]
    w2_sb = sb("w2_sb", [128, 16 * 128], BF16)
    b1_sb = sb("b1_sb", [128, 16])
    b2_sb = sb("b2_sb", [128, 1])
    pos_sb = sb("pos_sb", [128, 16])
    mem_sb = sb("mem_sb", [128, 2])
    ckin = sb("ckin", [128, 16 * 256], BF16)
    hid = sb("hid", [128, 16 * 256], BF16)
    impw = sb("impw", [128, 257])
    impl = sb("impl", [128, 257])
    bm8 = sb("bm8", [128, 32 * 256], U8)
    bmneg_t = sb("bmneg_t", [128, 256])
    dneg = sb("dneg", [128, 128])
    cmtf = sb("cmtf", [128, 257])
    ba_sb = sb("ba_sb", [128, 256])
    bz_sb = sb("bz_sb", [128, 128])
    dg_sb = sb("dg_sb", [128, 128])
    gt_it = sb("gt_it", [128, 24])
    fsm = sb("fsm", [128, LSEQ])
    fp_sb = sb("fp_sb", [128, LSEQ], BF16)
    pts = sb("pts", [128, 32 * 128], BF16)
    stats = sb("stats", [128, 10])
    m8 = sb("m8", [128, 8])
    mwork = sb("mwork", [128, 256])
    fo_sb = sb("fo_sb", [128, 128])
    so_sb = sb("so_sb", [128, 128])
    co_sb = sb("co_sb", [128, 128])
    abuf = sb("abuf", [128, 128], BF16)
    aold = sb("aold", [128, 128], BF16)
    sid = sb("sid", [128, 128], BF16)
    pp = [ps(f"pp{i}", [128, 512]) for i in range(4)]
    ptb = [ps(f"ptb{i}", [128, 1024], BF16) for i in range(2)]

    if True:
        E = p.op
        D = p.dma
        eid = D("sync", lambda e: e.dma_start(out=sid[:], in_=identb), slot="id")
        ecos = D("sync", lambda e: e.dma_start(out=cos_sb[:], in_=cosf), slot="cos")
        esin = D("sync", lambda e: e.dma_start(out=sin_sb[:], in_=sinf), slot="sin")
        eba = D("sync", lambda e: e.dma_start(out=ba_sb[:], in_=baneg), slot="ba")
        ebz = D("sync", lambda e: e.dma_start(out=bz_sb[:], in_=bzneg), slot="bz")
        edg = D("sync", lambda e: e.dma_start(out=dg_sb[:], in_=diagf), slot="dg")
        ekv = []
        for u in range(2):
            ev_k, ev_v = load_kv(p, u, k_sb[u], v_sb[u])
            ekv.append(ev_k)
            ekv.append(ev_v)

        ck_done = [None, None]
        cv_done = [None, None]
        w_cons = None
        pos_cons = None
        mem_cons = None
        pbc = [None] * 4
        ptc = [None] * 2
        for which in ("k", "v"):
            wsrcT, bb1, wsrc2, bb2, possrc, memsrc = ((kw1T, kb1, kw2T, kb2, kpos, memk) if which == "k"
                                                      else (vw1T, vb1, vw2T, vb2, vpos, memv))
            ew1l = [None, None]
            ew2 = D("gpsimd", lambda e, wsrc2=wsrc2: e.dma_start(
                out=w2_sb[:].rearrange("p (kt m) -> p kt m", kt=16),
                in_=wsrc2.rearrange("(kt p) m -> p kt m", p=128)), deps=[w_cons], slot="w2")
            eb1 = D("sync", lambda e, bb1=bb1: e.dma_start(out=b1_sb[:], in_=bb1.rearrange("(kt p) o -> p (kt o)", p=128),
                                                           allow_slow_non_contiguous=True), deps=[mem_cons], slot="b1")
            eb2 = D("sync", lambda e, bb2=bb2: e.dma_start(out=b2_sb[:], in_=bb2), deps=[mem_cons], slot="b2")
            for u in range(2):
                src = k_sb[u] if which == "k" else v_sb[u]
                epos = D("sync", lambda e, u=u, possrc=possrc: e.dma_start(out=pos_sb[:], in_=possrc[u]), deps=[pos_cons], slot="pos")
                emem = D("sync", lambda e, u=u, memsrc=memsrc: e.dma_start(out=mem_sb[:, 0:1], in_=memsrc[u]), deps=[mem_cons], slot="mem")
                eci = []
                for c in range(16):
                    eci.append(E("vector", lambda e, c=c, src=src: e.tensor_scalar(
                        out=ckin[:, c * 256:(c + 1) * 256],
                        in0=src[:].rearrange("p (nc cb) -> p cb nc", cb=16)[:, c, :],
                        scalar1=pos_sb[:, c:c + 1], scalar2=None, op0=ALU.add), [epos, ekv[u * 2 + (0 if which == "k" else 1)]]))
                pos_cons = eci[-1]
                ehid = []
                for mt2 in range(16):
                    bk = mt2 % 2
                    wsl = mt2 % 2
                    ew1 = D("gpsimd", lambda e, mt2=mt2, wsl=wsl, wsrcT=wsrcT: e.dma_start(
                        out=w1_sb[wsl][:].rearrange("p (kt m) -> p kt m", kt=16),
                        in_=wsrcT.rearrange("(kt p) m -> p kt m", p=128)[:, :, mt2 * 128:(mt2 + 1) * 128]),
                        deps=[ew1l[wsl], w_cons], slot=f"w1{wsl}")
                    mm = None
                    for kt in range(16):
                        mm = E("tensor", lambda e, mt2=mt2, kt=kt, bk=bk, wsl=wsl: e.matmul(
                            out=pp[bk][:, 0:256], lhsT=w1_sb[wsl][:, kt * 128:(kt + 1) * 128],
                            rhs=ckin[:, kt * 256:(kt + 1) * 256], start=(kt == 0), stop=(kt == 15)),
                            deps=([ew1, pbc[bk]] + eci) if kt == 0 else None, chain=(kt == 0))
                    ew1l[wsl] = Ev("c_tensor", p.sem_count["c_tensor"])
                    ehid.append(E("scalar", lambda e, mt2=mt2, bk=bk: e.activation(
                        out=hid[:, mt2 * 256:(mt2 + 1) * 256], in_=pp[bk][:, 0:256], func=AF.Relu,
                        bias=b1_sb[:, mt2:mt2 + 1]), [mm, eb1]))
                    pbc[bk] = ehid[-1]
                mm = None
                for kt in range(16):
                    mm = E("tensor", lambda e, kt=kt: e.matmul(
                        out=pp[2][:, 0:256], lhsT=w2_sb[:, kt * 128:(kt + 1) * 128],
                        rhs=hid[:, kt * 256:(kt + 1) * 256], start=(kt == 0), stop=(kt == 15)),
                        deps=([ew2, pbc[2]] + ehid) if kt == 0 else None, chain=(kt == 0))
                dst = ck_sb[u] if which == "k" else cv_sb[u]
                ec1 = E("scalar", lambda e, dst=dst: e.activation(out=dst[:, 1:257], in_=pp[2][:, 0:256], func=AF.Identity, bias=b2_sb[:]), [mm, eb2])
                pbc[2] = ec1
                ec2 = E("vector", lambda e, dst=dst: e.tensor_copy(out=dst[:, 0:1], in_=mem_sb[:, 0:1]), [emem, ec1])
                mem_cons = ec2
                if which == "k":
                    ck_done[u] = ec2
                else:
                    cv_done[u] = ec2
            w_cons = Ev("c_tensor", p.sem_count["c_tensor"])
        cvt_done = [None, None]
        for u in range(2):
            ee = []
            for jt, (o, n) in enumerate(((0, 128), (128, 128))):
                etr = E("tensor", lambda e, u=u, o=o: e.transpose(out=ptb[0][:, 0:128], in_=cv_sb[u][:, o:o + 128], identity=sid[:]),
                        [cv_done[u], eid, cvt_done[u], ptc[0]])
                ee.append(E("scalar", lambda e, u=u, jt=jt: e.copy(out=cvt_a[u][:, jt * 128:(jt + 1) * 128], in_=ptb[0][:, 0:128]), [etr]))
                ptc[0] = ee[-1]
            etr = E("tensor", lambda e, u=u: e.transpose(out=ptb[0][0:1, 0:128], in_=cv_sb[u][:, 256:257], identity=sid[:]), ee + [ptc[0]])
            ee.append(E("scalar", lambda e, u=u: e.copy(out=cvt_b[u][:], in_=ptb[0][0:1, 0:128]), [etr]))
            ptc[0] = ee[-1]
            cvt_done[u] = ee[-1]

        def softmax_f(W, deps, scl, ofs=0):
            e1 = E("vector", lambda e, W=W, ofs=ofs: e.tensor_reduce(out=stats[:, ofs:ofs + 1], in_=fsm[:, 0:W], axis=AX.X, op=ALU.max), deps)
            e2 = E("vector", lambda e, ofs=ofs: e.tensor_scalar(out=stats[:, ofs + 1:ofs + 2], in0=stats[:, ofs:ofs + 1], scalar1=-scl, scalar2=None, op0=ALU.mult))
            e3 = E("scalar", lambda e, W=W, scl=scl, ofs=ofs: e.activation(out=fp_sb[:, 0:W], in_=fsm[:, 0:W], func=AF.Exp,
                                                                           scale=scl, bias=stats[:, ofs + 1:ofs + 2], accum_out=stats[:, ofs + 2:ofs + 3]), [e2])
            e4 = E("vector", lambda e, ofs=ofs: e.reciprocal(out=stats[:, ofs + 3:ofs + 4], in_=stats[:, ofs + 2:ofs + 3]), [e3])
            return e3, e4

        a_st = {}
        rope_cons = None
        qg_cons = None
        unit_mm = None
        vt_cons = None
        imp_ld_cons = None
        imp_st_cons = None
        cmt_cons = None
        gt_cons = None
        aold_cons = None
        abuf_cons = None
        impd_guard = None
        for u in range(2):
            er1 = E("vector", lambda e, u=u: e.tensor_copy(out=qsw_sb[0:64, :], in_=k_sb[u][64:128, :]), [ekv[u * 2], rope_cons])
            er2 = E("vector", lambda e, u=u: e.tensor_copy(out=qsw_sb[64:128, :], in_=k_sb[u][0:64, :]))
            er3 = E("vector", lambda e, u=u: e.tensor_tensor(out=fsm[:], in0=k_sb[u][:], in1=cos_sb[:], op=ALU.mult), [ecos])
            er4 = E("vector", lambda e: e.tensor_tensor(out=rtmp2[:], in0=qsw_sb[:], in1=sin_sb[:], op=ALU.mult), [esin])
            er5 = E("vector", lambda e, u=u: e.tensor_tensor(out=rk_sb[u][:], in0=fsm[:], in1=rtmp2[:], op=ALU.add))
            evt = []
            for jt in range(32):
                etr = E("tensor", lambda e, u=u, jt=jt: e.transpose(out=ptb[jt % 2][:, 0:128], in_=v_sb[u][:, jt * 128:(jt + 1) * 128], identity=sid[:]),
                        [ekv[u * 2 + 1], eid, vt_cons, ptc[jt % 2]])
                evt.append(E("scalar", lambda e, jt=jt: e.copy(out=vt_sb[:, jt * 128:(jt + 1) * 128], in_=ptb[jt % 2][:, 0:128]), [etr]))
                ptc[jt % 2] = evt[-1]
            evt_done = evt[-1]
            imp_ev = [impd_guard] * 32
            for g in range(4):
                eq = load_qg(p, u, g, qg_sb, [qg_cons])
                for it in range(32):
                    ecm = D("sync", lambda e, it=it: e.dma_start(out=cmtf[:], in_=cmneg[it * 128:(it + 1) * 128, :]), slot="cmt",
                            deps=[cmt_cons])
                    mm = E("tensor", lambda e, u=u, g=g, it=it: e.matmul(
                        out=pp[0][:, 0:257], lhsT=qg_sb[:, it * 128:(it + 1) * 128], rhs=ck_sb[u][:],
                        start=True, stop=True), [eq, ck_done[u], unit_mm, pbc[0]])
                    esel = E("vector", lambda e: e.tensor_tensor(out=fsm[:, 0:257], in0=pp[0][:, 0:257], in1=cmtf[:], op=ALU.add), [mm, ecm])
                    pbc[0] = esel
                    unit_mm = esel
                    cmt_cons = esel
                    if g == 0:
                        imp_ev[it] = D("scalar", lambda e, it=it: e.dma_start(out=imp_d[it], in_=fsm[:, 0:257]), deps=[esel, imp_ev[it]], slot="imps")
                        imp_st_cons = imp_ev[it]
                    else:
                        eil = D("sync", lambda e, it=it: e.dma_start(out=impl[:], in_=imp_d[it]), deps=[imp_ev[it], imp_ld_cons], slot="impl")
                        eia = E("vector", lambda e: e.tensor_tensor(out=impw[:], in0=impl[:], in1=fsm[:, 0:257], op=ALU.add), [eil, esel, imp_st_cons])
                        imp_ld_cons = eia
                        imp_ev[it] = D("scalar", lambda e, it=it: e.dma_start(out=imp_d[it], in_=impw[:]), deps=[eia], slot="imps")
                        imp_st_cons = imp_ev[it]
                    e3, e4 = softmax_f(257, [esel], SCL)
                    eptl = []
                    for jt, (o, n) in enumerate(((0, 128), (128, 128), (256, 1))):
                        etr = E("tensor", lambda e, o=o, n=n, jt=jt: e.transpose(
                            out=ptb[jt % 2][0:n, 0:128], in_=fp_sb[:, o:o + n], identity=sid[:]), [e3, ptc[jt % 2]])
                        eptl.append(E("scalar", lambda e, jt=jt, n=n: e.copy(out=pts[0:n, jt * 128:(jt + 1) * 128], in_=ptb[jt % 2][0:n, 0:128]), [etr]))
                        ptc[jt % 2] = eptl[-1]
                    mmv = E("tensor", lambda e, u=u: e.matmul(out=pp[1][:, 0:128], lhsT=pts[:, 0:128], rhs=cvt_a[u][:, 0:128], start=True, stop=False), [eptl[0], cvt_done[u], pbc[1]])
                    mmv = E("tensor", lambda e, u=u: e.matmul(out=pp[1][:, 0:128], lhsT=pts[:, 128:256], rhs=cvt_a[u][:, 128:256], start=False, stop=False), [eptl[1]])
                    mmv = E("tensor", lambda e, u=u: e.matmul(out=pp[1][:, 0:128], lhsT=pts[0:1, 256:384], rhs=cvt_b[u][:], start=False, stop=True), [eptl[2]])
                    egt = D("sync", lambda e, it=it: e.dma_start(out=gt_it[:], in_=gt_ap(e, p, it)), slot="gtit",
                            deps=[gt_cons])
                    eco = E("scalar", lambda e: e.copy(out=co_sb[:], in_=pp[1][:, 0:128]), [mmv])
                    pbc[1] = eco
                    col = (u * 4 + g) * 3
                    eab = E("vector", lambda e, col=col: e.tensor_scalar(
                        out=abuf[:], in0=co_sb[:], scalar1=stats[:, 3:4], scalar2=gt_it[:, col:col + 1], op0=ALU.mult, op1=ALU.mult), [eco, e4, egt, abuf_cons])
                    gt_cons = eab
                    a_st[("a", u, g, it)] = D("scalar", lambda e, u=u, g=g, it=it: e.dma_start(
                        out=A_out[it * 128:(it + 1) * 128, (u * 4 + g) * 128:(u * 4 + g + 1) * 128], in_=abuf[:]), deps=[eab], slot="ast")
                    abuf_cons = a_st[("a", u, g, it)]
                qg_cons = Ev("c_tensor", p.sem_count["c_tensor"])
            t4ld = None
            for it in range(32):
                eil = D("sync", lambda e, it=it: e.dma_start(out=impw[:], in_=imp_d[it]),
                        deps=[imp_ev[it], t4ld, imp_ld_cons, imp_st_cons], slot="impl")
                et1 = E("vector", lambda e: e.memset(impw[:, 0:1], -1000.0 / (0.25 * SCL)), [eil])
                et2 = E("vector", lambda e: e.tensor_scalar(out=impw[:], in0=impw[:], scalar1=0.25 * SCL, scalar2=None, op0=ALU.mult))
                et3 = E("vector", lambda e: e.tensor_reduce(out=stats[:, 0:1], in_=impw[:], axis=AX.X, op=ALU.max))
                et4 = E("vector", lambda e: e.tensor_scalar(out=stats[:, 1:2], in0=stats[:, 0:1], scalar1=-1.0, scalar2=None, op0=ALU.mult))
                et5 = E("scalar", lambda e: e.activation(out=impw[:], in_=impw[:], func=AF.Exp,
                                                         bias=stats[:, 1:2], accum_out=stats[:, 2:3]), [et4])
                et6 = E("vector", lambda e: e.reciprocal(out=stats[:, 3:4], in_=stats[:, 2:3]), [et5])
                et7 = E("vector", lambda e: e.tensor_scalar(out=impw[:], in0=impw[:], scalar1=stats[:, 3:4], scalar2=None, op0=ALU.mult), [et6])
                et8 = E("vector", lambda e: e.max(out=m8[:], in_=impw[:, 1:257]), [et7])
                et9 = E("vector", lambda e: e.tensor_scalar(out=mwork[:], in0=impw[:, 1:257], scalar1=m8[:, 3:4], scalar2=None, op0=ALU.is_ge), [et8])
                eta = E("vector", lambda e: e.tensor_scalar(out=fsm[:, 0:256], in0=impw[:, 1:257], scalar1=1e-10, scalar2=None, op0=ALU.is_gt))
                t4ld = eta
                etb = E("vector", lambda e: e.tensor_tensor(out=mwork[:], in0=mwork[:], in1=fsm[:, 0:256], op=ALU.mult))
                E("vector", lambda e, it=it: e.tensor_scalar(out=bm8[:, it * 256:(it + 1) * 256], in0=mwork[:], scalar1=0.5, scalar2=None, op0=ALU.is_gt))
                impd_guard = eil
            bm_done = Ev("c_vector", p.sem_count["c_vector"])
            for g in range(4):
                eq = load_qg(p, u, g, qg_sb, [qg_cons])
                eq1 = E("vector", lambda e: e.tensor_copy(out=qsw_sb[0:64, :], in_=qg_sb[64:128, :]), [eq, rope_cons])
                eq2 = E("vector", lambda e: e.tensor_copy(out=qsw_sb[64:128, :], in_=qg_sb[0:64, :]))
                eq3 = E("vector", lambda e: e.tensor_tensor(out=fsm[:], in0=qg_sb[:], in1=cos_sb[:], op=ALU.mult))
                eq4 = E("vector", lambda e: e.tensor_tensor(out=rtmp2[:], in0=qsw_sb[:], in1=sin_sb[:], op=ALU.mult))
                eq5 = E("vector", lambda e: e.tensor_tensor(out=rq_sb[:], in0=fsm[:], in1=rtmp2[:], op=ALU.add))
                rope_cons = eq5
                bmn_cons = None
                dneg_cons = None
                for it in range(32):
                    W = (it + 1) * 128
                    ebm = E("vector", lambda e, it=it: e.tensor_scalar(
                        out=bmneg_t[:], in0=bm8[:, it * 256:(it + 1) * 256], scalar1=1e30, scalar2=-1e30,
                        op0=ALU.mult, op1=ALU.add), [bm_done, bmn_cons])
                    edc = E("vector", lambda e, it=it: e.tensor_tensor(
                        out=dneg[:].rearrange("p (nb c) -> p nb c", c=16),
                        in0=dg_sb[:].rearrange("p (nb c) -> p nb c", c=16),
                        in1=bm8[:, it * 256 + it * 8: it * 256 + it * 8 + 8].unsqueeze(2).broadcast_to([128, 8, 16]),
                        op=ALU.max), [edg, bm_done, dneg_cons])
                    edc2 = E("vector", lambda e: e.tensor_scalar(out=dneg[:], in0=dneg[:], scalar1=1e30, scalar2=-1e30,
                                                                 op0=ALU.mult, op1=ALU.add))
                    esels = []
                    for jc in range((W + 511) // 512):
                        wd = min(512, W - jc * 512)
                        nb = wd // 16
                        mm = E("tensor", lambda e, u=u, it=it, jc=jc, wd=wd: e.matmul(
                            out=pp[0][:, 0:wd], lhsT=rq_sb[:, it * 128:(it + 1) * 128],
                            rhs=rk_sb[u][:, jc * 512:jc * 512 + wd], start=True, stop=True), [eq5, er5, unit_mm, pbc[0]])
                        esel = E("vector", lambda e, it=it, jc=jc, wd=wd, nb=nb: e.tensor_tensor(
                            out=fsm[:, jc * 512:jc * 512 + wd].rearrange("p (nb c) -> p nb c", c=16),
                            in0=pp[0][:, 0:wd].rearrange("p (nb c) -> p nb c", c=16),
                            in1=bmneg_t[:, jc * 32:jc * 32 + nb].unsqueeze(2).broadcast_to([128, nb, 16]),
                            op=ALU.add), [ebm, mm])
                        pbc[0] = esel
                        unit_mm = esel
                        esels.append(esel)
                    bmn_cons = esels[-1]
                    dof = (W - 128) % 512
                    jcl = (W - 128) // 512
                    esel2 = E("vector", lambda e, W=W, dof=dof, jcl=jcl: e.tensor_tensor(
                        out=fsm[:, W - 128:W], in0=pp[0][:, dof:dof + 128], in1=dneg[:], op=ALU.add), [edc2, esels[-1]])
                    dneg_cons = esel2
                    pbc[0] = esel2
                    e3, e4 = softmax_f(W, [esel2] + esels, SCL)
                    eptl = []
                    for jt in range(it + 1):
                        pj = jt % 2
                        etr = E("tensor", lambda e, jt=jt, pj=pj: e.transpose(
                            out=ptb[pj][:, 0:128], in_=fp_sb[:, jt * 128:(jt + 1) * 128], identity=sid[:]),
                            [e3, ptc[pj]])
                        eptl.append(E("scalar", lambda e, jt=jt, pj=pj: e.copy(out=pts[:, jt * 128:(jt + 1) * 128], in_=ptb[pj][:, 0:128]), [etr]))
                        ptc[pj] = eptl[-1]
                    mmv = None
                    for jt in range(it + 1):
                        mmv = E("tensor", lambda e, jt=jt, it=it: e.matmul(
                            out=pp[1][:, 0:128], lhsT=pts[:, jt * 128:(jt + 1) * 128], rhs=vt_sb[:, jt * 128:(jt + 1) * 128],
                            start=(jt == 0), stop=(jt == it)), deps=([evt_done, pbc[1]] + eptl) if jt == 0 else None, chain=(jt == 0))
                    efo = E("scalar", lambda e: e.copy(out=fo_sb[:], in_=pp[1][:, 0:128]), [mmv])
                    pbc[1] = efo
                    col = (u * 4 + g) * 3
                    egt = D("sync", lambda e, it=it: e.dma_start(out=gt_it[:], in_=gt_ap(e, p, it)), slot="gtit",
                            deps=[gt_cons])
                    ef1 = E("vector", lambda e, col=col: e.tensor_scalar(
                        out=fo_sb[:], in0=fo_sb[:], scalar1=stats[:, 3:4], scalar2=gt_it[:, col + 1:col + 2], op0=ALU.mult, op1=ALU.mult), [efo, e4, egt])
                    if it == 0:
                        mm = E("tensor", lambda e, u=u: e.matmul(out=pp[2][:, 0:128], lhsT=rq_sb[:, 0:128], rhs=rk_sb[u][:, 0:128], start=True, stop=True), [eq5, er5, ef1, pbc[2]])
                        esel = E("vector", lambda e: e.tensor_tensor(out=fsm[:, 0:128], in0=pp[2][:, 0:128], in1=bz_sb[:], op=ALU.add), [ebz, mm])
                        sW = 128
                    else:
                        mm = E("tensor", lambda e, u=u, it=it: e.matmul(out=pp[2][:, 0:256], lhsT=rq_sb[:, it * 128:(it + 1) * 128],
                                                                        rhs=rk_sb[u][:, (it - 1) * 128:(it + 1) * 128], start=True, stop=True), [eq5, er5, ef1, pbc[2]])
                        esel = E("vector", lambda e: e.tensor_tensor(out=fsm[:, 0:256], in0=pp[2][:, 0:256], in1=ba_sb[:], op=ALU.add), [eba, mm])
                        sW = 256
                    pbc[2] = esel
                    unit_mm = esel
                    e3s, e4s = softmax_f(sW, [esel], SCL, ofs=4)
                    eptl2 = []
                    for jt in range(sW // 128):
                        pj = jt % 2
                        etr = E("tensor", lambda e, jt=jt, pj=pj: e.transpose(
                            out=ptb[pj][:, 0:128], in_=fp_sb[:, jt * 128:(jt + 1) * 128], identity=sid[:]),
                            [e3s, ptc[pj]])
                        eptl2.append(E("scalar", lambda e, jt=jt, pj=pj: e.copy(out=pts[:, jt * 128:(jt + 1) * 128], in_=ptb[pj][:, 0:128]), [etr]))
                        ptc[pj] = eptl2[-1]
                    mmv = None
                    for jt in range(sW // 128):
                        vjt = it - 1 + jt if it > 0 else 0
                        mmv = E("tensor", lambda e, jt=jt, vjt=vjt, sW=sW: e.matmul(
                            out=pp[3][:, 0:128], lhsT=pts[:, jt * 128:(jt + 1) * 128], rhs=vt_sb[:, vjt * 128:(vjt + 1) * 128],
                            start=(jt == 0), stop=(jt == sW // 128 - 1)), deps=([evt_done, pbc[3]] + eptl2) if jt == 0 else None, chain=(jt == 0))
                    eso = E("scalar", lambda e: e.copy(out=so_sb[:], in_=pp[3][:, 0:128]), [mmv])
                    pbc[3] = eso
                    es5 = E("vector", lambda e, col=col: e.tensor_scalar(
                        out=so_sb[:], in0=so_sb[:], scalar1=stats[:, 7:8], scalar2=gt_it[:, col + 2:col + 3], op0=ALU.mult, op1=ALU.mult), [eso, e4s])
                    gt_cons = es5
                    eao = D("sync", lambda e, u=u, g=g, it=it: e.dma_start(
                        out=aold[:], in_=A_out[it * 128:(it + 1) * 128, (u * 4 + g) * 128:(u * 4 + g + 1) * 128]),
                        deps=[a_st[("a", u, g, it)], aold_cons], slot="aold")
                    ec1 = E("vector", lambda e: e.tensor_tensor(out=fo_sb[:], in0=fo_sb[:], in1=so_sb[:], op=ALU.add), [ef1, es5])
                    ec2 = E("vector", lambda e: e.tensor_tensor(out=abuf[:], in0=fo_sb[:], in1=aold[:], op=ALU.add), [eao, abuf_cons])
                    aold_cons = ec2
                    a_st[("a2", u, g, it)] = D("scalar", lambda e, u=u, g=g, it=it: e.dma_start(
                        out=A_out[it * 128:(it + 1) * 128, (u * 4 + g) * 128:(u * 4 + g + 1) * 128], in_=abuf[:]), deps=[ec2], slot="ast2")
                    abuf_cons = a_st[("a2", u, g, it)]
                qg_cons = Ev("c_tensor", p.sem_count["c_tensor"])
            vt_cons = Ev("c_tensor", p.sem_count["c_tensor"])
    ctx.close()


def build_l1_nsa():
    nc = _mk()
    io = {}
    for nm, shp, dt in [("kw1T", [2048, 2048], BF16), ("kb1", [2048, 1], F32), ("kw2T", [2048, 128], F32),
                        ("kb2", [128, 1], F32), ("vw1T", [2048, 2048], BF16), ("vb1", [2048, 1], F32),
                        ("vw2T", [2048, 128], F32), ("vb2", [128, 1], F32), ("kpos", [2, 128, 16], F32),
                        ("vpos", [2, 128, 16], F32), ("memk", [2, 128, 1], F32), ("memv", [2, 128, 1], F32),
                        ("cosf", [128, LSEQ], BF16), ("sinf", [128, LSEQ], BF16), ("cmneg", [LSEQ, 257], F32),
                        ("baneg", [128, 256], F32), ("bzneg", [128, 128], F32), ("diagf", [128, 128], F32),
                        ("identb", [128, 128], BF16)]:
        io[nm] = nc.dram_tensor(nm, shp, dt, kind="ExternalInput").ap()
    io["A_out"] = nc.dram_tensor("A_out", [LSEQ, 1024], BF16, kind="ExternalOutput").ap()
    io["imp_d"] = nc.dram_tensor("imp_d", [32, 128, 257], F32).ap()
    q_in = nc.dram_tensor("q_in", [2, 512, LSEQ], BF16, kind="ExternalInput").ap()
    k_in = nc.dram_tensor("k_in", [2, 128, LSEQ], BF16, kind="ExternalInput").ap()
    v_in = nc.dram_tensor("v_in", [2, 128, LSEQ], BF16, kind="ExternalInput").ap()
    gtb = nc.dram_tensor("gtb", [LSEQ, 96], F32, kind="ExternalInput").ap()
    io["load_qg"] = lambda p, u, g, dst, deps: p.dma("sync", lambda e: e.dma_start(out=dst[:], in_=q_in[u, g * 128:(g + 1) * 128, :]), deps=deps, slot="qg")
    def _load_kv(p, u, k_dst, v_dst):
        ev_k = p.dma("sync", lambda e: e.dma_start(out=k_dst[:], in_=k_in[u]), slot=f"k{u}")
        ev_v = p.dma("sync", lambda e: e.dma_start(out=v_dst[:], in_=v_in[u]), slot=f"v{u}")
        return ev_k, ev_v
    io["load_kv"] = _load_kv
    io["gt_ap"] = lambda e, p, it: gtb[it * 128:(it + 1) * 128, 0:24]
    p = Prog(nc)
    emit_l1_nsa(p, io)
    p.build()
    return nc


def emit_l1_tail(p, io, pref="P9_"):
    nc = p.nc
    woT2, xn1T, kern, grow = io["woT2"], io["xn1T"], io["kern"], io["grow"]
    loraA, loraBT, shAT, shBT = io["loraA"], io["loraBT"], io["shAT"], io["shBT"]
    fbias1, xnew, ident, onesb, outf = io["fbias1"], io["xnew"], io["ident"], io["onesb"], io["outf"]
    load_at = io["load_at"]
    ctx = contextlib.ExitStack()
    sb = lambda n, s, d=F32: ctx.enter_context(nc.sbuf_tensor(pref + n, s, d))
    ps = lambda n, s, d=F32: ctx.enter_context(nc.psum_tensor(pref + n, s, d))
    at_tg = sb("at_tg", [128, 32 * 512], BF16)
    xn_tg = sb("xn_tg", [128, 32 * 512], BF16)
    wot = [sb(f"wot{i}", [128, 512], BF16) for i in range(4)]
    ket = [sb(f"ket{i}", [128, 512], BF16) for i in range(4)]
    la_t = sb("la_t", [32, 512], BF16)
    sid = sb("sid", [128, 128], BF16)
    ones_sb = sb("ones_sb", [1, 32], BF16)
    grow_b = sb("grow_b", [1, 16], BF16)
    grep = sb("grep", [32, 16])
    i32 = sb("i32", [32, 32], BF16)
    ign = sb("ign", [32, 32], BF16)
    aeff = sb("aeff", [32, DM], BF16)
    aefT = sb("aefT", [128, 32 * 32], BF16)
    befT = sb("befT", [32, DM], BF16)
    shb_sb = sb("shb_sb", [32, DM], BF16)
    sha_t = sb("sha_t", [128, 32 * 32], BF16)
    ax_sb = sb("ax_sb", [32, TSH], BF16)
    axs_sb = sb("axs_sb", [32, TSH], BF16)
    bias_bc = sb("bias_bc", [128, DM])
    xt = [sb(f"xt{i}", [128, 512]) for i in range(4)]
    a_ev = [sb(f"a_ev{i}", [128, 512]) for i in range(4)]
    xo = [sb(f"xo{i}", [128, 512]) for i in range(4)]
    pa = [ps(f"pa{i}", [128, 512]) for i in range(4)]
    pf = [ps(f"pf{i}", [128, 512]) for i in range(4)]

    if True:
        E = p.op
        D = p.dma
        eid = D("sync", lambda e: e.dma_start(out=sid[:], in_=ident), slot="id")
        eon = D("sync", lambda e: e.dma_start(out=ones_sb[:], in_=onesb), slot="on")
        ebias = D("gpsimd", lambda e: e.dma_start(out=bias_bc[:], in_=fbias1.partition_broadcast(128).squeeze(1)), slot="fb")
        egr = D("gpsimd", lambda e: e.dma_start(out=grow_b[:], in_=grow), slot="gr")
        eshb = D("gpsimd", lambda e: e.dma_start(out=shb_sb[:], in_=shBT), slot="shb")
        esha = D("gpsimd", lambda e: e.dma_start(out=sha_t[:].rearrange("p (kt r) -> p kt r", kt=32),
                                                 in_=shAT.rearrange("(kt p) r -> p kt r", p=128)), slot="sha")
        exn = []
        for tg in range(2):
            for q in range(4):
                exn.append(D("sync", lambda e, tg=tg, q=q: e.dma_start(
                    out=xn_tg[:].rearrange("p (kt t) -> p kt t", kt=32)[:, q * 8:(q + 1) * 8, tg * 0:512] if False else
                        xn_tg[:].rearrange("p (kt t) -> p kt t", kt=32)[:, q * 8:(q + 1) * 8, :],
                    in_=xn1T.rearrange("(kt p) t -> p kt t", p=128)[:, q * 8:(q + 1) * 8, tg * 512:(tg + 1) * 512]), slot=f"xn{q}"))
            break
        mmg = E("tensor", lambda e: e.matmul(out=pa[0][0:32, 0:16], lhsT=ones_sb[:], rhs=grow_b[:], start=True, stop=True), [eon, egr])
        egrep = E("scalar", lambda e: e.copy(out=grep[:], in_=pa[0][0:32, 0:16]), [mmg])
        ei32 = D("sync", lambda e: e.dma_start(out=i32[:], in_=ident[0:32, 0:32]), slot="i32")
        la_cons = None
        ae_done = []
        for dst, srcT in ((aeff, loraA), (befT, loraBT)):
            for och in range(8):
                mm = None
                for n in range(16):
                    eg = E("vector", lambda e, n=n: e.tensor_scalar(out=ign[:], in0=i32[:], scalar1=grep[:, n:n + 1], scalar2=None, op0=ALU.mult), [egrep, ei32, la_cons])
                    el = D("gpsimd", lambda e, n=n, och=och, srcT=srcT: e.dma_start(
                        out=la_t[:], in_=srcT[n, :, och * 512:(och + 1) * 512]), deps=[la_cons], slot="la")
                    mm = E("tensor", lambda e, n=n, och=och, dst=dst: e.matmul(
                        out=pa[1][0:32, :], lhsT=ign[:], rhs=la_t[:], start=(n == 0), stop=(n == 15)),
                        [eg, el] + ([ae_done[-1]] if (n == 0 and ae_done) else []))
                    la_cons = mm
                ae_done.append(E("scalar", lambda e, och=och, dst=dst: e.copy(out=dst[:, och * 512:(och + 1) * 512], in_=pa[1][0:32, :]), [mm]))
        atr = []
        for kt in range(32):
            etr = E("tensor", lambda e, kt=kt: e.transpose(out=pa[2][:].bitcast(BF16)[0:128, 0:32], in_=aeff[:, kt * 128:(kt + 1) * 128], identity=sid[0:32, 0:32]),
                    [ae_done[7], eid] + ([atr[-1]] if atr else []))
            atr.append(E("scalar", lambda e, kt=kt: e.copy(out=aefT[:, kt * 32:(kt + 1) * 32], in_=pa[2][:].bitcast(BF16)[0:128, 0:32]), [etr]))
        at_cons = None
        xn_cons = exn[-1]
        w_cons = [None] * 4
        ke_cons = [None] * 4
        xt_cons = [None] * 4
        pa_cons = [None, [atr[-1]], [la_cons, egrep], [atr[-1]]]
        pa_cons = [None, atr[-1], la_cons, atr[-1]]
        pf_cons = [None] * 4
        aev_cons = [None] * 4
        xo_cons = [None] * 4
        ax_done = None
        for tg in range(2):
            if tg == 1:
                exn = []
                for q in range(4):
                    exn.append(D("sync", lambda e, q=q: e.dma_start(
                        out=xn_tg[:].rearrange("p (kt t) -> p kt t", kt=32)[:, q * 8:(q + 1) * 8, :],
                        in_=xn1T.rearrange("(kt p) t -> p kt t", p=128)[:, q * 8:(q + 1) * 8, 512:1024]),
                        deps=[xn_cons], slot=f"xn{q}"))
            eats = load_at(p, tg, at_tg, [at_cons], (sid, eid))
            mm = None
            for kt in range(32):
                mm = E("tensor", lambda e, kt=kt: e.matmul(out=pa[2][0:32, :], lhsT=aefT[:, kt * 32:(kt + 1) * 32],
                                                           rhs=xn_tg[:, kt * 512:(kt + 1) * 512], start=(kt == 0), stop=(kt == 31)),
                       deps=([pa_cons[2], atr[-1]] + exn) if kt == 0 else None, chain=(kt == 0))
            eax = E("scalar", lambda e, tg=tg: e.copy(out=ax_sb[:, tg * 512:(tg + 1) * 512], in_=pa[2][0:32, :]), [mm])
            mm = None
            for kt in range(32):
                mm = E("tensor", lambda e, kt=kt: e.matmul(out=pa[2][0:32, :], lhsT=sha_t[:, kt * 32:(kt + 1) * 32],
                                                           rhs=xn_tg[:, kt * 512:(kt + 1) * 512], start=(kt == 0), stop=(kt == 31)),
                       deps=[eax, esha] if kt == 0 else None, chain=(kt == 0))
            eaxs = E("scalar", lambda e, tg=tg: e.copy(out=axs_sb[:, tg * 512:(tg + 1) * 512], in_=pa[2][0:32, :]), [mm])
            pa_cons[2] = eaxs
            ax_done = eaxs
            for ch in range(8):
                mma = [None] * 4
                for kt in range(32):
                    sl = kt % 4
                    qn = "gpsimd" if kt % 2 == 0 else "sync"
                    ew = D(qn, lambda e, kt=kt, ch=ch, sl=sl: e.dma_start(
                        out=wot[sl][:], in_=woT2[kt * 128:(kt + 1) * 128, ch * 512:(ch + 1) * 512]),
                        deps=[w_cons[sl]], slot=f"w{sl}")
                    for tt in range(4):
                        mma[tt] = E("tensor", lambda e, kt=kt, tt=tt, sl=sl: e.matmul(
                            out=pa[tt][:], lhsT=at_tg[:, kt * 512 + tt * 128: kt * 512 + (tt + 1) * 128],
                            rhs=wot[sl][:], start=(kt == 0), stop=(kt == 31)),
                            deps=([ew, pa_cons[tt]] + eats) if kt == 0 else [ew], chain=(kt == 0))
                    w_cons[sl] = Ev("c_tensor", p.sem_count["c_tensor"])
                mmf = [None] * 4
                for kt in range(32):
                    sl = kt % 4
                    qn = "scalar" if kt % 2 == 0 else "gpsimd"
                    ew = D(qn, lambda e, kt=kt, ch=ch, sl=sl: e.dma_start(
                        out=ket[sl][:], in_=kern[kt * 128:(kt + 1) * 128, ch * 512:(ch + 1) * 512]),
                        deps=[ke_cons[sl]], slot=f"v{sl}")
                    for tt in range(4):
                        mmf[tt] = E("tensor", lambda e, kt=kt, tt=tt, sl=sl: e.matmul(
                            out=pf[tt][:], lhsT=xn_tg[:, kt * 512 + tt * 128: kt * 512 + (tt + 1) * 128],
                            rhs=ket[sl][:], start=(kt == 0), stop=False),
                            deps=([ew, pf_cons[tt]] + exn) if kt == 0 else [ew], chain=(kt == 0))
                    ke_cons[sl] = Ev("c_tensor", p.sem_count["c_tensor"])
                for tt in range(4):
                    mmf[tt] = E("tensor", lambda e, tt=tt, tg=tg, ch=ch: e.matmul(
                        out=pf[tt][:], lhsT=ax_sb[:, tg * 512 + tt * 128: tg * 512 + (tt + 1) * 128],
                        rhs=befT[:, ch * 512:(ch + 1) * 512], start=False, stop=False),
                        deps=[ax_done, ae_done[-1]] if tt == 0 else None, chain=(tt == 0))
                for tt in range(4):
                    mmf[tt] = E("tensor", lambda e, tt=tt, tg=tg, ch=ch: e.matmul(
                        out=pf[tt][:], lhsT=axs_sb[:, tg * 512 + tt * 128: tg * 512 + (tt + 1) * 128],
                        rhs=shb_sb[:, ch * 512:(ch + 1) * 512], start=False, stop=True),
                        deps=[ax_done, eshb] if tt == 0 else None, chain=(tt == 0))
                for tt in range(4):
                    ex = D("sync", lambda e, tg=tg, tt=tt, ch=ch: e.dma_start(
                        out=xt[tt][:], in_=xnew[tg * 512 + tt * 128: tg * 512 + (tt + 1) * 128, ch * 512:(ch + 1) * 512]),
                        deps=[xt_cons[tt]], slot=f"x{tt}")
                    ea = E("scalar", lambda e, tt=tt: e.copy(out=a_ev[tt][:], in_=pa[tt][:]), [mma[tt], aev_cons[tt]])
                    pa_cons[tt] = ea
                    e1 = E("vector", lambda e, tt=tt: e.tensor_tensor(out=xo[tt][:], in0=a_ev[tt][:], in1=pf[tt][:], op=ALU.add), [ea, mmf[tt], xo_cons[tt]])
                    pf_cons[tt] = e1
                    aev_cons[tt] = e1
                    e2 = E("vector", lambda e, tt=tt: e.tensor_tensor(out=xo[tt][:], in0=xo[tt][:], in1=xt[tt][:], op=ALU.add), [ex])
                    e3 = E("vector", lambda e, tt=tt, ch=ch: e.tensor_tensor(out=xo[tt][:], in0=xo[tt][:], in1=bias_bc[:, ch * 512:(ch + 1) * 512], op=ALU.add), [ebias])
                    xt_cons[tt] = e2
                    xo_cons[tt] = D("scalar", lambda e, tg=tg, tt=tt, ch=ch: e.dma_start(
                        out=outf[tg * 512 + tt * 128: tg * 512 + (tt + 1) * 128, ch * 512:(ch + 1) * 512], in_=xo[tt][:]),
                        deps=[e3], slot=f"xo{tt}")
            at_cons = Ev("c_tensor", p.sem_count["c_tensor"])
            xn_cons = at_cons
    ctx.close()


def build_l1_tail():
    nc = _mk()
    io = {}
    for nm, shp, dt in [("woT2", [DM, DM], BF16), ("xn1T", [DM, TSH], BF16), ("kern", [DM, DM], BF16),
                        ("grow", [1, 16], F32), ("loraA", [16, 32, DM], F32), ("loraBT", [16, 32, DM], F32),
                        ("shAT", [DM, 32], F32), ("shBT", [32, DM], F32), ("fbias1", [1, DM], F32),
                        ("xnew", [TSH, DM], F32), ("ident", [128, 128], BF16), ("onesb", [1, 32], BF16)]:
        io[nm] = nc.dram_tensor(nm, shp, dt, kind="ExternalInput").ap()
    io["outf"] = nc.dram_tensor("outf", [TSH, DM], F32, kind="ExternalOutput").ap()
    AT = nc.dram_tensor("AT", [DM, TSH], BF16, kind="ExternalInput").ap()
    def _load_at(p, tg, at_tg, deps, helpers):
        evs = []
        for q in range(4):
            evs.append(p.dma("sync", lambda e, tg=tg, q=q: e.dma_start(
                out=at_tg[:].rearrange("p (kt t) -> p kt t", kt=32)[:, q * 8:(q + 1) * 8, :],
                in_=AT.rearrange("(kt p) t -> p kt t", p=128)[:, q * 8:(q + 1) * 8, tg * 512:(tg + 1) * 512]),
                deps=deps, slot=f"at{q}"))
        return evs
    io["load_at"] = _load_at
    p = Prog(nc)
    emit_l1_tail(p, io)
    p.build()
    return nc


PERM128 = np.concatenate([np.arange(0, 128, 2), np.arange(1, 128, 2)])
ONES32 = np.ones((1, 32), dtype=NP_BF16)


def _nsa_consts():
    inv = 1.0 / 10000.0 ** (np.arange(0, 128, 2, dtype=np.float64) / 128)
    ang = np.arange(LSEQ, dtype=np.float64)[:, None] * inv[None, :]
    cos = np.cos(ang).astype(np.float32).T
    sin = np.sin(ang).astype(np.float32).T
    cosf = np.concatenate([cos, cos], 0).astype(NP_BF16)
    sinf = np.concatenate([-sin, sin], 0).astype(NP_BF16)
    ck_seq = np.concatenate([[-1], (np.arange(256) + 1) * 16 - 1])
    cmneg = np.where(ck_seq[None, :] < np.arange(LSEQ)[:, None], 0.0, -1e30).astype(np.float32)
    r = np.arange(128)[:, None]
    jj = np.arange(256)[None, :]
    baneg = np.where((jj >= r + 64) & (jj <= r + 128), 0.0, -1e30).astype(np.float32)
    jz = np.arange(128)[None, :]
    bzneg = np.where((jz <= r) & (jz >= r - 64), 0.0, -1e30).astype(np.float32)
    diagf = (((r // 16) == (jz // 16)) & (jz <= r)).astype(np.float32)
    return cosf, sinf, cmneg, baneg, bzneg, diagf


def kernel(**ins):
    C = np.ascontiguousarray
    x = np.asarray(ins["L0_attn_hidden_states"]).reshape(TOK, DM)
    qk_perm = np.arange(6144)
    for h in range(40):
        qk_perm[h * 128:(h + 1) * 128] = h * 128 + PERM128
    kw1p = ins["L1_attn_k_w1"].reshape(2048, 16, 128)[:, :, PERM128].reshape(2048, 2048)
    cast_src = {
        "woT": C(ins["L0_attn_out_proj_w"].T),
        "woT2": C(ins["L1_attn_w_o"].T),
        "kern": np.asarray(ins["L1_ffn_kernel"]),
        "wqT": C(ins["L1_attn_w_qkv"][qk_perm].T),
        "utT": C(ins["L0_ffn_U_tail"].T),
        "kw1T": C(kw1p.T),
        "vw1T": C(ins["L1_attn_v_w1"].T),
    }
    nc = build_l0_prep()
    in_maps = []
    for c in range(NC):
        d = {"x": x[c * TSH:(c + 1) * TSH], "ident": IDENT_BF16}
        for nm, rows, cols in CAST_SPECS:
            d["ci_" + nm] = C(cast_src[nm][c * rows:(c + 1) * rows])
        in_maps.append(d)
    res = _run(nc, in_maps)
    xnT_full = np.concatenate([np.asarray(r["xnT"]) for r in res], axis=1)
    wbf = {nm: np.concatenate([np.asarray(r["co_" + nm]) for r in res], axis=0)
           for nm, _, _ in CAST_SPECS}
    W = ins["L0_attn_in_proj_w"]
    nc = build_l0_inproj()
    res = _run(nc, [{
        "xnT": xnT_full,
        "w_x": C(W[:, c * DSH:(c + 1) * DSH]),
        "w_z": C(W[:, DIN + c * DSH: DIN + (c + 1) * DSH]),
        "cw": C(ins["L0_attn_conv_w"][c * DSH:(c + 1) * DSH]),
        "cb": C(ins["L0_attn_conv_b"][c * DSH:(c + 1) * DSH]),
        "wxpT": C(ins["L0_attn_x_proj_w"].T[c * DSH:(c + 1) * DSH]),
    } for c in range(NC)])
    xc_sh = [np.asarray(r["xc"]) for r in res]
    zs_sh = [np.asarray(r["zs"]) for r in res]
    ssp_all = np.stack([np.asarray(r["ssp"]) for r in res])
    lnw = np.concatenate([ins["L0_attn_dt_ln_w"], ins["L0_attn_B_ln_w"], ins["L0_attn_C_ln_w"]])[None, :].astype(np.float32)
    dtWT = C(ins["L0_attn_dt_proj_w"].T)
    nc = build_l0_ssmmid()
    res = _run(nc, [{
        "pst": C(ssp_all[:, :, c * TSH:(c + 1) * TSH]), "lnw": lnw, "dtWT": dtWT,
        "dtb": ins["L0_attn_dt_bias"], "ident": IDENT_BF16, "identf": IDENT_F32,
    } for c in range(NC)])
    dt_full = np.concatenate([np.asarray(r["dt_o"]) for r in res], axis=1)
    Bn_full = np.concatenate([np.asarray(r["Bn_o"]) for r in res], axis=1)
    Cn_full = np.concatenate([np.asarray(r["Cn_o"]) for r in res], axis=1)
    ISH = DM // NC
    nc = build_l0_ffnbuild()
    res = _run(nc, [{
        "vh": C(ins["L0_ffn_Vh_tail"][:, c * ISH:(c + 1) * ISH]), "st": ins["L0_ffn_S_tail"],
        "utT": wbf["utT"],
        "vtop": C(ins["L0_ffn_Vh_top"][:, c * ISH:(c + 1) * ISH]), "utopT": C(ins["L0_ffn_U_top"].T),
        "stop_": ins["L0_ffn_S_top"][:, None], "alpha": ins["L0_ffn_alpha"][:, None],
        "beta": ins["L0_ffn_beta"][:, None],
        "vhr": C(ins["L0_ffn_Vh_tail_r"][:, c * ISH:(c + 1) * ISH]), "urT": C(ins["L0_ffn_U_tail_r"].T),
        "srd": np.stack([ins["L0_ffn_S_tail_r"], ins["L0_ffn_D"]], axis=1), "rmat": ins["L0_ffn_R"],
    } for c in range(NC)])
    weT_full = np.concatenate([np.asarray(r["weT"]) for r in res], axis=0)
    nc = build_l0_scan()
    res = _run(nc, [{
        "xc": xc_sh[c], "zsd": zs_sh[c], "dtd": C(dt_full[c * DSH:(c + 1) * DSH]),
        "alog": C(ins["L0_attn_A_log"][c * DSH:(c + 1) * DSH]),
        "dpar": C(ins["L0_attn_D_param"][c * DSH:(c + 1) * DSH]),
        "Bn": Bn_full, "Cn": Cn_full,
    } for c in range(NC)])
    yg_full = np.concatenate([np.asarray(r["yg"]) for r in res], axis=0)
    nc = build_l0_tail()
    res = _run(nc, [{
        "ygd": C(yg_full[:, c * TSH:(c + 1) * TSH]), "woT": wbf["woT"], "weT": weT_full,
        "xnT": C(xnT_full[:, c * TSH:(c + 1) * TSH]), "x": x[c * TSH:(c + 1) * TSH],
        "fbias": ins["L0_ffn_bias"][None, :], "ident": IDENT_BF16,
    } for c in range(NC)])
    xnew_sh = [np.asarray(r["xnew"]) for r in res]
    xn1T_full = np.concatenate([np.asarray(r["xn1T"]) for r in res], axis=1)
    wgT = C(ins["L1_attn_w_gate"].T)
    nc = build_l1_qkv()
    res = _run(nc, [{
        "xn1T": C(xn1T_full[:, c * TSH:(c + 1) * TSH]), "wqT": wbf["wqT"], "wgT": wgT,
        "bg": ins["L1_attn_b_gate"][:, None], "identf": IDENT_F32,
    } for c in range(NC)])
    qkvT_full = np.concatenate([np.asarray(r["qkv"]) for r in res], axis=1)
    gates_full = np.concatenate([np.asarray(r["gatesT"]) for r in res], axis=0)
    COSF, SINF, CMNEG, BANEG, BZNEG, DIAGF = _nsa_consts()
    kw2p = ins["L1_attn_k_w2"][PERM128]
    wcom = {
        "kw1T": wbf["kw1T"], "kb1": ins["L1_attn_k_b1"][:, None],
        "kw2T": C(kw2p.T), "kb2": ins["L1_attn_k_b2"][PERM128][:, None],
        "vw1T": wbf["vw1T"], "vb1": ins["L1_attn_v_b1"][:, None],
        "vw2T": C(ins["L1_attn_v_w2"].T), "vb2": ins["L1_attn_v_b2"][:, None],
        "cosf": COSF, "sinf": SINF, "cmneg": CMNEG, "baneg": BANEG, "bzneg": BZNEG,
        "diagf": DIAGF, "identb": IDENT_BF16,
    }
    in_maps = []
    for c in range(NC):
        b = c // 4
        kvhs = [2 * (c % 4), 2 * (c % 4) + 1]
        cols = slice(b * LSEQ, (b + 1) * LSEQ)
        q_in = np.zeros((2, 512, LSEQ), dtype=NP_BF16)
        k_in = np.zeros((2, 128, LSEQ), dtype=NP_BF16)
        v_in = np.zeros((2, 128, LSEQ), dtype=NP_BF16)
        kpos = np.zeros((2, 128, 16), np.float32)
        vpos = np.zeros((2, 128, 16), np.float32)
        memk = np.zeros((2, 128, 1), np.float32)
        memv = np.zeros((2, 128, 1), np.float32)
        for u, kvh in enumerate(kvhs):
            for g in range(4):
                hq = kvh * 4 + g
                q_in[u, g * 128:(g + 1) * 128] = qkvT_full[hq * 128:(hq + 1) * 128, cols]
            k_in[u] = qkvT_full[4096 + kvh * 128: 4096 + (kvh + 1) * 128, cols]
            v_in[u] = qkvT_full[5120 + kvh * 128: 5120 + (kvh + 1) * 128, cols]
            kpos[u] = ins["L1_attn_k_pos"][kvh].T[PERM128]
            vpos[u] = ins["L1_attn_v_pos"][kvh].T
            memk[u] = ins["L1_attn_mem_kv"][0, kvh].T[PERM128]
            memv[u] = ins["L1_attn_mem_kv"][1, kvh].T
        colperm = []
        for u2, kvh2 in enumerate(kvhs):
            for g2 in range(4):
                for c2 in range(3):
                    colperm.append((kvh2 * 4 + g2) * 3 + c2)
        colperm += [i for i in range(96) if i not in colperm]
        d = {"q_in": q_in, "k_in": k_in, "v_in": v_in,
             "gtb": C(gates_full[cols][:, colperm]),
             "kpos": kpos, "vpos": vpos, "memk": memk, "memv": memv}
        d.update(wcom)
        in_maps.append(d)
    nc = build_l1_nsa()
    res = _run(nc, in_maps)
    A_tok = np.zeros((TOK, DM), dtype=NP_BF16)
    for c in range(NC):
        b = c // 4
        kvhs = [2 * (c % 4), 2 * (c % 4) + 1]
        a = np.asarray(res[c]["A_out"])
        for u, kvh in enumerate(kvhs):
            for g in range(4):
                hq = kvh * 4 + g
                A_tok[b * LSEQ:(b + 1) * LSEQ, hq * 128:(hq + 1) * 128] = a[:, (u * 4 + g) * 128:(u * 4 + g + 1) * 128]
    AT_full = C(A_tok.T)
    loraBT = C(np.transpose(ins["L1_ffn_lora_B"], (0, 2, 1)))
    shAT = C(ins["L1_ffn_lora_A_shared"].T)
    nc = build_l1_tail()
    res = _run(nc, [{
        "AT": C(AT_full[:, c * TSH:(c + 1) * TSH]), "woT2": wbf["woT2"],
        "xn1T": C(xn1T_full[:, c * TSH:(c + 1) * TSH]), "kern": wbf["kern"],
        "grow": ins["L1_ffn_gate"][c][None, :], "loraA": ins["L1_ffn_lora_A"],
        "loraBT": loraBT, "shAT": shAT, "shBT": C(ins["L1_ffn_lora_B_shared"].T),
        "fbias1": ins["L1_ffn_bias"][None, :], "xnew": xnew_sh[c],
        "ident": IDENT_BF16, "onesb": ONES32,
    } for c in range(NC)])
    out = np.concatenate([np.asarray(r["outf"]) for r in res], axis=0)
    return out.reshape(2, LSEQ, DM).astype(np.float32)


# version 07 (final)
# speedup vs baseline: 1.2767x; 1.1181x over previous
import contextlib
from collections import defaultdict

import numpy as np
import ml_dtypes

import concourse.bass as bass
import concourse.mybir as mybir
from concourse.bass_utils import run_bass_kernel_spmd

F32 = mybir.dt.float32
BF16 = mybir.dt.bfloat16
U8 = mybir.dt.uint8
AF = mybir.ActivationFunctionType
ALU = mybir.AluOpType
AX = mybir.AxisListType
NP_BF16 = ml_dtypes.bfloat16

NC = 8
DM = 4096
DIN = 8192
NST = 16
TOK = 8192
TSH = TOK // NC
LSEQ = 4096
DSH = DIN // NC


class Ev:
    __slots__ = ("sem_key", "val")

    def __init__(self, sem_key, val):
        self.sem_key = sem_key
        self.val = val


class Prog:

    AUTO = {"vector", "scalar", "gpsimd", "tensor"}

    def __init__(self, nc):
        self.nc = nc
        self.items = defaultdict(list)
        self.sem_count = defaultdict(int)
        self.sem_keys = set()
        self.last = {}
        self.floor = []
        self._pid = {}

    def barrier(self):
        self.floor = [Ev(k, v) for k, v in self.sem_count.items()]

    def pid(self, e):
        key = id(e)
        if key not in self._pid:
            self._pid[key] = e.partition_id()
        return self._pid[key]

    def _rec(self, engine, fn, deps, sem_key, inc, chain):
        deps = [d for d in (deps or []) if d is not None] + self.floor
        if chain and engine in self.AUTO and engine in self.last:
            deps.append(self.last[engine])
        self.sem_keys.add(sem_key)
        self.sem_count[sem_key] += inc
        ev = Ev(sem_key, self.sem_count[sem_key])
        self.items[engine].append((fn, deps, sem_key, inc))
        if engine in self.AUTO:
            self.last[engine] = ev
        return ev

    def op(self, engine, fn, deps=None, chain=True):
        return self._rec(engine, fn, deps, "c_" + engine, 1, chain)

    def dma(self, queue, fn, deps=None, slot=None):
        key = "d_" + queue + ("" if slot is None else "_" + str(slot))
        return self._rec(queue, fn, deps, key, 16, False)

    def cc(self, fn, deps=None):
        return self._rec("gpsimd", fn, deps, "cc", 1, True)

    def build(self, final_engine="sync"):
        nc = self.nc
        ctx = contextlib.ExitStack()
        sems = {k: ctx.enter_context(nc.semaphore(k)) for k in sorted(self.sem_keys)}
        finals = sorted(self.sem_count.items())

        def emit_stream(eng, engine_name):
            hw = defaultdict(int)
            for (fn, deps, sem_key, inc) in self.items[engine_name]:
                need = {}
                for d in deps:
                    if d.val > need.get(d.sem_key, 0):
                        need[d.sem_key] = d.val
                for k, v in sorted(need.items()):
                    if v > hw[k]:
                        eng.wait_ge(sems[k], v)
                        hw[k] = v
                inst = fn(eng)
                if inst is None:
                    raise RuntimeError(f"op on {engine_name} returned None")
                inst.then_inc(sems[sem_key], inc)

        with ctx, nc.Block() as block:
            for engine_name in ["sync", "vector", "scalar", "tensor", "gpsimd"]:
                if not self.items[engine_name] and engine_name != final_engine:
                    continue
                dec = getattr(block, engine_name)

                def mk(en):
                    def _f(eng):
                        emit_stream(eng, en)
                        if en == final_engine:
                            for k, v in finals:
                                eng.wait_ge(sems[k], v)
                    return _f
                dec(mk(engine_name))
        return nc


def _mk():
    return bass.Bass("TRN2", target_bir_lowering=False)


def _t(a):
    return np.ascontiguousarray(a.T)


def _run(nc, in_maps):
    return run_bass_kernel_spmd(nc, in_maps, core_ids=list(range(NC))).results


IDENT_BF16 = np.eye(128, dtype=NP_BF16)
IDENT_F32 = np.eye(128, dtype=np.float32)


def emit_l0_prep(p, io, pref="P1_"):
    nc = p.nc
    x, ident, xnT = io["x"], io["ident"], io["xnT"]
    ctx = contextlib.ExitStack()
    sb = lambda n, s, d=F32: ctx.enter_context(nc.sbuf_tensor(pref + n, s, d))
    ps = lambda n, s, d=F32: ctx.enter_context(nc.psum_tensor(pref + n, s, d))
    NT = TSH // 128
    xt = [sb(f"xt{i}", [128, DM]) for i in range(2)]
    sq = sb("sq", [128, DM])
    stat = sb("stat", [128, 8])
    xnt = [sb(f"xnt{i}", [128, DM], BF16) for i in range(2)]
    sid = sb("sid", [128, 128], BF16)
    ptr = [ps(f"ptr{i}", [128, 1024], BF16) for i in range(2)]
    ot = [sb(f"ot{i}", [128, 128], BF16) for i in range(4)]
    scst = sb("scst", [128, 2])

    if True:
        eid = p.dma("sync", lambda e: e.dma_start(out=sid[:], in_=ident), slot="id")
        ecst = p.op("vector", lambda e: e.memset(scst[:, 0:1], 1e-5))
        cons = [None, None]
        xn_cons = [None, None]
        tr_cons = [None, None]
        st_ev = [None] * 4
        for i in range(NT):
            sl = i % 2
            xb, xnb = xt[sl], xnt[sl]
            eld = p.dma("sync", lambda e, xb=xb, i=i: e.dma_start(out=xb[:], in_=x[i * 128:(i + 1) * 128, :]),
                        deps=[cons[sl]], slot=f"x{sl}")
            es1 = p.op("vector", lambda e, xb=xb: e.tensor_reduce(out=stat[:, 0:1], in_=xb[:], axis=AX.X, op=ALU.add), [eld])
            esq = p.op("scalar", lambda e, xb=xb: e.activation(out=sq[:], in_=xb[:], func=AF.Square, accum_out=stat[:, 1:2]), [eld, es1])
            em = p.op("vector", lambda e: e.tensor_scalar(out=stat[:, 2:3], in0=stat[:, 0:1], scalar1=1.0 / DM, scalar2=None, op0=ALU.mult), [esq])
            em2 = p.op("vector", lambda e: e.tensor_scalar(out=stat[:, 3:4], in0=stat[:, 1:2], scalar1=1.0 / DM, scalar2=None, op0=ALU.mult), [esq])
            emm = p.op("vector", lambda e: e.tensor_tensor(out=stat[:, 4:5], in0=stat[:, 2:3], in1=stat[:, 2:3], op=ALU.mult))
            evar = p.op("vector", lambda e: e.tensor_tensor(out=stat[:, 5:6], in0=stat[:, 3:4], in1=stat[:, 4:5], op=ALU.subtract))
            esd = p.op("scalar", lambda e: e.activation(out=stat[:, 6:7], in_=stat[:, 5:6], func=AF.Sqrt, bias=scst[:, 0:1]), [evar, ecst])
            ers = p.op("vector", lambda e: e.reciprocal(out=stat[:, 7:8], in_=stat[:, 6:7]), [esd])
            exn = p.op("vector", lambda e, xb=xb, xnb=xnb: e.tensor_scalar(out=xnb[:], in0=xb[:], scalar1=stat[:, 2:3], scalar2=stat[:, 7:8], op0=ALU.subtract, op1=ALU.mult), [xn_cons[sl]])
            cons[sl] = exn
            last_tr = None
            for j in range(DM // 128):
                pj = j % 2
                etr = p.op("tensor", lambda e, j=j, pj=pj, xnb=xnb: e.transpose(out=ptr[pj][:, 0:128], in_=xnb[:, j * 128:(j + 1) * 128], identity=sid[:]),
                           [exn, eid, tr_cons[pj]])
                k = (i * 32 + j) % 4
                eev = p.op("scalar", lambda e, k=k, pj=pj: e.copy(out=ot[k][:], in_=ptr[pj][:, 0:128]), [etr, st_ev[k]])
                tr_cons[pj] = eev
                st_ev[k] = p.dma("scalar", lambda e, k=k, i=i, j=j: e.dma_start(out=xnT[j * 128:(j + 1) * 128, i * 128:(i + 1) * 128], in_=ot[k][:]),
                                 deps=[eev], slot=f"st{k}")
                last_tr = eev
            xn_cons[sl] = last_tr
    ctx.close()


def emit_casts(p, pairs, pref="PC_"):
    nc = p.nc
    ctx = contextlib.ExitStack()
    wmax = max(c for (_, _, _, c) in pairs)
    cb = [ctx.enter_context(nc.sbuf_tensor(pref + f"cb{i}", [128, wmax], BF16)) for i in range(2)]
    ld = [None, None]
    st = [None, None]
    k = 0
    for (iap, oap, rows, cols) in pairs:
        for r in range(0, rows, 128):
            rsz = min(128, rows - r)
            sl = k % 2
            eld = p.dma("gpsimd", lambda e, sl=sl, rsz=rsz, r=r, iap=iap, cols=cols: e.dma_start(
                out=cb[sl][0:rsz, 0:cols], in_=iap[r:r + rsz, :]), deps=[st[sl]], slot=f"ci{sl}")
            ld[sl] = eld
            st[sl] = p.dma("scalar", lambda e, sl=sl, rsz=rsz, r=r, oap=oap, cols=cols: e.dma_start(
                out=oap[r:r + rsz, :], in_=cb[sl][0:rsz, 0:cols]), deps=[eld], slot=f"co{sl}")
            k += 1
    ctx.close()


CAST_SPECS = [
    ("woT", DIN // NC, DM),
    ("woT2", DM // NC, DM),
    ("kern", DM // NC, DM),
    ("wqT", DM // NC, 6144),
    ("utT", 4032 // NC, DM),
    ("kw1T", 2048 // NC, 2048),
    ("vw1T", 2048 // NC, 2048),
]


def build_l0_prep():
    nc = _mk()
    io = {
        "x": nc.dram_tensor("x", [TSH, DM], F32, kind="ExternalInput").ap(),
        "ident": nc.dram_tensor("ident", [128, 128], BF16, kind="ExternalInput").ap(),
        "xnT": nc.dram_tensor("xnT", [DM, TSH], BF16, kind="ExternalOutput").ap(),
    }
    pairs = []
    for nm, rows, cols in CAST_SPECS:
        iap = nc.dram_tensor("ci_" + nm, [rows, cols], F32, kind="ExternalInput").ap()
        oap = nc.dram_tensor("co_" + nm, [rows, cols], BF16, kind="ExternalOutput").ap()
        pairs.append((iap, oap, rows, cols))
    p = Prog(nc)
    emit_l0_prep(p, io)
    p.barrier()
    emit_casts(p, pairs)
    p.build()
    return nc


def host_prep_inputs(inputs):
    x = np.asarray(inputs["L0_attn_hidden_states"]).reshape(TOK, DM)
    return x


def emit_l0_inproj(p, io, pref="P2_"):
    nc = p.nc
    xnT, w_x, w_z, cw, cb, wxpT = io["xnT"], io["w_x"], io["w_z"], io["cw"], io["cb"], io["wxpT"]
    xc, zs = io["xc"], io["zs"]
    ssp_store_ap = io["ssp_store_ap"]
    ctx = contextlib.ExitStack()
    sb = lambda n, s, d=F32: ctx.enter_context(nc.sbuf_tensor(pref + n, s, d))
    ps = lambda n, s, d=F32: ctx.enter_context(nc.psum_tensor(pref + n, s, d))
    NTC = 16
    TC = 512
    w_sb = sb("w_sb", [128, 32 * 1024], BF16)
    wxp_sb = sb("wxp_sb", [128, 8 * 288], BF16)
    cw_sb = sb("cw_sb", [128, 8 * 4])
    cb_sb = sb("cb_sb", [128, 8])
    xk = [sb(f"xk{i}", [128, 32 * TC], BF16) for i in range(2)]
    xcp = [sb(f"xcp{i}", [128, TC + 3]) for i in range(8)]
    cs1 = sb("cs1", [128, TC])
    cs2 = sb("cs2", [128, TC])
    xcs = [sb(f"xcs{i}", [128, TC], BF16) for i in range(8)]
    zst = [sb(f"zst{i}", [128, TC], BF16) for i in range(2)]
    ssps = [sb(f"ssps{i}", [128, TC]) for i in range(3)]
    pb = [ps(f"pb{i}", [128, 512]) for i in range(4)]

    if True:
        ewx = p.dma("gpsimd", lambda e: e.dma_start(
            out=w_sb[:].rearrange("p (ht e) -> p ht e", ht=32), in_=w_x.rearrange("(ht p) e -> p ht e", p=128)), slot="w")
        ewxp = p.dma("gpsimd", lambda e: e.dma_start(
            out=wxp_sb[:].rearrange("p (dt e) -> p dt e", dt=8), in_=wxpT.rearrange("(dt p) e -> p dt e", p=128)), slot="wxp")
        ecw = p.dma("sync", lambda e: e.dma_start(
            out=cw_sb[:].rearrange("p (dt k) -> p dt k", dt=8), in_=cw.rearrange("(dt p) k -> p dt k", p=128)), slot="cw")
        ecb = p.dma("sync", lambda e: e.dma_start(
            out=cb_sb[:], in_=cb.rearrange("(dt p) -> p dt", p=128), allow_slow_non_contiguous=True), slot="cb")
        einit = []
        for et in range(8):
            einit.append(p.op("vector", lambda e, et=et: e.memset(xcp[et][:, 0:3], 0.0)))
        xk_cons = [None, None]
        bank_cons = [None] * 4
        xcs_cons = [None] * 8
        zst_cons = [None, None]
        ssp_store = [None] * 3
        xc_store = [None] * 8

        def xn_load(tc, sl, deps):
            evs = []
            for q in range(4):
                evs.append(p.dma("sync", lambda e, tc=tc, sl=sl, q=q: e.dma_start(
                    out=xk[sl][:].rearrange("p (ht t) -> p ht t", ht=32)[:, q * 8:(q + 1) * 8, :],
                    in_=xnT.rearrange("(ht p) t -> p ht t", p=128)[:, q * 8:(q + 1) * 8, tc * TC:(tc + 1) * TC]),
                    deps=deps, slot=f"xk{sl}{q}"))
            return evs

        for tc in range(NTC):
            sl = tc % 2
            elds = xn_load(tc, sl, [xk_cons[sl]])
            last_mm_of_chunk = None
            chunk_sil = []
            for et in range(8):
                bk = et % 4
                mm = None
                for ht in range(32):
                    mm = p.op("tensor", lambda e, ht=ht, et=et, sl=sl, bk=bk: e.matmul(
                        out=pb[bk][:],
                        lhsT=w_sb[:, ht * 1024 + et * 128: ht * 1024 + et * 128 + 128],
                        rhs=xk[sl][:, ht * TC:(ht + 1) * TC],
                        start=(ht == 0), stop=(ht == 31)),
                        deps=([ewx, bank_cons[bk]] + elds) if ht == 0 else None, chain=(ht == 0))
                last_mm_of_chunk = mm
                eev = p.op("scalar", lambda e, et=et, bk=bk: e.copy(out=xcp[et][:, 3:3 + TC], in_=pb[bk][:]), [mm])
                bank_cons[bk] = eev
                e1 = p.op("vector", lambda e, et=et: e.tensor_scalar(
                    out=cs1[:], in0=xcp[et][:, 3:3 + TC], scalar1=cw_sb[:, et * 4 + 3:et * 4 + 4],
                    scalar2=cb_sb[:, et:et + 1], op0=ALU.mult, op1=ALU.add), [eev, ecw, ecb])
                e2 = p.op("vector", lambda e, et=et: e.scalar_tensor_tensor(
                    out=cs2[:], in0=xcp[et][:, 2:2 + TC], scalar=cw_sb[:, et * 4 + 2:et * 4 + 3],
                    in1=cs1[:], op0=ALU.mult, op1=ALU.add))
                e3 = p.op("vector", lambda e, et=et: e.scalar_tensor_tensor(
                    out=cs1[:], in0=xcp[et][:, 1:1 + TC], scalar=cw_sb[:, et * 4 + 1:et * 4 + 2],
                    in1=cs2[:], op0=ALU.mult, op1=ALU.add))
                e4 = p.op("vector", lambda e, et=et: e.scalar_tensor_tensor(
                    out=cs2[:], in0=xcp[et][:, 0:0 + TC], scalar=cw_sb[:, et * 4 + 0:et * 4 + 1],
                    in1=cs1[:], op0=ALU.mult, op1=ALU.add))
                if tc == NTC // 2 - 1:
                    etl = p.op("vector", lambda e, et=et: e.memset(xcp[et][:, 0:3], 0.0), [e4])
                else:
                    etl = p.op("vector", lambda e, et=et: e.tensor_copy(out=xcp[et][:, 0:3], in_=xcp[et][:, TC:TC + 3]), [e4])
                esl = p.op("scalar", lambda e, et=et: e.activation(out=xcs[et][:], in_=cs2[:], func=AF.Silu), [e4, xcs_cons[et]])
                chunk_sil.append(esl)
                xc_store[et] = p.dma("scalar", lambda e, et=et, tc=tc: e.dma_start(
                    out=xc[et * 128:(et + 1) * 128, tc * TC:(tc + 1) * TC], in_=xcs[et][:]), deps=[esl], slot=f"xc{et}")
            for mt, msz in ((0, 128), (1, 128), (2, 32)):
                bk = mt % 4
                mm = None
                for kt in range(8):
                    mm = p.op("tensor", lambda e, mt=mt, msz=msz, kt=kt, bk=bk: e.matmul(
                        out=pb[bk][0:msz, :],
                        lhsT=wxp_sb[:, kt * 288 + mt * 128: kt * 288 + mt * 128 + msz],
                        rhs=xcs[kt][:],
                        start=(kt == 0), stop=(kt == 7)),
                        deps=([ewxp, bank_cons[bk], last_mm_of_chunk] + chunk_sil) if kt == 0 else None,
                        chain=(kt == 0))
                eev = p.op("scalar", lambda e, mt=mt, msz=msz, bk=bk: e.copy(out=ssps[mt][0:msz, :], in_=pb[bk][0:msz, :]), [mm, ssp_store[mt]])
                bank_cons[bk] = eev
                ssp_store[mt] = p.dma("scalar", lambda e, mt=mt, msz=msz, tc=tc: e.dma_start(
                    out=ssp_store_ap(mt, msz, tc), in_=ssps[mt][0:msz, :]), deps=[eev], slot=f"sp{mt}")
            for et in range(8):
                xcs_cons[et] = Ev("c_tensor", p.sem_count["c_tensor"])
            xk_cons[sl] = Ev("c_tensor", p.sem_count["c_tensor"])
        ewz = p.dma("gpsimd", lambda e: e.dma_start(
            out=w_sb[:].rearrange("p (ht e) -> p ht e", ht=32), in_=w_z.rearrange("(ht p) e -> p ht e", p=128)),
            deps=[Ev("c_tensor", p.sem_count["c_tensor"])], slot="w")
        for tc in range(NTC):
            sl = tc % 2
            elds = xn_load(tc, sl, [xk_cons[sl]])
            for et in range(8):
                bk = et % 4
                mm = None
                for ht in range(32):
                    mm = p.op("tensor", lambda e, ht=ht, et=et, sl=sl, bk=bk: e.matmul(
                        out=pb[bk][:],
                        lhsT=w_sb[:, ht * 1024 + et * 128: ht * 1024 + et * 128 + 128],
                        rhs=xk[sl][:, ht * TC:(ht + 1) * TC],
                        start=(ht == 0), stop=(ht == 31)),
                        deps=([ewz, bank_cons[bk]] + elds) if ht == 0 else None, chain=(ht == 0))
                zsl = et % 2
                esl = p.op("scalar", lambda e, zsl=zsl, bk=bk: e.activation(out=zst[zsl][:], in_=pb[bk][:], func=AF.Silu), [mm, zst_cons[zsl]])
                bank_cons[bk] = esl
                zst_cons[zsl] = p.dma("scalar", lambda e, zsl=zsl, et=et, tc=tc: e.dma_start(
                    out=zs[et * 128:(et + 1) * 128, tc * TC:(tc + 1) * TC], in_=zst[zsl][:]), deps=[esl], slot=f"zs{zsl}")
            xk_cons[sl] = Ev("c_tensor", p.sem_count["c_tensor"])
    ctx.close()


def build_l0_inproj():
    nc = _mk()
    io = {
        "xnT": nc.dram_tensor("xnT", [DM, TOK], BF16, kind="ExternalInput").ap(),
        "w_x": nc.dram_tensor("w_x", [DM, DSH], F32, kind="ExternalInput").ap(),
        "w_z": nc.dram_tensor("w_z", [DM, DSH], F32, kind="ExternalInput").ap(),
        "cw": nc.dram_tensor("cw", [DSH, 4], F32, kind="ExternalInput").ap(),
        "cb": nc.dram_tensor("cb", [DSH], F32, kind="ExternalInput").ap(),
        "wxpT": nc.dram_tensor("wxpT", [DSH, 288], F32, kind="ExternalInput").ap(),
        "xc": nc.dram_tensor("xc", [DSH, TOK], BF16, kind="ExternalOutput").ap(),
        "zs": nc.dram_tensor("zs", [DSH, TOK], BF16, kind="ExternalOutput").ap(),
    }
    ssp = nc.dram_tensor("ssp", [288, TOK], F32, kind="ExternalOutput").ap()
    io["ssp_store_ap"] = lambda mt, msz, tc: ssp[mt * 128:mt * 128 + msz, tc * 512:(tc + 1) * 512]
    p = Prog(nc)
    emit_l0_inproj(p, io)
    p.build()
    return nc


def emit_l0_ssmmid(p, io, pref="P3_"):
    nc = p.nc
    lnw, dtWT, dtb = io["lnw"], io["dtWT"], io["dtb"]
    ident, identf = io["ident"], io["identf"]
    dt_o, Bn_o, Cn_o = io["dt_o"], io["Bn_o"], io["Cn_o"]
    pst = io.get("pst")
    ssp_my = io.get("ssp_my")
    ctx = contextlib.ExitStack()
    sb = lambda n, s, d=F32: ctx.enter_context(nc.sbuf_tensor(pref + n, s, d))
    ps = lambda n, s, d=F32: ctx.enter_context(nc.psum_tensor(pref + n, s, d))
    NT = TSH // 128
    acc = [sb(f"acc{i}", [128, TSH]) for i in range(3)]
    prt = sb("prt", [128, TSH])
    sid = sb("sid", [128, 128], BF16)
    sidf = sb("sidf", [128, 128], F32)
    ptr = [ps(f"ptr{i}", [128, 512]) for i in range(2)]
    ptb = [ps(f"ptb{i}", [128, 1024], BF16) for i in range(2)]
    stok = [sb(f"stok{i}", [128, 288]) for i in range(2)]
    sq = sb("sq", [128, 288])
    st8 = sb("st8", [128, 12])
    lnb = sb("lnb", [128, 288])
    nrm = sb("nrm", [128, 288])
    nrmb = sb("nrmb", [128, 288], BF16)
    dtr = sb("dtr", [128, 2 * TSH], BF16)
    bcn = sb("bcn", [32, TSH])
    dtw = sb("dtw", [128, 2 * DIN], BF16)
    dtbs = sb("dtbs", [128, 64])
    one = sb("one", [128, 1])
    sp1 = sb("sp1", [128, 512])
    sp2 = sb("sp2", [128, 512])
    sp3 = sb("sp3", [128, 512])
    dttile = [sb(f"dtt{i}", [128, TSH]) for i in range(2)]
    pdt = [ps(f"pdt{i}", [128, 512]) for i in range(2)]

    if True:
        eid = p.dma("sync", lambda e: e.dma_start(out=sid[:], in_=ident), slot="id")
        eidf = p.dma("sync", lambda e: e.dma_start(out=sidf[:], in_=identf), slot="idf")
        elnb = p.dma("gpsimd", lambda e: e.dma_start(out=lnb[:], in_=lnw.partition_broadcast(128).squeeze(1)), slot="lnb")
        edtw = p.dma("gpsimd", lambda e: e.dma_start(out=dtw[:].rearrange("p (kt e) -> p kt e", kt=2),
                                                     in_=dtWT.rearrange("(kt p) e -> p kt e", p=128)), slot="dtw")
        edtb = p.dma("sync", lambda e: e.dma_start(out=dtbs[:], in_=dtb.rearrange("(mt p) -> p mt", p=128),
                                                   allow_slow_non_contiguous=True), slot="dtb")
        eone = p.op("vector", lambda e: e.memset(one[:], 1.0))
        eacc = []
        prt_cons = None
        if ssp_my is not None:
            for rt, rsz in ((0, 128), (1, 128), (2, 32)):
                eacc.append(p.dma("sync", lambda e, rt=rt, rsz=rsz: e.dma_start(
                    out=acc[rt][0:rsz, :], in_=ssp_my[rt * 128:rt * 128 + rsz, :]), slot=f"pa{rt}"))
        else:
            for rt, rsz in ((0, 128), (1, 128), (2, 32)):
                ea = p.dma("sync", lambda e, rt=rt, rsz=rsz: e.dma_start(out=acc[rt][0:rsz, :], in_=pst[0, rt * 128:rt * 128 + rsz, :]), slot=f"pa{rt}")
                ev = None
                for c in range(1, NC):
                    el = p.dma("sync", lambda e, c=c, rt=rt, rsz=rsz: e.dma_start(out=prt[0:rsz, :], in_=pst[c, rt * 128:rt * 128 + rsz, :]),
                               deps=[prt_cons], slot=f"pb{rt}")
                    ev = p.op("vector", lambda e, rt=rt, rsz=rsz: e.tensor_tensor(out=acc[rt][0:rsz, :], in0=acc[rt][0:rsz, :], in1=prt[0:rsz, :], op=ALU.add),
                              [el, ea])
                    prt_cons = ev
                eacc.append(ev)
        dtr_evs = []
        bcn_evs = []
        stok_cons = [None, None]
        ptr_cons = [None, None]
        ptb_cons = [None, None]
        for it in range(NT):
            sl = it % 2
            etr_ev = []
            for rt, rsz in ((0, 128), (1, 128), (2, 32)):
                pj = rt % 2
                etr = p.op("tensor", lambda e, rt=rt, rsz=rsz, it=it, pj=pj: e.transpose(
                    out=ptr[pj][0:128, 0:rsz], in_=acc[rt][0:rsz, it * 128:(it + 1) * 128], identity=sidf[0:rsz, 0:rsz]),
                    [eacc[rt], eidf, stok_cons[sl], ptr_cons[pj]])
                eev = p.op("scalar", lambda e, rt=rt, rsz=rsz, sl=sl, pj=pj: e.copy(
                    out=stok[sl][:, rt * 128:rt * 128 + rsz], in_=ptr[pj][0:128, 0:rsz]), [etr])
                ptr_cons[pj] = eev
                etr_ev.append(eev)
            esq = p.op("scalar", lambda e, sl=sl: e.activation(out=sq[:], in_=stok[sl][:], func=AF.Square), etr_ev)
            segs = ((0, 256, 0), (256, 16, 1), (272, 16, 2))
            einv = []
            for (o, n, ix) in segs:
                er = p.op("vector", lambda e, o=o, n=n, ix=ix: e.tensor_reduce(out=st8[:, ix:ix + 1], in_=sq[:, o:o + n], axis=AX.X, op=ALU.add), [esq])
                em = p.op("vector", lambda e, n=n, ix=ix: e.tensor_scalar(out=st8[:, 3 + ix:4 + ix], in0=st8[:, ix:ix + 1], scalar1=1.0 / n, scalar2=1e-6, op0=ALU.mult, op1=ALU.add))
                es = p.op("scalar", lambda e, ix=ix: e.activation(out=st8[:, 6 + ix:7 + ix], in_=st8[:, 3 + ix:4 + ix], func=AF.Sqrt), [em])
                ei = p.op("vector", lambda e, ix=ix: e.reciprocal(out=st8[:, 9 + ix:10 + ix], in_=st8[:, 6 + ix:7 + ix]), [es])
                einv.append(ei)
            en = []
            for (o, n, ix) in segs:
                en.append(p.op("vector", lambda e, o=o, n=n, ix=ix, sl=sl: e.tensor_scalar(
                    out=nrm[:, o:o + n], in0=stok[sl][:, o:o + n], scalar1=st8[:, 9 + ix:10 + ix], scalar2=None, op0=ALU.mult), [einv[ix]]))
            stok_cons[sl] = en[-1]
            ew = p.op("vector", lambda e: e.tensor_tensor(out=nrm[:], in0=nrm[:], in1=lnb[:], op=ALU.mult), en + [elnb])
            ebf = p.op("vector", lambda e: e.tensor_copy(out=nrmb[:], in_=nrm[:]), [ew])
            for j in range(2):
                pj = j % 2
                etr = p.op("tensor", lambda e, j=j, pj=pj: e.transpose(out=ptb[pj][:, 0:128], in_=nrmb[:, j * 128:(j + 1) * 128], identity=sid[:]), [ebf, ptb_cons[pj], eid])
                ee = p.op("scalar", lambda e, j=j, it=it, pj=pj: e.copy(out=dtr[:, j * TSH + it * 128: j * TSH + (it + 1) * 128], in_=ptb[pj][:, 0:128]), [etr])
                ptb_cons[pj] = ee
                dtr_evs.append(ee)
            etrb = p.op("tensor", lambda e: e.transpose(out=ptr[0][0:32, 0:128], in_=nrm[:, 256:288], identity=sidf[:]), [ew, ptr_cons[0]])
            eeb = p.op("scalar", lambda e, it=it: e.copy(out=bcn[:, it * 128:(it + 1) * 128], in_=ptr[0][0:32, 0:128]), [etrb])
            ptr_cons[0] = eeb
            bcn_evs.append(eeb)
        eb_st = p.dma("sync", lambda e: e.dma_start(out=Bn_o, in_=bcn[0:16, :]), bcn_evs, slot="bno")
        ec_st = p.dma("sync", lambda e: e.dma_start(out=Cn_o, in_=bcn[16:32, :]), bcn_evs, slot="cno")
        dt_cons = [None, None]
        pdt_cons = [None, None]
        prev_ed = None
        for mt in range(DIN // 128):
            sl = mt % 2
            for tch in range(2):
                bk = tch
                mm = None
                for kt in range(2):
                    mm = p.op("tensor", lambda e, mt=mt, tch=tch, kt=kt, bk=bk: e.matmul(
                        out=pdt[bk][:], lhsT=dtw[:, kt * DIN + mt * 128: kt * DIN + (mt + 1) * 128],
                        rhs=dtr[:, kt * TSH + tch * 512: kt * TSH + (tch + 1) * 512],
                        start=(kt == 0), stop=(kt == 1)),
                        deps=([edtw, pdt_cons[bk]] + dtr_evs) if kt == 0 else None, chain=(kt == 0))
                er = p.op("scalar", lambda e, mt=mt, bk=bk: e.activation(out=sp1[:], in_=pdt[bk][:], func=AF.Relu, bias=dtbs[:, mt:mt + 1]), [mm, edtb, prev_ed])
                ea = p.op("scalar", lambda e, mt=mt, bk=bk: e.activation(out=sp2[:], in_=pdt[bk][:], func=AF.Abs, bias=dtbs[:, mt:mt + 1]))
                pdt_cons[bk] = ea
                ee = p.op("scalar", lambda e: e.activation(out=sp3[:], in_=sp2[:], func=AF.Exp, scale=-1.0))
                el = p.op("scalar", lambda e: e.activation(out=sp2[:], in_=sp3[:], func=AF.Ln, bias=one[:, 0:1]), [eone])
                ed = p.op("vector", lambda e, sl=sl, tch=tch: e.tensor_tensor(out=dttile[sl][:, tch * 512:(tch + 1) * 512], in0=sp1[:], in1=sp2[:], op=ALU.add),
                          [el, dt_cons[sl]])
                prev_ed = ed
                if tch == 1:
                    dt_cons[sl] = p.dma("scalar", lambda e, mt=mt, sl=sl: e.dma_start(
                        out=dt_o[mt * 128:(mt + 1) * 128, :], in_=dttile[sl][:]), deps=[ed], slot=f"dt{sl}")
    ctx.close()


def build_l0_ssmmid():
    nc = _mk()
    io = {
        "pst": nc.dram_tensor("pst", [NC, 288, TSH], F32, kind="ExternalInput").ap(),
        "lnw": nc.dram_tensor("lnw", [1, 288], F32, kind="ExternalInput").ap(),
        "dtWT": nc.dram_tensor("dtWT", [256, DIN], F32, kind="ExternalInput").ap(),
        "dtb": nc.dram_tensor("dtb", [DIN], F32, kind="ExternalInput").ap(),
        "ident": nc.dram_tensor("ident", [128, 128], BF16, kind="ExternalInput").ap(),
        "identf": nc.dram_tensor("identf", [128, 128], F32, kind="ExternalInput").ap(),
        "dt_o": nc.dram_tensor("dt_o", [DIN, TSH], F32, kind="ExternalOutput").ap(),
        "Bn_o": nc.dram_tensor("Bn_o", [NST, TSH], F32, kind="ExternalOutput").ap(),
        "Cn_o": nc.dram_tensor("Cn_o", [NST, TSH], F32, kind="ExternalOutput").ap(),
    }
    p = Prog(nc)
    emit_l0_ssmmid(p, io)
    p.build()
    return nc


def emit_l0_ffnbuild(p, io, pref="P4_"):
    nc = p.nc
    ISH = DM // NC
    RT = 4032
    vh, st, utT, vtop, utopT = io["vh"], io["st"], io["utT"], io["vtop"], io["utopT"]
    stop_, alpha, beta, vhr, urT, srd, rmat = io["stop_"], io["alpha"], io["beta"], io["vhr"], io["urT"], io["srd"], io["rmat"]
    weT = io["weT"]
    ctx = contextlib.ExitStack()
    sb = lambda n, s, d=F32: ctx.enter_context(nc.sbuf_tensor(pref + n, s, d))
    ps = lambda n, s, d=F32: ctx.enter_context(nc.psum_tensor(pref + n, s, d))
    vh_sb = sb("vh_sb", [128, 32 * ISH], BF16)
    st_sb = sb("st_sb", [128, 32])
    vtop_sb = sb("vtop_sb", [64, ISH], BF16)
    sh_sb = sb("sh_sb", [64, 4])
    vhr_sb = sb("vhr_sb", [4, ISH], BF16)
    ur_sb = sb("ur_sb", [4, DM], BF16)
    srd_sb = sb("srd_sb", [4, 3])
    r_sb = sb("r_sb", [4, 4], BF16)
    mt_sb = sb("mt_sb", [4, DM], BF16)
    rhs_t = [sb(f"rhs{i}", [128, 512], BF16) for i in range(2)]
    rtop = sb("rtop", [64, 512], BF16)
    outt = [sb(f"outt{i}", [128, 512], BF16) for i in range(4)]
    pmm = [ps(f"pmm{i}", [128, 512]) for i in range(5)]

    if True:
        evh1 = p.dma("gpsimd", lambda e: e.dma_start(
            out=vh_sb[:].rearrange("p (kt i) -> p kt i", kt=32)[:, 0:31, :],
            in_=vh[0:3968, :].rearrange("(kt p) i -> p kt i", p=128)), slot="vh")
        evh2 = p.dma("gpsimd", lambda e: e.dma_start(
            out=vh_sb[0:64, :].rearrange("p (kt i) -> p kt i", kt=32)[:, 31:32, :],
            in_=vh[3968:4032, :].rearrange("(kt p) i -> p kt i", p=64)), slot="vh")
        est1 = p.dma("sync", lambda e: e.dma_start(out=st_sb[:, 0:31], in_=st[0:3968].rearrange("(kt p) -> p kt", p=128),
                                                   allow_slow_non_contiguous=True), slot="st")
        est2 = p.dma("sync", lambda e: e.dma_start(out=st_sb[0:64, 31:32], in_=st[3968:4032].rearrange("(kt p) -> p kt", p=64),
                                                   allow_slow_non_contiguous=True), slot="st")
        evt = p.dma("gpsimd", lambda e: e.dma_start(out=vtop_sb[:], in_=vtop), slot="vt")
        esh = p.dma("sync", lambda e: e.dma_start(out=sh_sb[:, 0:1], in_=stop_), slot="sh")
        esa = p.dma("sync", lambda e: e.dma_start(out=sh_sb[:, 1:2], in_=alpha), slot="sh2")
        esb2 = p.dma("sync", lambda e: e.dma_start(out=sh_sb[:, 2:3], in_=beta), slot="sh3")
        evr = p.dma("gpsimd", lambda e: e.dma_start(out=vhr_sb[:], in_=vhr), slot="vr")
        eur = p.dma("gpsimd", lambda e: e.dma_start(out=ur_sb[:], in_=urT), slot="ur")
        esd = p.dma("sync", lambda e: e.dma_start(out=srd_sb[:, 0:2], in_=srd), slot="sd")
        erm = p.dma("gpsimd", lambda e: e.dma_start(out=r_sb[:], in_=rmat), slot="rm")
        escl = []
        for kt in range(32):
            kz = 128 if kt < 31 else 64
            escl.append(p.op("vector", lambda e, kt=kt, kz=kz: e.tensor_scalar(
                out=vh_sb[0:kz, kt * ISH:(kt + 1) * ISH], in0=vh_sb[0:kz, kt * ISH:(kt + 1) * ISH],
                scalar1=st_sb[0:kz, kt:kt + 1], scalar2=None, op0=ALU.mult), [evh1, evh2, est1, est2]))
        e1 = p.op("vector", lambda e: e.tensor_tensor(out=sh_sb[:, 3:4], in0=sh_sb[:, 0:1], in1=sh_sb[:, 1:2], op=ALU.mult), [esh, esa, esb2])
        e2 = p.op("vector", lambda e: e.tensor_tensor(out=sh_sb[:, 3:4], in0=sh_sb[:, 3:4], in1=sh_sb[:, 2:3], op=ALU.add))
        e3 = p.op("scalar", lambda e: e.activation(out=sh_sb[:, 3:4], in_=sh_sb[:, 3:4], func=AF.Relu), [e2])
        e4 = p.op("vector", lambda e: e.tensor_scalar(out=vtop_sb[:], in0=vtop_sb[:], scalar1=sh_sb[:, 3:4], scalar2=None, op0=ALU.mult), [e3, evt])
        e5 = p.op("scalar", lambda e: e.activation(out=srd_sb[:, 2:3], in_=srd_sb[:, 1:2], func=AF.Relu), [esd])
        e6 = p.op("vector", lambda e: e.tensor_tensor(out=srd_sb[:, 2:3], in0=srd_sb[:, 2:3], in1=srd_sb[:, 0:1], op=ALU.mult), [e5])
        e7 = p.op("vector", lambda e: e.tensor_scalar(out=ur_sb[:], in0=ur_sb[:], scalar1=srd_sb[:, 2:3], scalar2=None, op0=ALU.mult), [e6, eur])
        emt_ev = []
        for oc in range(8):
            emm = p.op("tensor", lambda e, oc=oc: e.matmul(out=pmm[4][0:4, :], lhsT=r_sb[:], rhs=ur_sb[:, oc * 512:(oc + 1) * 512],
                                                           start=True, stop=True), [e7, erm] + ([emt_ev[-1]] if emt_ev else []))
            emt_ev.append(p.op("scalar", lambda e, oc=oc: e.copy(out=mt_sb[:, oc * 512:(oc + 1) * 512], in_=pmm[4][0:4, :]), [emm]))
        rhs_cons = [None, None]
        outt_cons = [None] * 4
        bank_cons = [None] * 4
        rt_cons = None
        for oc in range(8):
            mm_last = [None] * 4
            for kt in range(32):
                sl = kt % 2
                ksz = 128 if kt < 31 else 64
                eld = p.dma("gpsimd", lambda e, kt=kt, oc=oc, sl=sl, ksz=ksz: e.dma_start(
                    out=rhs_t[sl][0:ksz, :], in_=utT[kt * 128:kt * 128 + ksz, oc * 512:(oc + 1) * 512]),
                    deps=[rhs_cons[sl]], slot=f"r{sl}")
                for mt in range(4):
                    mm_last[mt] = p.op("tensor", lambda e, kt=kt, mt=mt, sl=sl, ksz=ksz: e.matmul(
                        out=pmm[mt][:], lhsT=vh_sb[0:ksz, kt * ISH + mt * 128: kt * ISH + (mt + 1) * 128],
                        rhs=rhs_t[sl][0:ksz, :], start=(kt == 0), stop=False),
                        deps=([eld] + escl + [bank_cons[mt]]) if kt == 0 else [eld], chain=(kt == 0))
                rhs_cons[sl] = Ev("c_tensor", p.sem_count["c_tensor"])
            sl = 0
            eldt = p.dma("gpsimd", lambda e, oc=oc: e.dma_start(out=rtop[:], in_=utopT[:, oc * 512:(oc + 1) * 512]),
                         deps=[rt_cons], slot="rt")
            for mt in range(4):
                mm_last[mt] = p.op("tensor", lambda e, mt=mt, oc=oc: e.matmul(
                    out=pmm[mt][:], lhsT=vtop_sb[:, mt * 128:(mt + 1) * 128], rhs=rtop[:],
                    start=False, stop=False), deps=[eldt, e4] if mt == 0 else None, chain=(mt == 0))
            rt_cons = Ev("c_tensor", p.sem_count["c_tensor"])
            for mt in range(4):
                mm_last[mt] = p.op("tensor", lambda e, mt=mt, oc=oc: e.matmul(
                    out=pmm[mt][:], lhsT=vhr_sb[:, mt * 128:(mt + 1) * 128], rhs=mt_sb[:, oc * 512:(oc + 1) * 512],
                    start=False, stop=True), deps=([evr] + emt_ev) if mt == 0 else None, chain=(mt == 0))
            for mt in range(4):
                eev = p.op("scalar", lambda e, mt=mt: e.copy(out=outt[mt][:], in_=pmm[mt][:]), [mm_last[mt], outt_cons[mt]])
                bank_cons[mt] = eev
                outt_cons[mt] = p.dma("scalar", lambda e, mt=mt, oc=oc: e.dma_start(
                    out=weT[mt * 128:(mt + 1) * 128, oc * 512:(oc + 1) * 512], in_=outt[mt][:]), deps=[eev], slot=f"o{mt}")
    ctx.close()


def build_l0_ffnbuild():
    nc = _mk()
    ISH = DM // NC
    RT = 4032
    io = {
        "vh": nc.dram_tensor("vh", [RT, ISH], F32, kind="ExternalInput").ap(),
        "st": nc.dram_tensor("st", [RT], F32, kind="ExternalInput").ap(),
        "utT": nc.dram_tensor("utT", [RT, DM], BF16, kind="ExternalInput").ap(),
        "vtop": nc.dram_tensor("vtop", [64, ISH], F32, kind="ExternalInput").ap(),
        "utopT": nc.dram_tensor("utopT", [64, DM], F32, kind="ExternalInput").ap(),
        "stop_": nc.dram_tensor("stop_", [64, 1], F32, kind="ExternalInput").ap(),
        "alpha": nc.dram_tensor("alpha", [64, 1], F32, kind="ExternalInput").ap(),
        "beta": nc.dram_tensor("beta", [64, 1], F32, kind="ExternalInput").ap(),
        "vhr": nc.dram_tensor("vhr", [4, ISH], F32, kind="ExternalInput").ap(),
        "urT": nc.dram_tensor("urT", [4, DM], F32, kind="ExternalInput").ap(),
        "srd": nc.dram_tensor("srd", [4, 2], F32, kind="ExternalInput").ap(),
        "rmat": nc.dram_tensor("rmat", [4, 4], F32, kind="ExternalInput").ap(),
        "weT": nc.dram_tensor("weT", [ISH, DM], BF16, kind="ExternalOutput").ap(),
    }
    p = Prog(nc)
    emit_l0_ffnbuild(p, io)
    p.build()
    return nc


def emit_l0_scan(p, io, pref="P5_"):
    nc = p.nc
    xc, zsd, alog, dpar, yg = io["xc"], io["zsd"], io["alog"], io["dpar"], io["yg"]
    dt_ap = io["dt_ap"]
    bn_ap = io["bn_ap"]
    cn_ap = io["cn_ap"]
    ctx = contextlib.ExitStack()
    sb = lambda n, s, d=F32: ctx.enter_context(nc.sbuf_tensor(pref + n, s, d))
    TC = 512
    NCH = TOK // TC
    a_sb = sb("a_sb", [128, 8 * NST])
    d_sb = sb("d_sb", [128, 8])
    bb = sb("bb", [128, NST * TC])
    cb2 = sb("cb2", [128, NST * TC])
    dtc = sb("dtc", [128, TC])
    xcc = sb("xcc", [128, TC], BF16)
    zsc = sb("zsc", [128, TC], BF16)
    ub = sb("ub", [128, TC])
    dab = sb("dab", [128, NST * TC])
    dbx = sb("dbx", [128, NST * TC])
    hb = sb("hb", [128, NST * TC])
    tmp16 = sb("tmp16", [128, NST])
    hinit = sb("hinit", [128, 8 * NST])
    yb = sb("yb", [128, TC])
    g1 = sb("g1", [128, TC])
    ygt = sb("ygt", [128, TC], BF16)

    if True:
        ea = p.dma("sync", lambda e: e.dma_start(out=a_sb[:].rearrange("p (dt n) -> p dt n", dt=8),
                                                 in_=alog.rearrange("(dt p) n -> p dt n", p=128)), slot="al")
        eax = p.op("scalar", lambda e: e.activation(out=a_sb[:], in_=a_sb[:], func=AF.Exp), [ea])
        ean = p.op("vector", lambda e: e.tensor_scalar(out=a_sb[:], in0=a_sb[:], scalar1=-1.0, scalar2=None, op0=ALU.mult), [eax])
        ed = p.dma("sync", lambda e: e.dma_start(out=d_sb[:], in_=dpar.rearrange("(dt p) -> p dt", p=128),
                                                 allow_slow_non_contiguous=True), slot="dp")
        bc_cons = None
        ld_cons = [[], [], []]
        st_cons = None
        buf_cons = None
        for b in range(2):
            ehz = p.op("vector", lambda e: e.memset(hinit[:], 0.0), [buf_cons])
            for ch in range(NCH // 2):
                off = b * (TOK // 2) + ch * TC
                ebb = p.dma("gpsimd", lambda e, off=off: e.dma_start(
                    out=bb[:].rearrange("p (n t) -> p n t", n=NST),
                    in_=bn_ap(off).partition_broadcast(128)), deps=[bc_cons], slot="bb")
                ecb = p.dma("gpsimd", lambda e, off=off: e.dma_start(
                    out=cb2[:].rearrange("p (n t) -> p n t", n=NST),
                    in_=cn_ap(off).partition_broadcast(128)), deps=[bc_cons], slot="cb")
                for dt_i in range(8):
                    ro = dt_i * 128
                    e_dt = p.dma("sync", lambda e, ro=ro, off=off: e.dma_start(out=dtc[:], in_=dt_ap(e, p, ro, off)),
                                 deps=ld_cons[0], slot="dt")
                    e_xc = p.dma("sync", lambda e, ro=ro, off=off: e.dma_start(out=xcc[:], in_=xc[ro:ro + 128, off:off + TC]),
                                 deps=ld_cons[1], slot="xc")
                    e_zs = p.dma("sync", lambda e, ro=ro, off=off: e.dma_start(out=zsc[:], in_=zsd[ro:ro + 128, off:off + TC]),
                                 deps=ld_cons[2], slot="zs")
                    edas = []
                    for n in range(NST):
                        edas.append(p.op("scalar", lambda e, n=n, dt_i=dt_i: e.activation(
                            out=dab[:, n * TC:(n + 1) * TC], in_=dtc[:], func=AF.Exp,
                            scale=a_sb[:, dt_i * NST + n: dt_i * NST + n + 1]), [e_dt, ean, buf_cons]))
                    eu = p.op("vector", lambda e: e.tensor_tensor(out=ub[:], in0=dtc[:], in1=xcc[:], op=ALU.mult), [e_dt, e_xc, buf_cons])
                    edbx = p.op("vector", lambda e: e.tensor_tensor(
                        out=dbx[:].rearrange("p (n t) -> p n t", n=NST),
                        in0=bb[:].rearrange("p (n t) -> p n t", n=NST),
                        in1=ub[:].unsqueeze(1).broadcast_to([128, NST, TC]), op=ALU.mult), [eu, ebb])
                    ef1 = p.op("vector", lambda e, dt_i=dt_i: e.tensor_tensor(
                        out=tmp16[:].unsqueeze(2), in0=dab[:].rearrange("p (n t) -> p n t", n=NST)[:, :, 0:1],
                        in1=hinit[:, dt_i * NST:(dt_i + 1) * NST].unsqueeze(2), op=ALU.mult), edas + [edbx])
                    ef2 = p.op("vector", lambda e: e.tensor_tensor(
                        out=dbx[:].rearrange("p (n t) -> p n t", n=NST)[:, :, 0:1],
                        in0=dbx[:].rearrange("p (n t) -> p n t", n=NST)[:, :, 0:1],
                        in1=tmp16[:].unsqueeze(2), op=ALU.add))
                    ef3 = p.op("vector", lambda e: e.memset(dab[:].rearrange("p (n t) -> p n t", n=NST)[:, :, 0:1], 0.0))
                    esc = p.op("vector", lambda e: e.tensor_tensor_scan(
                        out=hb[:], data0=dab[:], data1=dbx[:], initial=0.0, op0=ALU.mult, op1=ALU.add), [ef3, ef2])
                    eh = p.op("vector", lambda e, dt_i=dt_i: e.tensor_copy(
                        out=hinit[:, dt_i * NST:(dt_i + 1) * NST].unsqueeze(2),
                        in_=hb[:].rearrange("p (n t) -> p n t", n=NST)[:, :, TC - 1:TC]), [esc])
                    ep = p.op("vector", lambda e: e.tensor_tensor(
                        out=dbx[:].rearrange("p (n t) -> p n t", n=NST),
                        in0=hb[:].rearrange("p (n t) -> p n t", n=NST),
                        in1=cb2[:].rearrange("p (n t) -> p n t", n=NST), op=ALU.mult), [esc, ecb])
                    ey = p.op("vector", lambda e: e.tensor_reduce(
                        out=yb[:], in_=dbx[:].rearrange("p (n t) -> p t n", n=NST), axis=AX.X, op=ALU.add), [ep])
                    eg1 = p.op("vector", lambda e, dt_i=dt_i: e.scalar_tensor_tensor(
                        out=g1[:], in0=xcc[:], scalar=d_sb[:, dt_i:dt_i + 1], in1=yb[:], op0=ALU.mult, op1=ALU.add), [ey, ed])
                    eg2 = p.op("vector", lambda e: e.tensor_tensor(out=ygt[:], in0=g1[:], in1=zsc[:], op=ALU.mult), [e_zs, st_cons])
                    st_cons = p.dma("scalar", lambda e, ro=ro, off=off: e.dma_start(
                        out=yg[ro:ro + 128, off:off + TC], in_=ygt[:]), deps=[eg2], slot="st")
                    ld_cons = [[eu, edas[-1]], [eg1], [eg2]]
                    buf_cons = eg1
                bc_cons = Ev("c_vector", p.sem_count["c_vector"])
    ctx.close()


def build_l0_scan():
    nc = _mk()
    io = {
        "xc": nc.dram_tensor("xc", [DSH, TOK], BF16, kind="ExternalInput").ap(),
        "zsd": nc.dram_tensor("zsd", [DSH, TOK], BF16, kind="ExternalInput").ap(),
        "alog": nc.dram_tensor("alog", [DSH, NST], F32, kind="ExternalInput").ap(),
        "dpar": nc.dram_tensor("dpar", [DSH], F32, kind="ExternalInput").ap(),
        "yg": nc.dram_tensor("yg", [DSH, TOK], BF16, kind="ExternalOutput").ap(),
    }
    dtd = nc.dram_tensor("dtd", [DSH, TOK], F32, kind="ExternalInput").ap()
    Bn = nc.dram_tensor("Bn", [NST, TOK], F32, kind="ExternalInput").ap()
    Cn = nc.dram_tensor("Cn", [NST, TOK], F32, kind="ExternalInput").ap()
    io["dt_ap"] = lambda e, p, ro, off: dtd[ro:ro + 128, off:off + 512]
    io["bn_ap"] = lambda off: Bn[:, off:off + 512]
    io["cn_ap"] = lambda off: Cn[:, off:off + 512]
    p = Prog(nc)
    emit_l0_scan(p, io)
    p.build()
    return nc


def emit_l0_tail(p, io, pref="P6_"):
    nc = p.nc
    woT, weT, xnT, x, fbias, ident = io["woT"], io["weT"], io["xnT"], io["x"], io["fbias"], io["ident"]
    xnew, xn1T = io["xnew"], io["xn1T"]
    ygd_ap = io["ygd_ap"]
    ctx = contextlib.ExitStack()
    sb = lambda n, s, d=F32: ctx.enter_context(nc.sbuf_tensor(pref + n, s, d))
    ps = lambda n, s, d=F32: ctx.enter_context(nc.psum_tensor(pref + n, s, d))
    yg_tg = sb("yg_tg", [128, 64 * 512], BF16)
    xn_tg = sb("xn_tg", [128, 32 * 512], BF16)
    wot = [sb(f"wot{i}", [128, 512], BF16) for i in range(4)]
    wet = [sb(f"wet{i}", [128, 512], BF16) for i in range(4)]
    bias_bc = sb("bias_bc", [128, DM])
    xt = [sb(f"xt{i}", [128, 512]) for i in range(4)]
    a_ev = [sb(f"a_ev{i}", [128, 512]) for i in range(4)]
    xo = [sb(f"xo{i}", [128, 512]) for i in range(4)]
    sqs = sb("sqs", [128, 512])
    stacc = sb("stacc", [128, 16])
    stmp = sb("stmp", [128, 8])
    pa = [ps(f"pa{i}", [128, 512]) for i in range(4)]
    pf = [ps(f"pf{i}", [128, 512]) for i in range(4)]
    xnw = [sb(f"xnw{i}", [128, DM]) for i in range(2)]
    stat = sb("stat", [128, 8])
    x1b = [sb(f"x1b{i}", [128, DM], BF16) for i in range(2)]
    sid = sb("sid", [128, 128], BF16)
    otb = [sb(f"otb{i}", [128, 1024], BF16) for i in range(2)]
    scst = sb("scst", [128, 1])
    ptb = [pa[0][:].bitcast(BF16), pa[1][:].bitcast(BF16)]

    if True:
        eid = p.dma("sync", lambda e: e.dma_start(out=sid[:], in_=ident), slot="id")
        ebias = p.dma("gpsimd", lambda e: e.dma_start(out=bias_bc[:], in_=fbias.partition_broadcast(128).squeeze(1)), slot="fb")
        ecst = p.op("vector", lambda e: e.memset(scst[:], 1e-5))
        eacc0 = p.op("vector", lambda e: e.memset(stacc[:], 0.0))
        yg_cons = None
        xn_cons = None
        w_cons = [None] * 4
        we_cons = [None] * 4
        xt_cons = [None] * 4
        pa_cons = [None] * 4
        pf_cons = [None] * 4
        aev_cons = [None] * 4
        xo_cons = [None] * 4
        st_cons = [None] * 4
        stq = [None] * 4
        acc_done = [None] * 8
        row_done = [[None] * 4 for _ in range(2)]
        for tg in range(2):
            eygs = []
            for q in range(8):
                eygs.append(p.dma("sync", lambda e, tg=tg, q=q: e.dma_start(
                    out=yg_tg[:].rearrange("p (kt t) -> p kt t", kt=64)[:, q * 8:(q + 1) * 8, :],
                    in_=ygd_ap(e, p, q, tg)),
                    deps=[yg_cons], slot=f"yg{q}"))
            exns = []
            for q in range(4):
                exns.append(p.dma("sync", lambda e, tg=tg, q=q: e.dma_start(
                    out=xn_tg[:].rearrange("p (kt t) -> p kt t", kt=32)[:, q * 8:(q + 1) * 8, :],
                    in_=xnT.rearrange("(kt p) t -> p kt t", p=128)[:, q * 8:(q + 1) * 8, tg * 512:(tg + 1) * 512]),
                    deps=[xn_cons], slot=f"xn{q}"))
            for ch in range(8):
                mma = [None] * 4
                for kt in range(64):
                    sl = kt % 4
                    qn = "gpsimd" if kt % 2 == 0 else "sync"
                    ew = p.dma(qn, lambda e, kt=kt, ch=ch, sl=sl: e.dma_start(
                        out=wot[sl][:], in_=woT[kt * 128:(kt + 1) * 128, ch * 512:(ch + 1) * 512]),
                        deps=[w_cons[sl]], slot=f"w{sl}")
                    for tt in range(4):
                        mma[tt] = p.op("tensor", lambda e, kt=kt, tt=tt, sl=sl: e.matmul(
                            out=pa[tt][:], lhsT=yg_tg[:, kt * 512 + tt * 128: kt * 512 + (tt + 1) * 128],
                            rhs=wot[sl][:], start=(kt == 0), stop=(kt == 63)),
                            deps=([ew, pa_cons[tt]] + eygs) if kt == 0 else [ew], chain=(kt == 0))
                    w_cons[sl] = Ev("c_tensor", p.sem_count["c_tensor"])
                mmf = [None] * 4
                for kt in range(32):
                    sl = kt % 4
                    qn = "scalar" if kt % 2 == 0 else "gpsimd"
                    ew = p.dma(qn, lambda e, kt=kt, ch=ch, sl=sl: e.dma_start(
                        out=wet[sl][:], in_=weT[kt * 128:(kt + 1) * 128, ch * 512:(ch + 1) * 512]),
                        deps=[we_cons[sl]], slot=f"v{sl}")
                    for tt in range(4):
                        mmf[tt] = p.op("tensor", lambda e, kt=kt, tt=tt, sl=sl: e.matmul(
                            out=pf[tt][:], lhsT=xn_tg[:, kt * 512 + tt * 128: kt * 512 + (tt + 1) * 128],
                            rhs=wet[sl][:], start=(kt == 0), stop=(kt == 31)),
                            deps=([ew, pf_cons[tt]] + exns) if kt == 0 else [ew], chain=(kt == 0))
                    we_cons[sl] = Ev("c_tensor", p.sem_count["c_tensor"])
                for tt in range(4):
                    i = tg * 4 + tt
                    ex = p.dma("sync", lambda e, tg=tg, tt=tt, ch=ch: e.dma_start(
                        out=xt[tt][:], in_=x[tg * 512 + tt * 128: tg * 512 + (tt + 1) * 128, ch * 512:(ch + 1) * 512]),
                        deps=[xt_cons[tt]], slot=f"x{tt}")
                    ea = p.op("scalar", lambda e, tt=tt: e.copy(out=a_ev[tt][:], in_=pa[tt][:]), [mma[tt], aev_cons[tt]])
                    pa_cons[tt] = ea
                    e1 = p.op("vector", lambda e, tt=tt: e.tensor_tensor(out=xo[tt][:], in0=a_ev[tt][:], in1=pf[tt][:], op=ALU.add), [ea, mmf[tt], xo_cons[tt], st_cons[tt]])
                    pf_cons[tt] = e1
                    aev_cons[tt] = e1
                    e2 = p.op("vector", lambda e, tt=tt: e.tensor_tensor(out=xo[tt][:], in0=xo[tt][:], in1=xt[tt][:], op=ALU.add), [ex])
                    e3 = p.op("vector", lambda e, tt=tt, ch=ch: e.tensor_tensor(out=xo[tt][:], in0=xo[tt][:], in1=bias_bc[:, ch * 512:(ch + 1) * 512], op=ALU.add), [ebias])
                    xt_cons[tt] = e2
                    est = p.dma("scalar", lambda e, tg=tg, tt=tt, ch=ch: e.dma_start(
                        out=xnew[tg * 512 + tt * 128: tg * 512 + (tt + 1) * 128, ch * 512:(ch + 1) * 512], in_=xo[tt][:]),
                        deps=[e3], slot=f"xo{tt}")
                    st_cons[tt] = est
                    ev1 = p.op("vector", lambda e, tt=tt: e.tensor_reduce(out=stmp[:, tt:tt + 1], in_=xo[tt][:], axis=AX.X, op=ALU.add), [e3])
                    ev2 = p.op("vector", lambda e, i=i, tt=tt: e.tensor_tensor(out=stacc[:, 2 * i:2 * i + 1], in0=stacc[:, 2 * i:2 * i + 1], in1=stmp[:, tt:tt + 1], op=ALU.add), [ev1, eacc0])
                    es1 = p.op("scalar", lambda e, tt=tt: e.activation(out=sqs[:], in_=xo[tt][:], func=AF.Square, accum_out=stmp[:, 4 + tt:5 + tt]), [e3, stq[tt]])
                    ev3 = p.op("vector", lambda e, i=i, tt=tt: e.tensor_tensor(out=stacc[:, 2 * i + 1:2 * i + 2], in0=stacc[:, 2 * i + 1:2 * i + 2], in1=stmp[:, 4 + tt:5 + tt], op=ALU.add), [es1, eacc0])
                    stq[tt] = ev3
                    xo_cons[tt] = ev3
                    acc_done[i] = ev3
                    row_done[tg][tt] = est
            yg_cons = Ev("c_tensor", p.sem_count["c_tensor"])
            xn_cons = yg_cons
        cons2 = [None, None]
        x1_cons = [None, None]
        tr_cons = [None, None]
        ot_cons = [None, None]
        for i in range(TSH // 128):
            sl = i % 2
            xb, xnb = xnw[sl], x1b[sl]
            eld = p.dma("sync", lambda e, xb=xb, i=i: e.dma_start(out=xb[:], in_=xnew[i * 128:(i + 1) * 128, :]),
                        deps=[cons2[sl], row_done[i // 4][i % 4]], slot=f"r{sl}")
            em = p.op("vector", lambda e, i=i: e.tensor_scalar(out=stat[:, 2:3], in0=stacc[:, 2 * i:2 * i + 1], scalar1=1.0 / DM, scalar2=None, op0=ALU.mult), [acc_done[i]])
            em2 = p.op("vector", lambda e, i=i: e.tensor_scalar(out=stat[:, 3:4], in0=stacc[:, 2 * i + 1:2 * i + 2], scalar1=1.0 / DM, scalar2=None, op0=ALU.mult))
            emm = p.op("vector", lambda e: e.tensor_tensor(out=stat[:, 4:5], in0=stat[:, 2:3], in1=stat[:, 2:3], op=ALU.mult))
            evar = p.op("vector", lambda e: e.tensor_tensor(out=stat[:, 5:6], in0=stat[:, 3:4], in1=stat[:, 4:5], op=ALU.subtract))
            esd = p.op("scalar", lambda e: e.activation(out=stat[:, 6:7], in_=stat[:, 5:6], func=AF.Sqrt, bias=scst[:, 0:1]), [evar, ecst])
            ers = p.op("vector", lambda e: e.reciprocal(out=stat[:, 7:8], in_=stat[:, 6:7]), [esd])
            exn = p.op("vector", lambda e, xb=xb, xnb=xnb: e.tensor_scalar(out=xnb[:], in0=xb[:], scalar1=stat[:, 2:3], scalar2=stat[:, 7:8], op0=ALU.subtract, op1=ALU.mult), [x1_cons[sl], eld, ers])
            cons2[sl] = exn
            last = None
            for jg in range(4):
                og = jg % 2
                evs = []
                for jj in range(8):
                    j = jg * 8 + jj
                    pj = j % 2
                    etr = p.op("tensor", lambda e, j=j, pj=pj, xnb=xnb: e.transpose(out=ptb[pj][:, 0:128], in_=xnb[:, j * 128:(j + 1) * 128], identity=sid[:]),
                               [exn, eid, tr_cons[pj], pa_cons[pj]])
                    eev = p.op("scalar", lambda e, jj=jj, og=og, pj=pj: e.copy(out=otb[og][:, jj * 128:(jj + 1) * 128], in_=ptb[pj][:, 0:128]),
                               [etr, ot_cons[og] if jj == 0 else None])
                    pa_cons[pj] = eev
                    tr_cons[pj] = eev
                    evs.append(eev)
                    last = eev
                ot_cons[og] = p.dma("scalar", lambda e, jg=jg, og=og, i=i: e.dma_start(
                    out=xn1T.rearrange("(jt p) t -> p jt t", p=128)[:, jg * 8:(jg + 1) * 8, i * 128:(i + 1) * 128],
                    in_=otb[og][:].rearrange("p (jt t) -> p jt t", jt=8)), deps=evs, slot=f"s{og}")
            x1_cons[sl] = last
    ctx.close()


def build_l0_tail():
    nc = _mk()
    io = {
        "woT": nc.dram_tensor("woT", [DIN, DM], BF16, kind="ExternalInput").ap(),
        "weT": nc.dram_tensor("weT", [DM, DM], BF16, kind="ExternalInput").ap(),
        "xnT": nc.dram_tensor("xnT", [DM, TSH], BF16, kind="ExternalInput").ap(),
        "x": nc.dram_tensor("x", [TSH, DM], F32, kind="ExternalInput").ap(),
        "fbias": nc.dram_tensor("fbias", [1, DM], F32, kind="ExternalInput").ap(),
        "ident": nc.dram_tensor("ident", [128, 128], BF16, kind="ExternalInput").ap(),
        "xnew": nc.dram_tensor("xnew", [TSH, DM], F32, kind="ExternalOutput").ap(),
        "xn1T": nc.dram_tensor("xn1T", [DM, TSH], BF16, kind="ExternalOutput").ap(),
    }
    ygd = nc.dram_tensor("ygd", [DIN, TSH], BF16, kind="ExternalInput").ap()
    io["ygd_ap"] = lambda e, p, q, tg: ygd.rearrange("(kt p) t -> p kt t", p=128)[:, q * 8:(q + 1) * 8, tg * 512:(tg + 1) * 512]
    p = Prog(nc)
    emit_l0_tail(p, io)
    p.build()
    return nc


def emit_l1_qkv(p, io, pref="P7_"):
    nc = p.nc
    xn1T, wqT, wgT, bg, identf = io["xn1T"], io["wqT"], io["wgT"], io["bg"], io["identf"]
    qkv, gatesT = io["qkv"], io["gatesT"]
    ctx = contextlib.ExitStack()
    sb = lambda n, s, d=F32: ctx.enter_context(nc.sbuf_tensor(pref + n, s, d))
    ps = lambda n, s, d=F32: ctx.enter_context(nc.psum_tensor(pref + n, s, d))
    xn_sb = sb("xn_sb", [128, 32 * TSH], BF16)
    wq_t = [sb(f"wq{i}", [128, 32 * 128], BF16) for i in range(2)]
    wg_sb = sb("wg_sb", [128, 32 * 96], BF16)
    bg_sb = sb("bg_sb", [96, 1])
    out_t = [sb(f"out{i}", [128, 512], BF16) for i in range(4)]
    gt_sb = sb("gt_sb", [96, TSH])
    gto = sb("gto", [128, 96])
    sidf = sb("sidf", [128, 128], F32)
    pq = [ps(f"pq{i}", [128, 512]) for i in range(2)]
    pg = ps("pg", [128, 512])

    if True:
        eidf = p.dma("sync", lambda e: e.dma_start(out=sidf[:], in_=identf), slot="idf")
        exn = []
        for q in range(4):
            exn.append(p.dma("sync", lambda e, q=q: e.dma_start(
                out=xn_sb[:].rearrange("p (kt t) -> p kt t", kt=32)[:, q * 8:(q + 1) * 8, :],
                in_=xn1T.rearrange("(kt p) t -> p kt t", p=128)[:, q * 8:(q + 1) * 8, :]), slot=f"xn{q}"))
        ewg = p.dma("gpsimd", lambda e: e.dma_start(
            out=wg_sb[:].rearrange("p (kt m) -> p kt m", kt=32),
            in_=wgT.rearrange("(kt p) m -> p kt m", p=128)), slot="wg")
        ebg = p.dma("sync", lambda e: e.dma_start(out=bg_sb[:], in_=bg), slot="bg")
        wq_cons = [None, None]
        out_cons = [None] * 4
        pq_cons = [None, None]
        for mt in range(48):
            sl = mt % 2
            ew = p.dma("gpsimd", lambda e, mt=mt, sl=sl: e.dma_start(
                out=wq_t[sl][:].rearrange("p (kt m) -> p kt m", kt=32),
                in_=wqT.rearrange("(kt p) m -> p kt m", p=128)[:, :, mt * 128:(mt + 1) * 128]),
                deps=[wq_cons[sl]], slot=f"wq{sl}")
            for tch in range(2):
                bk = tch
                mm = None
                for kt in range(32):
                    mm = p.op("tensor", lambda e, kt=kt, mt=mt, tch=tch, sl=sl, bk=bk: e.matmul(
                        out=pq[bk][:], lhsT=wq_t[sl][:, kt * 128:(kt + 1) * 128],
                        rhs=xn_sb[:, kt * TSH + tch * 512: kt * TSH + (tch + 1) * 512],
                        start=(kt == 0), stop=(kt == 31)),
                        deps=([ew, pq_cons[bk]] + exn) if kt == 0 else None, chain=(kt == 0))
                ob = (mt * 2 + tch) % 4
                eev = p.op("scalar", lambda e, ob=ob, bk=bk: e.copy(out=out_t[ob][:], in_=pq[bk][:]), [mm, out_cons[ob]])
                pq_cons[bk] = eev
                out_cons[ob] = p.dma("scalar", lambda e, ob=ob, mt=mt, tch=tch: e.dma_start(
                    out=qkv[mt * 128:(mt + 1) * 128, tch * 512:(tch + 1) * 512], in_=out_t[ob][:]), deps=[eev], slot=f"o{ob}")
            wq_cons[sl] = Ev("c_tensor", p.sem_count["c_tensor"])
        eg_ev = []
        for tch in range(2):
            mm = None
            for kt in range(32):
                mm = p.op("tensor", lambda e, kt=kt, tch=tch: e.matmul(
                    out=pg[0:96, :], lhsT=wg_sb[:, kt * 96:(kt + 1) * 96],
                    rhs=xn_sb[:, kt * TSH + tch * 512: kt * TSH + (tch + 1) * 512],
                    start=(kt == 0), stop=(kt == 31)),
                    deps=([ewg] + exn + eg_ev) if kt == 0 else None, chain=(kt == 0))
            eg_ev.append(p.op("scalar", lambda e, tch=tch: e.activation(
                out=gt_sb[:, tch * 512:(tch + 1) * 512], in_=pg[0:96, :], func=AF.Sigmoid, bias=bg_sb[:]), [mm, ebg]))
        gt_st = None
        for it2 in range(TSH // 128):
            etr = p.op("tensor", lambda e, it2=it2: e.transpose(out=pg[0:128, 0:96], in_=gt_sb[:, it2 * 128:(it2 + 1) * 128],
                                                               identity=sidf[0:96, 0:96]), eg_ev + [eidf, gt_st])
            eev = p.op("scalar", lambda e: e.copy(out=gto[:, 0:96], in_=pg[0:128, 0:96]), [etr])
            gt_st = p.dma("scalar", lambda e, it2=it2: e.dma_start(out=gatesT[it2 * 128:(it2 + 1) * 128, :], in_=gto[:, 0:96]), deps=[eev], slot="gt")
    ctx.close()


def build_l1_qkv():
    nc = _mk()
    io = {
        "xn1T": nc.dram_tensor("xn1T", [DM, TSH], BF16, kind="ExternalInput").ap(),
        "wqT": nc.dram_tensor("wqT", [DM, 6144], BF16, kind="ExternalInput").ap(),
        "wgT": nc.dram_tensor("wgT", [DM, 96], F32, kind="ExternalInput").ap(),
        "bg": nc.dram_tensor("bg", [96, 1], F32, kind="ExternalInput").ap(),
        "identf": nc.dram_tensor("identf", [128, 128], F32, kind="ExternalInput").ap(),
        "qkv": nc.dram_tensor("qkv", [6144, TSH], BF16, kind="ExternalOutput").ap(),
        "gatesT": nc.dram_tensor("gatesT", [TSH, 96], F32, kind="ExternalOutput").ap(),
    }
    p = Prog(nc)
    emit_l1_qkv(p, io)
    p.build()
    return nc


NEGC = -1.0e30
SCL = 0.08838834764831845


def emit_l1_nsa(p, io, pref="P8_"):
    nc = p.nc
    load_qg = io["load_qg"]
    load_kv = io["load_kv"]
    gt_ap = io["gt_ap"]
    kw1T = io["kw1T"]
    kb1 = io["kb1"]
    kw2T = io["kw2T"]
    kb2 = io["kb2"]
    vw1T = io["vw1T"]
    vb1 = io["vb1"]
    vw2T = io["vw2T"]
    vb2 = io["vb2"]
    kpos = io["kpos"]
    vpos = io["vpos"]
    memk = io["memk"]
    memv = io["memv"]
    cosf = io["cosf"]
    sinf = io["sinf"]
    cmneg = io["cmneg"]
    baneg = io["baneg"]
    bzneg = io["bzneg"]
    diagf = io["diagf"]
    identb = io["identb"]
    A_out = io["A_out"]
    imp_d = io["imp_d"]

    ctx = contextlib.ExitStack()
    sb = lambda n, s, d=F32: ctx.enter_context(nc.sbuf_tensor(pref + n, s, d))
    ps = lambda n, s, d=F32: ctx.enter_context(nc.psum_tensor(pref + n, s, d))
    k_sb = [sb(f"k{u}", [128, LSEQ], BF16) for u in range(2)]
    v_sb = [sb(f"v{u}", [128, LSEQ], BF16) for u in range(2)]
    rk_sb = [sb(f"rk{u}", [128, LSEQ], BF16) for u in range(2)]
    ck_sb = [sb(f"ck{u}", [128, 257], BF16) for u in range(2)]
    cv_sb = [sb(f"cv{u}", [128, 257], BF16) for u in range(2)]
    cvt_a = [sb(f"cvta{u}", [128, 2 * 128], BF16) for u in range(2)]
    cvt_b = [sb(f"cvtb{u}", [1, 128], BF16) for u in range(2)]
    vt_sb = sb("vt_sb", [128, 32 * 128], BF16)
    cos_sb = sb("cos_sb", [128, LSEQ], BF16)
    sin_sb = sb("sin_sb", [128, LSEQ], BF16)
    qg_sb = sb("qg_sb", [128, LSEQ], BF16)
    qsw_sb = sb("qsw_sb", [128, LSEQ], BF16)
    rtmp2 = sb("rtmp2", [128, LSEQ])
    rq_sb = sb("rq_sb", [128, LSEQ], BF16)
    w1_sb = [sb(f"w1_{i}", [128, 16 * 128], BF16) for i in range(2)]
    w2_sb = sb("w2_sb", [128, 16 * 128], BF16)
    b1_sb = sb("b1_sb", [128, 16])
    b2_sb = sb("b2_sb", [128, 1])
    pos_sb = sb("pos_sb", [128, 16])
    mem_sb = sb("mem_sb", [128, 2])
    ckin = sb("ckin", [128, 16 * 256], BF16)
    hid = sb("hid", [128, 16 * 256], BF16)
    impw = sb("impw", [128, 257])
    impl = sb("impl", [128, 257])
    bm8 = sb("bm8", [128, 32 * 256], U8)
    bmneg_t = sb("bmneg_t", [128, 256])
    dneg = sb("dneg", [128, 128])
    cmtf = sb("cmtf", [128, 257])
    ba_sb = sb("ba_sb", [128, 256])
    bz_sb = sb("bz_sb", [128, 128])
    dg_sb = sb("dg_sb", [128, 128])
    gt_it = sb("gt_it", [128, 24])
    fsm = sb("fsm", [128, LSEQ])
    fp_sb = sb("fp_sb", [128, LSEQ], BF16)
    pts = sb("pts", [128, 32 * 128], BF16)
    stats = sb("stats", [128, 10])
    m8 = sb("m8", [128, 8])
    mwork = sb("mwork", [128, 256])
    fo_sb = sb("fo_sb", [128, 128])
    so_sb = sb("so_sb", [128, 128])
    co_sb = sb("co_sb", [128, 128])
    abuf = sb("abuf", [128, 128], BF16)
    aold = sb("aold", [128, 128], BF16)
    sid = sb("sid", [128, 128], BF16)
    pp = [ps(f"pp{i}", [128, 512]) for i in range(4)]
    ptb = [ps(f"ptb{i}", [128, 1024], BF16) for i in range(2)]

    if True:
        E = p.op
        D = p.dma
        eid = D("sync", lambda e: e.dma_start(out=sid[:], in_=identb), slot="id")
        ecos = D("sync", lambda e: e.dma_start(out=cos_sb[:], in_=cosf), slot="cos")
        esin = D("sync", lambda e: e.dma_start(out=sin_sb[:], in_=sinf), slot="sin")
        eba = D("sync", lambda e: e.dma_start(out=ba_sb[:], in_=baneg), slot="ba")
        ebz = D("sync", lambda e: e.dma_start(out=bz_sb[:], in_=bzneg), slot="bz")
        edg = D("sync", lambda e: e.dma_start(out=dg_sb[:], in_=diagf), slot="dg")
        ekv = []
        for u in range(2):
            ev_k, ev_v = load_kv(p, u, k_sb[u], v_sb[u])
            ekv.append(ev_k)
            ekv.append(ev_v)

        ck_done = [None, None]
        cv_done = [None, None]
        w_cons = None
        pos_cons = None
        mem_cons = None
        pbc = [None] * 4
        ptc = [None] * 2
        for which in ("k", "v"):
            wsrcT, bb1, wsrc2, bb2, possrc, memsrc = ((kw1T, kb1, kw2T, kb2, kpos, memk) if which == "k"
                                                      else (vw1T, vb1, vw2T, vb2, vpos, memv))
            ew1l = [None, None]
            ew2 = D("gpsimd", lambda e, wsrc2=wsrc2: e.dma_start(
                out=w2_sb[:].rearrange("p (kt m) -> p kt m", kt=16),
                in_=wsrc2.rearrange("(kt p) m -> p kt m", p=128)), deps=[w_cons], slot="w2")
            eb1 = D("sync", lambda e, bb1=bb1: e.dma_start(out=b1_sb[:], in_=bb1.rearrange("(kt p) o -> p (kt o)", p=128),
                                                           allow_slow_non_contiguous=True), deps=[mem_cons], slot="b1")
            eb2 = D("sync", lambda e, bb2=bb2: e.dma_start(out=b2_sb[:], in_=bb2), deps=[mem_cons], slot="b2")
            for u in range(2):
                src = k_sb[u] if which == "k" else v_sb[u]
                epos = D("sync", lambda e, u=u, possrc=possrc: e.dma_start(out=pos_sb[:], in_=possrc[u]), deps=[pos_cons], slot="pos")
                emem = D("sync", lambda e, u=u, memsrc=memsrc: e.dma_start(out=mem_sb[:, 0:1], in_=memsrc[u]), deps=[mem_cons], slot="mem")
                eci = []
                for c in range(16):
                    eci.append(E("vector", lambda e, c=c, src=src: e.tensor_scalar(
                        out=ckin[:, c * 256:(c + 1) * 256],
                        in0=src[:].rearrange("p (nc cb) -> p cb nc", cb=16)[:, c, :],
                        scalar1=pos_sb[:, c:c + 1], scalar2=None, op0=ALU.add), [epos, ekv[u * 2 + (0 if which == "k" else 1)]]))
                pos_cons = eci[-1]
                ehid = []
                for mt2 in range(16):
                    bk = mt2 % 2
                    wsl = mt2 % 2
                    ew1 = D("gpsimd", lambda e, mt2=mt2, wsl=wsl, wsrcT=wsrcT: e.dma_start(
                        out=w1_sb[wsl][:].rearrange("p (kt m) -> p kt m", kt=16),
                        in_=wsrcT.rearrange("(kt p) m -> p kt m", p=128)[:, :, mt2 * 128:(mt2 + 1) * 128]),
                        deps=[ew1l[wsl], w_cons], slot=f"w1{wsl}")
                    mm = None
                    for kt in range(16):
                        mm = E("tensor", lambda e, mt2=mt2, kt=kt, bk=bk, wsl=wsl: e.matmul(
                            out=pp[bk][:, 0:256], lhsT=w1_sb[wsl][:, kt * 128:(kt + 1) * 128],
                            rhs=ckin[:, kt * 256:(kt + 1) * 256], start=(kt == 0), stop=(kt == 15)),
                            deps=([ew1, pbc[bk]] + eci) if kt == 0 else None, chain=(kt == 0))
                    ew1l[wsl] = Ev("c_tensor", p.sem_count["c_tensor"])
                    ehid.append(E("scalar", lambda e, mt2=mt2, bk=bk: e.activation(
                        out=hid[:, mt2 * 256:(mt2 + 1) * 256], in_=pp[bk][:, 0:256], func=AF.Relu,
                        bias=b1_sb[:, mt2:mt2 + 1]), [mm, eb1]))
                    pbc[bk] = ehid[-1]
                mm = None
                for kt in range(16):
                    mm = E("tensor", lambda e, kt=kt: e.matmul(
                        out=pp[2][:, 0:256], lhsT=w2_sb[:, kt * 128:(kt + 1) * 128],
                        rhs=hid[:, kt * 256:(kt + 1) * 256], start=(kt == 0), stop=(kt == 15)),
                        deps=([ew2, pbc[2]] + ehid) if kt == 0 else None, chain=(kt == 0))
                dst = ck_sb[u] if which == "k" else cv_sb[u]
                ec1 = E("scalar", lambda e, dst=dst: e.activation(out=dst[:, 1:257], in_=pp[2][:, 0:256], func=AF.Identity, bias=b2_sb[:]), [mm, eb2])
                pbc[2] = ec1
                ec2 = E("vector", lambda e, dst=dst: e.tensor_copy(out=dst[:, 0:1], in_=mem_sb[:, 0:1]), [emem, ec1])
                mem_cons = ec2
                if which == "k":
                    ck_done[u] = ec2
                else:
                    cv_done[u] = ec2
            w_cons = Ev("c_tensor", p.sem_count["c_tensor"])
        cvt_done = [None, None]
        for u in range(2):
            ee = []
            for jt, (o, n) in enumerate(((0, 128), (128, 128))):
                etr = E("tensor", lambda e, u=u, o=o: e.transpose(out=ptb[0][:, 0:128], in_=cv_sb[u][:, o:o + 128], identity=sid[:]),
                        [cv_done[u], eid, cvt_done[u], ptc[0]])
                ee.append(E("scalar", lambda e, u=u, jt=jt: e.copy(out=cvt_a[u][:, jt * 128:(jt + 1) * 128], in_=ptb[0][:, 0:128]), [etr]))
                ptc[0] = ee[-1]
            etr = E("tensor", lambda e, u=u: e.transpose(out=ptb[0][0:1, 0:128], in_=cv_sb[u][:, 256:257], identity=sid[:]), ee + [ptc[0]])
            ee.append(E("scalar", lambda e, u=u: e.copy(out=cvt_b[u][:], in_=ptb[0][0:1, 0:128]), [etr]))
            ptc[0] = ee[-1]
            cvt_done[u] = ee[-1]

        def softmax_f(W, deps, scl, ofs=0):
            e1 = E("vector", lambda e, W=W, ofs=ofs: e.tensor_reduce(out=stats[:, ofs:ofs + 1], in_=fsm[:, 0:W], axis=AX.X, op=ALU.max), deps)
            e2 = E("vector", lambda e, ofs=ofs: e.tensor_scalar(out=stats[:, ofs + 1:ofs + 2], in0=stats[:, ofs:ofs + 1], scalar1=-scl, scalar2=None, op0=ALU.mult))
            e3 = E("scalar", lambda e, W=W, scl=scl, ofs=ofs: e.activation(out=fp_sb[:, 0:W], in_=fsm[:, 0:W], func=AF.Exp,
                                                                           scale=scl, bias=stats[:, ofs + 1:ofs + 2], accum_out=stats[:, ofs + 2:ofs + 3]), [e2])
            e4 = E("vector", lambda e, ofs=ofs: e.reciprocal(out=stats[:, ofs + 3:ofs + 4], in_=stats[:, ofs + 2:ofs + 3]), [e3])
            return e3, e4

        a_st = {}
        rope_cons = None
        qg_cons = None
        unit_mm = None
        vt_cons = None
        imp_ld_cons = None
        imp_st_cons = None
        cmt_cons = None
        gt_cons = None
        aold_cons = None
        abuf_cons = None
        impd_guard = None
        for u in range(2):
            er1 = E("vector", lambda e, u=u: e.tensor_copy(out=qsw_sb[0:64, :], in_=k_sb[u][64:128, :]), [ekv[u * 2], rope_cons])
            er2 = E("vector", lambda e, u=u: e.tensor_copy(out=qsw_sb[64:128, :], in_=k_sb[u][0:64, :]))
            er3 = E("vector", lambda e, u=u: e.tensor_tensor(out=fsm[:], in0=k_sb[u][:], in1=cos_sb[:], op=ALU.mult), [ecos])
            er4 = E("vector", lambda e: e.tensor_tensor(out=rtmp2[:], in0=qsw_sb[:], in1=sin_sb[:], op=ALU.mult), [esin])
            er5 = E("vector", lambda e, u=u: e.tensor_tensor(out=rk_sb[u][:], in0=fsm[:], in1=rtmp2[:], op=ALU.add))
            evt = []
            for jt in range(32):
                etr = E("tensor", lambda e, u=u, jt=jt: e.transpose(out=ptb[jt % 2][:, 0:128], in_=v_sb[u][:, jt * 128:(jt + 1) * 128], identity=sid[:]),
                        [ekv[u * 2 + 1], eid, vt_cons, ptc[jt % 2]])
                evt.append(E("scalar", lambda e, jt=jt: e.copy(out=vt_sb[:, jt * 128:(jt + 1) * 128], in_=ptb[jt % 2][:, 0:128]), [etr]))
                ptc[jt % 2] = evt[-1]
            evt_done = evt[-1]
            imp_ev = [impd_guard] * 32
            for g in range(4):
                eq = load_qg(p, u, g, qg_sb, [qg_cons])
                for it in range(32):
                    ecm = D("sync", lambda e, it=it: e.dma_start(out=cmtf[:], in_=cmneg[it * 128:(it + 1) * 128, :]), slot="cmt",
                            deps=[cmt_cons])
                    mm = E("tensor", lambda e, u=u, g=g, it=it: e.matmul(
                        out=pp[0][:, 0:257], lhsT=qg_sb[:, it * 128:(it + 1) * 128], rhs=ck_sb[u][:],
                        start=True, stop=True), [eq, ck_done[u], unit_mm, pbc[0]])
                    esel = E("vector", lambda e: e.tensor_tensor(out=fsm[:, 0:257], in0=pp[0][:, 0:257], in1=cmtf[:], op=ALU.add), [mm, ecm])
                    pbc[0] = esel
                    unit_mm = esel
                    cmt_cons = esel
                    if g == 0:
                        imp_ev[it] = D("scalar", lambda e, it=it: e.dma_start(out=imp_d[it], in_=fsm[:, 0:257]), deps=[esel, imp_ev[it]], slot="imps")
                        imp_st_cons = imp_ev[it]
                    else:
                        eil = D("sync", lambda e, it=it: e.dma_start(out=impl[:], in_=imp_d[it]), deps=[imp_ev[it], imp_ld_cons], slot="impl")
                        eia = E("vector", lambda e: e.tensor_tensor(out=impw[:], in0=impl[:], in1=fsm[:, 0:257], op=ALU.add), [eil, esel, imp_st_cons])
                        imp_ld_cons = eia
                        imp_ev[it] = D("scalar", lambda e, it=it: e.dma_start(out=imp_d[it], in_=impw[:]), deps=[eia], slot="imps")
                        imp_st_cons = imp_ev[it]
                    e3, e4 = softmax_f(257, [esel], SCL)
                    eptl = []
                    for jt, (o, n) in enumerate(((0, 128), (128, 128), (256, 1))):
                        etr = E("tensor", lambda e, o=o, n=n, jt=jt: e.transpose(
                            out=ptb[jt % 2][0:n, 0:128], in_=fp_sb[:, o:o + n], identity=sid[:]), [e3, ptc[jt % 2]])
                        eptl.append(E("scalar", lambda e, jt=jt, n=n: e.copy(out=pts[0:n, jt * 128:(jt + 1) * 128], in_=ptb[jt % 2][0:n, 0:128]), [etr]))
                        ptc[jt % 2] = eptl[-1]
                    mmv = E("tensor", lambda e, u=u: e.matmul(out=pp[1][:, 0:128], lhsT=pts[:, 0:128], rhs=cvt_a[u][:, 0:128], start=True, stop=False), [eptl[0], cvt_done[u], pbc[1]])
                    mmv = E("tensor", lambda e, u=u: e.matmul(out=pp[1][:, 0:128], lhsT=pts[:, 128:256], rhs=cvt_a[u][:, 128:256], start=False, stop=False), [eptl[1]])
                    mmv = E("tensor", lambda e, u=u: e.matmul(out=pp[1][:, 0:128], lhsT=pts[0:1, 256:384], rhs=cvt_b[u][:], start=False, stop=True), [eptl[2]])
                    egt = D("sync", lambda e, it=it: e.dma_start(out=gt_it[:], in_=gt_ap(e, p, it)), slot="gtit",
                            deps=[gt_cons])
                    eco = E("scalar", lambda e: e.copy(out=co_sb[:], in_=pp[1][:, 0:128]), [mmv])
                    pbc[1] = eco
                    col = (u * 4 + g) * 3
                    eab = E("vector", lambda e, col=col: e.tensor_scalar(
                        out=abuf[:], in0=co_sb[:], scalar1=stats[:, 3:4], scalar2=gt_it[:, col:col + 1], op0=ALU.mult, op1=ALU.mult), [eco, e4, egt, abuf_cons])
                    gt_cons = eab
                    a_st[("a", u, g, it)] = D("scalar", lambda e, u=u, g=g, it=it: e.dma_start(
                        out=A_out[it * 128:(it + 1) * 128, (u * 4 + g) * 128:(u * 4 + g + 1) * 128], in_=abuf[:]), deps=[eab], slot="ast")
                    abuf_cons = a_st[("a", u, g, it)]
                qg_cons = Ev("c_tensor", p.sem_count["c_tensor"])
            t4ld = None
            for it in range(32):
                eil = D("sync", lambda e, it=it: e.dma_start(out=impw[:], in_=imp_d[it]),
                        deps=[imp_ev[it], t4ld, imp_ld_cons, imp_st_cons], slot="impl")
                et1 = E("vector", lambda e: e.memset(impw[:, 0:1], -1000.0 / (0.25 * SCL)), [eil])
                et2 = E("vector", lambda e: e.tensor_scalar(out=impw[:], in0=impw[:], scalar1=0.25 * SCL, scalar2=None, op0=ALU.mult))
                et3 = E("vector", lambda e: e.tensor_reduce(out=stats[:, 0:1], in_=impw[:], axis=AX.X, op=ALU.max))
                et4 = E("vector", lambda e: e.tensor_scalar(out=stats[:, 1:2], in0=stats[:, 0:1], scalar1=-1.0, scalar2=None, op0=ALU.mult))
                et5 = E("scalar", lambda e: e.activation(out=impw[:], in_=impw[:], func=AF.Exp,
                                                         bias=stats[:, 1:2], accum_out=stats[:, 2:3]), [et4])
                et6 = E("vector", lambda e: e.reciprocal(out=stats[:, 3:4], in_=stats[:, 2:3]), [et5])
                et7 = E("vector", lambda e: e.tensor_scalar(out=impw[:], in0=impw[:], scalar1=stats[:, 3:4], scalar2=None, op0=ALU.mult), [et6])
                et8 = E("vector", lambda e: e.max(out=m8[:], in_=impw[:, 1:257]), [et7])
                et9 = E("vector", lambda e: e.tensor_scalar(out=mwork[:], in0=impw[:, 1:257], scalar1=m8[:, 3:4], scalar2=None, op0=ALU.is_ge), [et8])
                eta = E("vector", lambda e: e.tensor_scalar(out=fsm[:, 0:256], in0=impw[:, 1:257], scalar1=1e-10, scalar2=None, op0=ALU.is_gt))
                t4ld = eta
                etb = E("vector", lambda e: e.tensor_tensor(out=mwork[:], in0=mwork[:], in1=fsm[:, 0:256], op=ALU.mult))
                E("vector", lambda e, it=it: e.tensor_scalar(out=bm8[:, it * 256:(it + 1) * 256], in0=mwork[:], scalar1=0.5, scalar2=None, op0=ALU.is_gt))
                impd_guard = eil
            bm_done = Ev("c_vector", p.sem_count["c_vector"])
            for g in range(4):
                eq = load_qg(p, u, g, qg_sb, [qg_cons])
                eq1 = E("vector", lambda e: e.tensor_copy(out=qsw_sb[0:64, :], in_=qg_sb[64:128, :]), [eq, rope_cons])
                eq2 = E("vector", lambda e: e.tensor_copy(out=qsw_sb[64:128, :], in_=qg_sb[0:64, :]))
                eq3 = E("vector", lambda e: e.tensor_tensor(out=fsm[:], in0=qg_sb[:], in1=cos_sb[:], op=ALU.mult))
                eq4 = E("vector", lambda e: e.tensor_tensor(out=rtmp2[:], in0=qsw_sb[:], in1=sin_sb[:], op=ALU.mult))
                eq5 = E("vector", lambda e: e.tensor_tensor(out=rq_sb[:], in0=fsm[:], in1=rtmp2[:], op=ALU.add))
                rope_cons = eq5
                bmn_cons = None
                dneg_cons = None
                for it in range(32):
                    W = (it + 1) * 128
                    ebm = E("vector", lambda e, it=it: e.tensor_scalar(
                        out=bmneg_t[:], in0=bm8[:, it * 256:(it + 1) * 256], scalar1=1e30, scalar2=-1e30,
                        op0=ALU.mult, op1=ALU.add), [bm_done, bmn_cons])
                    edc = E("vector", lambda e, it=it: e.tensor_tensor(
                        out=dneg[:].rearrange("p (nb c) -> p nb c", c=16),
                        in0=dg_sb[:].rearrange("p (nb c) -> p nb c", c=16),
                        in1=bm8[:, it * 256 + it * 8: it * 256 + it * 8 + 8].unsqueeze(2).broadcast_to([128, 8, 16]),
                        op=ALU.max), [edg, bm_done, dneg_cons])
                    edc2 = E("vector", lambda e: e.tensor_scalar(out=dneg[:], in0=dneg[:], scalar1=1e30, scalar2=-1e30,
                                                                 op0=ALU.mult, op1=ALU.add))
                    esels = []
                    for jc in range((W + 511) // 512):
                        wd = min(512, W - jc * 512)
                        nb = wd // 16
                        mm = E("tensor", lambda e, u=u, it=it, jc=jc, wd=wd: e.matmul(
                            out=pp[0][:, 0:wd], lhsT=rq_sb[:, it * 128:(it + 1) * 128],
                            rhs=rk_sb[u][:, jc * 512:jc * 512 + wd], start=True, stop=True), [eq5, er5, unit_mm, pbc[0]])
                        esel = E("vector", lambda e, it=it, jc=jc, wd=wd, nb=nb: e.tensor_tensor(
                            out=fsm[:, jc * 512:jc * 512 + wd].rearrange("p (nb c) -> p nb c", c=16),
                            in0=pp[0][:, 0:wd].rearrange("p (nb c) -> p nb c", c=16),
                            in1=bmneg_t[:, jc * 32:jc * 32 + nb].unsqueeze(2).broadcast_to([128, nb, 16]),
                            op=ALU.add), [ebm, mm])
                        pbc[0] = esel
                        unit_mm = esel
                        esels.append(esel)
                    bmn_cons = esels[-1]
                    dof = (W - 128) % 512
                    jcl = (W - 128) // 512
                    esel2 = E("vector", lambda e, W=W, dof=dof, jcl=jcl: e.tensor_tensor(
                        out=fsm[:, W - 128:W], in0=pp[0][:, dof:dof + 128], in1=dneg[:], op=ALU.add), [edc2, esels[-1]])
                    dneg_cons = esel2
                    pbc[0] = esel2
                    e3, e4 = softmax_f(W, [esel2] + esels, SCL)
                    eptl = []
                    for jt in range(it + 1):
                        pj = jt % 2
                        etr = E("tensor", lambda e, jt=jt, pj=pj: e.transpose(
                            out=ptb[pj][:, 0:128], in_=fp_sb[:, jt * 128:(jt + 1) * 128], identity=sid[:]),
                            [e3, ptc[pj]])
                        eptl.append(E("scalar", lambda e, jt=jt, pj=pj: e.copy(out=pts[:, jt * 128:(jt + 1) * 128], in_=ptb[pj][:, 0:128]), [etr]))
                        ptc[pj] = eptl[-1]
                    mmv = None
                    for jt in range(it + 1):
                        mmv = E("tensor", lambda e, jt=jt, it=it: e.matmul(
                            out=pp[1][:, 0:128], lhsT=pts[:, jt * 128:(jt + 1) * 128], rhs=vt_sb[:, jt * 128:(jt + 1) * 128],
                            start=(jt == 0), stop=(jt == it)), deps=([evt_done, pbc[1]] + eptl) if jt == 0 else None, chain=(jt == 0))
                    efo = E("scalar", lambda e: e.copy(out=fo_sb[:], in_=pp[1][:, 0:128]), [mmv])
                    pbc[1] = efo
                    col = (u * 4 + g) * 3
                    egt = D("sync", lambda e, it=it: e.dma_start(out=gt_it[:], in_=gt_ap(e, p, it)), slot="gtit",
                            deps=[gt_cons])
                    ef1 = E("vector", lambda e, col=col: e.tensor_scalar(
                        out=fo_sb[:], in0=fo_sb[:], scalar1=stats[:, 3:4], scalar2=gt_it[:, col + 1:col + 2], op0=ALU.mult, op1=ALU.mult), [efo, e4, egt])
                    if it == 0:
                        mm = E("tensor", lambda e, u=u: e.matmul(out=pp[2][:, 0:128], lhsT=rq_sb[:, 0:128], rhs=rk_sb[u][:, 0:128], start=True, stop=True), [eq5, er5, ef1, pbc[2]])
                        esel = E("vector", lambda e: e.tensor_tensor(out=fsm[:, 0:128], in0=pp[2][:, 0:128], in1=bz_sb[:], op=ALU.add), [ebz, mm])
                        sW = 128
                    else:
                        mm = E("tensor", lambda e, u=u, it=it: e.matmul(out=pp[2][:, 0:256], lhsT=rq_sb[:, it * 128:(it + 1) * 128],
                                                                        rhs=rk_sb[u][:, (it - 1) * 128:(it + 1) * 128], start=True, stop=True), [eq5, er5, ef1, pbc[2]])
                        esel = E("vector", lambda e: e.tensor_tensor(out=fsm[:, 0:256], in0=pp[2][:, 0:256], in1=ba_sb[:], op=ALU.add), [eba, mm])
                        sW = 256
                    pbc[2] = esel
                    unit_mm = esel
                    e3s, e4s = softmax_f(sW, [esel], SCL, ofs=4)
                    eptl2 = []
                    for jt in range(sW // 128):
                        pj = jt % 2
                        etr = E("tensor", lambda e, jt=jt, pj=pj: e.transpose(
                            out=ptb[pj][:, 0:128], in_=fp_sb[:, jt * 128:(jt + 1) * 128], identity=sid[:]),
                            [e3s, ptc[pj]])
                        eptl2.append(E("scalar", lambda e, jt=jt, pj=pj: e.copy(out=pts[:, jt * 128:(jt + 1) * 128], in_=ptb[pj][:, 0:128]), [etr]))
                        ptc[pj] = eptl2[-1]
                    mmv = None
                    for jt in range(sW // 128):
                        vjt = it - 1 + jt if it > 0 else 0
                        mmv = E("tensor", lambda e, jt=jt, vjt=vjt, sW=sW: e.matmul(
                            out=pp[3][:, 0:128], lhsT=pts[:, jt * 128:(jt + 1) * 128], rhs=vt_sb[:, vjt * 128:(vjt + 1) * 128],
                            start=(jt == 0), stop=(jt == sW // 128 - 1)), deps=([evt_done, pbc[3]] + eptl2) if jt == 0 else None, chain=(jt == 0))
                    eso = E("scalar", lambda e: e.copy(out=so_sb[:], in_=pp[3][:, 0:128]), [mmv])
                    pbc[3] = eso
                    es5 = E("vector", lambda e, col=col: e.tensor_scalar(
                        out=so_sb[:], in0=so_sb[:], scalar1=stats[:, 7:8], scalar2=gt_it[:, col + 2:col + 3], op0=ALU.mult, op1=ALU.mult), [eso, e4s])
                    gt_cons = es5
                    eao = D("sync", lambda e, u=u, g=g, it=it: e.dma_start(
                        out=aold[:], in_=A_out[it * 128:(it + 1) * 128, (u * 4 + g) * 128:(u * 4 + g + 1) * 128]),
                        deps=[a_st[("a", u, g, it)], aold_cons], slot="aold")
                    ec1 = E("vector", lambda e: e.tensor_tensor(out=fo_sb[:], in0=fo_sb[:], in1=so_sb[:], op=ALU.add), [ef1, es5])
                    ec2 = E("vector", lambda e: e.tensor_tensor(out=abuf[:], in0=fo_sb[:], in1=aold[:], op=ALU.add), [eao, abuf_cons])
                    aold_cons = ec2
                    a_st[("a2", u, g, it)] = D("scalar", lambda e, u=u, g=g, it=it: e.dma_start(
                        out=A_out[it * 128:(it + 1) * 128, (u * 4 + g) * 128:(u * 4 + g + 1) * 128], in_=abuf[:]), deps=[ec2], slot="ast2")
                    abuf_cons = a_st[("a2", u, g, it)]
                qg_cons = Ev("c_tensor", p.sem_count["c_tensor"])
            vt_cons = Ev("c_tensor", p.sem_count["c_tensor"])
    ctx.close()


def build_l1_nsa():
    nc = _mk()
    io = {}
    for nm, shp, dt in [("kw1T", [2048, 2048], BF16), ("kb1", [2048, 1], F32), ("kw2T", [2048, 128], F32),
                        ("kb2", [128, 1], F32), ("vw1T", [2048, 2048], BF16), ("vb1", [2048, 1], F32),
                        ("vw2T", [2048, 128], F32), ("vb2", [128, 1], F32), ("kpos", [2, 128, 16], F32),
                        ("vpos", [2, 128, 16], F32), ("memk", [2, 128, 1], F32), ("memv", [2, 128, 1], F32),
                        ("cosf", [128, LSEQ], BF16), ("sinf", [128, LSEQ], BF16), ("cmneg", [LSEQ, 257], F32),
                        ("baneg", [128, 256], F32), ("bzneg", [128, 128], F32), ("diagf", [128, 128], F32),
                        ("identb", [128, 128], BF16)]:
        io[nm] = nc.dram_tensor(nm, shp, dt, kind="ExternalInput").ap()
    io["A_out"] = nc.dram_tensor("A_out", [LSEQ, 1024], BF16, kind="ExternalOutput").ap()
    io["imp_d"] = nc.dram_tensor("imp_d", [32, 128, 257], F32).ap()
    q_in = nc.dram_tensor("q_in", [2, 512, LSEQ], BF16, kind="ExternalInput").ap()
    k_in = nc.dram_tensor("k_in", [2, 128, LSEQ], BF16, kind="ExternalInput").ap()
    v_in = nc.dram_tensor("v_in", [2, 128, LSEQ], BF16, kind="ExternalInput").ap()
    gtb = nc.dram_tensor("gtb", [LSEQ, 96], F32, kind="ExternalInput").ap()
    io["load_qg"] = lambda p, u, g, dst, deps: p.dma("sync", lambda e: e.dma_start(out=dst[:], in_=q_in[u, g * 128:(g + 1) * 128, :]), deps=deps, slot="qg")
    def _load_kv(p, u, k_dst, v_dst):
        ev_k = p.dma("sync", lambda e: e.dma_start(out=k_dst[:], in_=k_in[u]), slot=f"k{u}")
        ev_v = p.dma("sync", lambda e: e.dma_start(out=v_dst[:], in_=v_in[u]), slot=f"v{u}")
        return ev_k, ev_v
    io["load_kv"] = _load_kv
    io["gt_ap"] = lambda e, p, it: gtb[it * 128:(it + 1) * 128, 0:24]
    p = Prog(nc)
    emit_l1_nsa(p, io)
    p.build()
    return nc


class RegTrack:

    def __init__(self):
        self.d = {}

    def w(self, key, ev):
        self.d[key] = [ev, []]

    def r(self, key, ev):
        self.d.setdefault(key, [None, []])[1].append(ev)

    def dw(self, key):
        wr, rds = self.d.get(key, (None, []))
        return [wr] + rds

    def dr(self, key):
        wr, _ = self.d.get(key, (None, []))
        return [wr]


def emit_l1_nsa2(p, io, pref="P8_"):
    nc = p.nc
    q_in, k_in, v_in, gtb = io["q_in"], io["k_in"], io["v_in"], io["gtb"]
    kw1T, kb1, kw2T, kb2 = io["kw1T"], io["kb1"], io["kw2T"], io["kb2"]
    vw1T, vb1, vw2T, vb2 = io["vw1T"], io["vb1"], io["vw2T"], io["vb2"]
    kpos, vpos, memk, memv = io["kpos"], io["vpos"], io["memk"], io["memv"]
    cosf, sinf, cmneg = io["cosf"], io["sinf"], io["cmneg"]
    baneg, bzneg, diagf, identb = io["baneg"], io["bzneg"], io["diagf"], io["identb"]
    A_out = io["A_out"]

    ctx = contextlib.ExitStack()
    sb = lambda n, s, d=F32: ctx.enter_context(nc.sbuf_tensor(pref + n, s, d))
    ps = lambda n, s, d=F32: ctx.enter_context(nc.psum_tensor(pref + n, s, d))
    kvq = sb("kvq", [128, 4 * LSEQ], BF16)
    aux = sb("aux", [128, 4 * LSEQ], BF16)
    rk2 = sb("rk2", [128, 2 * LSEQ], BF16)
    vt2 = sb("vt2", [128, 2 * LSEQ], BF16)
    cos_sb = sb("cos", [128, LSEQ], BF16)
    sin_sb = sb("sin", [128, LSEQ], BF16)
    fsm = sb("fsm", [128, LSEQ])
    fp = sb("fp", [128, LSEQ], BF16)
    pts = sb("pts", [128, LSEQ], BF16)
    ssm = sb("ssm", [128, 4 * 256])
    sfp = sb("sfp", [128, 4 * 256], BF16)
    spts = sb("spts", [128, 4 * 256], BF16)
    ck2 = sb("ck2", [128, 2 * 257], BF16)
    cv2 = sb("cv2", [128, 2 * 257], BF16)
    cvta = sb("cvta", [128, 2 * 256], BF16)
    cvtb = sb("cvtb", [1, 2 * 128], BF16)
    w1b = [sb(f"w1b{i}", [128, 16 * 128], BF16) for i in range(2)]
    w2b = sb("w2b", [128, 16 * 128], BF16)
    b1s = sb("b1s", [128, 16])
    b2s = sb("b2s", [128, 1])
    poss = sb("poss", [128, 16])
    mems = sb("mems", [128, 2])
    gts = sb("gts", [128, 768])
    cmt = [sb(f"cmt{i}", [128, 257]) for i in range(2)]
    ba_sb = sb("ba", [128, 256])
    bz_sb = sb("bz", [128, 128])
    dg_sb = sb("dg", [128, 128])
    imp = sb("imp", [128, 257])
    iw = sb("iw", [128, 256])
    iw2 = sb("iw2", [128, 256])
    m8 = sb("m8", [128, 8])
    bm8 = sb("bm8", [128, 256], U8)
    bmn = sb("bmn", [128, 256])
    dneg = sb("dneg", [128, 128])
    aout = sb("aout", [128, 512])
    co4 = sb("co4", [128, 512])
    fo4 = sb("fo4", [128, 512])
    so4 = sb("so4", [128, 512])
    abuf = sb("abuf", [128, 512], BF16)
    stats = sb("stats", [128, 48])
    sid = sb("sid", [128, 128], BF16)
    psc = [ps(f"psc{i}", [128, 512]) for i in range(2)]
    ppv = [ps(f"ppv{i}", [128, 512]) for i in range(2)]
    ptb = [ps(f"ptb{i}", [128, 1024], BF16) for i in range(2)]
    pss = ps("pss", [128, 512])
    psv = ps("psv", [128, 512])

    E = p.op
    D = p.dma
    R = RegTrack()

    def qspan(lo, hi):
        return range(lo // 1024, (hi + 1023) // 1024)

    def qdw(name, lo, hi):
        ds = []
        for qq in qspan(lo, hi):
            ds += R.dw((name, qq))
        return ds

    def qw(name, lo, hi, ev):
        for qq in qspan(lo, hi):
            R.w((name, qq), ev)

    def qr(name, lo, hi, ev):
        for qq in qspan(lo, hi):
            R.r((name, qq), ev)

    eid = D("sync", lambda e: e.dma_start(out=sid[:], in_=identb), slot="id")
    ecos = D("sync", lambda e: e.dma_start(out=cos_sb[:], in_=cosf), slot="cos")
    esin = D("sync", lambda e: e.dma_start(out=sin_sb[:], in_=sinf), slot="sin")
    eba = D("sync", lambda e: e.dma_start(out=ba_sb[:], in_=baneg), slot="ba")
    ebz = D("sync", lambda e: e.dma_start(out=bz_sb[:], in_=bzneg), slot="bz")
    edg = D("sync", lambda e: e.dma_start(out=dg_sb[:], in_=diagf), slot="dg")
    egt = D("gpsimd", lambda e: e.dma_start(out=gts[:], in_=gtb), slot="gt")
    ecm0 = D("gpsimd", lambda e: e.dma_start(out=cmt[0][:], in_=cmneg[0:128, :]), slot="cm0")
    cm_ev = [ecm0, None]
    ekv = []
    for qd in range(4):
        src = k_in[qd % 2] if qd < 2 else v_in[qd % 2]
        ekv.append(D("sync", lambda e, qd=qd, src=src: e.dma_start(
            out=kvq[:, qd * LSEQ:(qd + 1) * LSEQ], in_=src), slot=f"kv{qd}"))

    rsw = sb("rsw", [128, 512], BF16)

    def rope_chunk(srcb, kb, dstb, db, o, deps0):
        c1 = E("vector", lambda e: e.tensor_copy(out=rsw[0:64, :], in_=srcb[64:128, kb + o:kb + o + 512]),
               deps0 + R.dw(("rsw", 0)))
        E("vector", lambda e: e.tensor_copy(out=rsw[64:128, :], in_=srcb[0:64, kb + o:kb + o + 512]))
        v1 = E("vector", lambda e: e.tensor_tensor(
            out=pss[:, :], in0=rsw[:, :], in1=sin_sb[:, o:o + 512], op=ALU.mult),
            [esin] + R.dw(("pss", 0)))
        R.w(("rsw", 0), v1)
        v3 = E("vector", lambda e: e.tensor_tensor(
            out=fsm[:, 0:512], in0=srcb[:, kb + o:kb + o + 512], in1=cos_sb[:, o:o + 512], op=ALU.mult),
            [ecos] + qdw("fsm", 0, 512))
        v4 = E("vector", lambda e: e.tensor_tensor(
            out=dstb[:, db + o:db + o + 512], in0=fsm[:, 0:512], in1=pss[:, :], op=ALU.add))
        R.w(("pss", 0), v4)
        qw("fsm", 0, 512, v4)
        return v4

    rk_done = [None, None]
    for u in range(2):
        last = None
        for cc in range(LSEQ // 512):
            last = rope_chunk(kvq, u * LSEQ, rk2, u * LSEQ, cc * 512, [ekv[u]])
        rk_done[u] = last
    vt_done = [None, None]
    for u in range(2):
        vb = (2 + u) * LSEQ
        evs = []
        for jt in range(32):
            pj = jt % 2
            etr = E("tensor", lambda e, vb=vb, jt=jt, pj=pj: e.transpose(
                out=ptb[pj][:, 0:128], in_=kvq[:, vb + jt * 128:vb + (jt + 1) * 128], identity=sid[:]),
                [ekv[2 + u], eid] + R.dw(("ptb", pj)))
            eev = E("scalar", lambda e, u=u, jt=jt, pj=pj: e.copy(
                out=vt2[:, u * LSEQ + jt * 128:u * LSEQ + (jt + 1) * 128], in_=ptb[pj][:, 0:128]), [etr])
            R.w(("ptb", pj), eev)
            evs.append(eev)
        vt_done[u] = evs[-1]

    ck_done = [None, None]
    cv_done = [None, None]
    w2_cons = None
    for wi, which in enumerate(("k", "v")):
        wsrcT, bb1, wsrc2, bb2, possrc, memsrc = ((kw1T, kb1, kw2T, kb2, kpos, memk) if which == "k"
                                                  else (vw1T, vb1, vw2T, vb2, vpos, memv))
        ew2 = D("gpsimd", lambda e, wsrc2=wsrc2: e.dma_start(
            out=w2b[:].rearrange("p (kt m) -> p kt m", kt=16),
            in_=wsrc2.rearrange("(kt p) m -> p kt m", p=128)), deps=[w2_cons], slot="w2")
        eb1 = D("sync", lambda e, bb1=bb1: e.dma_start(out=b1s[:], in_=bb1.rearrange("(kt p) o -> p (kt o)", p=128),
                                                       allow_slow_non_contiguous=True), deps=R.dw(("b1s", 0)), slot="b1")
        eb2 = D("sync", lambda e, bb2=bb2: e.dma_start(out=b2s[:], in_=bb2), deps=R.dw(("b2s", 0)), slot="b2")
        R.w(("b1s", 0), eb1)
        R.w(("b2s", 0), eb2)
        eci_all = []
        emems = []
        for u in range(2):
            kvbase = (wi * 2 + u) * LSEQ
            epos = D("sync", lambda e, u=u, possrc=possrc: e.dma_start(out=poss[:], in_=possrc[u]),
                     deps=R.dw(("poss", 0)), slot="pos")
            R.w(("poss", 0), epos)
            emem = D("sync", lambda e, u=u, memsrc=memsrc: e.dma_start(out=mems[:, u:u + 1], in_=memsrc[u]),
                     deps=R.dw(("mems", u)), slot="mem")
            R.w(("mems", u), emem)
            emems.append(emem)
            eci = []
            for c in range(16):
                ev = E("vector", lambda e, c=c, u=u, kvbase=kvbase: e.tensor_scalar(
                    out=aux[:, u * LSEQ + c * 256:u * LSEQ + (c + 1) * 256],
                    in0=kvq[:, kvbase:kvbase + LSEQ].rearrange("p (ncc cb) -> p cb ncc", cb=16)[:, c, :],
                    scalar1=poss[:, c:c + 1], scalar2=None, op0=ALU.add),
                    [epos, ekv[wi * 2 + u]] + (R.dw(("aux", u)) if c == 0 else []))
                eci.append(ev)
                R.r(("poss", 0), ev)
            R.w(("aux", u), eci[-1])
            eci_all.append(eci)
        hid_done = [[], []]
        for mt2 in range(16):
            wsl = mt2 % 2
            ew1 = D("gpsimd", lambda e, mt2=mt2, wsl=wsl, wsrcT=wsrcT: e.dma_start(
                out=w1b[wsl][:].rearrange("p (kt m) -> p kt m", kt=16),
                in_=wsrcT.rearrange("(kt p) m -> p kt m", p=128)[:, :, mt2 * 128:(mt2 + 1) * 128]),
                deps=R.dw(("w1b", wsl)), slot=f"w1{wsl}")
            R.w(("w1b", wsl), ew1)
            for u in range(2):
                bk = u
                mm = None
                for kt in range(16):
                    mm = E("tensor", lambda e, kt=kt, u=u, bk=bk, wsl=wsl: e.matmul(
                        out=psc[bk][:, 0:256], lhsT=w1b[wsl][:, kt * 128:(kt + 1) * 128],
                        rhs=aux[:, u * LSEQ + kt * 256:u * LSEQ + (kt + 1) * 256],
                        start=(kt == 0), stop=(kt == 15)),
                        deps=([ew1] + eci_all[u] + R.dw(("psc", bk))) if kt == 0 else None, chain=(kt == 0))
                R.r(("w1b", wsl), mm)
                R.r(("aux", u), mm)
                eh = E("scalar", lambda e, mt2=mt2, u=u, bk=bk: e.activation(
                    out=aux[:, (2 + u) * LSEQ + mt2 * 256:(2 + u) * LSEQ + (mt2 + 1) * 256],
                    in_=psc[bk][:, 0:256], func=AF.Relu, bias=b1s[:, mt2:mt2 + 1]),
                    [mm, eb1] + (R.dw(("aux", 2 + u)) if mt2 == 0 else []))
                R.w(("psc", bk), eh)
                R.r(("b1s", 0), eh)
                hid_done[u].append(eh)
        R.w(("aux", 2), hid_done[0][-1])
        R.w(("aux", 3), hid_done[1][-1])
        for u in range(2):
            mm = None
            for kt in range(16):
                mm = E("tensor", lambda e, kt=kt, u=u: e.matmul(
                    out=ppv[u][:, 0:256], lhsT=w2b[:, kt * 128:(kt + 1) * 128],
                    rhs=aux[:, (2 + u) * LSEQ + kt * 256:(2 + u) * LSEQ + (kt + 1) * 256],
                    start=(kt == 0), stop=(kt == 15)),
                    deps=([ew2] + hid_done[u] + R.dw(("ppv", u))) if kt == 0 else None, chain=(kt == 0))
            w2_cons = mm
            R.r(("aux", 2 + u), mm)
            dst = ck2 if which == "k" else cv2
            ec1 = E("scalar", lambda e, u=u, dst=dst: e.activation(
                out=dst[:, u * 257 + 1:u * 257 + 257], in_=ppv[u][:, 0:256], func=AF.Identity, bias=b2s[:]), [mm, eb2])
            R.w(("ppv", u), ec1)
            R.r(("b2s", 0), ec1)
            ec2 = E("vector", lambda e, u=u, dst=dst: e.tensor_copy(out=dst[:, u * 257:u * 257 + 1], in_=mems[:, u:u + 1]),
                    [emems[u], ec1])
            R.r(("mems", u), ec2)
            if which == "k":
                ck_done[u] = ec2
            else:
                cv_done[u] = ec2
    cvt_done = [None, None]
    for u in range(2):
        evs = []
        for jt in range(2):
            etr = E("tensor", lambda e, u=u, jt=jt: e.transpose(
                out=ptb[0][:, 0:128], in_=cv2[:, u * 257 + jt * 128:u * 257 + (jt + 1) * 128], identity=sid[:]),
                [cv_done[u], eid] + R.dw(("ptb", 0)))
            eev = E("scalar", lambda e, u=u, jt=jt: e.copy(
                out=cvta[:, u * 256 + jt * 128:u * 256 + (jt + 1) * 128], in_=ptb[0][:, 0:128]), [etr])
            R.w(("ptb", 0), eev)
            evs.append(eev)
        etr = E("tensor", lambda e, u=u: e.transpose(
            out=ptb[0][0:1, 0:128], in_=cv2[:, u * 257 + 256:u * 257 + 257], identity=sid[:]), evs + R.dw(("ptb", 0)))
        eev = E("scalar", lambda e, u=u: e.copy(out=cvtb[:, u * 128:(u + 1) * 128], in_=ptb[0][0:1, 0:128]), [etr])
        R.w(("ptb", 0), eev)
        cvt_done[u] = eev

    def smax(srcb, so, dstb, do, W, deps, sc):
        e1 = E("vector", lambda e: e.tensor_reduce(out=stats[:, sc:sc + 1], in_=srcb[:, so:so + W], axis=AX.X, op=ALU.max),
               deps + R.dw(("st", sc)))
        e2 = E("vector", lambda e: e.tensor_scalar(out=stats[:, sc + 1:sc + 2], in0=stats[:, sc:sc + 1], scalar1=-SCL, scalar2=None, op0=ALU.mult))
        e3 = E("scalar", lambda e: e.activation(out=dstb[:, do:do + W], in_=srcb[:, so:so + W], func=AF.Exp,
                                                scale=SCL, bias=stats[:, sc + 1:sc + 2], accum_out=stats[:, sc + 2:sc + 3]), [e2])
        e4 = E("vector", lambda e: e.reciprocal(out=stats[:, sc + 3:sc + 4], in_=stats[:, sc + 2:sc + 3]), [e3])
        R.w(("st", sc), e4)
        return e3, e4

    for u in range(2):
        if u == 0:
            qdeps = [rk_done[0], rk_done[1], vt_done[0], vt_done[1], ck_done[0], ck_done[1], cv_done[0], cv_done[1]]
            for uu in range(4):
                qdeps += R.dw(("aux", uu))
        else:
            qdeps = [Ev("c_tensor", p.sem_count["c_tensor"])]
        eqs = []
        for g in range(4):
            eqs.append(D("sync", lambda e, u=u, g=g: e.dma_start(
                out=kvq[:, g * LSEQ:(g + 1) * LSEQ], in_=q_in[u, g * 128:(g + 1) * 128, :]),
                deps=qdeps, slot=f"q{g}"))
        rq_done = []
        aux_guard = []
        if u == 0:
            for uu in range(4):
                aux_guard += R.dw(("aux", uu))
        else:
            aux_guard = qdeps
        for g in range(4):
            last = None
            for cc in range(LSEQ // 512):
                extra = aux_guard if (g == 0 and cc == 0) else []
                last = rope_chunk(kvq, g * LSEQ, aux, g * LSEQ, cc * 512, [eqs[g]] + extra)
            rq_done.append(last)

        for it in range(32):
            sl = it % 2
            if not (u == 1 and it == 31):
                nt_ = (it + 1) % 32
                nsl = nt_ % 2
                cm_ev[nsl] = D("gpsimd", lambda e, nt_=nt_: e.dma_start(
                    out=cmt[nt_ % 2][:], in_=cmneg[nt_ * 128:(nt_ + 1) * 128, :]),
                    deps=R.dw(("cmt", nsl)), slot=f"cm{nsl}")
                R.w(("cmt", nsl), cm_ev[nsl])
            c_sm = []
            for g in range(4):
                rb = g * 1024
                bk = g % 2
                mm = E("tensor", lambda e, u=u, g=g, it=it, bk=bk: e.matmul(
                    out=psc[bk][:, 0:257], lhsT=kvq[:, g * LSEQ + it * 128:g * LSEQ + (it + 1) * 128],
                    rhs=ck2[:, u * 257:(u + 1) * 257], start=True, stop=True),
                    [eqs[g], ck_done[u]] + R.dw(("psc", bk)))
                esel = E("vector", lambda e, rb=rb, bk=bk, sl=sl: e.tensor_tensor(
                    out=fsm[:, rb:rb + 257], in0=psc[bk][:, 0:257], in1=cmt[sl][:], op=ALU.add),
                    [mm, cm_ev[sl]] + qdw("fsm", rb, rb + 257))
                R.w(("psc", bk), esel)
                qw("fsm", rb, rb + 257, esel)
                R.r(("cmt", sl), esel)
                if g == 0:
                    ei = E("vector", lambda e, rb=rb: e.tensor_copy(out=imp[:], in_=fsm[:, rb:rb + 257]), [esel] + R.dw(("imp", 0)))
                else:
                    ei = E("vector", lambda e, rb=rb: e.tensor_tensor(out=imp[:], in0=imp[:], in1=fsm[:, rb:rb + 257], op=ALU.add), [esel])
                R.w(("imp", 0), ei)
                qr("fsm", rb, rb + 257, ei)
                e3, e4 = smax(fsm, rb, fp, rb, 257, [esel] + qdw("fp", rb, rb + 257), g * 4)
                qr("fsm", rb, rb + 257, e3)
                qw("fp", rb, rb + 257, e3)
                c_sm.append((e3, e4))
            for g in range(4):
                rb = g * 1024
                e3, e4 = c_sm[g]
                eptl = []
                for jt, (o, n) in enumerate(((0, 128), (128, 128), (256, 1))):
                    pj = jt % 2
                    etr = E("tensor", lambda e, rb=rb, o=o, n=n, pj=pj: e.transpose(
                        out=ptb[pj][0:n, 0:128], in_=fp[:, rb + o:rb + o + n], identity=sid[:]),
                        [e3] + R.dw(("ptb", pj)))
                    qr("fp", rb + o, rb + o + n, etr)
                    eev = E("scalar", lambda e, rb=rb, jt=jt, n=n, pj=pj: e.copy(
                        out=pts[0:n, rb + jt * 128:rb + (jt + 1) * 128], in_=ptb[pj][0:n, 0:128]),
                        [etr] + (qdw("pts", rb, rb + 384) if jt == 0 else []))
                    R.w(("ptb", pj), eev)
                    eptl.append(eev)
                qw("pts", rb, rb + 384, eptl[-1])
                bk = g % 2
                mmv = E("tensor", lambda e, u=u, rb=rb, bk=bk: e.matmul(
                    out=ppv[bk][:, 0:128], lhsT=pts[:, rb:rb + 128], rhs=cvta[:, u * 256:u * 256 + 128],
                    start=True, stop=False), [eptl[0], cvt_done[u]] + R.dw(("ppv", bk)))
                mmv = E("tensor", lambda e, u=u, rb=rb, bk=bk: e.matmul(
                    out=ppv[bk][:, 0:128], lhsT=pts[:, rb + 128:rb + 256], rhs=cvta[:, u * 256 + 128:u * 256 + 256],
                    start=False, stop=False), [eptl[1]])
                mmv = E("tensor", lambda e, u=u, rb=rb, bk=bk: e.matmul(
                    out=ppv[bk][:, 0:128], lhsT=pts[0:1, rb + 256:rb + 384], rhs=cvtb[:, u * 128:(u + 1) * 128],
                    start=False, stop=True), [eptl[2]])
                qr("pts", rb, rb + 384, mmv)
                eco = E("scalar", lambda e, g=g, bk=bk: e.copy(out=co4[:, g * 128:(g + 1) * 128], in_=ppv[bk][:, 0:128]),
                        [mmv] + R.dw(("co4", g)))
                R.w(("ppv", bk), eco)
                R.w(("co4", g), eco)
                col = it * 24 + (u * 4 + g) * 3
                eab = E("vector", lambda e, g=g, col=col: e.tensor_scalar(
                    out=aout[:, g * 128:(g + 1) * 128], in0=co4[:, g * 128:(g + 1) * 128],
                    scalar1=stats[:, g * 4 + 3:g * 4 + 4], scalar2=gts[:, col:col + 1], op0=ALU.mult, op1=ALU.mult),
                    [eco, c_sm[g][1], egt] + R.dw(("aout", g)))
                R.w(("aout", g), eab)
                R.r(("co4", g), eab)
                R.r(("st", g * 4), eab)
            et1 = E("vector", lambda e: e.memset(imp[:, 0:1], -1000.0 / (0.25 * SCL)), R.dw(("imp", 0)))
            et2 = E("vector", lambda e: e.tensor_scalar(out=imp[:], in0=imp[:], scalar1=0.25 * SCL, scalar2=None, op0=ALU.mult))
            et3 = E("vector", lambda e: e.tensor_reduce(out=stats[:, 44:45], in_=imp[:], axis=AX.X, op=ALU.max), R.dw(("st", 44)))
            et4 = E("vector", lambda e: e.tensor_scalar(out=stats[:, 45:46], in0=stats[:, 44:45], scalar1=-1.0, scalar2=None, op0=ALU.mult))
            et5 = E("scalar", lambda e: e.activation(out=imp[:], in_=imp[:], func=AF.Exp,
                                                     bias=stats[:, 45:46], accum_out=stats[:, 46:47]), [et4])
            et6 = E("vector", lambda e: e.reciprocal(out=stats[:, 47:48], in_=stats[:, 46:47]), [et5])
            et7 = E("vector", lambda e: e.tensor_scalar(out=imp[:], in0=imp[:], scalar1=stats[:, 47:48], scalar2=None, op0=ALU.mult), [et6])
            R.w(("imp", 0), et7)
            R.w(("st", 44), et7)
            et8 = E("vector", lambda e: e.max(out=m8[:], in_=imp[:, 1:257]), [et7])
            et9 = E("vector", lambda e: e.tensor_scalar(out=iw[:], in0=imp[:, 1:257], scalar1=m8[:, 3:4], scalar2=None, op0=ALU.is_ge), [et8])
            eta = E("vector", lambda e: e.tensor_scalar(out=iw2[:], in0=imp[:, 1:257], scalar1=1e-10, scalar2=None, op0=ALU.is_gt))
            etb = E("vector", lambda e: e.tensor_tensor(out=iw[:], in0=iw[:], in1=iw2[:], op=ALU.mult), [eta])
            etc = E("vector", lambda e: e.tensor_scalar(out=bm8[:], in0=iw[:], scalar1=0.5, scalar2=None, op0=ALU.is_gt),
                    [etb] + R.dw(("bm8", 0)))
            R.w(("bm8", 0), etc)
            ebm = E("vector", lambda e: e.tensor_scalar(out=bmn[:], in0=bm8[:], scalar1=1e30, scalar2=-1e30,
                                                        op0=ALU.mult, op1=ALU.add), [etc] + R.dw(("bmn", 0)))
            R.w(("bmn", 0), ebm)
            edc = E("vector", lambda e, it=it: e.tensor_tensor(
                out=dneg[:].rearrange("p (nb c) -> p nb c", c=16),
                in0=dg_sb[:].rearrange("p (nb c) -> p nb c", c=16),
                in1=bm8[:, it * 8:it * 8 + 8].unsqueeze(2).broadcast_to([128, 8, 16]),
                op=ALU.max), [edg, etc] + R.dw(("dneg", 0)))
            edc2 = E("vector", lambda e: e.tensor_scalar(out=dneg[:], in0=dneg[:], scalar1=1e30, scalar2=-1e30,
                                                         op0=ALU.mult, op1=ALU.add))
            R.w(("dneg", 0), edc2)
            W = (it + 1) * 128
            nreg = 4 if it < 8 else (2 if it < 16 else 1)
            rwid = LSEQ // nreg

            def f_scores(g):
                rb = (g % nreg) * rwid
                esels = []
                for jc in range((W + 511) // 512):
                    wd = min(512, W - jc * 512)
                    nb = wd // 16
                    bk = jc % 2
                    mm = E("tensor", lambda e, u=u, g=g, it=it, jc=jc, wd=wd, bk=bk: e.matmul(
                        out=psc[bk][:, 0:wd], lhsT=aux[:, g * LSEQ + it * 128:g * LSEQ + (it + 1) * 128],
                        rhs=rk2[:, u * LSEQ + jc * 512:u * LSEQ + jc * 512 + wd], start=True, stop=True),
                        [rq_done[g], rk_done[u]] + R.dw(("psc", bk)))
                    esel = E("vector", lambda e, rb=rb, jc=jc, wd=wd, nb=nb, bk=bk: e.tensor_tensor(
                        out=fsm[:, rb + jc * 512:rb + jc * 512 + wd].rearrange("p (nb c) -> p nb c", c=16),
                        in0=psc[bk][:, 0:wd].rearrange("p (nb c) -> p nb c", c=16),
                        in1=bmn[:, jc * 32:jc * 32 + nb].unsqueeze(2).broadcast_to([128, nb, 16]),
                        op=ALU.add), [ebm, mm] + qdw("fsm", rb + jc * 512, rb + jc * 512 + wd))
                    R.w(("psc", bk), esel)
                    qw("fsm", rb + jc * 512, rb + jc * 512 + wd, esel)
                    R.r(("bmn", 0), esel)
                    esels.append(esel)
                dof = (W - 128) % 512
                lbk = ((W - 1) // 512) % 2
                esel2 = E("vector", lambda e, rb=rb, W=W, dof=dof, lbk=lbk: e.tensor_tensor(
                    out=fsm[:, rb + W - 128:rb + W], in0=psc[lbk][:, dof:dof + 128], in1=dneg[:], op=ALU.add),
                    [edc2, esels[-1]])
                qw("fsm", rb + W - 128, rb + W, esel2)
                R.r(("dneg", 0), esel2)
                R.r(("psc", lbk), esel2)
                e3, e4 = smax(fsm, rb, fp, rb, W, [esel2] + esels + qdw("fp", rb, rb + W), 16 + g * 4)
                qr("fsm", rb, rb + W, e3)
                qw("fp", rb, rb + W, e3)
                return e3, e4

            def f_pv(g, e3, e4):
                rb = (g % nreg) * rwid
                nt = it + 1
                eptl = []
                for jt in range(nt):
                    pj = (jt // 4) % 2
                    jj4 = jt % 4
                    dtr = [e3]
                    if jj4 == 0:
                        dtr += R.dw(("ptb", pj))
                    etr = E("tensor", lambda e, rb=rb, jt=jt, pj=pj, jj4=jj4: e.transpose(
                        out=ptb[pj][:, jj4 * 128:(jj4 + 1) * 128], in_=fp[:, rb + jt * 128:rb + (jt + 1) * 128], identity=sid[:]),
                        dtr)
                    qr("fp", rb + jt * 128, rb + (jt + 1) * 128, etr)
                    if jj4 == 3 or jt == nt - 1:
                        lo = (jt // 4) * 512
                        ln = (jj4 + 1) * 128
                        eev = E("scalar", lambda e, rb=rb, lo=lo, ln=ln, pj=pj: e.copy(
                            out=pts[:, rb + lo:rb + lo + ln], in_=ptb[pj][:, 0:ln]),
                            [etr] + qdw("pts", rb + lo, rb + lo + ln))
                        R.w(("ptb", pj), eev)
                        qw("pts", rb + lo, rb + lo + ln, eev)
                        eptl.append(eev)
                bk = g % 2
                mmv = None
                for jt in range(nt):
                    mmv = E("tensor", lambda e, u=u, rb=rb, jt=jt, nt=nt, bk=bk: e.matmul(
                        out=ppv[bk][:, 0:128], lhsT=pts[:, rb + jt * 128:rb + (jt + 1) * 128],
                        rhs=vt2[:, u * LSEQ + jt * 128:u * LSEQ + (jt + 1) * 128],
                        start=(jt == 0), stop=(jt == nt - 1)),
                        deps=(eptl + [vt_done[u]] + R.dw(("ppv", bk))) if jt == 0 else None, chain=(jt == 0))
                qr("pts", rb, rb + nt * 128, mmv)
                efo = E("scalar", lambda e, g=g, bk=bk: e.copy(out=fo4[:, g * 128:(g + 1) * 128], in_=ppv[bk][:, 0:128]),
                        [mmv] + R.dw(("fo4", g)))
                R.w(("ppv", bk), efo)
                R.w(("fo4", g), efo)
                col = it * 24 + (u * 4 + g) * 3 + 1
                ef1 = E("vector", lambda e, g=g, col=col: e.tensor_scalar(
                    out=fo4[:, g * 128:(g + 1) * 128], in0=fo4[:, g * 128:(g + 1) * 128],
                    scalar1=stats[:, 16 + g * 4 + 3:16 + g * 4 + 4], scalar2=gts[:, col:col + 1], op0=ALU.mult, op1=ALU.mult),
                    [efo, e4])
                R.w(("fo4", g), ef1)
                R.r(("st", 16 + g * 4), ef1)

            for w in range(4 // nreg):
                f_sm = [f_scores(w * nreg + r) for r in range(nreg)]
                for r in range(nreg):
                    f_pv(w * nreg + r, *f_sm[r])
            sW = 128 if it == 0 else 256
            s_sm = []
            for g in range(4):
                sb_ = g * 256
                mm = E("tensor", lambda e, u=u, g=g, it=it, sW=sW: e.matmul(
                    out=pss[:, 0:sW], lhsT=aux[:, g * LSEQ + it * 128:g * LSEQ + (it + 1) * 128],
                    rhs=rk2[:, u * LSEQ + max(0, (it - 1)) * 128:u * LSEQ + (it + 1) * 128],
                    start=True, stop=True), [rq_done[g], rk_done[u]] + R.dw(("pss", 0)))
                esel = E("vector", lambda e, sb_=sb_, sW=sW: e.tensor_tensor(
                    out=ssm[:, sb_:sb_ + sW], in0=pss[:, 0:sW], in1=(bz_sb[:] if sW == 128 else ba_sb[:]), op=ALU.add),
                    [mm, eba, ebz] + R.dw(("ssm", g)))
                R.w(("pss", 0), esel)
                R.w(("ssm", g), esel)
                e3, e4 = smax(ssm, sb_, sfp, sb_, sW, [esel] + R.dw(("sfp", g)), 32 + g * 4)
                R.r(("ssm", g), e3)
                R.w(("sfp", g), e3)
                s_sm.append((e3, e4))
            for g in range(4):
                sb_ = g * 256
                e3, e4 = s_sm[g]
                eptl = []
                for jt in range(sW // 128):
                    pj = jt % 2
                    etr = E("tensor", lambda e, sb_=sb_, jt=jt, pj=pj: e.transpose(
                        out=ptb[pj][:, 0:128], in_=sfp[:, sb_ + jt * 128:sb_ + (jt + 1) * 128], identity=sid[:]),
                        [e3] + R.dw(("ptb", pj)))
                    R.r(("sfp", g), etr)
                    eev = E("scalar", lambda e, sb_=sb_, jt=jt, pj=pj: e.copy(
                        out=spts[:, sb_ + jt * 128:sb_ + (jt + 1) * 128], in_=ptb[pj][:, 0:128]),
                        [etr] + (R.dw(("spts", g)) if jt == 0 else []))
                    R.w(("ptb", pj), eev)
                    eptl.append(eev)
                R.w(("spts", g), eptl[-1])
                mmv = None
                for jt in range(sW // 128):
                    vjt = it - 1 + jt if it > 0 else 0
                    mmv = E("tensor", lambda e, u=u, sb_=sb_, jt=jt, vjt=vjt, sW=sW: e.matmul(
                        out=psv[:, 0:128], lhsT=spts[:, sb_ + jt * 128:sb_ + (jt + 1) * 128],
                        rhs=vt2[:, u * LSEQ + vjt * 128:u * LSEQ + (vjt + 1) * 128],
                        start=(jt == 0), stop=(jt == sW // 128 - 1)),
                        deps=(eptl + [vt_done[u]] + R.dw(("psv", 0))) if jt == 0 else None, chain=(jt == 0))
                R.r(("spts", g), mmv)
                eso = E("scalar", lambda e, g=g: e.copy(out=so4[:, g * 128:(g + 1) * 128], in_=psv[:, 0:128]),
                        [mmv] + R.dw(("so4", g)))
                R.w(("psv", 0), eso)
                R.w(("so4", g), eso)
                col = it * 24 + (u * 4 + g) * 3 + 2
                es5 = E("vector", lambda e, g=g, col=col: e.tensor_scalar(
                    out=so4[:, g * 128:(g + 1) * 128], in0=so4[:, g * 128:(g + 1) * 128],
                    scalar1=stats[:, 32 + g * 4 + 3:32 + g * 4 + 4], scalar2=gts[:, col:col + 1], op0=ALU.mult, op1=ALU.mult),
                    [eso, e4])
                R.w(("so4", g), es5)
                R.r(("st", 32 + g * 4), es5)
            deps_c = []
            for g in range(4):
                deps_c += R.dr(("aout", g)) + R.dr(("fo4", g))
            ec1 = E("vector", lambda e: e.tensor_tensor(out=aout[:], in0=aout[:], in1=fo4[:], op=ALU.add), deps_c)
            deps_s = []
            for g in range(4):
                deps_s += R.dr(("so4", g))
            ec2 = E("vector", lambda e: e.tensor_tensor(out=aout[:], in0=aout[:], in1=so4[:], op=ALU.add), deps_s)
            for g in range(4):
                R.w(("fo4", g), ec1)
                R.w(("so4", g), ec2)
            ecb = E("vector", lambda e: e.tensor_copy(out=abuf[:], in_=aout[:]), R.dw(("abuf", 0)) + [ec2])
            for g in range(4):
                R.w(("aout", g), ecb)
            est = D("scalar", lambda e, u=u, it=it: e.dma_start(
                out=A_out[it * 128:(it + 1) * 128, u * 512:(u + 1) * 512], in_=abuf[:]), deps=[ecb], slot="ast")
            R.w(("abuf", 0), est)
    ctx.close()


def build_l1_nsa2():
    nc = _mk()
    io = {}
    for nm, shp, dt in [("kw1T", [2048, 2048], BF16), ("kb1", [2048, 1], F32), ("kw2T", [2048, 128], F32),
                        ("kb2", [128, 1], F32), ("vw1T", [2048, 2048], BF16), ("vb1", [2048, 1], F32),
                        ("vw2T", [2048, 128], F32), ("vb2", [128, 1], F32), ("kpos", [2, 128, 16], F32),
                        ("vpos", [2, 128, 16], F32), ("memk", [2, 128, 1], F32), ("memv", [2, 128, 1], F32),
                        ("cosf", [128, LSEQ], BF16), ("sinf", [128, LSEQ], BF16), ("cmneg", [LSEQ, 257], F32),
                        ("baneg", [128, 256], F32), ("bzneg", [128, 128], F32), ("diagf", [128, 128], F32),
                        ("identb", [128, 128], BF16),
                        ("q_in", [2, 512, LSEQ], BF16), ("k_in", [2, 128, LSEQ], BF16),
                        ("v_in", [2, 128, LSEQ], BF16), ("gtb", [128, 768], F32)]:
        io[nm] = nc.dram_tensor(nm, shp, dt, kind="ExternalInput").ap()
    io["A_out"] = nc.dram_tensor("A_out", [LSEQ, 1024], BF16, kind="ExternalOutput").ap()
    p = Prog(nc)
    emit_l1_nsa2(p, io)
    p.build()
    return nc


def emit_l1_tail(p, io, pref="P9_"):
    nc = p.nc
    woT2, xn1T, kern, grow = io["woT2"], io["xn1T"], io["kern"], io["grow"]
    loraA, loraBT, shAT, shBT = io["loraA"], io["loraBT"], io["shAT"], io["shBT"]
    fbias1, xnew, ident, onesb, outf = io["fbias1"], io["xnew"], io["ident"], io["onesb"], io["outf"]
    load_at = io["load_at"]
    ctx = contextlib.ExitStack()
    sb = lambda n, s, d=F32: ctx.enter_context(nc.sbuf_tensor(pref + n, s, d))
    ps = lambda n, s, d=F32: ctx.enter_context(nc.psum_tensor(pref + n, s, d))
    at_tg = sb("at_tg", [128, 32 * 512], BF16)
    xn_tg = sb("xn_tg", [128, 32 * 512], BF16)
    wot = [sb(f"wot{i}", [128, 512], BF16) for i in range(4)]
    ket = [sb(f"ket{i}", [128, 512], BF16) for i in range(4)]
    la_t = sb("la_t", [32, 512], BF16)
    sid = sb("sid", [128, 128], BF16)
    ones_sb = sb("ones_sb", [1, 32], BF16)
    grow_b = sb("grow_b", [1, 16], BF16)
    grep = sb("grep", [32, 16])
    i32 = sb("i32", [32, 32], BF16)
    ign = sb("ign", [32, 32], BF16)
    aeff = sb("aeff", [32, DM], BF16)
    aefT = sb("aefT", [128, 32 * 32], BF16)
    befT = sb("befT", [32, DM], BF16)
    shb_sb = sb("shb_sb", [32, DM], BF16)
    sha_t = sb("sha_t", [128, 32 * 32], BF16)
    ax_sb = sb("ax_sb", [32, TSH], BF16)
    axs_sb = sb("axs_sb", [32, TSH], BF16)
    bias_bc = sb("bias_bc", [128, DM])
    xt = [sb(f"xt{i}", [128, 512]) for i in range(4)]
    a_ev = [sb(f"a_ev{i}", [128, 512]) for i in range(4)]
    xo = [sb(f"xo{i}", [128, 512]) for i in range(4)]
    pa = [ps(f"pa{i}", [128, 512]) for i in range(4)]
    pf = [ps(f"pf{i}", [128, 512]) for i in range(4)]

    if True:
        E = p.op
        D = p.dma
        eid = D("sync", lambda e: e.dma_start(out=sid[:], in_=ident), slot="id")
        eon = D("sync", lambda e: e.dma_start(out=ones_sb[:], in_=onesb), slot="on")
        ebias = D("gpsimd", lambda e: e.dma_start(out=bias_bc[:], in_=fbias1.partition_broadcast(128).squeeze(1)), slot="fb")
        egr = D("gpsimd", lambda e: e.dma_start(out=grow_b[:], in_=grow), slot="gr")
        eshb = D("gpsimd", lambda e: e.dma_start(out=shb_sb[:], in_=shBT), slot="shb")
        esha = D("gpsimd", lambda e: e.dma_start(out=sha_t[:].rearrange("p (kt r) -> p kt r", kt=32),
                                                 in_=shAT.rearrange("(kt p) r -> p kt r", p=128)), slot="sha")
        exn = []
        for tg in range(2):
            for q in range(4):
                exn.append(D("sync", lambda e, tg=tg, q=q: e.dma_start(
                    out=xn_tg[:].rearrange("p (kt t) -> p kt t", kt=32)[:, q * 8:(q + 1) * 8, tg * 0:512] if False else
                        xn_tg[:].rearrange("p (kt t) -> p kt t", kt=32)[:, q * 8:(q + 1) * 8, :],
                    in_=xn1T.rearrange("(kt p) t -> p kt t", p=128)[:, q * 8:(q + 1) * 8, tg * 512:(tg + 1) * 512]), slot=f"xn{q}"))
            break
        mmg = E("tensor", lambda e: e.matmul(out=pa[0][0:32, 0:16], lhsT=ones_sb[:], rhs=grow_b[:], start=True, stop=True), [eon, egr])
        egrep = E("scalar", lambda e: e.copy(out=grep[:], in_=pa[0][0:32, 0:16]), [mmg])
        ei32 = D("sync", lambda e: e.dma_start(out=i32[:], in_=ident[0:32, 0:32]), slot="i32")
        la_cons = None
        ae_done = []
        for dst, srcT in ((aeff, loraA), (befT, loraBT)):
            for och in range(8):
                mm = None
                for n in range(16):
                    eg = E("vector", lambda e, n=n: e.tensor_scalar(out=ign[:], in0=i32[:], scalar1=grep[:, n:n + 1], scalar2=None, op0=ALU.mult), [egrep, ei32, la_cons])
                    el = D("gpsimd", lambda e, n=n, och=och, srcT=srcT: e.dma_start(
                        out=la_t[:], in_=srcT[n, :, och * 512:(och + 1) * 512]), deps=[la_cons], slot="la")
                    mm = E("tensor", lambda e, n=n, och=och, dst=dst: e.matmul(
                        out=pa[1][0:32, :], lhsT=ign[:], rhs=la_t[:], start=(n == 0), stop=(n == 15)),
                        [eg, el] + ([ae_done[-1]] if (n == 0 and ae_done) else []))
                    la_cons = mm
                ae_done.append(E("scalar", lambda e, och=och, dst=dst: e.copy(out=dst[:, och * 512:(och + 1) * 512], in_=pa[1][0:32, :]), [mm]))
        atr = []
        for kt in range(32):
            etr = E("tensor", lambda e, kt=kt: e.transpose(out=pa[2][:].bitcast(BF16)[0:128, 0:32], in_=aeff[:, kt * 128:(kt + 1) * 128], identity=sid[0:32, 0:32]),
                    [ae_done[7], eid] + ([atr[-1]] if atr else []))
            atr.append(E("scalar", lambda e, kt=kt: e.copy(out=aefT[:, kt * 32:(kt + 1) * 32], in_=pa[2][:].bitcast(BF16)[0:128, 0:32]), [etr]))
        at_cons = None
        xn_cons = exn[-1]
        w_cons = [None] * 4
        ke_cons = [None] * 4
        xt_cons = [None] * 4
        pa_cons = [None, [atr[-1]], [la_cons, egrep], [atr[-1]]]
        pa_cons = [None, atr[-1], la_cons, atr[-1]]
        pf_cons = [None] * 4
        aev_cons = [None] * 4
        xo_cons = [None] * 4
        ax_done = None
        for tg in range(2):
            if tg == 1:
                exn = []
                for q in range(4):
                    exn.append(D("sync", lambda e, q=q: e.dma_start(
                        out=xn_tg[:].rearrange("p (kt t) -> p kt t", kt=32)[:, q * 8:(q + 1) * 8, :],
                        in_=xn1T.rearrange("(kt p) t -> p kt t", p=128)[:, q * 8:(q + 1) * 8, 512:1024]),
                        deps=[xn_cons], slot=f"xn{q}"))
            eats = load_at(p, tg, at_tg, [at_cons], (sid, eid))
            mm = None
            for kt in range(32):
                mm = E("tensor", lambda e, kt=kt: e.matmul(out=pa[2][0:32, :], lhsT=aefT[:, kt * 32:(kt + 1) * 32],
                                                           rhs=xn_tg[:, kt * 512:(kt + 1) * 512], start=(kt == 0), stop=(kt == 31)),
                       deps=([pa_cons[2], atr[-1]] + exn) if kt == 0 else None, chain=(kt == 0))
            eax = E("scalar", lambda e, tg=tg: e.copy(out=ax_sb[:, tg * 512:(tg + 1) * 512], in_=pa[2][0:32, :]), [mm])
            mm = None
            for kt in range(32):
                mm = E("tensor", lambda e, kt=kt: e.matmul(out=pa[2][0:32, :], lhsT=sha_t[:, kt * 32:(kt + 1) * 32],
                                                           rhs=xn_tg[:, kt * 512:(kt + 1) * 512], start=(kt == 0), stop=(kt == 31)),
                       deps=[eax, esha] if kt == 0 else None, chain=(kt == 0))
            eaxs = E("scalar", lambda e, tg=tg: e.copy(out=axs_sb[:, tg * 512:(tg + 1) * 512], in_=pa[2][0:32, :]), [mm])
            pa_cons[2] = eaxs
            ax_done = eaxs
            for ch in range(8):
                mma = [None] * 4
                for kt in range(32):
                    sl = kt % 4
                    qn = "gpsimd" if kt % 2 == 0 else "sync"
                    ew = D(qn, lambda e, kt=kt, ch=ch, sl=sl: e.dma_start(
                        out=wot[sl][:], in_=woT2[kt * 128:(kt + 1) * 128, ch * 512:(ch + 1) * 512]),
                        deps=[w_cons[sl]], slot=f"w{sl}")
                    for tt in range(4):
                        mma[tt] = E("tensor", lambda e, kt=kt, tt=tt, sl=sl: e.matmul(
                            out=pa[tt][:], lhsT=at_tg[:, kt * 512 + tt * 128: kt * 512 + (tt + 1) * 128],
                            rhs=wot[sl][:], start=(kt == 0), stop=(kt == 31)),
                            deps=([ew, pa_cons[tt]] + eats) if kt == 0 else [ew], chain=(kt == 0))
                    w_cons[sl] = Ev("c_tensor", p.sem_count["c_tensor"])
                mmf = [None] * 4
                for kt in range(32):
                    sl = kt % 4
                    qn = "scalar" if kt % 2 == 0 else "gpsimd"
                    ew = D(qn, lambda e, kt=kt, ch=ch, sl=sl: e.dma_start(
                        out=ket[sl][:], in_=kern[kt * 128:(kt + 1) * 128, ch * 512:(ch + 1) * 512]),
                        deps=[ke_cons[sl]], slot=f"v{sl}")
                    for tt in range(4):
                        mmf[tt] = E("tensor", lambda e, kt=kt, tt=tt, sl=sl: e.matmul(
                            out=pf[tt][:], lhsT=xn_tg[:, kt * 512 + tt * 128: kt * 512 + (tt + 1) * 128],
                            rhs=ket[sl][:], start=(kt == 0), stop=False),
                            deps=([ew, pf_cons[tt]] + exn) if kt == 0 else [ew], chain=(kt == 0))
                    ke_cons[sl] = Ev("c_tensor", p.sem_count["c_tensor"])
                for tt in range(4):
                    mmf[tt] = E("tensor", lambda e, tt=tt, tg=tg, ch=ch: e.matmul(
                        out=pf[tt][:], lhsT=ax_sb[:, tg * 512 + tt * 128: tg * 512 + (tt + 1) * 128],
                        rhs=befT[:, ch * 512:(ch + 1) * 512], start=False, stop=False),
                        deps=[ax_done, ae_done[-1]] if tt == 0 else None, chain=(tt == 0))
                for tt in range(4):
                    mmf[tt] = E("tensor", lambda e, tt=tt, tg=tg, ch=ch: e.matmul(
                        out=pf[tt][:], lhsT=axs_sb[:, tg * 512 + tt * 128: tg * 512 + (tt + 1) * 128],
                        rhs=shb_sb[:, ch * 512:(ch + 1) * 512], start=False, stop=True),
                        deps=[ax_done, eshb] if tt == 0 else None, chain=(tt == 0))
                for tt in range(4):
                    ex = D("sync", lambda e, tg=tg, tt=tt, ch=ch: e.dma_start(
                        out=xt[tt][:], in_=xnew[tg * 512 + tt * 128: tg * 512 + (tt + 1) * 128, ch * 512:(ch + 1) * 512]),
                        deps=[xt_cons[tt]], slot=f"x{tt}")
                    ea = E("scalar", lambda e, tt=tt: e.copy(out=a_ev[tt][:], in_=pa[tt][:]), [mma[tt], aev_cons[tt]])
                    pa_cons[tt] = ea
                    e1 = E("vector", lambda e, tt=tt: e.tensor_tensor(out=xo[tt][:], in0=a_ev[tt][:], in1=pf[tt][:], op=ALU.add), [ea, mmf[tt], xo_cons[tt]])
                    pf_cons[tt] = e1
                    aev_cons[tt] = e1
                    e2 = E("vector", lambda e, tt=tt: e.tensor_tensor(out=xo[tt][:], in0=xo[tt][:], in1=xt[tt][:], op=ALU.add), [ex])
                    e3 = E("vector", lambda e, tt=tt, ch=ch: e.tensor_tensor(out=xo[tt][:], in0=xo[tt][:], in1=bias_bc[:, ch * 512:(ch + 1) * 512], op=ALU.add), [ebias])
                    xt_cons[tt] = e2
                    xo_cons[tt] = D("scalar", lambda e, tg=tg, tt=tt, ch=ch: e.dma_start(
                        out=outf[tg * 512 + tt * 128: tg * 512 + (tt + 1) * 128, ch * 512:(ch + 1) * 512], in_=xo[tt][:]),
                        deps=[e3], slot=f"xo{tt}")
            at_cons = Ev("c_tensor", p.sem_count["c_tensor"])
            xn_cons = at_cons
    ctx.close()


def build_l1_tail():
    nc = _mk()
    io = {}
    for nm, shp, dt in [("woT2", [DM, DM], BF16), ("xn1T", [DM, TSH], BF16), ("kern", [DM, DM], BF16),
                        ("grow", [1, 16], F32), ("loraA", [16, 32, DM], F32), ("loraBT", [16, 32, DM], F32),
                        ("shAT", [DM, 32], F32), ("shBT", [32, DM], F32), ("fbias1", [1, DM], F32),
                        ("xnew", [TSH, DM], F32), ("ident", [128, 128], BF16), ("onesb", [1, 32], BF16)]:
        io[nm] = nc.dram_tensor(nm, shp, dt, kind="ExternalInput").ap()
    io["outf"] = nc.dram_tensor("outf", [TSH, DM], F32, kind="ExternalOutput").ap()
    AT = nc.dram_tensor("AT", [DM, TSH], BF16, kind="ExternalInput").ap()
    def _load_at(p, tg, at_tg, deps, helpers):
        evs = []
        for q in range(4):
            evs.append(p.dma("sync", lambda e, tg=tg, q=q: e.dma_start(
                out=at_tg[:].rearrange("p (kt t) -> p kt t", kt=32)[:, q * 8:(q + 1) * 8, :],
                in_=AT.rearrange("(kt p) t -> p kt t", p=128)[:, q * 8:(q + 1) * 8, tg * 512:(tg + 1) * 512]),
                deps=deps, slot=f"at{q}"))
        return evs
    io["load_at"] = _load_at
    p = Prog(nc)
    emit_l1_tail(p, io)
    p.build()
    return nc


PERM128 = np.concatenate([np.arange(0, 128, 2), np.arange(1, 128, 2)])
ONES32 = np.ones((1, 32), dtype=NP_BF16)


def _nsa_consts():
    inv = 1.0 / 10000.0 ** (np.arange(0, 128, 2, dtype=np.float64) / 128)
    ang = np.arange(LSEQ, dtype=np.float64)[:, None] * inv[None, :]
    cos = np.cos(ang).astype(np.float32).T
    sin = np.sin(ang).astype(np.float32).T
    cosf = np.concatenate([cos, cos], 0).astype(NP_BF16)
    sinf = np.concatenate([-sin, sin], 0).astype(NP_BF16)
    ck_seq = np.concatenate([[-1], (np.arange(256) + 1) * 16 - 1])
    cmneg = np.where(ck_seq[None, :] < np.arange(LSEQ)[:, None], 0.0, -1e30).astype(np.float32)
    r = np.arange(128)[:, None]
    jj = np.arange(256)[None, :]
    baneg = np.where((jj >= r + 64) & (jj <= r + 128), 0.0, -1e30).astype(np.float32)
    jz = np.arange(128)[None, :]
    bzneg = np.where((jz <= r) & (jz >= r - 64), 0.0, -1e30).astype(np.float32)
    diagf = (((r // 16) == (jz // 16)) & (jz <= r)).astype(np.float32)
    return cosf, sinf, cmneg, baneg, bzneg, diagf


def kernel(**ins):
    C = np.ascontiguousarray
    x = np.asarray(ins["L0_attn_hidden_states"]).reshape(TOK, DM)
    qk_perm = np.arange(6144)
    for h in range(40):
        qk_perm[h * 128:(h + 1) * 128] = h * 128 + PERM128
    kw1p = ins["L1_attn_k_w1"].reshape(2048, 16, 128)[:, :, PERM128].reshape(2048, 2048)
    cast_src = {
        "woT": C(ins["L0_attn_out_proj_w"].T),
        "woT2": C(ins["L1_attn_w_o"].T),
        "kern": np.asarray(ins["L1_ffn_kernel"]),
        "wqT": C(ins["L1_attn_w_qkv"][qk_perm].T),
        "utT": C(ins["L0_ffn_U_tail"].T),
        "kw1T": C(kw1p.T),
        "vw1T": C(ins["L1_attn_v_w1"].T),
    }
    nc = build_l0_prep()
    in_maps = []
    for c in range(NC):
        d = {"x": x[c * TSH:(c + 1) * TSH], "ident": IDENT_BF16}
        for nm, rows, cols in CAST_SPECS:
            d["ci_" + nm] = C(cast_src[nm][c * rows:(c + 1) * rows])
        in_maps.append(d)
    res = _run(nc, in_maps)
    xnT_full = np.concatenate([np.asarray(r["xnT"]) for r in res], axis=1)
    wbf = {nm: np.concatenate([np.asarray(r["co_" + nm]) for r in res], axis=0)
           for nm, _, _ in CAST_SPECS}
    W = ins["L0_attn_in_proj_w"]
    nc = build_l0_inproj()
    res = _run(nc, [{
        "xnT": xnT_full,
        "w_x": C(W[:, c * DSH:(c + 1) * DSH]),
        "w_z": C(W[:, DIN + c * DSH: DIN + (c + 1) * DSH]),
        "cw": C(ins["L0_attn_conv_w"][c * DSH:(c + 1) * DSH]),
        "cb": C(ins["L0_attn_conv_b"][c * DSH:(c + 1) * DSH]),
        "wxpT": C(ins["L0_attn_x_proj_w"].T[c * DSH:(c + 1) * DSH]),
    } for c in range(NC)])
    xc_sh = [np.asarray(r["xc"]) for r in res]
    zs_sh = [np.asarray(r["zs"]) for r in res]
    ssp_all = np.stack([np.asarray(r["ssp"]) for r in res])
    lnw = np.concatenate([ins["L0_attn_dt_ln_w"], ins["L0_attn_B_ln_w"], ins["L0_attn_C_ln_w"]])[None, :].astype(np.float32)
    dtWT = C(ins["L0_attn_dt_proj_w"].T)
    nc = build_l0_ssmmid()
    res = _run(nc, [{
        "pst": C(ssp_all[:, :, c * TSH:(c + 1) * TSH]), "lnw": lnw, "dtWT": dtWT,
        "dtb": ins["L0_attn_dt_bias"], "ident": IDENT_BF16, "identf": IDENT_F32,
    } for c in range(NC)])
    dt_full = np.concatenate([np.asarray(r["dt_o"]) for r in res], axis=1)
    Bn_full = np.concatenate([np.asarray(r["Bn_o"]) for r in res], axis=1)
    Cn_full = np.concatenate([np.asarray(r["Cn_o"]) for r in res], axis=1)
    ISH = DM // NC
    nc = build_l0_ffnbuild()
    res = _run(nc, [{
        "vh": C(ins["L0_ffn_Vh_tail"][:, c * ISH:(c + 1) * ISH]), "st": ins["L0_ffn_S_tail"],
        "utT": wbf["utT"],
        "vtop": C(ins["L0_ffn_Vh_top"][:, c * ISH:(c + 1) * ISH]), "utopT": C(ins["L0_ffn_U_top"].T),
        "stop_": ins["L0_ffn_S_top"][:, None], "alpha": ins["L0_ffn_alpha"][:, None],
        "beta": ins["L0_ffn_beta"][:, None],
        "vhr": C(ins["L0_ffn_Vh_tail_r"][:, c * ISH:(c + 1) * ISH]), "urT": C(ins["L0_ffn_U_tail_r"].T),
        "srd": np.stack([ins["L0_ffn_S_tail_r"], ins["L0_ffn_D"]], axis=1), "rmat": ins["L0_ffn_R"],
    } for c in range(NC)])
    weT_full = np.concatenate([np.asarray(r["weT"]) for r in res], axis=0)
    nc = build_l0_scan()
    res = _run(nc, [{
        "xc": xc_sh[c], "zsd": zs_sh[c], "dtd": C(dt_full[c * DSH:(c + 1) * DSH]),
        "alog": C(ins["L0_attn_A_log"][c * DSH:(c + 1) * DSH]),
        "dpar": C(ins["L0_attn_D_param"][c * DSH:(c + 1) * DSH]),
        "Bn": Bn_full, "Cn": Cn_full,
    } for c in range(NC)])
    yg_full = np.concatenate([np.asarray(r["yg"]) for r in res], axis=0)
    nc = build_l0_tail()
    res = _run(nc, [{
        "ygd": C(yg_full[:, c * TSH:(c + 1) * TSH]), "woT": wbf["woT"], "weT": weT_full,
        "xnT": C(xnT_full[:, c * TSH:(c + 1) * TSH]), "x": x[c * TSH:(c + 1) * TSH],
        "fbias": ins["L0_ffn_bias"][None, :], "ident": IDENT_BF16,
    } for c in range(NC)])
    xnew_sh = [np.asarray(r["xnew"]) for r in res]
    xn1T_full = np.concatenate([np.asarray(r["xn1T"]) for r in res], axis=1)
    wgT = C(ins["L1_attn_w_gate"].T)
    nc = build_l1_qkv()
    res = _run(nc, [{
        "xn1T": C(xn1T_full[:, c * TSH:(c + 1) * TSH]), "wqT": wbf["wqT"], "wgT": wgT,
        "bg": ins["L1_attn_b_gate"][:, None], "identf": IDENT_F32,
    } for c in range(NC)])
    qkvT_full = np.concatenate([np.asarray(r["qkv"]) for r in res], axis=1)
    gates_full = np.concatenate([np.asarray(r["gatesT"]) for r in res], axis=0)
    COSF, SINF, CMNEG, BANEG, BZNEG, DIAGF = _nsa_consts()
    kw2p = ins["L1_attn_k_w2"][PERM128]
    wcom = {
        "kw1T": wbf["kw1T"], "kb1": ins["L1_attn_k_b1"][:, None],
        "kw2T": C(kw2p.T), "kb2": ins["L1_attn_k_b2"][PERM128][:, None],
        "vw1T": wbf["vw1T"], "vb1": ins["L1_attn_v_b1"][:, None],
        "vw2T": C(ins["L1_attn_v_w2"].T), "vb2": ins["L1_attn_v_b2"][:, None],
        "cosf": COSF, "sinf": SINF, "cmneg": CMNEG, "baneg": BANEG, "bzneg": BZNEG,
        "diagf": DIAGF, "identb": IDENT_BF16,
    }
    in_maps = []
    for c in range(NC):
        b = c // 4
        kvhs = [2 * (c % 4), 2 * (c % 4) + 1]
        cols = slice(b * LSEQ, (b + 1) * LSEQ)
        q_in = np.zeros((2, 512, LSEQ), dtype=NP_BF16)
        k_in = np.zeros((2, 128, LSEQ), dtype=NP_BF16)
        v_in = np.zeros((2, 128, LSEQ), dtype=NP_BF16)
        kpos = np.zeros((2, 128, 16), np.float32)
        vpos = np.zeros((2, 128, 16), np.float32)
        memk = np.zeros((2, 128, 1), np.float32)
        memv = np.zeros((2, 128, 1), np.float32)
        for u, kvh in enumerate(kvhs):
            for g in range(4):
                hq = kvh * 4 + g
                q_in[u, g * 128:(g + 1) * 128] = qkvT_full[hq * 128:(hq + 1) * 128, cols]
            k_in[u] = qkvT_full[4096 + kvh * 128: 4096 + (kvh + 1) * 128, cols]
            v_in[u] = qkvT_full[5120 + kvh * 128: 5120 + (kvh + 1) * 128, cols]
            kpos[u] = ins["L1_attn_k_pos"][kvh].T[PERM128]
            vpos[u] = ins["L1_attn_v_pos"][kvh].T
            memk[u] = ins["L1_attn_mem_kv"][0, kvh].T[PERM128]
            memv[u] = ins["L1_attn_mem_kv"][1, kvh].T
        colperm = []
        for u2, kvh2 in enumerate(kvhs):
            for g2 in range(4):
                for c2 in range(3):
                    colperm.append((kvh2 * 4 + g2) * 3 + c2)
        g24 = gates_full[cols][:, colperm]
        gtb = C(g24.reshape(32, 128, 24).transpose(1, 0, 2).reshape(128, 768))
        d = {"q_in": q_in, "k_in": k_in, "v_in": v_in, "gtb": gtb,
             "kpos": kpos, "vpos": vpos, "memk": memk, "memv": memv}
        d.update(wcom)
        in_maps.append(d)
    nc = build_l1_nsa2()
    res = _run(nc, in_maps)
    A_tok = np.zeros((TOK, DM), dtype=NP_BF16)
    for c in range(NC):
        b = c // 4
        kvhs = [2 * (c % 4), 2 * (c % 4) + 1]
        a = np.asarray(res[c]["A_out"])
        for u, kvh in enumerate(kvhs):
            for g in range(4):
                hq = kvh * 4 + g
                A_tok[b * LSEQ:(b + 1) * LSEQ, hq * 128:(hq + 1) * 128] = a[:, (u * 4 + g) * 128:(u * 4 + g + 1) * 128]
    AT_full = C(A_tok.T)
    loraBT = C(np.transpose(ins["L1_ffn_lora_B"], (0, 2, 1)))
    shAT = C(ins["L1_ffn_lora_A_shared"].T)
    nc = build_l1_tail()
    res = _run(nc, [{
        "AT": C(AT_full[:, c * TSH:(c + 1) * TSH]), "woT2": wbf["woT2"],
        "xn1T": C(xn1T_full[:, c * TSH:(c + 1) * TSH]), "kern": wbf["kern"],
        "grow": ins["L1_ffn_gate"][c][None, :], "loraA": ins["L1_ffn_lora_A"],
        "loraBT": loraBT, "shAT": shAT, "shBT": C(ins["L1_ffn_lora_B_shared"].T),
        "fbias1": ins["L1_ffn_bias"][None, :], "xnew": xnew_sh[c],
        "ident": IDENT_BF16, "onesb": ONES32,
    } for c in range(NC)])
    out = np.concatenate([np.asarray(r["outf"]) for r in res], axis=0)
    return out.reshape(2, LSEQ, DM).astype(np.float32)
